# version 04
# baseline (speedup 1.0000x reference)
import numpy as np
import ml_dtypes
import bass_rust
import concourse.bass as bass
import concourse.mybir as mybir
from concourse.bass_utils import run_bass_kernel_spmd
from concourse.bass_types import DynSlice
from concourse.bass_core_rust import IndirectOffsetOnAxis
from contextlib import ExitStack

F32 = mybir.dt.float32
BF16 = mybir.dt.bfloat16
I16 = mybir.dt.int16
I32 = mybir.dt.int32
U8 = mybir.dt.uint8
U32 = mybir.dt.uint32
AF = mybir.ActivationFunctionType
ALU = mybir.AluOpType
AX = mybir.AxisListType

NCORES = 8
TLOC = 4096
D = 1024
SEQ = 2048
CHUNK = 512
NCHUNK = TLOC // CHUNK

_DMA_OPS = ("dma_start", "dma_start_transpose", "indirect_dma_start")
_PASS = ("wait_ge", "value_load", "load_library", "register", "reg_load",
         "alloc_register", "free_register", "nop")


class Ctr:
    def __init__(self, sem):
        self.sem, self.n = sem, 0


class Eng:

    def __init__(self, raw, c):
        self.raw, self.c = raw, c

    def __getattr__(self, nm):
        fn = getattr(self.raw, nm)
        if nm in _PASS:
            return fn
        if nm in _DMA_OPS:
            def dcall(*a, sem=None, **k):
                inst = fn(*a, **k)
                assert sem is not None, f"dma {nm} needs sem="
                sem.n += 16
                return inst.then_inc(sem.sem, 16)
            return dcall

        def ccall(*a, **k):
            inst = fn(*a, **k)
            self.c.n += 1
            return inst.then_inc(self.c.sem, 1)
        return ccall

    def fence(self):
        self.raw.wait_ge(self.c.sem, self.c.n)

    def wait(self, ctr, n=None):
        self.raw.wait_ge(ctr.sem, ctr.n if n is None else n)


class P:

    def __init__(self, nc, ctx, blk):
        self.nc, self.ctx, self.blk = nc, ctx, blk
        self.ctrs = {}
        self.shared = {}
        self.scope = ctx
        for nm in ("v", "a", "t", "g"):
            self.ctr(nm)

    def ctr(self, name):
        if name not in self.ctrs:
            h = self.ctx.enter_context(self.nc.semaphore(name))
            self.ctrs[name] = Ctr(h)
        return self.ctrs[name]

    def sbuf(self, name, shape, dt):
        return self.scope.enter_context(self.nc.sbuf_tensor(name, shape, dt))

    def psum(self, name, shape, dt=F32):
        return self.scope.enter_context(self.nc.psum_tensor(name, shape, dt))

    def sync(self, fn):
        self.blk.sync(lambda e: fn(Eng(e, self.ctr("sy"))))

    def vector(self, fn):
        self.blk.vector(lambda e: fn(Eng(e, self.ctr("v"))))

    def scalar(self, fn):
        self.blk.scalar(lambda e: fn(Eng(e, self.ctr("a"))))

    def tensor(self, fn):
        self.blk.tensor(lambda e: fn(Eng(e, self.ctr("t"))))

    def gpsimd(self, fn):
        self.blk.gpsimd(lambda e: fn(Eng(e, self.ctr("g"))))

    def barrier(self, engines="vatg"):
        vals = [(c, c.n) for c in self.ctrs.values()]
        emap = {"v": self.vector, "a": self.scalar, "t": self.tensor,
                "g": self.gpsimd, "s": self.sync}
        for ch in engines:
            def mk(e, vals=tuple(vals)):
                for ctr, n in vals:
                    if n:
                        e.wait(ctr, n)
            emap[ch](mk)


MM_T, YEV, A_SQ, V_SQRD, SLAB_T, PS_EV = {}, {}, {}, {}, {}, {}
V_UBF, V_CONVD, V_YD, V_RES, V_RESNG, MM_OUT = {}, {}, {}, {}, {}, {}


def _reset_markers():
    for dd in (MM_T, YEV, A_SQ, V_SQRD, SLAB_T, PS_EV,
               V_UBF, V_CONVD, V_YD, V_RES, V_RESNG, MM_OUT):
        dd.clear()




def emit_l0_attn(p, io, V):
    nc = p.nc
    _p1s = ExitStack()
    _oldscope = p.scope
    p.scope = _p1s
    x_in = io["x_in"]
    x1_hbm, xr_hbm, stats_hbm = io["x1_hbm"], io["xr_hbm"], io["stats_hbm"]
    inw_h, parw_h = io["w_inw_bf"], io["w_parw_bf"]

    outw = p.sbuf("p1_outw", [128, 8, D], BF16)
    wsl = [p.sbuf(f"p1_wsl{i}", [128, 8, 768], BF16) for i in range(2)]
    normw_f = p.sbuf("p1_normwf", [128, 8], F32)
    inb = p.sbuf("p1_inb", [128, 16], F32)
    convw = p.sbuf("p1_convw", [128, 8, 4], F32)
    convb = p.sbuf("p1_convb", [128, 8], F32)
    parb = p.sbuf("p1_parb", [128, 24], F32)
    skipw = p.sbuf("p1_skipw", [128, 8], F32)
    outb_row = p.sbuf("p1_outb", [1, D], BF16)
    ones_col = p.sbuf("p1_ones", [1, 128], BF16)

    v, a, t, g = p.ctr("v"), p.ctr("a"), p.ctr("t"), p.ctr("g")
    d_ld = p.ctr("d_ld")
    d_wv = p.ctr("d_wv")
    d_w = [p.ctr("d_w0"), p.ctr("d_w1")]
    d_tr = [p.ctr("d_tr0"), p.ctr("d_tr1")]
    d_xc = [p.ctr("d_xc0"), p.ctr("d_xc1")]
    d_st = p.ctr("d_st")
    d_sx1 = [p.ctr("d_sx10"), p.ctr("d_sx11")]
    d_g = p.ctr("d_g")

    def loads(e):
        e.dma_start(out=inb[:], in_=V["l0_inb_fm"], sem=d_wv)
        e.dma_start(out=convw[:], in_=V["l0_convw_fm"], sem=d_wv)
        e.dma_start(out=convb[:], in_=V["l0_convb_fm"], sem=d_wv)
        e.dma_start(out=parb[:], in_=V["l0_parb_fm"], sem=d_wv)
        e.dma_start(out=skipw[:], in_=V["l0_skip_fm"], sem=d_wv)
        e.dma_start(out=normw_f[:], in_=V["l0_normw_fm"], sem=d_wv)
    p.sync(loads)
    LD_W = d_wv.n

    def gload(e):
        e.dma_start(out=outb_row[:], in_=V["l0_out_b"], sem=d_g)
        e.dma_start(out=outw[:], in_=V["l0_out_w"].rearrange("(j q) e -> q j e", q=128), sem=d_g)
    p.gpsimd(gload)
    GD_OUTB = d_g.n

    with ExitStack() as actx:
        def asb(name, shape, dt):
            return actx.enter_context(nc.sbuf_tensor(name, shape, dt))
        xs = asb("p1_xs", [128, 8, D], F32)
        xrs = asb("p1_xrs", [128, 8, D], BF16)
        sqs = asb("p1_sqs", [128, 8, D], F32)
        bnag = asb("p1_bnag", [128, 8, 2], F32)
        stA = asb("p1_stA", [128, 8, 2], F32)
        z1 = asb("p1_z1", [128, 8], F32)
        z2 = asb("p1_z2", [128, 8], F32)
        z3 = asb("p1_z3", [128, 8], F32)
        s1 = asb("p1_s1", [128, 8], F32)
        s3 = asb("p1_s3", [128, 8], F32)
        Amul = asb("p1_Am", [128, 8], F32)
        rstd = asb("p1_rs", [128, 8], F32)

        def vinit(e):
            e.memset(ones_col[:], 1.0)
        p.vector(vinit)

        for grp in range(4):
            def ldx(e, grp=grp):
                if grp > 0:
                    e.wait(v)
                e.dma_start(out=xs[:], in_=x_in.rearrange("(q s) d -> q s d", q=128)[:, grp * 8:(grp + 1) * 8, :], sem=d_ld)
            p.sync(ldx)

            def v1(e, grp=grp):
                e.wait(d_ld)
                for k in range(8):
                    e.reduce_sum(bnag[:, k, 0:1], xs[:, k, :], axis=AX.X)
                e.fence()
            p.vector(v1)

            def v1b(e, grp=grp):
                for k in range(8):
                    e.tensor_tensor(sqs[:, k, :], xs[:, k, :], xs[:, k, :], ALU.mult)
                    e.fence()
                    e.reduce_sum(bnag[:, k, 1:2], sqs[:, k, :], axis=AX.X)
                e.fence()
                V_SQRD[(grp, 7)] = p.ctr("v").n
                e.tensor_scalar_mul(bnag[:, :, 0], bnag[:, :, 0], 1.0 / D)
                e.fence()
                e.tensor_scalar_mul(bnag[:, :, 1], bnag[:, :, 1], 1.0 / D)
                e.fence()
                e.scalar_tensor_tensor(z1[:], bnag[:, :, 0], -1.0, bnag[:, :, 0], ALU.mult, ALU.mult)
                e.fence()
                e.tensor_tensor(bnag[:, :, 1], bnag[:, :, 1], z1[:], ALU.add)
                e.fence()
                e.tensor_scalar_add(z1[:], bnag[:, :, 1], 1e-5)
                e.fence()
                e.reciprocal(z2[:], z1[:])
                e.fence()
                e.tensor_tensor(z2[:], bnag[:, :, 1], z2[:], ALU.mult)
                e.fence()
                e.tensor_scalar_add(z2[:], z2[:], 1e-6)
                e.fence()
                e.tensor_tensor(z3[:], z1[:], z2[:], ALU.mult)
                e.fence()
            p.vector(v1b)

            def a1(e, grp=grp):
                e.wait(v)
                e.activation(s3[:], z3[:], AF.Sqrt)
                e.activation(s1[:], z1[:], AF.Sqrt)
            p.scalar(a1)

            def v2(e, grp=grp):
                e.wait(a)
                e.wait(d_st)
                e.reciprocal(Amul[:], s3[:])
                e.reciprocal(rstd[:], s1[:])
                e.fence()
                e.tensor_scalar_add(stA[:, :, 0], rstd[:], 1.0)
                e.scalar_tensor_tensor(stA[:, :, 1], bnag[:, :, 0], -1.0, rstd[:], ALU.mult, ALU.mult)
                for k in range(8):
                    e.tensor_scalar(xrs[:, k, :], xs[:, k, :], bnag[:, k, 0:1], Amul[:, k:k + 1],
                                    ALU.subtract, ALU.mult)
                e.fence()
            p.vector(v2)

            def st1(e, grp=grp):
                e.wait(v)
                e.dma_start(out=xr_hbm.rearrange("(q s) d -> q s d", q=128)[:, grp * 8:(grp + 1) * 8, :], in_=xrs[:], sem=d_st)
                e.dma_start(out=stats_hbm.rearrange("(q s) c -> q s c", q=128)[:, grp * 8:(grp + 1) * 8, :], in_=stA[:], sem=d_st)
            p.scalar(st1)
    ST_PASSA = d_st.n
    p.barrier("vats")

    xr_fm = [p.sbuf(f"p1_xrfm{i}", [128, 8, CHUNK], BF16) for i in range(2)]
    u_pad = p.sbuf("p1_upad", [128, 8, CHUNK + 3], F32)
    tail = p.sbuf("p1_tail", [128, 8, 3], F32)
    uc = p.sbuf("p1_uc", [128, 8, CHUNK], F32)
    u_bf = p.sbuf("p1_ubf", [128, 8, CHUNK], BF16)
    gs_bf = p.sbuf("p1_gsbf", [128, 8, CHUNK], BF16)
    a_t = p.sbuf("p1_a", [128, 8, CHUNK], F32)
    bu_t = p.sbuf("p1_bu", [128, 8, CHUNK], F32)
    c_t = p.sbuf("p1_c", [128, 8, CHUNK], F32)
    stn = p.sbuf("p1_stn", [128, 8, CHUNK], F32)
    y_bf = p.sbuf("p1_ybf", [128, 8, CHUNK], BF16)
    carry = p.sbuf("p1_carry", [128, 8], F32)
    xc = [p.sbuf(f"p1_xc{i}", [128, D], F32) for i in range(2)]
    x1c = [p.sbuf(f"p1_x1c{i}", [128, D], F32) for i in range(2)]
    ab_t = p.sbuf("p1_ab", [128, 4, 2], F32)
    ps_a = [p.psum(f"p1_psa{i}", [128, 512]) for i in range(6)]
    ps_o = [p.psum(f"p1_pso{i}", [128, 512]) for i in range(2)]

    def vinit2(e):
        e.memset(carry[:], 0.0)
        e.memset(tail[:], 0.0)
        e.fence()
    p.vector(vinit2)

    SLABS = [("in", 0, 6), ("in", 6, 12), ("in", 12, 16),
             ("par", 0, 6), ("par", 6, 12), ("par", 12, 18), ("par", 18, 24)]

    for c in range(NCHUNK):
        t0 = c * CHUNK
        b = c % 2
        seq_start = (c % 4 == 0)

        def ldtr(e, c=c, t0=t0, b=b):
            e.wait(d_st, ST_PASSA)
            if c >= 2:
                e.wait(t)
            for j in range(8):
                e.dma_start_transpose(xr_fm[b][:, j, :], xr_hbm[t0:t0 + CHUNK, j * 128:(j + 1) * 128], sem=d_tr[b])
        p.sync(ldtr)

        def ldab(e, c=c, t0=t0):
            if c >= 1:
                e.wait(v)
            e.dma_start(out=ab_t[:], in_=stats_hbm[t0:t0 + CHUNK, :].rearrange("(k q) c -> q k c", q=128), sem=d_ld)
        p.sync(ldab)
        LD_AB = d_ld.n

        def vnw(e, c=c, b=b):
            e.wait(d_tr[b])
            e.wait(d_wv, LD_W)
            for j in range(8):
                e.tensor_scalar_mul(xr_fm[b][:, j, :], xr_fm[b][:, j, :], normw_f[:, j:j + 1])
            e.fence()
        p.vector(vnw)
        V_NW = v.n

        evac_plan = []
        for si, (kind, pf0, pf1) in enumerate(SLABS):
            sb = si % 2

            def ldsl(e, si=si, sb=sb, kind=kind, pf0=pf0, pf1=pf1, c=c):
                if si >= 2:
                    e.wait(t, SLAB_T[(c, si - 2)])
                elif c > 0:
                    e.wait(t, SLAB_T[(c - 1, len(SLABS) - 2 + si)])
                src = inw_h if kind == "in" else parw_h
                e.dma_start(out=wsl[sb][:, :, 0:(pf1 - pf0) * 128],
                            in_=src.rearrange("(j q) e -> q j e", q=128)[:, :, pf0 * 128:pf1 * 128], sem=d_w[sb])
            p.sync(ldsl)

            def mmsl(e, si=si, sb=sb, kind=kind, pf0=pf0, pf1=pf1, c=c, b=b):
                e.wait(d_w[sb])
                if kind == "in":
                    e.wait(v, V_NW)
                else:
                    e.wait(v, V_UBF[c])
                for pf in range(pf0, pf1):
                    gi = (pf if kind == "in" else 16 + pf)
                    pi = gi % 6
                    if gi >= 6:
                        e.wait(a, PS_EV[(c, gi - 6)])
                    elif c > 0:
                        e.wait(a, PS_EV[(c - 1, 34 + gi)])
                    for j in range(8):
                        e.matmul(ps_a[pi][:], wsl[sb][:, j, (pf - pf0) * 128:(pf - pf0 + 1) * 128],
                                 xr_fm[b][:, j, :] if kind == "in" else u_bf[:, j, :],
                                 start=(j == 0), stop=(j == 7))
                    MM_T[(c, gi)] = t.n
                SLAB_T[(c, si)] = t.n
            p.tensor(mmsl)

            def evsl(e, si=si, kind=kind, pf0=pf0, pf1=pf1, c=c):
                for pf in range(pf0, pf1):
                    gi = (pf if kind == "in" else 16 + pf)
                    pi = gi % 6
                    e.wait(t, MM_T[(c, gi)])
                    if kind == "in":
                        if pf < 8:
                            if c >= 1 and pf == 0:
                                e.wait(v, V_CONVD[c - 1])
                            e.activation(u_pad[:, pf, 3:], ps_a[pi][:], AF.Identity, bias=inb[:, pf:pf + 1])
                        else:
                            if c >= 1 and pf == 8:
                                e.wait(v, V_YD[c - 1])
                            e.activation(gs_bf[:, pf - 8, :], ps_a[pi][:], AF.Silu, bias=inb[:, pf:pf + 1])
                    else:
                        j = pf % 8
                        if pf < 8:
                            if c >= 1 and pf == 0:
                                e.wait(v, V_YD[c - 1])
                            e.activation(a_t[:, j, :], ps_a[pi][:], AF.Sigmoid, bias=parb[:, pf:pf + 1])
                        elif pf < 16:
                            e.activation(bu_t[:, j, :], ps_a[pi][:], AF.Tanh, bias=parb[:, pf:pf + 1])
                        else:
                            e.activation(c_t[:, j, :], ps_a[pi][:], AF.Sigmoid, bias=parb[:, pf:pf + 1])
                    PS_EV[(c, gi)] = a.n
            p.scalar(evsl)

            if si == 2:
                def convv(e, c=c, seq_start=seq_start):
                    e.wait(a)
                    if seq_start:
                        e.memset(u_pad[:, :, 0:3], 0.0)
                    else:
                        e.tensor_copy(u_pad[:, :, 0:3], tail[:])
                    e.fence()
                    for j in range(8):
                        e.tensor_scalar(uc[:, j, :], u_pad[:, j, 0:CHUNK], convw[:, j, 0:1], convb[:, j:j + 1], ALU.mult, ALU.add)
                    e.fence()
                    for k in range(1, 4):
                        for j in range(8):
                            e.scalar_tensor_tensor(uc[:, j, :], u_pad[:, j, k:k + CHUNK], convw[:, j, k:k + 1], uc[:, j, :], ALU.mult, ALU.add)
                        e.fence()
                    e.tensor_copy(tail[:], u_pad[:, :, CHUNK:CHUNK + 3])
                    e.fence()
                p.vector(convv)
                V_CONVD[c] = p.ctr("v").n

                def silua(e, c=c):
                    e.wait(v)
                    e.activation(uc[:], uc[:], AF.Silu)
                p.scalar(silua)

                def ubfv(e, c=c):
                    e.wait(a)
                    e.tensor_copy(u_bf[:], uc[:])
                    e.fence()
                p.vector(ubfv)
                V_UBF[c] = v.n

        def scanv(e, c=c, seq_start=seq_start):
            e.wait(a)
            if seq_start:
                e.memset(carry[:], 0.0)
                e.fence()
            for j in range(8):
                e.tensor_tensor(bu_t[:, j, :], bu_t[:, j, :], uc[:, j, :], ALU.mult)
            e.fence()
            for j in range(8):
                e.tensor_tensor_scan(stn[:, j, :], a_t[:, j, :], bu_t[:, j, :], carry[:, j:j + 1], ALU.mult, ALU.add)
                e.fence()
                e.tensor_copy(carry[:, j:j + 1], stn[:, j, CHUNK - 1:CHUNK])
                e.fence()
            for j in range(8):
                e.tensor_tensor(c_t[:, j, :], c_t[:, j, :], stn[:, j, :], ALU.mult)
            e.fence()
            for j in range(8):
                e.scalar_tensor_tensor(stn[:, j, :], uc[:, j, :], skipw[:, j:j + 1], c_t[:, j, :], ALU.mult, ALU.add)
            e.fence()
            for j in range(8):
                e.tensor_tensor(y_bf[:, j, :], stn[:, j, :], gs_bf[:, j, :], ALU.mult)
            e.fence()
        p.vector(scanv)
        V_YD[c] = v.n

        for tt_i in range(4):
            bb = tt_i % 2

            def ldxc(e, c=c, t0=t0, tt_i=tt_i, bb=bb):
                if c * 4 + tt_i >= 2:
                    e.wait(v, V_RES[(c * 4 + tt_i) - 2])
                e.dma_start(out=xc[bb][:], in_=x_in[t0 + tt_i * 128:t0 + (tt_i + 1) * 128, :], sem=d_xc[bb])
            p.sync(ldxc)

            def mmout(e, c=c, tt_i=tt_i):
                e.wait(v, V_YD[c])
                if c == 0 and tt_i == 0:
                    e.wait(d_g, GD_OUTB)
                for ng in range(2):
                    gi = tt_i * 2 + ng
                    if c * 8 + gi >= 2:
                        e.wait(v, V_RESNG[(c * 8 + gi) - 2])
                    for j in range(8):
                        e.matmul(ps_o[gi % 2][:], y_bf[:, j, tt_i * 128:(tt_i + 1) * 128],
                                 outw[:, j, ng * 512:(ng + 1) * 512], start=(j == 0), stop=False)
                    e.matmul(ps_o[gi % 2][:], ones_col[:], outb_row[0:1, ng * 512:(ng + 1) * 512],
                             start=False, stop=True)
                    MM_OUT[(c, gi)] = t.n
            p.tensor(mmout)

            def resv(e, c=c, tt_i=tt_i, bb=bb):
                e.wait(d_ld, LD_AB)
                e.wait(d_xc[bb])
                e.wait(d_sx1[bb])
                for ng in range(2):
                    gi = tt_i * 2 + ng
                    e.wait(t, MM_OUT[(c, gi)])
                    sl = slice(ng * 512, (ng + 1) * 512)
                    e.tensor_scalar(x1c[bb][:, sl], xc[bb][:, sl], ab_t[:, tt_i, 0:1], ab_t[:, tt_i, 1:2], ALU.mult, ALU.add)
                    e.fence()
                    e.tensor_tensor(x1c[bb][:, sl], x1c[bb][:, sl], ps_o[gi % 2][:], ALU.add)
                    e.fence()
                    V_RESNG[c * 8 + gi] = v.n
                V_RES[c * 4 + tt_i] = v.n
            p.vector(resv)

            def stx1(e, c=c, t0=t0, tt_i=tt_i, bb=bb):
                e.wait(v, V_RES[c * 4 + tt_i])
                e.dma_start(out=x1_hbm[t0 + tt_i * 128:t0 + (tt_i + 1) * 128, :], in_=x1c[bb][:], sem=d_sx1[bb])
            p.scalar(stx1)

    p.scope = _oldscope
    _p1s.close()




def emit_ln_plain(p, src_hbm, xn_hbm, pref, gates=()):
    nc = p.nc
    v, a = p.ctr("v"), p.ctr("a")
    d_ld = p.ctr("d_ld")
    d_st = p.ctr("d_st")
    with ExitStack() as sc:
        old = p.scope
        p.scope = sc
        xs = p.sbuf(pref + "_xs", [128, 8, D], F32)
        xrs = p.sbuf(pref + "_xrs", [128, 8, D], BF16)
        sqs = p.sbuf(pref + "_sqs", [128, 8, D], F32)
        mv = p.sbuf(pref + "_mv", [128, 8, 2], F32)
        z1 = p.sbuf(pref + "_z1", [128, 8], F32)
        s1 = p.sbuf(pref + "_s1", [128, 8], F32)
        Am = p.sbuf(pref + "_Am", [128, 8], F32)
        M = {}
        for grp in range(4):
            def ldx(e, grp=grp):
                for gnm in gates:
                    e.wait(p.ctr(gnm))
                if grp > 0:
                    e.wait(v)
                e.dma_start(out=xs[:], in_=src_hbm.rearrange("(q s) d -> q s d", q=128)[:, grp * 8:(grp + 1) * 8, :], sem=d_ld)
            p.sync(ldx)

            def v1(e, grp=grp):
                e.wait(d_ld)
                if grp > 0:
                    e.wait(d_st)
                for k in range(8):
                    e.reduce_sum(mv[:, k, 0:1], xs[:, k, :], axis=AX.X)
                e.fence()
            p.vector(v1)

            def asq(e, grp=grp):
                e.wait(d_ld)
                e.wait(v)
                for k in range(8):
                    e.tensor_tensor(sqs[:, k, :], xs[:, k, :], xs[:, k, :], ALU.mult)
                e.fence()
                M[("sq", grp)] = p.ctr("v").n
            p.vector(asq)

            def v2(e, grp=grp):
                e.wait(v, M[("sq", grp)])
                for k in range(8):
                    e.reduce_sum(mv[:, k, 1:2], sqs[:, k, :], axis=AX.X)
                e.fence()
                e.tensor_scalar_mul(mv[:, :, 0], mv[:, :, 0], 1.0 / D)
                e.fence()
                e.tensor_scalar_mul(mv[:, :, 1], mv[:, :, 1], 1.0 / D)
                e.fence()
                e.scalar_tensor_tensor(z1[:], mv[:, :, 0], -1.0, mv[:, :, 0], ALU.mult, ALU.mult)
                e.fence()
                e.tensor_tensor(z1[:], mv[:, :, 1], z1[:], ALU.add)
                e.fence()
                e.tensor_scalar_add(z1[:], z1[:], 1e-5)
                e.fence()
            p.vector(v2)

            def a2(e, grp=grp):
                e.wait(v)
                e.activation(s1[:], z1[:], AF.Sqrt)
            p.scalar(a2)

            def v3(e, grp=grp):
                e.wait(a)
                e.reciprocal(Am[:], s1[:])
                e.fence()
                for k in range(8):
                    e.tensor_scalar(xrs[:, k, :], xs[:, k, :], mv[:, k, 0:1], Am[:, k:k + 1],
                                    ALU.subtract, ALU.mult)
                e.fence()
            p.vector(v3)

            def st1(e, grp=grp):
                e.wait(v)
                e.dma_start(out=xn_hbm.rearrange("(q s) d -> q s d", q=128)[:, grp * 8:(grp + 1) * 8, :], in_=xrs[:], sem=d_st)
            p.scalar(st1)
        p.scope = old
    return d_st.n


N_ITER = 28
CPAY = 1024
PAYR = 8 * TLOC + 128
LCAP = 1024
PAYR4 = 64 * LCAP + 128
BIG = 1.0e6


def emit_l0_ffn(p, io, V, dbg=False):
    nc = p.nc
    v, a, t, g = p.ctr("v"), p.ctr("a"), p.ctr("t"), p.ctr("g")
    d_ld = p.ctr("d_ld")
    d_st = p.ctr("d_st")
    d_tr = [p.ctr("d_tr0"), p.ctr("d_tr1")]
    d_gg = p.ctr("p2_dgg")
    d_z = p.ctr("p2_dz")
    d_w = [p.ctr("d_w0"), p.ctr("d_w1")]
    d_y = [p.ctr("p2_dy0"), p.ctr("p2_dy1")]
    d_rp = p.ctr("p2_drp")

    x1_hbm, x2_hbm, xn2_hbm = io["x1_hbm"], io["x2_hbm"], io["xn2_hbm"]
    xall, sc_in, sc_all = io["p2_xall"], io["p2_scin"], io["p2_scall"]
    pay_in, pay_ar = io["p2_payin"], io["p2_payar"]
    mypay_hbm, xe_hbm = io["p2_mypay"], io["p2_xe"]
    y_in, y_all = io["p2_yin"], io["p2_yall"]
    acc_hbm, dw_hbm = io["p2_acc"], io["p2_dw"]
    w1t_h, w3t_h, w2t_h = io["w_w1t_bf"], io["w_w3t_bf"], io["w_w2t_bf"]

    me_i = p.sbuf("p2_mei", [1, 2], I32)
    REG = {}
    gidt = p.sbuf("p2_gidt", [128, 32], F32)
    riota = p.sbuf("p2_riota", [128, 8], F32)
    baset = p.sbuf("p2_baset", [128, 1], F32)
    dumpt = p.sbuf("p2_dumpt", [128, 8], F32)
    ones816 = p.sbuf("p2_o816", [8, 16], F32)
    ones88 = p.sbuf("p2_o88", [8, 8], F32)
    ctile = p.sbuf("p2_ctile", [8, 16], F32)
    cnt1 = p.sbuf("p2_cnt1", [128, 1], F32)
    thr128 = p.sbuf("p2_thr", [128, 1], F32)
    loB = [p.sbuf(f"p2_lo{i}", [8, 1], F32) for i in range(2)]
    hiB = [p.sbuf(f"p2_hi{i}", [8, 1], F32) for i in range(2)]
    loE, hiE = loB[0], hiB[0]
    midE = p.sbuf("p2_mid", [8, 1], F32)
    mid16 = p.sbuf("p2_mid16", [8, 16], F32)
    geE = p.sbuf("p2_ge", [8, 1], F32)
    ltE = p.sbuf("p2_lt", [8, 1], F32)
    sc8a = p.sbuf("p2_sc8a", [8, 1], F32)
    sc8b = p.sbuf("p2_sc8b", [8, 1], F32)
    cntE = p.sbuf("p2_cntE", [8, 1], F32)
    mE = p.sbuf("p2_mE", [8, 1], F32)
    fillE = p.sbuf("p2_fillE", [8, 1], F32)
    maxE = p.sbuf("p2_maxE", [8, 1], F32)
    nmaxE = p.sbuf("p2_nmaxE", [8, 1], F32)
    denE = p.sbuf("p2_denE", [8, 1], F32)
    rdenE = p.sbuf("p2_rdenE", [8, 1], F32)
    tievE = p.sbuf("p2_tievE", [8, 1], F32)
    mce = p.sbuf("p2_mce", [128, 1], F32)
    tice = p.sbuf("p2_tice", [128, 1], F32)
    hi128 = p.sbuf("p2_hi128", [128, 1], F32)
    lo128 = p.sbuf("p2_lo128", [128, 1], F32)
    nmx128 = p.sbuf("p2_nmx128", [128, 1], F32)
    tie_ec = p.sbuf("p2_tieec", [8, 16], F32)
    gt_ec = p.sbuf("p2_gtec", [8, 16], F32)
    tie_e8 = p.sbuf("p2_tie8", [8, 8], F32)
    tiecum = p.sbuf("p2_tcum", [8, 8], F32)
    take_ec = p.sbuf("p2_takeec", [8, 8], F32)
    m_e8 = p.sbuf("p2_me8", [8, 8], F32)
    off_gcT = p.sbuf("p2_offgcT", [8, 8], F32)
    take_my = p.sbuf("p2_takemy", [8, 1], F32)
    og_meE = p.sbuf("p2_ogmeE", [8, 1], F32)
    ogP = p.sbuf("p2_ogP", [8, 1], F32)
    epad8 = p.sbuf("p2_epad8", [8, 1], F32)
    smod = p.sbuf("p2_smod", [8, TLOC], F32)
    oge_row = p.sbuf("p2_ogerow", [1, 8], F32)
    ones_col = p.sbuf("p2_onescol", [1, 128], F32)
    ogbc = p.sbuf("p2_ogbc", [128, 8], F32)
    ccol64 = p.sbuf("p2_ccol64", [128, 64], F32)
    idx64 = p.sbuf("p2_idx64", [128, 64], F32)
    idx64_i = p.sbuf("p2_idx64i", [128, 64], I32)
    wg_tm2 = p.sbuf("p2_wgtm2", [128, 32, 2], F32)
    gidr = p.sbuf("p2_gidr", [128, 8], F32)
    tlr = p.sbuf("p2_tlr", [128, 8], F32)
    v1m = p.sbuf("p2_v1m", [128, 8], F32)
    v2m = p.sbuf("p2_v2m", [128, 8], F32)
    idxs2 = p.sbuf("p2_idxs2", [128, 8], I32)
    mask8 = p.sbuf("p2_mask8", [8, 8], F32)
    tl8 = p.sbuf("p2_tl8", [1, 8], F32)
    ztile = p.sbuf("p2_z", [128, D], F32)
    zb = p.sbuf("p2_zb", [128, 1040], BF16)
    p.shared.update(me_i=me_i, gidt=gidt, riota=riota, baset=baset, dumpt=dumpt,
                    tl8=tl8, epad8=epad8, smod=smod, ones_col=ones_col, ccol64=ccol64,
                    ztile=ztile, zb=zb, mask8=mask8, ones88=ones88)

    def ldconst(e):
        e.wait(p.ctr("d_sx10"))
        e.wait(p.ctr("d_sx11"))
        e.dma_start(out=me_i[:], in_=V["me_i"], sem=d_ld)
        e.dma_start(out=gidt[:], in_=V["gid_c"], sem=d_ld)
        e.dma_start(out=riota[:], in_=V["riota_c"], sem=d_ld)
        e.dma_start(out=baset[:], in_=V["base_c"], sem=d_ld)
        e.dma_start(out=mask8[:], in_=V["mask8_c"], sem=d_ld)
        e.dma_start(out=tl8[:], in_=V["tl8_c"], sem=d_ld)
        e.dma_start(out=epad8[:], in_=V["epad8_c"], sem=d_ld)
        e.dma_start(out=smod[:], in_=V["smod_c"], sem=d_ld)
        e.dma_start(out=ccol64[:], in_=V["ccol64_c"], sem=d_ld)
    p.sync(ldconst)
    LD_CONST = d_ld.n

    def vconst(e):
        e.memset(ones816[:], 1.0)
        e.memset(ones88[:], 1.0)
        for k in range(8):
            e.memset(dumpt[:, k:k + 1], float(TLOC + k))
        e.memset(ztile[:], 0.0)
        e.memset(zb[:], 0.0)
        e.memset(ones_col[:], 1.0)
        e.fence()
    p.vector(vconst)
    V_CONST = v.n

    def stz(e):
        e.wait(v, V_CONST)
        for i in range(33):
            e.dma_start(out=acc_hbm[i * 128:(i + 1) * 128, :], in_=ztile[:], sem=d_z)
        payv = pay_in.rearrange("(q x) f -> q x f", q=128)
        for j in range(4):
            e.dma_start(out=payv[:, j * 64:(j + 1) * 64, :], in_=ztile[:], sem=d_z)
        e.dma_start(out=payv[:, 256:257, :], in_=ztile[:, 0:16], sem=d_z)
        for i in range(CPAY // 128):
            e.dma_start(out=y_all[8 * TLOC + i * 128:8 * TLOC + (i + 1) * 128, :], in_=zb[:], sem=d_z)
    p.gpsimd(stz)
    ST_ZERO = d_z.n

    emit_ln_plain(p, x1_hbm, xn2_hbm, "p2ln", gates=("d_sx10", "d_sx11"))
    ST_LN = d_st.n

    with ExitStack() as ssc:
        old = p.scope; p.scope = ssc
        wgt = p.sbuf("p2_wgt", [128, 8, 8], F32)
        wgt_b = p.sbuf("p2_wgtb", [128, 8, 8], BF16)
        xnf = [p.sbuf(f"p2_xnf{i}", [128, 8, 512], BF16) for i in range(2)]
        ssb = p.sbuf("p2_ssb", [8, TLOC], F32)
        ps_s = p.psum("p2_pss", [8, 512])

        def lds(e):
            e.wait(d_st, ST_LN)
            e.dma_start(out=wgt[:], in_=V["wgt_fm"], sem=d_ld)
        p.sync(lds)
        LD_WGT = d_ld.n

        def vcvt(e):
            e.wait(d_ld, LD_WGT)
            e.tensor_copy(wgt_b[:], wgt[:])
            e.fence()
        p.vector(vcvt)
        V_WGT = v.n

        for cg in range(8):
            bb = cg % 2

            def ldtr(e, cg=cg, bb=bb):
                e.wait(d_st, ST_LN)
                if cg >= 2:
                    e.wait(t)
                for j in range(8):
                    e.dma_start_transpose(xnf[bb][:, j, :], xn2_hbm[cg * 512:(cg + 1) * 512, j * 128:(j + 1) * 128], sem=d_tr[bb])
            p.sync(ldtr)

            def mms(e, cg=cg, bb=bb):
                e.wait(d_tr[bb])
                e.wait(v, V_WGT)
                if cg >= 1:
                    e.wait(v)
                for j in range(8):
                    e.matmul(ps_s[:], wgt_b[:, j, :], xnf[bb][:, j, :], start=(j == 0), stop=(j == 7))
            p.tensor(mms)

            def evs(e, cg=cg):
                e.wait(t)
                e.tensor_copy(ssb[:, cg * 512:(cg + 1) * 512], ps_s[:])
                e.fence()
            p.vector(evs)

        def sts(e):
            e.wait(v)
            e.dma_start(out=sc_in, in_=ssb[:], sem=d_st)
        p.scalar(sts)
        p.scope = old
    ST_SC = d_st.n

    def ag_sc(e):
        e.wait(d_st, ST_SC)
        e.collective_compute("AllGather", ALU.bypass, replica_groups=[list(range(NCORES))],
                             ins=[sc_in], outs=[sc_all])
    p.gpsimd(ag_sc)
    CC_SC = g.n

    def ag_x(e):
        e.wait(d_st, ST_LN)
        e.collective_compute("AllGather", ALU.bypass, replica_groups=[list(range(NCORES))],
                             ins=[xn2_hbm], outs=[xall])
    p.gpsimd(ag_x)
    CC_X = g.n

    with ExitStack() as asc:
        old = p.scope; p.scope = asc
        srch = p.sbuf("p2_srch", [128, 2048], F32)
        cmpb = p.sbuf("p2_cmp", [128, 2048], BF16)
        gtb = p.sbuf("p2_gtb", [128, 2048], BF16)
        expb = p.sbuf("p2_expb", [128, 2048], F32)

        def ldsr(e):
            e.wait(g, CC_SC)
            e.dma_start(out=srch[:], in_=sc_all.rearrange("c eh f -> eh c f"), sem=d_ld)
        p.sync(ldsr)
        LD_SR = d_ld.n

        def vinit(e):
            e.wait(d_ld, LD_SR)
            e.reduce_max(cnt1[:], srch[:], axis=AX.X, apply_absolute_value=True)
            e.fence()
        p.vector(vinit)

        def rp0(e):
            e.wait(v)
            e.dma_start(out=ctile[:], in_=cnt1[:], sem=d_rp)
        p.sync(rp0)

        def vinit2(e):
            e.wait(d_rp)
            e.wait(v, V_CONST)
            e.reduce_max(hiE[:], ctile[:], axis=AX.X)
            e.fence()
            e.tensor_scalar_add(hiE[:], hiE[:], 1.0)
            e.fence()
            e.tensor_scalar_mul(loE[:], hiE[:], -1.0)
            e.fence()
            e.tensor_tensor(midE[:], loE[:], hiE[:], ALU.add)
            e.fence()
            e.tensor_scalar_mul(midE[:], midE[:], 0.5)
            e.fence()
            e.tensor_scalar(mid16[:], ones816[:], midE[:, 0:1], 0.0, ALU.mult, ALU.add)
            e.fence()
        p.vector(vinit2)

        def rp1(e):
            e.wait(v)
            e.dma_start(out=thr128[:], in_=mid16[:], sem=d_rp)
        p.sync(rp1)

        for it in range(N_ITER):
            def vc(e, it=it):
                e.wait(d_rp)
                e.tensor_scalar(cmpb[:], srch[:], thr128[:, 0:1], 0.0, ALU.is_gt, ALU.add,
                                accum_out=cnt1[:])
                e.fence()
            p.vector(vc)

            def rp2(e, it=it):
                e.wait(v)
                e.dma_start(out=ctile[:], in_=cnt1[:], sem=d_rp)
            p.sync(rp2)

            def vu(e, it=it):
                sb, db = it % 2, 1 - it % 2
                e.wait(d_rp)
                e.reduce_sum(cntE[:], ctile[:], axis=AX.X)
                e.fence()
                e.tensor_scalar(geE[:], cntE[:], float(TLOC), 0.0, ALU.is_ge, ALU.add)
                e.fence()
                e.tensor_scalar(ltE[:], cntE[:], float(TLOC), 0.0, ALU.is_lt, ALU.add)
                e.fence()
                e.tensor_tensor(sc8a[:], midE[:], geE[:], ALU.mult)
                e.tensor_tensor(sc8b[:], loB[sb][:], ltE[:], ALU.mult)
                e.fence()
                e.tensor_tensor(loB[db][:], sc8a[:], sc8b[:], ALU.add)
                e.fence()
                e.tensor_tensor(sc8a[:], hiB[sb][:], geE[:], ALU.mult)
                e.tensor_tensor(sc8b[:], midE[:], ltE[:], ALU.mult)
                e.fence()
                e.tensor_tensor(hiB[db][:], sc8a[:], sc8b[:], ALU.add)
                e.fence()
                e.tensor_tensor(midE[:], loB[db][:], hiB[db][:], ALU.add)
                e.fence()
                e.tensor_scalar_mul(midE[:], midE[:], 0.5)
                e.fence()
                e.tensor_scalar(mid16[:], ones816[:], midE[:, 0:1], 0.0, ALU.mult, ALU.add)
                e.fence()
            p.vector(vu)

            def rp3(e, it=it):
                e.wait(v)
                e.dma_start(out=thr128[:], in_=mid16[:], sem=d_rp)
            p.sync(rp3)

        def vhb(e):
            e.wait(d_rp)
            e.tensor_scalar(mid16[:], ones816[:], hiE[:, 0:1], 0.0, ALU.mult, ALU.add)
            e.fence()
        p.vector(vhb)

        def rp4(e):
            e.wait(v)
            e.dma_start(out=hi128[:], in_=mid16[:], sem=d_rp)
        p.sync(rp4)

        def vf1(e):
            e.wait(d_rp)
            e.tensor_scalar(gtb[:], srch[:], hi128[:, 0:1], 0.0, ALU.is_gt, ALU.add, accum_out=mce[:])
            e.fence()
            e.reduce_max(cnt1[:], srch[:], axis=AX.X)
            e.fence()
            e.tensor_scalar(mid16[:], ones816[:], loE[:, 0:1], 0.0, ALU.mult, ALU.add)
            e.fence()
        p.vector(vf1)

        def rp5(e):
            e.wait(v)
            e.dma_start(out=lo128[:], in_=mid16[:], sem=d_rp)
            e.dma_start(out=ctile[:], in_=cnt1[:], sem=d_rp)
            e.dma_start(out=gt_ec[:], in_=mce[:], sem=d_rp)
        p.sync(rp5)

        def vf2(e):
            e.wait(d_rp)
            e.reduce_max(maxE[:], ctile[:], axis=AX.X)
            e.fence()
            e.tensor_scalar_mul(nmaxE[:], maxE[:], -1.0)
            e.fence()
            e.tensor_scalar(mid16[:], ones816[:], nmaxE[:, 0:1], 0.0, ALU.mult, ALU.add)
            e.fence()
            e.tensor_scalar(cmpb[:], srch[:], lo128[:, 0:1], 0.0, ALU.is_gt, ALU.add, accum_out=tice[:])
            e.fence()
            e.tensor_tensor(tice[:], tice[:], mce[:], ALU.subtract)
            e.fence()
            e.reduce_sum(mE[:], gt_ec[:], axis=AX.X)
            e.fence()
            e.tensor_scalar(fillE[:], mE[:], float(TLOC), -1.0, ALU.subtract, ALU.mult)
            e.fence()
        p.vector(vf2)

        def rp6(e):
            e.wait(v)
            e.dma_start(out=nmx128[:], in_=mid16[:], sem=d_rp)
            e.dma_start(out=tie_ec[:], in_=tice[:], sem=d_rp)
        p.sync(rp6)

        def af1(e):
            e.wait(d_rp)
            e.activation(expb[:], srch[:], AF.Exp, bias=nmx128[:, 0:1])
        p.scalar(af1)

        def vf3(e):
            e.wait(a)
            e.tensor_tensor(expb[:], expb[:], gtb[:], ALU.mult)
            e.fence()
            e.reduce_sum(cnt1[:], expb[:], axis=AX.X)
            e.fence()
        p.vector(vf3)

        def rp7(e):
            e.wait(v)
            e.dma_start(out=ctile[:], in_=cnt1[:], sem=d_rp)
        p.sync(rp7)

        def vf4(e):
            e.wait(d_rp)
            e.reduce_sum(denE[:], ctile[:], axis=AX.X)
            e.fence()
            e.tensor_tensor(tievE[:], hiE[:], nmaxE[:], ALU.add)
            e.fence()
        p.vector(vf4)

        def af2(e):
            e.wait(v)
            e.activation(tievE[:], tievE[:], AF.Exp)
        p.scalar(af2)

        def vf5(e):
            e.wait(a)
            e.tensor_tensor(tievE[:], tievE[:], fillE[:], ALU.mult)
            e.fence()
            e.tensor_tensor(denE[:], denE[:], tievE[:], ALU.add)
            e.fence()
            e.reciprocal(rdenE[:], denE[:])
            e.fence()
            e.tensor_tensor(tie_e8[:], tie_ec[:, 0:8], tie_ec[:, 8:16], ALU.add)
            e.fence()
            e.tensor_tensor(m_e8[:], gt_ec[:, 0:8], gt_ec[:, 8:16], ALU.add)
            e.fence()
            e.tensor_tensor_scan(tiecum[:], ones88[:], tie_e8[:], 0.0, ALU.mult, ALU.add)
            e.fence()
            e.tensor_tensor(tiecum[:], tiecum[:], tie_e8[:], ALU.subtract)
            e.fence()
            e.tensor_scalar(take_ec[:], tiecum[:], fillE[:, 0:1], -1.0, ALU.subtract, ALU.mult)
            e.fence()
            e.tensor_scalar(take_ec[:], take_ec[:], 0.0, 0.0, ALU.max, ALU.add)
            e.fence()
            e.tensor_tensor(take_ec[:], take_ec[:], tie_e8[:], ALU.min)
            e.fence()
            e.tensor_tensor(m_e8[:], m_e8[:], take_ec[:], ALU.add)
            e.fence()
            e.tensor_tensor_scan(off_gcT[:], ones88[:], m_e8[:], 0.0, ALU.mult, ALU.add)
            e.fence()
            e.tensor_tensor(off_gcT[:], off_gcT[:], m_e8[:], ALU.subtract)
            e.fence()
        p.vector(vf5)
        V_STATS = v.n
        p.scope = old

    with ExitStack() as bsc:
        old = p.scope; p.scope = bsc
        s_my = p.sbuf("p2_smy", [8, TLOC], F32)
        gtw = p.sbuf("p2_gtw", [8, TLOC], F32)
        tied = p.sbuf("p2_tied", [8, TLOC], F32)
        sel_my = p.sbuf("p2_selmy", [8, TLOC], F32)
        rank_my = p.sbuf("p2_rankmy", [8, TLOC], F32)
        ones8 = p.sbuf("p2_ones8", [8, TLOC], F32)

        def vones(e):
            e.memset(ones8[:], 1.0)
            e.fence()
        p.vector(vones)

        def gcols(e):
            e.wait(g, CC_SC)
            e.wait(v, V_STATS)
            rh_me = e.alloc_register(name="p2mereg")
            REG["me"] = bass_rust.make_scalar_value(rh_me, min_val=0, max_val=7)
            REG["meh"] = rh_me
            e.reg_load(rh_me, me_i[0:1, 0:1])
            e.dma_start(out=s_my[:], in_=sc_all.rearrange("c eh f -> c (eh f)")[DynSlice(REG["me"], 1), :], sem=d_gg)
        p.gpsimd(gcols)
        GD_COLS = d_gg.n

        def vcols(e):
            e.wait(v, V_STATS)
            e.wait(d_ld)
            e.tensor_tensor(tiecum[:], take_ec[:], mask8[:], ALU.mult)
            e.fence()
            e.reduce_sum(take_my[:], tiecum[:], axis=AX.X)
            e.fence()
            e.tensor_tensor(tiecum[:], off_gcT[:], mask8[:], ALU.mult)
            e.fence()
            e.reduce_sum(og_meE[:], tiecum[:], axis=AX.X)
            e.fence()
            e.tensor_tensor(ogP[:], og_meE[:], epad8[:], ALU.add)
            e.fence()
        p.vector(vcols)
        V_COLS = v.n

        def rp8(e):
            e.wait(v, V_COLS)
            e.dma_start(out=oge_row[:], in_=og_meE[:], sem=d_rp)
        p.sync(rp8)
        RP_ROWS = d_rp.n

        def vpre(e):
            e.wait(d_rp, RP_ROWS)
            e.wait(d_ld)
            e.tensor_tensor(oge_row[:], oge_row[:], tl8[:], ALU.add)
            e.fence()
        p.vector(vpre)
        V_PRE = v.n

        if dbg:
            def stdbg(e):
                e.wait(v)
                e.wait(d_rp)
                e.wait(d_gg)
                dd = io["dbg"]
                for i, tt in enumerate([hiE, loE, mE, fillE, maxE, denE, rdenE, take_my,
                                        og_meE, ogP, midE, cntE]):
                    e.dma_start(out=dd[:, i:i + 1], in_=tt[:], sem=d_st, allow_slow_non_contiguous=True)
                e.dma_start(out=dd[:, 16:24], in_=m_e8[:], sem=d_st)
                e.dma_start(out=dd[:, 24:32], in_=take_ec[:], sem=d_st)
                e.dma_start(out=dd[:, 40:48], in_=off_gcT[:], sem=d_st)
                e.dma_start(out=dd[:, 48:56], in_=tie_e8[:], sem=d_st)
            p.scalar(stdbg)
            return

        def vmy1(e):
            e.wait(d_gg, GD_COLS)
            e.tensor_scalar(gtw[:], s_my[:], hiE[:, 0:1], 0.0, ALU.is_gt, ALU.add)
            e.fence()
            e.tensor_scalar(tied[:], s_my[:], loE[:, 0:1], 0.0, ALU.is_gt, ALU.add)
            e.fence()
            e.tensor_tensor(tied[:], tied[:], gtw[:], ALU.subtract)
            e.fence()
            e.tensor_tensor_scan(rank_my[:], ones8[:], tied[:], 0.0, ALU.mult, ALU.add)
            e.fence()
            e.tensor_tensor(rank_my[:], rank_my[:], tied[:], ALU.subtract)
            e.fence()
            e.tensor_scalar(sel_my[:], rank_my[:], take_my[:, 0:1], 0.0, ALU.is_lt, ALU.add)
            e.fence()
            e.tensor_tensor(sel_my[:], sel_my[:], tied[:], ALU.mult)
            e.fence()
            e.tensor_tensor(sel_my[:], sel_my[:], gtw[:], ALU.add)
            e.fence()
            e.tensor_tensor_scan(rank_my[:], ones8[:], sel_my[:], 0.0, ALU.mult, ALU.add)
            e.fence()
            e.tensor_tensor(rank_my[:], rank_my[:], sel_my[:], ALU.subtract)
            e.fence()
        p.vector(vmy1)
        V_MY1 = v.n

        def amy(e):
            e.wait(v, V_MY1)
            e.activation(gtw[:], s_my[:], AF.Exp, bias=nmaxE[:, 0:1])
        p.scalar(amy)
        A_WMY = a.n

        def vmy2(e):
            e.wait(a, A_WMY)
            e.wait(v, V_COLS)
            e.tensor_scalar_mul(gtw[:], gtw[:], rdenE[:, 0:1])
            e.fence()
            e.tensor_tensor(gtw[:], gtw[:], sel_my[:], ALU.mult)
            e.fence()
            e.tensor_scalar(tied[:], rank_my[:], ogP[:, 0:1], float(-8 * TLOC), ALU.add, ALU.add)
            e.fence()
            e.tensor_tensor(tied[:], tied[:], smod[:], ALU.subtract)
            e.fence()
            e.tensor_tensor(tied[:], tied[:], sel_my[:], ALU.mult)
            e.fence()
            e.tensor_tensor(tied[:], tied[:], smod[:], ALU.add)
            e.fence()
            e.tensor_scalar_add(tied[:], tied[:], float(8 * TLOC))
            e.fence()
        p.vector(vmy2)
        V_MY2 = v.n

        def stdw(e):
            e.wait(v, V_MY2)
            e.dma_start(out=dw_hbm[0], in_=tied[:], sem=d_st)
            e.dma_start(out=dw_hbm[1], in_=gtw[:], sem=d_st)
        p.scalar(stdw)
        p.scope = old
    ST_DW = d_st.n

    with ExitStack() as csc:
        old = p.scope; p.scope = csc
        dwt = p.sbuf("p2_dwt", [128, 2, 32], F32)
        dwi = p.sbuf("p2_dwi", [128, 32], I32)
        pay_src = p.sbuf("p2_psrc", [128, 32, 16], F32)

        def vpay0(e):
            e.wait(d_st, ST_DW)
            e.wait(d_ld)
            e.memset(pay_src[:], 0.0)
            e.fence()
            e.tensor_copy(pay_src[:, :, 0], gidt[:])
            e.fence()
        p.vector(vpay0)
        V_PAY0 = v.n

        for ee in range(8):
            def lddw(e, ee=ee):
                e.wait(d_st, ST_DW)
                if ee > 0:
                    e.wait(d_gg)
                e.dma_start(out=dwt[:], in_=dw_hbm.rearrange("k e t -> e t k")[ee].rearrange("(q s) k -> q k s", q=128), sem=d_ld)
            p.sync(lddw)
            LDDW = d_ld.n

            def vpay(e, ee=ee, LDDW=LDDW):
                e.wait(d_ld, LDDW)
                e.wait(v, V_PAY0)
                if ee > 0:
                    e.wait(d_gg)
                e.tensor_copy(dwi[:], dwt[:, 0, :])
                e.fence()
                e.tensor_copy(pay_src[:, :, 1], dwt[:, 1, :])
                e.fence()
            p.vector(vpay)
            VPAY = v.n

            def gpay(e, ee=ee, VPAY=VPAY):
                e.wait(v, VPAY)
                e.wait(d_z, ST_ZERO)
                for s in range(32):
                    e.indirect_dma_start(
                        out=pay_in, out_offset=IndirectOffsetOnAxis(dwi[:, s:s + 1], 0),
                        in_=pay_src[:, s, :], in_offset=None,
                        sem=d_gg)
            p.gpsimd(gpay)
        p.scope = old
    GD_PAY = d_gg.n

    def ag_pay(e):
        e.wait(d_gg, GD_PAY)
        e.collective_compute("AllReduce", ALU.add, replica_groups=[list(range(NCORES))],
                             ins=[pay_in], outs=[pay_ar])
    p.gpsimd(ag_pay)
    CC_PAY = g.n

    with ExitStack() as dsc:
        old = p.scope; p.scope = dsc
        mypay = p.sbuf("p2_mypaysb", [128, 32, 16], F32)
        idxg = p.sbuf("p2_idxg", [128, 32], I32)
        X_tm = p.sbuf("p2_xtm", [128, 32, D], BF16)

        def ldmp(e):
            e.wait(g, CC_PAY)
            e.dma_start(out=mypay[:], in_=pay_ar[0:8 * TLOC, :].rearrange("(b r) f -> b r f", b=8)[DynSlice(REG["me"], 1)].rearrange("o (q s) f -> o q s f", q=128), sem=d_gg)
        p.gpsimd(ldmp)
        GD_MP = d_gg.n

        def stmp(e):
            e.wait(d_gg, GD_MP)
            e.dma_start(out=mypay_hbm.rearrange("(q s) f -> q s f", q=128), in_=mypay[:], sem=d_st)
        p.scalar(stmp)
        LD_MP = d_gg.n

        def vidx(e):
            e.wait(d_gg, GD_MP)
            e.tensor_copy(idxg[:], mypay[:, :, 0])
            e.fence()
        p.vector(vidx)
        V_IDXG = v.n

        def ggx(e):
            e.wait(v, V_IDXG)
            for s in range(32):
                e.indirect_dma_start(
                    out=X_tm[:, s, :], out_offset=None,
                    in_=xall, in_offset=IndirectOffsetOnAxis(idxg[:, s:s + 1], 0),
                    sem=d_gg)
        p.gpsimd(ggx)
        GD_X = d_gg.n

        def stxe(e):
            e.wait(d_gg, GD_X)
            e.dma_start(out=xe_hbm.rearrange("(q s) d -> q s d", q=128), in_=X_tm[:], sem=d_st)
        p.scalar(stxe)
        p.scope = old
    ST_XE = d_st.n

    with ExitStack() as esc:
        old = p.scope; p.scope = esc
        w1t_sb = p.sbuf("p2_w1sb", [128, 8, 2048], BF16)
        w3t_sb = p.sbuf("p2_w3sb", [128, 8, 2048], BF16)
        w2t_sb = p.sbuf("p2_w2sb", [128, 16, D], BF16)
        xfc = [p.sbuf(f"p2_xfc{i}", [128, 8, 512], BF16) for i in range(2)]
        h_bf = p.sbuf("p2_hbf", [128, 16, 512], BF16)
        hs_f = p.sbuf("p2_hsf", [128, 512], F32)
        Y_cg = [p.sbuf(f"p2_ycg{i}", [128, 4, 1040], BF16) for i in range(2)]
        ps_h = [p.psum(f"p2_psh{i}", [128, 512]) for i in range(2)]
        ps_y = [p.psum(f"p2_psy{i}", [128, 512]) for i in range(2)]

        def ldw13(e):
            e.wait(d_st, ST_XE)
            e.dma_start(out=w1t_sb[:], in_=w1t_h.rearrange("(j q) h -> q j h", q=128), sem=d_w[0])
            e.dma_start(out=w3t_sb[:], in_=w3t_h.rearrange("(j q) h -> q j h", q=128), sem=d_w[0])
            e.dma_start(out=w2t_sb[:], in_=w2t_h.rearrange("(j q) d -> q j d", q=128), sem=d_w[0])
        p.sync(ldw13)
        LD_W13 = d_w[0].n

        def ldwg2(e):
            e.wait(d_st, ST_XE)
            e.dma_start(out=wg_tm2[:], in_=mypay_hbm[:, 0:2].rearrange("(s q) f -> q s f", q=128), sem=d_ld)
        p.sync(ldwg2)
        LD_WG2 = d_ld.n

        for cg in range(8):
            bb = cg % 2

            def ldxf(e, cg=cg, bb=bb):
                e.wait(d_st, ST_XE)
                if cg >= 2:
                    e.wait(t)
                for j in range(8):
                    e.dma_start_transpose(xfc[bb][:, j, :], xe_hbm[cg * 512:(cg + 1) * 512, j * 128:(j + 1) * 128], sem=d_tr[bb])
            p.sync(ldxf)
            LD_XF = d_tr[bb].n

            for hf in range(16):
                def mmh(e, cg=cg, hf=hf, bb=bb, LD_XF=LD_XF):
                    e.wait(d_tr[bb], LD_XF)
                    e.wait(d_w[0], LD_W13)
                    if cg > 0 or hf > 0:
                        e.wait(v)
                    for j in range(8):
                        e.matmul(ps_h[0][:], w1t_sb[:, j, hf * 128:(hf + 1) * 128], xfc[bb][:, j, :], start=(j == 0), stop=(j == 7))
                    for j in range(8):
                        e.matmul(ps_h[1][:], w3t_sb[:, j, hf * 128:(hf + 1) * 128], xfc[bb][:, j, :], start=(j == 0), stop=(j == 7))
                p.tensor(mmh)

                def evh(e):
                    e.wait(t)
                    e.activation(hs_f[:], ps_h[0][:], AF.Silu)
                p.scalar(evh)

                def evh2(e, hf=hf):
                    e.wait(a)
                    e.tensor_tensor(h_bf[:, hf, :], hs_f[:], ps_h[1][:], ALU.mult)
                    e.fence()
                p.vector(evh2)

            for tt_i in range(4):
                for ng in range(2):
                    gi = tt_i * 2 + ng

                    def mmy(e, cg=cg, tt_i=tt_i, ng=ng, gi=gi):
                        e.wait(v)
                        if cg * 8 + gi >= 2:
                            e.wait(v, YEV[cg * 8 + gi - 2])
                        for hf in range(16):
                            e.matmul(ps_y[gi % 2][:], h_bf[:, hf, tt_i * 128:(tt_i + 1) * 128], w2t_sb[:, hf, ng * 512:(ng + 1) * 512], start=(hf == 0), stop=(hf == 15))
                    p.tensor(mmy)

                    def evy(e, cg=cg, tt_i=tt_i, ng=ng, gi=gi, bb=bb):
                        e.wait(t)
                        e.wait(d_ld, LD_WG2)
                        if cg >= 2 and gi == 0:
                            e.wait(d_y[bb])
                        e.tensor_scalar_mul(Y_cg[bb][:, tt_i, ng * 512:(ng + 1) * 512], ps_y[gi % 2][:], wg_tm2[:, cg * 4 + tt_i, 1:2])
                        e.fence()
                        YEV[cg * 8 + gi] = p.ctr("v").n
                    p.vector(evy)

            def vecho(e, cg=cg, bb=bb):
                ycf = Y_cg[bb].bitcast(F32)
                for tt_i in range(4):
                    e.tensor_copy(ycf[:, tt_i, 512:513], wg_tm2[:, cg * 4 + tt_i, 0:1])
                e.memset(ycf[:, :, 513:520], 0.0)
                e.fence()
            p.vector(vecho)
            V_ECHO = v.n

            def sty(e, cg=cg, bb=bb, V_ECHO=V_ECHO):
                e.wait(v, V_ECHO)
                e.dma_start(out=y_in.rearrange("(s q) f -> q s f", q=128)[:, cg * 4:(cg + 1) * 4, :], in_=Y_cg[bb][:], sem=d_y[bb])
            p.scalar(sty)
        p.scope = old

    def ag_y(e):
        e.wait(d_y[0])
        e.wait(d_y[1])
        e.collective_compute("AllGather", ALU.bypass, replica_groups=[list(range(NCORES))],
                             ins=[y_in], outs=[y_all[0:8 * TLOC, :]])
    p.gpsimd(ag_y)
    CC_Y = g.n

    with ExitStack() as fsc:
        old = p.scope; p.scope = fsc
        slab = [p.sbuf(f"p2_slab{i}", [128, 8, 1040], BF16) for i in range(2)]
        yf = [p.sbuf(f"p2_yf{i}", [128, 8, D], F32) for i in range(2)]
        d_sl = [p.ctr("p2_dsl0"), p.ctr("p2_dsl1")]
        GSC2, VS2 = {}, {}
        ps_bc = p.psum("p2_psbc", [128, 8])

        def tbc(e):
            e.wait(v, V_PRE)
            e.matmul(ps_bc[:], ones_col[:], oge_row[:], start=True, stop=True)
        p.tensor(tbc)
        T_BC = t.n

        def vbc(e):
            e.wait(t, T_BC)
            e.tensor_copy(ogbc[:], ps_bc[:])
            e.fence()
            for ee in range(8):
                e.tensor_scalar(idx64[:, ee * 8:(ee + 1) * 8], ccol64[:, ee * 8:(ee + 1) * 8],
                                ogbc[:, ee:ee + 1], 0.0, ALU.add, ALU.add)
            e.fence()
            e.tensor_copy(idx64_i[:], idx64[:])
            e.fence()
        p.vector(vbc)
        V_BC = v.n

        idxs2b = [p.sbuf(f"p2_idxs2{i}", [128, 8], I32) for i in range(2)]
        for ee in range(8):
            cb = ee % 2

            def gslab(e, ee=ee, cb=cb):
                e.wait(g, CC_Y)
                e.wait(v, V_BC)
                if ee >= 2:
                    e.wait(d_gg, GSC2[ee - 2])
                for k in range(8):
                    e.indirect_dma_start(
                        out=slab[cb][:, k, :], out_offset=None,
                        in_=y_all, in_offset=IndirectOffsetOnAxis(idx64_i[:, ee * 8 + k:ee * 8 + k + 1], 0),
                        sem=d_sl[cb])
            p.gpsimd(gslab)
            GSLAB = d_sl[cb].n

            def vslab(e, ee=ee, cb=cb, GSLAB=GSLAB):
                e.wait(d_sl[cb], GSLAB)
                e.tensor_copy(yf[cb][:], slab[cb][:, :, 0:D])
                e.fence()
                e.tensor_copy(gidr[:], slab[cb].bitcast(F32)[:, :, 512:513].rearrange("q s o -> q (s o)"))
                e.fence()
                e.tensor_scalar(tlr[:], gidr[:], baset[:, 0:1], 0.0, ALU.subtract, ALU.add)
                e.fence()
                e.tensor_scalar(v1m[:], tlr[:], -0.5, 0.0, ALU.is_gt, ALU.add)
                e.fence()
                e.tensor_scalar(v2m[:], tlr[:], float(TLOC) - 0.5, 0.0, ALU.is_lt, ALU.add)
                e.fence()
                e.tensor_tensor(v1m[:], v1m[:], v2m[:], ALU.mult)
                e.fence()
                e.tensor_tensor(gidr[:], tlr[:], v1m[:], ALU.mult)
                e.fence()
                e.tensor_scalar(v2m[:], v1m[:], 1.0, -1.0, ALU.subtract, ALU.mult)
                e.fence()
                e.tensor_tensor(v2m[:], v2m[:], dumpt[:], ALU.mult)
                e.fence()
                e.tensor_tensor(gidr[:], gidr[:], v2m[:], ALU.add)
                e.fence()
                e.tensor_copy(idxs2b[cb][:], gidr[:])
                e.fence()
            p.vector(vslab)
            VSLAB = v.n

            def gsc(e, ee=ee, cb=cb, VSLAB=VSLAB):
                e.wait(v, VSLAB)
                e.wait(d_z, ST_ZERO)
                if ee > 0:
                    e.wait(d_gg)
                for k in range(8):
                    e.indirect_dma_start(
                        out=acc_hbm, out_offset=IndirectOffsetOnAxis(idxs2b[cb][:, k:k + 1], 0),
                        in_=yf[cb][:, k, :], in_offset=None, compute_op=ALU.add,
                        sem=d_gg)
                GSC2[ee] = d_gg.n
                e.wait(d_gg)
            p.gpsimd(gsc)
        p.scope = old
    GD_COMB = d_gg.n

    with ExitStack() as xsc:
        old = p.scope; p.scope = xsc
        xtile = [p.sbuf(f"p2_xt{i}", [128, D], F32) for i in range(2)]
        atile = [p.sbuf(f"p2_at{i}", [128, D], F32) for i in range(2)]
        d_x2 = [p.ctr("p2_dx20"), p.ctr("p2_dx21")]
        d_lx = [p.ctr("p2_dlx0"), p.ctr("p2_dlx1")]
        for i in range(32):
            bb = i % 2

            def ldx2(e, i=i, bb=bb):
                e.wait(d_gg, GD_COMB)
                if i >= 2:
                    e.wait(d_x2[bb])
                e.dma_start(out=xtile[bb][:], in_=x1_hbm[i * 128:(i + 1) * 128, :], sem=d_lx[bb])
                e.dma_start(out=atile[bb][:], in_=acc_hbm[i * 128:(i + 1) * 128, :], sem=d_lx[bb])
            p.sync(ldx2)
            LX2 = d_lx[bb].n

            def vx2(e, i=i, bb=bb, LX2=LX2):
                e.wait(d_lx[bb], LX2)
                e.tensor_tensor(xtile[bb][:], xtile[bb][:], atile[bb][:], ALU.add)
                e.fence()
            p.vector(vx2)
            VX2 = v.n

            def stx2(e, i=i, bb=bb, VX2=VX2):
                e.wait(v, VX2)
                e.dma_start(out=x2_hbm[i * 128:(i + 1) * 128, :], in_=xtile[bb][:], sem=d_x2[bb])
            p.scalar(stx2)
        p.scope = old


EPS32 = float(2.0 ** -23)


def _regions_of(qt):
    R = [(qt, "T", 128)]
    if qt >= 1:
        R.append((qt - 1, None, 128))
    if qt == 2:
        R.append((0, "BG", 128))
    if qt >= 3:
        R.append((qt - 2, "B2", 128))
        R.append((0, None, 64))
    return R


def emit_l1_attn(p, io, V):
    nc = p.nc
    v, a, t, g = p.ctr("v"), p.ctr("a"), p.ctr("t"), p.ctr("g")
    d_ld = p.ctr("d_ld")
    d_st = p.ctr("d_st")
    d_tr = [p.ctr("d_tr0"), p.ctr("d_tr1")]
    d_w = [p.ctr("d_w0"), p.ctr("d_w1")]
    d_y3 = [p.ctr("p3_dy0"), p.ctr("p3_dy1")]
    d_lx3 = [p.ctr("p3_dlx0"), p.ctr("p3_dlx1")]
    d_sx3 = [p.ctr("p3_dsx0"), p.ctr("p3_dsx1")]

    x2_hbm, x3_hbm, xn3_hbm = io["x2_hbm"], io["x3_hbm"], io["xn3_hbm"]
    qk_hbm, y_hbm = io["p3_qk"], io["p3_y"]
    wqkv_h, wo_h = io["w_qkv_bf"], io["w_wo_bf"]

    _p3s = ExitStack()
    _oldscope = p.scope
    p.scope = _p3s

    v_sb = p.sbuf("p3_vsb", [128, 32, 256], BF16)
    ones_cb = p.sbuf("p3_onescb", [128, 1], BF16)
    masks = p.sbuf("p3_masks", [128, 3, 128], F32)

    emit_ln_plain(p, x2_hbm, xn3_hbm, "p3ln", gates=("p2_dx20", "p2_dx21"))
    ST_LN3 = d_st.n

    d_msk = p.ctr("p3_dmsk")

    def ldc3(e):
        e.wait(p.ctr("p2_dx20"))
        e.wait(p.ctr("p2_dx21"))
        e.dma_start(out=masks[:], in_=V["masks_c"], sem=d_msk)
    p.sync(ldc3)
    LD_MSK = d_msk.n

    def vc3(e):
        e.wait(d_msk, LD_MSK)
        e.memset(ones_cb[:], 1.0)
        e.fence()
    p.vector(vc3)

    with ExitStack() as qsc:
        old = p.scope; p.scope = qsc
        wqkv_sb = p.sbuf("p3_wqkvsb", [128, 8, 1536], BF16)
        xnf3 = [p.sbuf(f"p3_xnf{i}", [128, 8, 512], BF16) for i in range(2)]
        qkvt = [p.sbuf(f"p3_qkvt{i}", [128, 1536], F32) for i in range(2)]
        sqv = p.sbuf("p3_sqv", [128, 1280], F32)
        ssq = p.sbuf("p3_ssq", [128, 20], F32)
        rs20 = p.sbuf("p3_rs20", [128, 20], F32)
        qkro = p.sbuf("p3_qkro", [128, 1280], F32)
        tmp640 = p.sbuf("p3_tmp640", [128, 640], F32)
        qk_bf = [p.sbuf(f"p3_qkbf{i}", [128, 1280], BF16) for i in range(2)]
        ct = [p.sbuf(f"p3_ct{i}", [128, 32], F32) for i in range(2)]
        st_ = [p.sbuf(f"p3_st{i}", [128, 32], F32) for i in range(2)]
        ps_qkv = [p.psum(f"p3_psqkv{i}", [128, 1536]) for i in range(2)]
        d_qs = [p.ctr("p3_dqs0"), p.ctr("p3_dqs1")]
        VQC, VRP = {}, {}

        def ldw3(e):
            e.wait(d_st, ST_LN3)
            e.dma_start(out=wqkv_sb[:], in_=wqkv_h.rearrange("(j q) f -> q j f", q=128), sem=d_w[0])
        p.sync(ldw3)
        LD_W3 = d_w[0].n

        for cg in range(8):
            bb = cg % 2

            def ldtr3(e, cg=cg, bb=bb):
                e.wait(d_st, ST_LN3)
                if cg >= 2:
                    e.wait(t)
                for j in range(8):
                    e.dma_start_transpose(xnf3[bb][:, j, :], xn3_hbm[cg * 512:(cg + 1) * 512, j * 128:(j + 1) * 128], sem=d_tr[bb])
            p.sync(ldtr3)
            LD_XT3 = d_tr[bb].n

            for tt_i in range(4):
                gt = cg * 4 + tt_i
                qtl = gt % 16
                qb = gt % 2

                def mmq(e, cg=cg, tt_i=tt_i, bb=bb, qb=qb, gt=gt, LD_XT3=LD_XT3):
                    e.wait(d_tr[bb], LD_XT3)
                    e.wait(d_w[0], LD_W3)
                    if gt >= 2:
                        e.wait(v, VRP[gt - 2])
                    for part in range(3):
                        for j in range(8):
                            e.matmul(ps_qkv[qb][:, part * 512:(part + 1) * 512],
                                     xnf3[bb][:, j, tt_i * 128:(tt_i + 1) * 128],
                                     wqkv_sb[:, j, part * 512:(part + 1) * 512],
                                     start=(j == 0), stop=(j == 7))
                    MM_T[("p3q", gt)] = p.ctr("t").n
                p.tensor(mmq)

                def ldrope(e, qtl=qtl, qb=qb, gt=gt):
                    if gt >= 2:
                        e.wait(v, VRP[gt - 2])
                    e.dma_start(out=ct[qb][:], in_=V["rope_cos_c"][qtl * 128:(qtl + 1) * 128, :], sem=d_qs[qb])
                    e.dma_start(out=st_[qb][:], in_=V["rope_sin_c"][qtl * 128:(qtl + 1) * 128, :], sem=d_qs[qb])
                p.sync(ldrope)
                LD_RP = d_qs[qb].n

                def vqc(e, gt=gt, qb=qb):
                    e.wait(t, MM_T[("p3q", gt)])
                    e.tensor_copy(qkvt[qb][:], ps_qkv[qb][:, 0:1536])
                    e.fence()
                    VQC[gt] = p.ctr("v").n
                p.vector(vqc)

                def vrms(e, qb=qb):
                    e.tensor_tensor(sqv[:], qkvt[qb][:, 0:1280], qkvt[qb][:, 0:1280], ALU.mult)
                    e.fence()
                    e.reduce_sum(ssq[:], sqv.rearrange("p (h d) -> p h d", d=64), axis=AX.X)
                    e.fence()
                    e.tensor_scalar(ssq[:], ssq[:], 1.0 / 64.0, EPS32, ALU.mult, ALU.add)
                    e.fence()
                p.vector(vrms)

                def asq(e):
                    e.wait(v)
                    e.activation(rs20[:], ssq[:], AF.Sqrt)
                p.scalar(asq)

                def vrope(e, gt=gt, qb=qb, LD_RP=LD_RP):
                    e.wait(a)
                    e.wait(d_qs[qb], LD_RP)
                    e.reciprocal(rs20[:], rs20[:])
                    e.fence()
                    e.tensor_scalar_mul(rs20[:, 0:16], rs20[:, 0:16], 0.125)
                    e.fence()
                    rsb = rs20.rearrange("p (h o) -> p h o", o=1).broadcast_to([128, 20, 64])
                    e.tensor_tensor(qkro.rearrange("p (h d) -> p h d", d=64),
                                    qkvt[qb][:, 0:1280].rearrange("p (h d) -> p h d", d=64),
                                    rsb, ALU.mult)
                    e.fence()
                    x4 = qkro.rearrange("p (h c d) -> p h c d", c=2, d=32)
                    t2 = tmp640.rearrange("p (h d) -> p h d", d=32)
                    cb = ct[qb].rearrange("p (o d) -> p o d", o=1).broadcast_to([128, 20, 32])
                    sb = st_[qb].rearrange("p (o d) -> p o d", o=1).broadcast_to([128, 20, 32])
                    e.tensor_tensor(t2, x4[:, :, 0, :], sb, ALU.mult)
                    e.fence()
                    e.tensor_tensor(x4[:, :, 0, :], x4[:, :, 0, :], cb, ALU.mult)
                    e.fence()
                    e.tensor_tensor(sqv.rearrange("p (h d) -> p h d", d=64)[:, :, 0:32],
                                    x4[:, :, 1, :], sb, ALU.mult)
                    e.fence()
                    e.tensor_tensor(x4[:, :, 1, :], x4[:, :, 1, :], cb, ALU.mult)
                    e.fence()
                    e.tensor_tensor(x4[:, :, 0, :], x4[:, :, 0, :],
                                    sqv.rearrange("p (h d) -> p h d", d=64)[:, :, 0:32], ALU.add)
                    e.fence()
                    e.tensor_tensor(x4[:, :, 1, :], x4[:, :, 1, :], t2, ALU.subtract)
                    e.fence()
                    if gt >= 2:
                        e.wait(d_st)
                    e.tensor_copy(qk_bf[qb][:], qkro[:])
                    e.fence()
                    e.tensor_copy(v_sb[:, gt, :], ps_qkv[qb][:, 1280:1536])
                    e.fence()
                    VRP[gt] = p.ctr("v").n
                p.vector(vrope)
                V_QK = v.n

                def stqk(e, gt=gt, qb=qb, V_QK=V_QK):
                    e.wait(v, V_QK)
                    e.dma_start(out=qk_hbm[gt * 128:(gt + 1) * 128, :], in_=qk_bf[qb][:], sem=d_st)
                p.scalar(stqk)
        p.scope = old
    ST_QK = d_st.n

    with ExitStack() as asc3:
        old = p.scope; p.scope = asc3
        k_fmA = p.sbuf("p3_kfmA", [128, 2, 2048], BF16)
        d_qf = [p.ctr("p3_dqf0"), p.ctr("p3_dqf1")]
        k_fm = p.sbuf("p3_kfm", [128, 4, 2048], BF16)
        q_fm = [p.sbuf(f"p3_qfm{i}", [128, 8, 128], BF16) for i in range(2)]
        expS = [p.sbuf(f"p3_exps{i}", [128, 4, 128], BF16) for i in range(2)]
        den_f = [p.sbuf(f"p3_den{i}", [128, 1], F32) for i in range(2)]
        y_sb = [p.sbuf(f"p3_ysb{i}", [128, 16, 64], BF16) for i in range(2)]
        ps_s = [p.psum(f"p3_pss{i}", [128, 512]) for i in range(2)]
        ps_den = [p.psum(f"p3_psden{i}", [128, 1]) for i in range(2)]
        ps_av = p.psum("p3_psav", [128, 128])

        TSC, AEXP, VMASK, TDA, VEV = {}, {}, {}, {}, {}

        for it in range(32):
            s2, qt = it // 16, it % 16
            yb = it % 2
            regs = _regions_of(qt)

            def ldkf(e, s2=s2, qt=qt, it=it):
                if qt == 0:
                    e.wait(d_st, ST_QK)
                    if it > 0:
                        e.wait(t)
                    for o in range(4):
                        for c in range(2):
                            e.dma_start_transpose(k_fmA[:, c, o * 512:(o + 1) * 512],
                                                  qk_hbm[s2 * 2048 + o * 512:s2 * 2048 + (o + 1) * 512, 1024 + c * 128:1024 + (c + 1) * 128], sem=d_tr[0])
                qb = it % 2
                e.wait(d_st, ST_QK)
                if it >= 2:
                    e.wait(t)
                for j in range(8):
                    e.dma_start_transpose(q_fm[qb][:, j, :],
                                          qk_hbm[s2 * 2048 + qt * 128:s2 * 2048 + (qt + 1) * 128, j * 128:(j + 1) * 128], sem=d_qf[qb])
            p.sync(ldkf)
            LD_QF = d_qf[it % 2].n
            LD_KFA = d_tr[0].n

            if qt == 0:
                def dupk(e, LD_KFA=LD_KFA):
                    e.wait(d_tr[0], LD_KFA)
                    for gg in range(4):
                        for half in range(2):
                            e.dma_start(out=k_fm[half * 64:(half + 1) * 64, gg, :],
                                        in_=k_fmA[64 * (gg % 2):64 * (gg % 2) + 64, gg // 2, :], sem=d_ld)
                p.sync(dupk)
            LD_KF = d_ld.n

            for h in range(16):
                hb = h % 2
                gkv = h // 4

                def tsc(e, s2=s2, qt=qt, h=h, hb=hb, gkv=gkv, regs=regs,
                        LD_QF=LD_QF, LD_KF=LD_KF, it=it):
                    e.wait(d_qf[it % 2], LD_QF)
                    e.wait(d_ld, LD_KF)
                    if it * 16 + h >= 2:
                        e.wait(a, AEXP[it * 16 + h - 2])
                    qb = it % 2
                    for r, (kt, msk, kp) in enumerate(regs):
                        e.matmul(ps_s[hb][0:kp, r * 128:(r + 1) * 128],
                                 k_fm[64 * (h % 2):64 * (h % 2) + 64, gkv, kt * 128:kt * 128 + kp],
                                 q_fm[qb][64 * (h % 2):64 * (h % 2) + 64, h // 2, :],
                                 start=True, stop=True)
                    TSC[it * 16 + h] = t.n
                p.tensor(tsc)

                def aexp(e, h=h, hb=hb, regs=regs, it=it):
                    e.wait(t, TSC[it * 16 + h])
                    if it * 16 + h >= 2:
                        e.wait(t, TDA[it * 16 + h - 2])
                    for r, (kt, msk, kp) in enumerate(regs):
                        e.activation(expS[hb][0:kp, r, :], ps_s[hb][0:kp, r * 128:(r + 1) * 128], AF.Exp)
                    AEXP[it * 16 + h] = a.n
                p.scalar(aexp)

                def vmask(e, h=h, hb=hb, regs=regs, it=it):
                    e.wait(a, AEXP[it * 16 + h])
                    for r, (kt, msk, kp) in enumerate(regs):
                        if msk is not None:
                            mi = {"T": 0, "B2": 1, "BG": 2}[msk]
                            e.tensor_tensor(expS[hb][0:kp, r, :], expS[hb][0:kp, r, :], masks[0:kp, mi, :], ALU.mult)
                    e.fence()
                    VMASK[it * 16 + h] = v.n
                p.vector(vmask)

                def tda(e, s2=s2, h=h, hb=hb, gkv=gkv, regs=regs, it=it):
                    e.wait(v, VMASK[it * 16 + h])
                    nr = len(regs)
                    for r, (kt, msk, kp) in enumerate(regs):
                        e.matmul(ps_den[hb][:], expS[hb][0:kp, r, :], ones_cb[0:kp, :],
                                 start=(r == 0), stop=(r == nr - 1))
                    for r, (kt, msk, kp) in enumerate(regs):
                        e.matmul(ps_av[:, (h % 2) * 64:(h % 2) * 64 + 64], expS[hb][0:kp, r, :],
                                 v_sb[0:kp, s2 * 16 + kt, gkv * 64:(gkv + 1) * 64],
                                 start=(r == 0), stop=(r == nr - 1))
                    TDA[it * 16 + h] = t.n
                p.tensor(tda)

                def vev(e, h=h, hb=hb, yb=yb, it=it):
                    e.wait(t, TDA[it * 16 + h])
                    if h == 0 and it >= 2:
                        e.wait(d_y3[yb])
                    e.tensor_copy(den_f[hb][:], ps_den[hb][:])
                    e.fence()
                    e.reciprocal(den_f[hb][:], den_f[hb][:])
                    e.fence()
                    e.tensor_scalar_mul(y_sb[yb][:, h, :], ps_av[:, (h % 2) * 64:(h % 2) * 64 + 64], den_f[hb][:, 0:1])
                    e.fence()
                    VEV[it * 16 + h] = v.n
                p.vector(vev)

            def sty3(e, it=it, yb=yb):
                e.wait(v, VEV[it * 16 + 15])
                e.dma_start(out=y_hbm[it * 128:(it + 1) * 128, :], in_=y_sb[yb].rearrange("p h d -> p (h d)"), sem=d_y3[yb])
            p.scalar(sty3)
        p.scope = old

    with ExitStack() as osc:
        old = p.scope; p.scope = osc
        wo_sb = p.sbuf("p3_wosb", [128, 8, D], BF16)
        y_fm = [p.sbuf(f"p3_yfm{i}", [128, 8, 512], BF16) for i in range(2)]
        xc3 = [p.sbuf(f"p3_xc{i}", [128, D], F32) for i in range(2)]
        x3c = [p.sbuf(f"p3_x3c{i}", [128, D], F32) for i in range(2)]
        ps_o = [p.psum(f"p3_pso{i}", [128, D]) for i in range(2)]

        def ldwo(e):
            e.wait(d_y3[0])
            e.wait(d_y3[1])
            e.dma_start(out=wo_sb[:], in_=wo_h.rearrange("(j q) f -> q j f", q=128), sem=d_w[1])
        p.sync(ldwo)
        LD_WO = d_w[1].n

        for cg in range(8):
            bb = cg % 2

            def ldtry(e, cg=cg, bb=bb):
                e.wait(d_y3[0])
                e.wait(d_y3[1])
                if cg >= 2:
                    e.wait(t)
                for j in range(8):
                    e.dma_start_transpose(y_fm[bb][:, j, :], y_hbm[cg * 512:(cg + 1) * 512, j * 128:(j + 1) * 128], sem=d_tr[bb])
            p.sync(ldtry)
            LD_YT = d_tr[bb].n

            for tt_i in range(4):
                i = cg * 4 + tt_i
                pb = i % 2

                def ldxc3(e, i=i, pb=pb):
                    e.wait(d_y3[0])
                    e.wait(d_y3[1])
                    if i >= 2:
                        e.wait(d_sx3[pb])
                    e.dma_start(out=xc3[pb][:], in_=x2_hbm[i * 128:(i + 1) * 128, :], sem=d_lx3[pb])
                p.sync(ldxc3)
                LD_XC3 = d_lx3[pb].n

                def mmo(e, cg=cg, tt_i=tt_i, bb=bb, pb=pb, LD_YT=LD_YT):
                    e.wait(d_tr[bb], LD_YT)
                    e.wait(d_w[1], LD_WO)
                    if cg * 4 + tt_i >= 2:
                        e.wait(v)
                    for ng in range(2):
                        for j in range(8):
                            e.matmul(ps_o[pb][:, ng * 512:(ng + 1) * 512],
                                     y_fm[bb][:, j, tt_i * 128:(tt_i + 1) * 128],
                                     wo_sb[:, j, ng * 512:(ng + 1) * 512],
                                     start=(j == 0), stop=(j == 7))
                p.tensor(mmo)

                def vres3(e, i=i, pb=pb, LD_XC3=LD_XC3):
                    e.wait(t)
                    e.wait(d_lx3[pb], LD_XC3)
                    e.tensor_tensor(x3c[pb][:], xc3[pb][:], ps_o[pb][:], ALU.add)
                    e.fence()
                p.vector(vres3)
                V_R3 = v.n

                def stx3(e, i=i, pb=pb, V_R3=V_R3):
                    e.wait(v, V_R3)
                    e.dma_start(out=x3_hbm[i * 128:(i + 1) * 128, :], in_=x3c[pb][:], sem=d_sx3[pb])
                p.scalar(stx3)
        p.scope = old

    p.scope = _oldscope
    _p3s.close()


def emit_l1_ffn(p, io, V):
    nc = p.nc
    v, a, t, g = p.ctr("v"), p.ctr("a"), p.ctr("t"), p.ctr("g")
    d_ld = p.ctr("d_ld")
    d_st = p.ctr("d_st")
    d_tr = [p.ctr("d_tr0"), p.ctr("d_tr1")]
    d_gg = p.ctr("p2_dgg")
    d_z = p.ctr("p2_dz")
    d_w = [p.ctr("d_w0"), p.ctr("d_w1")]
    d_y = [p.ctr("p4_dy0"), p.ctr("p4_dy1")]
    d_rp = p.ctr("p2_drp")
    d_c4 = p.ctr("p4_dc4")

    x3_hbm, x4_hbm, xn4_hbm = io["x3_hbm"], io["x4_hbm"], io["xn4_hbm"]
    xall4 = io["p4_xall"]
    ptok_hbm = io["p4_ptok"]
    cnt_in, cnt_all = io["p4_cntin"], io["p4_cntall"]
    pay_in4, pay_ar4 = io["p4_payin"], io["p4_payar"]
    mypay4_hbm, xe4_hbm = io["p4_mypay"], io["p4_xe"]
    y_in4, y_all4 = io["p4_yin"], io["p4_yall"]
    acc4_hbm, dw4_hbm = io["p4_acc"], io["p4_dw"]
    w1_h, w2_h = io["w_l1w1_bf"], io["w_l1w2_bf"]

    _p4s = ExitStack()
    _oldscope = p.scope
    p.scope = _p4s

    S = p.shared
    me_i, gidt, riota, baset, dumpt = S["me_i"], S["gidt"], S["riota"], S["baset"], S["dumpt"]
    tl8, epad8, smod, ones_col, ccol64 = S["tl8"], S["epad8"], S["smod"], S["ones_col"], S["ccol64"]
    ztile, zb, mask8, ones88 = S["ztile"], S["zb"], S["mask8"], S["ones88"]
    erow8 = p.sbuf("p4_erow8", [128, 8], F32)
    cmlt = p.sbuf("p4_cmlt", [8, 8], F32)
    cnt_ec = p.sbuf("p4_cntec", [8, 8], F32)
    cnt16 = p.sbuf("p4_cnt16", [16, 1], F32)
    off4 = p.sbuf("p4_off4", [8, 1], F32)
    len4 = p.sbuf("p4_len4", [8, 1], F32)
    og4 = p.sbuf("p4_og4", [8, 1], F32)
    sc81 = p.sbuf("p4_sc81", [8, 1], F32)
    oge4_row = p.sbuf("p4_oge4row", [1, 16], F32)
    cntrow = p.sbuf("p4_cntrow", [1, 16], F32)
    bcrow = p.sbuf("p4_bcrow", [1, 16], F32)
    cnt_cb = p.sbuf("p4_cntcb", [8, 16], F32)
    m_ce4 = p.sbuf("p4_mce4", [8, 8], F32)
    m_cme4 = p.sbuf("p4_mcme4", [8, 1], F32)
    ogrow = p.sbuf("p4_ogrow", [1, 16], F32)
    cmltc = p.sbuf("p4_cmltc", [8, 1], F32)
    bc128 = p.sbuf("p4_bc128", [128, 16], F32)
    baset4 = p.sbuf("p4_baset4", [128, 1], F32)
    srcidx = p.sbuf("p4_srcidx", [128, 32], F32)
    srct1 = p.sbuf("p4_srct1", [128, 32], F32)
    rloc4 = p.sbuf("p4_rloc", [128, 32], F32)

    ogbc4 = p.sbuf("p4_ogbc4", [128, 16], F32)
    idx644 = p.sbuf("p4_idx64", [128, 64], F32)
    idx644_i = p.sbuf("p4_idx64i", [128, 64], I32)
    wg4_tm2 = p.sbuf("p4_wgtm2", [128, 32, 2], F32)
    oh_em = p.sbuf("p4_ohem", [8, TLOC], BF16)
    idn_f = p.sbuf("p4_idnf", [128, 128], F32)
    idn_b = p.sbuf("p4_idnb", [128, 128], BF16)
    gidr4 = p.sbuf("p4_gidr", [128, 8], F32)
    tlr4 = p.sbuf("p4_tlr", [128, 8], F32)
    v1m4 = p.sbuf("p4_v1m", [128, 8], F32)
    v2m4 = p.sbuf("p4_v2m", [128, 8], F32)
    idxs4 = p.sbuf("p4_idxs", [128, 8], I32)
    d_cst4 = p.ctr("p4_dcst")

    def ldc4(e):
        e.wait(p.ctr("p3_dsx0"))
        e.wait(p.ctr("p3_dsx1"))
        e.dma_start(out=erow8[:], in_=V["erow8_c"], sem=d_cst4)
        e.dma_start(out=cmltc[:], in_=V["cmltc_c"], sem=d_cst4)
        e.dma_start(out=idn_f[:], in_=V["idn_c"], sem=d_cst4)
    p.sync(ldc4)
    LD_C4 = d_cst4.n

    def stz4(e):
        e.wait(d_cst4, LD_C4)
        for i in range(33):
            e.dma_start(out=acc4_hbm[i * 128:(i + 1) * 128, :], in_=ztile[:], sem=d_z)
        payv = pay_in4.rearrange("(q x) f -> q x f", q=128)
        for j in range(8):
            e.dma_start(out=payv[:, j * 64:(j + 1) * 64, :], in_=ztile[:], sem=d_z)
        e.dma_start(out=payv[:, 512:513, :], in_=ztile[:, 0:16], sem=d_z)
        for i in range(CPAY // 128):
            e.dma_start(out=y_all4[8 * TLOC + i * 128:8 * TLOC + (i + 1) * 128, :], in_=zb[:], sem=d_z)
    p.gpsimd(stz4)
    ST_Z4 = d_z.n

    emit_ln_plain(p, x3_hbm, xn4_hbm, "p4ln", gates=("p3_dsx0", "p3_dsx1"))
    ST_LN4 = d_st.n

    def ag_x4(e):
        e.wait(d_st, ST_LN4)
        e.collective_compute("AllGather", ALU.bypass, replica_groups=[list(range(NCORES))],
                             ins=[xn4_hbm], outs=[xall4])
    p.gpsimd(ag_x4)
    CC_X4 = g.n

    with ExitStack() as rsc:
        old = p.scope; p.scope = rsc
        wr_f = p.sbuf("p4_wrf", [128, 8, 8], F32)
        wr_b = p.sbuf("p4_wrb", [128, 8, 8], BF16)
        xnf4 = [p.sbuf(f"p4_xnf{i}", [128, 8, 512], BF16) for i in range(2)]
        lg = p.sbuf("p4_lg", [128, 8], F32)
        mx8 = p.sbuf("p4_mx8", [128, 8], F32)
        nmx = p.sbuf("p4_nmx", [128, 1], F32)
        midx = p.sbuf("p4_midx", [128, 8], U32)
        eidf = p.sbuf("p4_eidf", [128, 1], F32)
        exb = p.sbuf("p4_exb", [128, 8], F32)
        den1 = p.sbuf("p4_den1", [128, 1], F32)
        oh8 = [p.sbuf(f"p4_oh8{i}", [128, 8], BF16) for i in range(2)]
        ptk = [p.sbuf(f"p4_ptk{i}", [128, 1], F32) for i in range(2)]
        ps_r = p.psum("p4_psr", [128, 8])
        ps_t = p.psum("p4_pst", [8, 128])
        d_or = [p.ctr("p4_dor0"), p.ctr("p4_dor1")]
        TOH, VOH = {}, {}

        def vidn(e):
            e.wait(d_cst4, LD_C4)
            e.tensor_copy(idn_b[:], idn_f[:])
            e.fence()
        p.vector(vidn)
        V_IDN = v.n

        def ldwr(e):
            e.wait(d_st, ST_LN4)
            e.dma_start(out=wr_f[:], in_=V["p4_wr_fm"], sem=d_c4)
        p.sync(ldwr)
        LD_WR = d_c4.n

        def vwr(e):
            e.wait(d_c4, LD_WR)
            e.tensor_copy(wr_b[:], wr_f[:])
            e.fence()
        p.vector(vwr)
        V_WR = v.n

        for cg in range(8):
            bb = cg % 2

            def ldtr4(e, cg=cg, bb=bb):
                e.wait(d_st, ST_LN4)
                if cg >= 2:
                    e.wait(t)
                for j in range(8):
                    e.dma_start_transpose(xnf4[bb][:, j, :], xn4_hbm[cg * 512:(cg + 1) * 512, j * 128:(j + 1) * 128], sem=d_tr[bb])
            p.sync(ldtr4)
            LD_X4 = d_tr[bb].n

            for tt_i in range(4):
                i = cg * 4 + tt_i
                ob = i % 2

                def mmr(e, cg=cg, tt_i=tt_i, bb=bb, LD_X4=LD_X4):
                    e.wait(d_tr[bb], LD_X4)
                    e.wait(v, V_WR)
                    if cg * 4 + tt_i >= 1:
                        e.wait(v)
                    for j in range(8):
                        e.matmul(ps_r[:], xnf4[bb][:, j, tt_i * 128:(tt_i + 1) * 128], wr_b[:, j, :], start=(j == 0), stop=(j == 7))
                p.tensor(mmr)

                def vrt(e, i=i, ob=ob):
                    e.wait(t)
                    if i >= 2:
                        e.wait(d_or[ob])
                    e.tensor_copy(lg[:], ps_r[:])
                    e.fence()
                    e.max(out=mx8[:], in_=lg[:])
                    e.fence()
                    e.max_index(midx[:], mx8[:], lg[:])
                    e.fence()
                    e.tensor_copy(eidf[:], midx[:, 0:1])
                    e.fence()
                    e.tensor_scalar_mul(nmx[:], mx8[:, 0:1], -1.0)
                    e.fence()
                p.vector(vrt)

                def art(e):
                    e.wait(v)
                    e.activation(exb[:], lg[:], AF.Exp, bias=nmx[:, 0:1])
                p.scalar(art)

                def vrt2(e, i=i, ob=ob):
                    e.wait(a)
                    if i >= 2:
                        e.wait(t, TOH[i - 2])
                    e.reduce_sum(den1[:], exb[:], axis=AX.X)
                    e.fence()
                    e.reciprocal(ptk[ob][:], den1[:])
                    e.fence()
                    e.tensor_scalar(oh8[ob][:], erow8[:], eidf[:, 0:1], 0.0, ALU.is_equal, ALU.add)
                    e.fence()
                p.vector(vrt2)
                V_RT = v.n

                def tmoh(e, i=i, ob=ob, V_RT=V_RT):
                    e.wait(v, V_RT)
                    e.wait(v, V_IDN)
                    if i >= 1:
                        e.wait(v, VOH[i - 1])
                    e.matmul(ps_t[:], oh8[ob][:], idn_b[:], start=True, stop=True)
                    TOH[i] = p.ctr("t").n
                p.tensor(tmoh)

                def voh(e, i=i, ob=ob):
                    e.wait(t, TOH[i])
                    e.tensor_copy(oh_em[:, i * 128:(i + 1) * 128], ps_t[:])
                    e.fence()
                    VOH[i] = p.ctr("v").n
                p.vector(voh)

                def stor(e, i=i, ob=ob, V_RT=V_RT):
                    e.wait(v, V_RT)
                    e.dma_start(out=ptok_hbm[i * 128:(i + 1) * 128, :], in_=ptk[ob][:], sem=d_or[ob])
                p.scalar(stor)
        p.scope = old
        V_OHLAST = VOH[31]
    ST_RT0 = [p.ctr("p4_dor0").n, p.ctr("p4_dor1").n]

    with ExitStack() as bsc4:
        old = p.scope; p.scope = bsc4
        ohf = p.sbuf("p4_ohf", [8, TLOC], F32)
        rank4 = p.sbuf("p4_rank", [8, TLOC], F32)
        sel4 = p.sbuf("p4_sel", [8, TLOC], F32)
        dest4 = p.sbuf("p4_dest", [8, TLOC], F32)
        ones84 = p.sbuf("p4_ones8", [8, TLOC], F32)
        ones8c = p.sbuf("p4_ones8c", [8, 1], F32)
        drow = p.sbuf("p4_drow", [1, TLOC], F32)
        krow = p.sbuf("p4_krow", [1, TLOC], F32)
        epadL = p.sbuf("p4_epadL", [8, 1], F32)
        mebase4 = p.sbuf("p4_mebase", [8, 1], F32)
        ps_dk = p.psum("p4_psdk", [1, 1024])

        def ldc4b(e):
            e.dma_start(out=epadL[:], in_=V["epadL_c"], sem=d_c4)
            e.dma_start(out=mebase4[:], in_=V["mebase4_c"], sem=d_c4)
        p.sync(ldc4b)
        LD_C4B = d_c4.n

        def vrk1(e):
            e.wait(v, V_OHLAST)
            e.wait(p.ctr("p4_dor0"), ST_RT0[0])
            e.wait(p.ctr("p4_dor1"), ST_RT0[1])
            e.memset(ones84[:], 1.0)
            e.memset(cntrow[:], 0.0)
            e.fence()
            e.tensor_copy(ohf[:], oh_em[:, :])
            e.fence()
            e.reduce_sum(cnt16[0:8, :], ohf[:], axis=AX.X)
            e.fence()
        p.vector(vrk1)
        V_RK1 = v.n

        def rpc4(e):
            e.wait(v, V_RK1)
            e.dma_start(out=cntrow[:, 0:8], in_=cnt16[0:8, :], sem=d_rp)
        p.sync(rpc4)
        RP_C4 = d_rp.n

        def voff4(e):
            e.wait(d_c4, LD_C4B)
            e.wait(d_rp, RP_C4)
            e.tensor_tensor_scan(rank4[:], ones84[:], ohf[:], 0.0, ALU.mult, ALU.add)
            e.fence()
            e.tensor_tensor(rank4[:], rank4[:], ohf[:], ALU.subtract)
            e.fence()
            e.tensor_scalar(sel4[:], rank4[:], float(LCAP) - 0.5, 0.0, ALU.is_lt, ALU.add)
            e.fence()
            e.tensor_tensor(sel4[:], sel4[:], ohf[:], ALU.mult)
            e.fence()
            e.memset(ones8c[:], 1.0)
            e.fence()
            e.tensor_scalar(dest4[:], rank4[:], epadL[:, 0:1], 0.0, ALU.add, ALU.add)
            e.fence()
            e.tensor_scalar(dest4[:], dest4[:], mebase4[:, 0:1], 0.0, ALU.add, ALU.add)
            e.fence()
            e.tensor_tensor(dest4[:], dest4[:], sel4[:], ALU.mult)
            e.fence()
        p.vector(voff4)
        V_OFF4 = v.n

        DKT = {}
        for sl in range(8):
            def tdk(e, sl=sl):
                e.wait(v, V_OFF4)
                if sl >= 1:
                    e.wait(v)
                e.matmul(ps_dk[0:1, 0:512], ones8c[:], dest4[:, sl * 512:(sl + 1) * 512], start=True, stop=True)
                e.matmul(ps_dk[0:1, 512:1024], ones8c[:], sel4[:, sl * 512:(sl + 1) * 512], start=True, stop=True)
            p.tensor(tdk)

            def vdk(e, sl=sl):
                e.wait(t)
                e.tensor_copy(drow[:, sl * 512:(sl + 1) * 512], ps_dk[0:1, 0:512])
                e.tensor_copy(krow[:, sl * 512:(sl + 1) * 512], ps_dk[0:1, 512:1024])
                e.fence()
                DKT[sl] = p.ctr("v").n
            p.vector(vdk)

        def vfin4(e):
            e.wait(v, DKT[7])
            e.tensor_scalar(krow[:], krow[:], 1.0, -1.0, ALU.subtract, ALU.mult)
            e.fence()
            e.tensor_scalar(rank4[0:1, :], smod[0:1, :], float(64 * LCAP), 0.0, ALU.add, ALU.add)
            e.fence()
            e.tensor_tensor(krow[:], krow[:], rank4[0:1, :], ALU.mult)
            e.fence()
            e.tensor_tensor(drow[:], drow[:], krow[:], ALU.add)
            e.fence()
        p.vector(vfin4)
        V_OFF4b = v.n

        def stdw4(e):
            e.wait(v, V_OFF4b)
            e.dma_start(out=dw4_hbm[0][0], in_=drow[:], sem=d_st)
        p.scalar(stdw4)
        p.scope = old
    ST_DW4 = d_st.n

    with ExitStack() as csc4:
        old = p.scope; p.scope = csc4
        dwt4 = p.sbuf("p4_dwt", [128, 32], F32)
        dwi4 = p.sbuf("p4_dwi", [128, 32], I32)
        pay_src4 = p.sbuf("p4_psrc", [128, 32, 16], F32)
        ptt = p.sbuf("p4_ptt", [128, 32, 1], F32)

        def ldpt(e):
            e.wait(d_st, ST_DW4)
            e.dma_start(out=ptt[:], in_=ptok_hbm.rearrange("(q s) o -> q s o", q=128), sem=d_c4)
            e.dma_start(out=dwt4[:], in_=dw4_hbm[0][0].rearrange("(q s) -> q s", q=128), sem=d_c4)
        p.sync(ldpt)
        LD_PT = d_c4.n

        def vpay04(e):
            e.wait(d_c4, LD_PT)
            e.memset(pay_src4[:], 0.0)
            e.fence()
            e.tensor_copy(pay_src4[:, :, 0], gidt[:])
            e.fence()
            e.tensor_copy(pay_src4[:, :, 1], ptt[:, :, 0])
            e.fence()
            e.tensor_copy(dwi4[:], dwt4[:])
            e.fence()
        p.vector(vpay04)
        V_PAY04 = v.n

        def gpay4(e, V_PAY04=V_PAY04):
            e.wait(v, V_PAY04)
            e.wait(d_z, ST_Z4)
            rhc = e.alloc_register(name="p4cntreg")
            vc_ = bass_rust.make_scalar_value(rhc, min_val=64 * LCAP + 32, max_val=64 * LCAP + 40)
            e.reg_load(rhc, me_i[0:1, 1:2])
            e.dma_start(out=pay_in4[DynSlice(vc_, 1), :], in_=cntrow[:], sem=d_gg)
            for s in range(32):
                e.indirect_dma_start(
                    out=pay_in4, out_offset=IndirectOffsetOnAxis(dwi4[:, s:s + 1], 0),
                    in_=pay_src4[:, s, :], in_offset=None,
                    sem=d_gg)
        p.gpsimd(gpay4)
        p.scope = old
    GD_PAY4 = d_gg.n

    def ag_pay4(e):
        e.wait(d_gg, GD_PAY4)
        e.collective_compute("AllReduce", ALU.add, replica_groups=[list(range(NCORES))],
                             ins=[pay_in4], outs=[pay_ar4])
    p.gpsimd(ag_pay4)
    CC_PAY4 = g.n

    with ExitStack() as dsc4:
        old = p.scope; p.scope = dsc4
        mypay4 = p.sbuf("p4_mypaysb", [128, 32, 16], F32)
        idxg4 = p.sbuf("p4_idxg", [128, 32], I32)
        srci = p.sbuf("p4_srci", [128, 32], I32)
        X_tm4 = p.sbuf("p4_xtm", [128, 32, D], BF16)
        ps_bcr = p.psum("p4_psbcr", [128, 16])

        def ldcb4(e):
            e.wait(g, CC_PAY4)
            e.dma_start(out=cnt_cb[:], in_=pay_ar4[64 * LCAP + 32:64 * LCAP + 40, :], sem=d_c4)
        p.sync(ldcb4)
        LD_CB4 = d_c4.n

        def vmce4(e):
            e.wait(d_c4, LD_CB4)
            e.wait(d_cst4, LD_C4)
            e.tensor_scalar(m_ce4[:], cnt_cb[:, 0:8], float(LCAP), 0.0, ALU.min, ALU.add)
            e.fence()
            e.tensor_tensor(cnt_cb[:, 0:8], m_ce4[:], mask8[:], ALU.mult)
            e.fence()
            e.reduce_sum(m_cme4[:], cnt_cb[:, 0:8], axis=AX.X)
            e.fence()
            e.tensor_scalar(cnt_cb[:, 8:16], m_ce4[:], cmltc[:, 0:1], 0.0, ALU.mult, ALU.add)
            e.fence()
        p.vector(vmce4)
        V_MCE4 = v.n

        def tog4(e):
            e.wait(v, V_MCE4)
            e.matmul(ps_bcr[0:1, 0:8], ones8c[:], cnt_cb[:, 8:16], start=True, stop=True)
        p.tensor(tog4)
        T_OG4 = t.n

        def rpm4(e):
            e.wait(v, V_MCE4)
            e.dma_start(out=bcrow[:, 0:8], in_=m_cme4[:], sem=d_rp)
        p.sync(rpm4)
        RP_M4 = d_rp.n

        def vog4(e):
            e.wait(t, T_OG4)
            e.wait(d_rp, RP_M4)
            e.tensor_copy(ogrow[:, 0:8], ps_bcr[0:1, 0:8])
            e.fence()
            e.tensor_scalar(oge4_row[:, 0:8], ogrow[:, 0:8], float(TLOC), 0.0, ALU.min, ALU.add)
            e.fence()
            e.tensor_tensor(oge4_row[:, 0:8], oge4_row[:, 0:8], tl8[:], ALU.add)
            e.fence()
            e.tensor_scalar(oge4_row[:, 8:16], ogrow[:, 0:8], float(TLOC), -1.0, ALU.subtract, ALU.mult)
            e.fence()
            e.tensor_scalar(oge4_row[:, 8:16], oge4_row[:, 8:16], 0.0, 0.0, ALU.max, ALU.add)
            e.fence()
            e.tensor_scalar(cntrow[:, 8:16], cntrow[:, 0:8], float(LCAP), 0.0, ALU.min, ALU.add)
            e.fence()
            e.tensor_tensor(oge4_row[:, 8:16], oge4_row[:, 8:16], cntrow[:, 8:16], ALU.min)
            e.fence()
            e.tensor_tensor_scan(bcrow[:, 8:16], ones88[0:1, :], bcrow[:, 0:8], 0.0, ALU.mult, ALU.add)
            e.fence()
            e.tensor_scalar(bcrow[:, 0:8], bcrow[:, 0:8], float(8 * LCAP), -1.0, ALU.subtract, ALU.mult)
            e.fence()
            e.tensor_copy(srct1[0:1, 0:8], bcrow[:, 8:16])
            e.fence()
            e.tensor_copy(bcrow[:, 8:16], bcrow[:, 0:8])
            e.fence()
            e.tensor_copy(bcrow[:, 0:8], srct1[0:1, 0:8])
            e.fence()
        p.vector(vog4)
        V_OG4 = v.n
        V_FOLD4 = V_OG4

        def tbcr(e):
            e.wait(v, V_OG4)
            e.matmul(ps_bcr[:], ones_col[:], bcrow[:], start=True, stop=True)
        p.tensor(tbcr)
        T_BCR = t.n

        def vsrc4(e):
            e.wait(t, T_BCR)
            e.wait(d_cst4)
            e.tensor_copy(bc128[:], ps_bcr[:])
            e.fence()
            e.tensor_scalar_mul(baset4[:], baset[:, 0:1], 0.25)
            e.fence()
            e.tensor_scalar(rloc4[:], gidt[:], baset[:, 0:1], 0.0, ALU.subtract, ALU.add)
            e.fence()
            e.tensor_scalar(srcidx[:], rloc4[:], baset4[:, 0:1], 0.0, ALU.add, ALU.add)
            e.fence()
            for cc in range(7):
                e.tensor_scalar(srct1[:], rloc4[:], bc128[:, cc:cc + 1], 0.0, ALU.is_ge, ALU.add)
                e.fence()
                e.tensor_scalar(srct1[:], srct1[:], bc128[:, 8 + cc:9 + cc], 0.0, ALU.mult, ALU.add)
                e.fence()
                e.tensor_tensor(srcidx[:], srcidx[:], srct1[:], ALU.add)
                e.fence()
            e.tensor_scalar(srcidx[:], srcidx[:], float(PAYR4 - 1), 0.0, ALU.min, ALU.add)
            e.fence()
            e.tensor_copy(srci[:], srcidx[:])
            e.fence()
        p.vector(vsrc4)
        V_SRC4 = v.n

        def gmp4(e):
            e.wait(v, V_SRC4)
            for s in range(32):
                e.indirect_dma_start(
                    out=mypay4[:, s, :], out_offset=None,
                    in_=pay_ar4, in_offset=IndirectOffsetOnAxis(srci[:, s:s + 1], 0),
                    sem=d_gg)
        p.gpsimd(gmp4)
        GD_MP4 = d_gg.n

        def stmp4(e):
            e.wait(d_gg, GD_MP4)
            e.dma_start(out=mypay4_hbm.rearrange("(q s) f -> q s f", q=128), in_=mypay4[:], sem=d_st)
        p.scalar(stmp4)

        def vidx4(e):
            e.wait(d_gg, GD_MP4)
            e.tensor_copy(idxg4[:], mypay4[:, :, 0])
            e.fence()
        p.vector(vidx4)
        V_IDXG4 = v.n

        def ggx4(e):
            e.wait(v, V_IDXG4)
            for s in range(32):
                e.indirect_dma_start(
                    out=X_tm4[:, s, :], out_offset=None,
                    in_=xall4, in_offset=IndirectOffsetOnAxis(idxg4[:, s:s + 1], 0),
                    sem=d_gg)
        p.gpsimd(ggx4)
        GD_X4 = d_gg.n

        def stxe4(e):
            e.wait(d_gg, GD_X4)
            e.dma_start(out=xe4_hbm.rearrange("(q s) d -> q s d", q=128), in_=X_tm4[:], sem=d_st)
        p.scalar(stxe4)
        p.scope = old
    ST_XE4 = d_st.n

    with ExitStack() as esc4:
        old = p.scope; p.scope = esc4
        w2_sb = p.sbuf("p4_w2sb", [128, 32, D], BF16)
        w1sl = [p.sbuf(f"p4_w1sl{i}", [128, 8, 1024], BF16) for i in range(2)]
        xfc4 = [p.sbuf(f"p4_xfc{i}", [128, 8, 512], BF16) for i in range(2)]
        h_bf4 = p.sbuf("p4_hbf", [128, 32, 512], BF16)
        Y_cg4 = [p.sbuf(f"p4_ycg{i}", [128, 4, 1040], BF16) for i in range(2)]
        ps_h4 = [p.psum(f"p4_psh{i}", [128, 512]) for i in range(2)]
        ps_y4 = [p.psum(f"p4_psy{i}", [128, 512]) for i in range(2)]
        d_w1 = [p.ctr("p4_dw10"), p.ctr("p4_dw11")]

        def ldw24(e):
            e.wait(d_st, ST_XE4)
            e.dma_start(out=w2_sb[:], in_=w2_h.rearrange("(j q) f -> q j f", q=128), sem=d_w[1])
        p.sync(ldw24)
        LD_W24 = d_w[1].n

        def ldwg4(e):
            e.wait(d_st, ST_XE4)
            e.dma_start(out=wg4_tm2[:], in_=mypay4_hbm[:, 0:2].rearrange("(s q) f -> q s f", q=128), sem=d_ld)
        p.sync(ldwg4)
        LD_WG4 = d_ld.n

        for cg in range(8):
            bb = cg % 2

            def ldxf4(e, cg=cg, bb=bb):
                e.wait(d_st, ST_XE4)
                if cg >= 2:
                    e.wait(t)
                for j in range(8):
                    e.dma_start_transpose(xfc4[bb][:, j, :], xe4_hbm[cg * 512:(cg + 1) * 512, j * 128:(j + 1) * 128], sem=d_tr[bb])
            p.sync(ldxf4)
            LD_XF4 = d_tr[bb].n

            for hh in range(4):
                wb = hh % 2

                def ldw1(e, cg=cg, hh=hh, wb=wb):
                    e.wait(d_st, ST_XE4)
                    if cg > 0 or hh > 1:
                        e.wait(t)
                    e.dma_start(out=w1sl[wb][:], in_=w1_h[:, hh * 1024:(hh + 1) * 1024].rearrange("(j q) f -> q j f", q=128), sem=d_w1[wb])
                p.sync(ldw1)
                LW1 = d_w1[wb].n

                for hf in range(8):
                    hfg = hh * 8 + hf

                    def mmh4(e, cg=cg, hh=hh, hf=hf, bb=bb, wb=wb, LD_XF4=LD_XF4, LW1=LW1):
                        e.wait(d_tr[bb], LD_XF4)
                        e.wait(d_w1[wb], LW1)
                        if cg > 0 or hh > 0 or hf > 0:
                            e.wait(a)
                        for j in range(8):
                            e.matmul(ps_h4[hf % 2][:], w1sl[wb][:, j, hf * 128:(hf + 1) * 128], xfc4[bb][:, j, :], start=(j == 0), stop=(j == 7))
                    p.tensor(mmh4)

                    def evh4(e, hfg=hfg, hf=hf):
                        e.wait(t)
                        e.activation(h_bf4[:, hfg, :], ps_h4[hf % 2][:], AF.Relu)
                    p.scalar(evh4)
            A_H4 = a.n

            for tt_i in range(4):
                for ng in range(2):
                    gi = tt_i * 2 + ng

                    def mmy4(e, cg=cg, tt_i=tt_i, ng=ng, gi=gi, A_H4=A_H4, LD_W24=LD_W24):
                        e.wait(a, A_H4)
                        e.wait(d_w[1], LD_W24)
                        if cg * 8 + gi >= 2:
                            e.wait(v, YEV[1000 + cg * 8 + gi - 2])
                        for jf in range(32):
                            e.matmul(ps_y4[gi % 2][:], h_bf4[:, jf, tt_i * 128:(tt_i + 1) * 128], w2_sb[:, jf, ng * 512:(ng + 1) * 512], start=(jf == 0), stop=(jf == 31))
                    p.tensor(mmy4)

                    def evy4(e, cg=cg, tt_i=tt_i, ng=ng, gi=gi, bb=bb, LD_WG4=LD_WG4):
                        e.wait(t)
                        e.wait(d_ld, LD_WG4)
                        if cg >= 2 and gi == 0:
                            e.wait(d_y[bb])
                        e.tensor_scalar_mul(Y_cg4[bb][:, tt_i, ng * 512:(ng + 1) * 512], ps_y4[gi % 2][:], wg4_tm2[:, cg * 4 + tt_i, 1:2])
                        e.fence()
                        YEV[1000 + cg * 8 + gi] = p.ctr("v").n
                    p.vector(evy4)

            def vecho4(e, cg=cg, bb=bb):
                ycf = Y_cg4[bb].bitcast(F32)
                for tt_i in range(4):
                    e.tensor_copy(ycf[:, tt_i, 512:513], wg4_tm2[:, cg * 4 + tt_i, 0:1])
                e.memset(ycf[:, :, 513:520], 0.0)
                e.fence()
            p.vector(vecho4)
            V_ECHO4 = v.n

            def sty4(e, cg=cg, bb=bb, V_ECHO4=V_ECHO4):
                e.wait(v, V_ECHO4)
                e.dma_start(out=y_in4.rearrange("(s q) f -> q s f", q=128)[:, cg * 4:(cg + 1) * 4, :], in_=Y_cg4[bb][:], sem=d_y[bb])
            p.scalar(sty4)
        p.scope = old

    def ag_y4(e):
        e.wait(d_y[0])
        e.wait(d_y[1])
        e.collective_compute("AllGather", ALU.bypass, replica_groups=[list(range(NCORES))],
                             ins=[y_in4], outs=[y_all4[0:8 * TLOC, :]])
    p.gpsimd(ag_y4)
    CC_Y4 = g.n

    with ExitStack() as fsc4:
        old = p.scope; p.scope = fsc4
        slab4 = [p.sbuf(f"p4_slab{i}", [128, 8, 1040], BF16) for i in range(2)]
        yf4 = [p.sbuf(f"p4_yf{i}", [128, 8, D], F32) for i in range(2)]
        d_sl4 = [p.ctr("p4_dsl0"), p.ctr("p4_dsl1")]
        idxs4b = [p.sbuf(f"p4_idxs{i}", [128, 8], I32) for i in range(2)]
        GSC4, V_SL4 = {}, {}
        dsc8 = p.sbuf("p4_dsc8", [128, 8], F32)
        ps_bc4 = p.psum("p4_psbc", [128, 16])

        def tbc4(e):
            e.wait(v, V_FOLD4)
            e.matmul(ps_bc4[:], ones_col[:], oge4_row[:], start=True, stop=True)
        p.tensor(tbc4)
        T_BC4 = t.n

        def vbc4(e):
            e.wait(t, T_BC4)
            e.tensor_copy(ogbc4[:], ps_bc4[:])
            e.fence()
            for ee in range(8):
                e.tensor_scalar(idx644[:, ee * 8:(ee + 1) * 8], ccol64[:, ee * 8:(ee + 1) * 8],
                                ogbc4[:, ee:ee + 1], 0.0, ALU.add, ALU.add)
            e.fence()
            e.tensor_copy(idx644_i[:], idx644[:])
            e.fence()
        p.vector(vbc4)
        V_BC4 = v.n

        for ee in range(8):
            cb = ee % 2

            def gslab4(e, ee=ee, cb=cb):
                e.wait(g, CC_Y4)
                e.wait(v, V_BC4)
                if ee >= 2:
                    e.wait(d_gg, GSC4[ee - 2])
                for k in range(8):
                    e.indirect_dma_start(
                        out=slab4[cb][:, k, :], out_offset=None,
                        in_=y_all4, in_offset=IndirectOffsetOnAxis(idx644_i[:, ee * 8 + k:ee * 8 + k + 1], 0),
                        sem=d_sl4[cb])
            p.gpsimd(gslab4)
            GSLAB4 = d_sl4[cb].n

            def vslab4(e, ee=ee, cb=cb, GSLAB4=GSLAB4):
                e.wait(d_sl4[cb], GSLAB4)
                e.tensor_copy(yf4[cb][:], slab4[cb][:, :, 0:D])
                e.fence()
                e.tensor_copy(gidr4[:], slab4[cb].bitcast(F32)[:, :, 512:513].rearrange("q s o -> q (s o)"))
                e.fence()
                e.tensor_scalar(tlr4[:], gidr4[:], baset[:, 0:1], 0.0, ALU.subtract, ALU.add)
                e.fence()
                e.tensor_scalar(v1m4[:], tlr4[:], -0.5, 0.0, ALU.is_gt, ALU.add)
                e.fence()
                e.tensor_scalar(v2m4[:], tlr4[:], float(TLOC) - 0.5, 0.0, ALU.is_lt, ALU.add)
                e.fence()
                e.tensor_tensor(v1m4[:], v1m4[:], v2m4[:], ALU.mult)
                e.fence()
                e.tensor_scalar(v2m4[:], riota[:], ogbc4[:, 8 + ee:9 + ee], 0.0, ALU.is_lt, ALU.add)
                e.fence()
                e.tensor_tensor(v1m4[:], v1m4[:], v2m4[:], ALU.mult)
                e.fence()
                e.tensor_tensor(gidr4[:], tlr4[:], v1m4[:], ALU.mult)
                e.fence()
                e.tensor_scalar(v2m4[:], v1m4[:], 1.0, -1.0, ALU.subtract, ALU.mult)
                e.fence()
                e.tensor_scalar(dsc8[:], dumpt[:], float(8 * ee), 0.0, ALU.add, ALU.add)
                e.fence()
                e.tensor_tensor(v2m4[:], v2m4[:], dsc8[:], ALU.mult)
                e.fence()
                e.tensor_tensor(gidr4[:], gidr4[:], v2m4[:], ALU.add)
                e.fence()
                e.tensor_copy(idxs4b[cb][:], gidr4[:])
                e.fence()
            p.vector(vslab4)
            VSLAB4 = v.n

            def gsc4(e, ee=ee, cb=cb, VSLAB4=VSLAB4):
                e.wait(v, VSLAB4)
                e.wait(d_z, ST_Z4)
                for k in range(8):
                    e.indirect_dma_start(
                        out=acc4_hbm, out_offset=IndirectOffsetOnAxis(idxs4b[cb][:, k:k + 1], 0),
                        in_=yf4[cb][:, k, :], in_offset=None, compute_op=ALU.add,
                        sem=d_gg)
                GSC4[ee] = d_gg.n
                e.wait(d_gg)
            p.gpsimd(gsc4)
        p.scope = old
    GD_COMB4 = d_gg.n

    with ExitStack() as xsc4:
        old = p.scope; p.scope = xsc4
        xtile4 = [p.sbuf(f"p4_xt{i}", [128, D], F32) for i in range(2)]
        atile4 = [p.sbuf(f"p4_at{i}", [128, D], F32) for i in range(2)]
        d_x4 = [p.ctr("p4_dx40"), p.ctr("p4_dx41")]
        d_lx4 = [p.ctr("p4_dlx0"), p.ctr("p4_dlx1")]
        for i in range(32):
            bb = i % 2

            def ldx4(e, i=i, bb=bb):
                e.wait(d_gg, GD_COMB4)
                if i >= 2:
                    e.wait(d_x4[bb])
                e.dma_start(out=xtile4[bb][:], in_=x3_hbm[i * 128:(i + 1) * 128, :], sem=d_lx4[bb])
                e.dma_start(out=atile4[bb][:], in_=acc4_hbm[i * 128:(i + 1) * 128, :], sem=d_lx4[bb])
            p.sync(ldx4)
            LX4 = d_lx4[bb].n

            def vx4(e, i=i, bb=bb, LX4=LX4):
                e.wait(d_lx4[bb], LX4)
                e.tensor_tensor(xtile4[bb][:], xtile4[bb][:], atile4[bb][:], ALU.add)
                e.fence()
            p.vector(vx4)
            VX4 = v.n

            def stx4(e, i=i, bb=bb, VX4=VX4):
                e.wait(v, VX4)
                e.dma_start(out=x4_hbm[i * 128:(i + 1) * 128, :], in_=xtile4[bb][:], sem=d_x4[bb])
            p.scalar(stx4)
        p.scope = old

    p.scope = _oldscope
    _p4s.close()


def build_all(upto=4):
    _reset_markers()
    nc = bass.Bass("TRN2", target_bir_lowering=False, debug=False)
    ctx = ExitStack()
    blk = ctx.enter_context(nc.Block())
    p = P(nc, ctx, blk)

    io = {}
    V = {}
    io["x_in"] = nc.dram_tensor("x_in", [TLOC, D], F32, kind="ExternalInput").ap()
    for nm, shp in [
        ("l0_norm_w", [1, D]), ("l0_in_w", [D, 2 * D]), ("l0_in_b", [1, 2 * D]),
        ("l0_conv_w", [4, D]), ("l0_conv_b", [1, D]), ("l0_param_w", [D, 3 * D]),
        ("l0_param_b", [1, 3 * D]), ("l0_skip", [1, D]), ("l0_out_w", [D, D]),
        ("l0_out_b", [1, D]), ("l0_inb_fm", [128, 16]), ("l0_convw_fm", [128, 8, 4]),
        ("l0_convb_fm", [128, 8]), ("l0_parb_fm", [128, 24]), ("l0_skip_fm", [128, 8]),
        ("l0_normw_fm", [128, 8]),
    ]:
        V[nm] = nc.dram_tensor(nm, shp, F32, kind="ExternalInput").ap()
    io["x_out"] = nc.dram_tensor("x_out", [TLOC, D], F32, kind="ExternalOutput").ap()
    io["x1_hbm"] = nc.dram_tensor("x1_hbm", [TLOC, D], F32, kind="Internal").ap()
    io["xr_hbm"] = nc.dram_tensor("xr_hbm", [TLOC, D], BF16, kind="Internal").ap()
    io["stats_hbm"] = nc.dram_tensor("stats_hbm", [TLOC, 2], F32, kind="Internal").ap()
    io["w_inw_bf"] = nc.dram_tensor("w_inw_bf", [D, 2 * D], BF16, kind="Internal").ap()
    io["w_parw_bf"] = nc.dram_tensor("w_parw_bf", [D, 3 * D], BF16, kind="Internal").ap()

    if upto >= 2:
        for nm, shp in [
            ("wgt_fm", [128, 8, 8]), ("gid_c", [128, 32]), ("riota_c", [128, 8]),
            ("base_c", [128, 1]), ("mask8_c", [8, 8]), ("tl8_c", [1, 8]),
            ("epad8_c", [8, 1]), ("smod_c", [8, TLOC]), ("ccol64_c", [128, 64]),
            ("p2_w1t", [D, 2048]), ("p2_w3t", [D, 2048]), ("p2_w2t", [2048, D]),
        ]:
            V[nm] = nc.dram_tensor(nm, shp, F32, kind="ExternalInput").ap()
        V["me_i"] = nc.dram_tensor("me_i", [1, 2], I32, kind="ExternalInput").ap()
        io["x2_hbm"] = nc.dram_tensor("x2_hbm", [TLOC, D], F32, kind="Internal").ap()
        io["xn2_hbm"] = nc.dram_tensor("xn2_hbm", [TLOC, D], BF16, kind="Internal").ap()
        io["p2_xall"] = nc.dram_tensor("p2_xall", [8 * TLOC, D], BF16, kind="Internal").ap()
        io["p2_scin"] = nc.dram_tensor("p2_scin", [16, SEQ], F32, kind="Internal").ap()
        io["p2_scall"] = nc.dram_tensor("p2_scall", [8, 16, SEQ], F32, kind="Internal").ap()
        io["p2_payin"] = nc.dram_tensor("p2_payin", [PAYR, 16], F32, kind="Internal").ap()
        io["p2_payar"] = nc.dram_tensor("p2_payar", [PAYR, 16], F32, kind="Internal").ap()
        io["p2_mypay"] = nc.dram_tensor("p2_mypay", [TLOC, 16], F32, kind="Internal").ap()
        io["p2_xe"] = nc.dram_tensor("p2_xe", [TLOC, D], BF16, kind="Internal").ap()
        io["p2_yin"] = nc.dram_tensor("p2_yin", [TLOC, 1040], BF16, kind="Internal").ap()
        io["p2_yall"] = nc.dram_tensor("p2_yall", [8 * TLOC + CPAY, 1040], BF16, kind="Internal").ap()
        io["p2_acc"] = nc.dram_tensor("p2_acc", [TLOC + 128, D], F32, kind="Internal").ap()
        io["p2_dw"] = nc.dram_tensor("p2_dw", [2, 8, TLOC], F32, kind="Internal").ap()
        io["w_w1t_bf"] = nc.dram_tensor("w_w1t_bf", [D, 2048], BF16, kind="Internal").ap()
        io["w_w3t_bf"] = nc.dram_tensor("w_w3t_bf", [D, 2048], BF16, kind="Internal").ap()
        io["w_w2t_bf"] = nc.dram_tensor("w_w2t_bf", [2048, D], BF16, kind="Internal").ap()

    if upto >= 3 and upto != 15:
        for nm, shp in [
            ("p3_wqkv", [D, 1536]), ("p3_wo", [D, D]),
            ("rope_cos_c", [SEQ, 32]), ("rope_sin_c", [SEQ, 32]),
            ("masks_c", [128, 3, 128]),
        ]:
            V[nm] = nc.dram_tensor(nm, shp, F32, kind="ExternalInput").ap()
        io["x3_hbm"] = nc.dram_tensor("x3_hbm", [TLOC, D], F32, kind="Internal").ap()
        io["xn3_hbm"] = nc.dram_tensor("xn3_hbm", [TLOC, D], BF16, kind="Internal").ap()
        io["p3_qk"] = nc.dram_tensor("p3_qk", [TLOC, 1280], BF16, kind="Internal").ap()
        io["p3_y"] = nc.dram_tensor("p3_y", [TLOC, D], BF16, kind="Internal").ap()
        io["w_qkv_bf"] = nc.dram_tensor("w_qkv_bf", [D, 1536], BF16, kind="Internal").ap()
        io["w_wo_bf"] = nc.dram_tensor("w_wo_bf", [D, D], BF16, kind="Internal").ap()

    if upto >= 4 and upto != 15:
        for nm, shp in [
            ("p4_wr_fm", [128, 8, 8]), ("erow8_c", [128, 8]), ("cmlt_c", [8, 8]),
            ("idn_c", [128, 128]), ("epadL_c", [8, 1]), ("mebase4_c", [8, 1]),
            ("cmltc_c", [8, 1]),
            ("p4_w1", [D, 4096]), ("p4_w2", [4096, D]),
        ]:
            V[nm] = nc.dram_tensor(nm, shp, F32, kind="ExternalInput").ap()
        io["x4_hbm"] = nc.dram_tensor("x4_hbm", [TLOC, D], F32, kind="Internal").ap()
        io["xn4_hbm"] = nc.dram_tensor("xn4_hbm", [TLOC, D], BF16, kind="Internal").ap()
        io["p4_xall"] = nc.dram_tensor("p4_xall", [8 * TLOC, D], BF16, kind="Internal").ap()
        io["p4_ptok"] = nc.dram_tensor("p4_ptok", [TLOC, 1], F32, kind="Internal").ap()
        io["p4_cntin"] = nc.dram_tensor("p4_cntin", [1, 8], F32, kind="Internal").ap()
        io["p4_cntall"] = nc.dram_tensor("p4_cntall", [8, 8], F32, kind="Internal").ap()
        io["p4_payin"] = nc.dram_tensor("p4_payin", [PAYR4, 16], F32, kind="Internal").ap()
        io["p4_payar"] = nc.dram_tensor("p4_payar", [PAYR4, 16], F32, kind="Internal").ap()
        io["p4_mypay"] = nc.dram_tensor("p4_mypay", [TLOC, 16], F32, kind="Internal").ap()
        io["p4_xe"] = nc.dram_tensor("p4_xe", [TLOC, D], BF16, kind="Internal").ap()
        io["p4_yin"] = nc.dram_tensor("p4_yin", [TLOC, 1040], BF16, kind="Internal").ap()
        io["p4_yall"] = nc.dram_tensor("p4_yall", [8 * TLOC + CPAY, 1040], BF16, kind="Internal").ap()
        io["p4_acc"] = nc.dram_tensor("p4_acc", [TLOC + 128, D], F32, kind="Internal").ap()
        io["p4_dw"] = nc.dram_tensor("p4_dw", [1, 8, TLOC], F32, kind="Internal").ap()
        io["w_l1w1_bf"] = nc.dram_tensor("w_l1w1_bf", [D, 4096], BF16, kind="Internal").ap()
        io["w_l1w2_bf"] = nc.dram_tensor("w_l1w2_bf", [4096, D], BF16, kind="Internal").ap()

    d_g = p.ctr("d_g")

    def stage(e):
        e.dma_start(out=io["w_inw_bf"], in_=V["l0_in_w"], sem=d_g)
        e.dma_start(out=io["w_parw_bf"], in_=V["l0_param_w"], sem=d_g)
        if upto >= 2:
            e.dma_start(out=io["w_w1t_bf"], in_=V["p2_w1t"], sem=d_g)
            e.dma_start(out=io["w_w3t_bf"], in_=V["p2_w3t"], sem=d_g)
            e.dma_start(out=io["w_w2t_bf"], in_=V["p2_w2t"], sem=d_g)
        if upto >= 3 and upto != 15:
            e.dma_start(out=io["w_qkv_bf"], in_=V["p3_wqkv"], sem=d_g)
            e.dma_start(out=io["w_wo_bf"], in_=V["p3_wo"], sem=d_g)
        if upto >= 4 and upto != 15:
            e.dma_start(out=io["w_l1w1_bf"], in_=V["p4_w1"], sem=d_g)
            e.dma_start(out=io["w_l1w2_bf"], in_=V["p4_w2"], sem=d_g)
        e.wait(d_g)
    p.gpsimd(stage)

    emit_l0_attn(p, io, V)
    if upto >= 2:
        if upto == 15:
            io["dbg"] = nc.dram_tensor("dbg", [8, 64], F32, kind="ExternalOutput").ap()
            emit_l0_ffn(p, io, V, dbg=True)
        else:
            emit_l0_ffn(p, io, V)
    if upto >= 3 and upto != 15:
        emit_l1_attn(p, io, V)
    if upto >= 4 and upto != 15:
        emit_l1_ffn(p, io, V)

    d_fin = p.ctr("d_fin")
    if upto >= 4 and upto != 15:
        src = io["x4_hbm"]
    elif upto >= 3 and upto != 15:
        src = io["x3_hbm"]
    elif upto >= 2 and upto != 15:
        src = io["x2_hbm"]
    else:
        src = io["x1_hbm"]

    def fin(e):
        for ctr in list(p.ctrs.values()):
            if ctr.n:
                e.wait(ctr)
        e.dma_start(out=io["x_out"], in_=src, sem=d_fin)
        e.wait(d_fin)
    p.sync(fin)

    ctx.close()
    return nc


def _shard_inputs(inputs, upto=4):
    x = np.ascontiguousarray(inputs["L0_attn_x"].reshape(NCORES, TLOC, D))
    base = {
        "l0_norm_w": np.ascontiguousarray(inputs["L0_attn_norm_w"].reshape(1, D)),
        "l0_in_w": np.ascontiguousarray(inputs["L0_attn_in_w"]),
        "l0_in_b": np.ascontiguousarray(inputs["L0_attn_in_b"].reshape(1, 2 * D)),
        "l0_conv_w": np.ascontiguousarray(inputs["L0_attn_conv_w"]),
        "l0_conv_b": np.ascontiguousarray(inputs["L0_attn_conv_b"].reshape(1, D)),
        "l0_param_w": np.ascontiguousarray(inputs["L0_attn_param_w"]),
        "l0_param_b": np.ascontiguousarray(inputs["L0_attn_param_b"].reshape(1, 3 * D)),
        "l0_skip": np.ascontiguousarray(inputs["L0_attn_skip"].reshape(1, D)),
        "l0_out_w": np.ascontiguousarray(inputs["L0_attn_out_w"]),
        "l0_out_b": np.ascontiguousarray(inputs["L0_attn_out_b"].reshape(1, D)),
    }
    def fm_vec(vrow, nj):
        return np.ascontiguousarray(vrow.reshape(nj, 128).T)
    base["l0_inb_fm"] = fm_vec(inputs["L0_attn_in_b"], 16)
    base["l0_convb_fm"] = fm_vec(inputs["L0_attn_conv_b"], 8)
    base["l0_parb_fm"] = fm_vec(inputs["L0_attn_param_b"], 24)
    base["l0_skip_fm"] = fm_vec(inputs["L0_attn_skip"], 8)
    base["l0_normw_fm"] = fm_vec(inputs["L0_attn_norm_w"], 8)
    base["l0_convw_fm"] = np.ascontiguousarray(
        inputs["L0_attn_conv_w"].reshape(4, 8, 128).transpose(2, 1, 0))
    if upto >= 2:
        Wg = inputs["L0_ffn_Wg"]
        base["wgt_fm"] = np.ascontiguousarray(Wg.T.reshape(8, 128, 8).transpose(1, 0, 2))
        base["tl8_c"] = (np.arange(8, dtype=np.float32) * TLOC).reshape(1, 8)
        base["epad8_c"] = (np.arange(8, dtype=np.float32) * TLOC).reshape(8, 1)
        base["smod_c"] = np.broadcast_to(
            np.tile(np.arange(32, dtype=np.float32), TLOC // 32), (8, TLOC)).copy()
        base["ccol64_c"] = np.ascontiguousarray(
            (np.arange(128)[:, None] + 128 * np.tile(np.arange(8), 8)[None, :]).astype(np.float32))
        base["riota_c"] = np.ascontiguousarray(
            (np.arange(128)[:, None] + 128 * np.arange(8)[None, :]).astype(np.float32))
    if upto >= 3:
        base["p3_wqkv"] = np.ascontiguousarray(np.concatenate(
            [inputs["L1_attn_Wq"], inputs["L1_attn_Wk"], inputs["L1_attn_Wv"]], axis=1))
        base["p3_wo"] = np.ascontiguousarray(inputs["L1_attn_Wo"])
        inv_freq = (1.0 / 10000.0 ** (np.arange(0, 64, 2, dtype=np.float32) / 64.0)).astype(np.float32)
        freqs = np.arange(SEQ, dtype=np.float32)[:, None] * inv_freq[None, :]
        base["rope_cos_c"] = np.cos(freqs).astype(np.float32)
        base["rope_sin_c"] = np.sin(freqs).astype(np.float32)
        kk = np.arange(128)[:, None]; qq = np.arange(128)[None, :]
        mT = (kk <= qq).astype(np.float32)
        mB2 = (kk >= qq + 1).astype(np.float32)
        mBG = np.maximum(mB2, (kk < 64).astype(np.float32))
        base["masks_c"] = np.ascontiguousarray(np.stack([mT, mB2, mBG], axis=1))
    if upto >= 4:
        Wr = inputs["L1_ffn_Wr"]
        base["p4_wr_fm"] = np.ascontiguousarray(Wr.reshape(8, 128, 8).transpose(1, 0, 2))
        base["erow8_c"] = np.broadcast_to(np.arange(8, dtype=np.float32), (128, 8)).copy()
        base["idn_c"] = np.eye(128, dtype=np.float32)
        base["epadL_c"] = (np.arange(8, dtype=np.float32) * 1024.0).reshape(8, 1)
    maps = []
    for c in range(NCORES):
        m = dict(base, x_in=x[c])
        if upto >= 2:
            m["gid_c"] = np.ascontiguousarray(
                (c * TLOC + np.arange(128)[:, None] * 32 + np.arange(32)[None, :]).astype(np.float32))
            m["base_c"] = np.full((128, 1), float(c * TLOC), np.float32)
            mk = np.zeros((8, 8), np.float32); mk[:, c] = 1.0
            m["mask8_c"] = mk
            m["me_i"] = np.array([[c, 64 * 1024 + 32 + c]], np.int32)
            m["p2_w1t"] = np.ascontiguousarray(inputs["L0_ffn_W1"][c].T)
            m["p2_w3t"] = np.ascontiguousarray(inputs["L0_ffn_W3"][c].T)
            m["p2_w2t"] = np.ascontiguousarray(inputs["L0_ffn_W2"][c].T)
        if upto >= 4:
            m["p4_w1"] = np.ascontiguousarray(inputs["L1_ffn_W1"][c])
            m["p4_w2"] = np.ascontiguousarray(inputs["L1_ffn_W2"][c])
            cm = np.zeros((8, 8), np.float32); cm[:, :c] = 1.0
            m["cmlt_c"] = cm
            m["mebase4_c"] = np.full((8, 1), float(c * 8192), np.float32)
            m["cmltc_c"] = (np.arange(8) < c).astype(np.float32).reshape(8, 1)
        maps.append(m)
    return maps


def kernel(**inputs):
    nc = build_all()
    in_maps = _shard_inputs(inputs)
    res = run_bass_kernel_spmd(nc, in_maps, core_ids=list(range(NCORES)))
    out = np.concatenate([r["x_out"] for r in res.results], axis=0)
    return np.ascontiguousarray(out.reshape(16, SEQ, D))


# version 05
# speedup vs baseline: 1.0097x; 1.0097x over previous
import numpy as np
import ml_dtypes
import bass_rust
import concourse.bass as bass
import concourse.mybir as mybir
from concourse.bass_utils import run_bass_kernel_spmd
from concourse.bass_types import DynSlice
from concourse.bass_core_rust import IndirectOffsetOnAxis
from contextlib import ExitStack

F32 = mybir.dt.float32
BF16 = mybir.dt.bfloat16
I16 = mybir.dt.int16
I32 = mybir.dt.int32
U8 = mybir.dt.uint8
U32 = mybir.dt.uint32
AF = mybir.ActivationFunctionType
ALU = mybir.AluOpType
AX = mybir.AxisListType

NCORES = 8
TLOC = 4096
D = 1024
SEQ = 2048
CHUNK = 512
NCHUNK = TLOC // CHUNK

_DMA_OPS = ("dma_start", "dma_start_transpose", "indirect_dma_start")
_PASS = ("wait_ge", "value_load", "load_library", "register", "reg_load",
         "alloc_register", "free_register", "nop")


class Ctr:
    def __init__(self, sem):
        self.sem, self.n = sem, 0


class Eng:

    def __init__(self, raw, c):
        self.raw, self.c = raw, c

    def __getattr__(self, nm):
        fn = getattr(self.raw, nm)
        if nm in _PASS:
            return fn
        if nm in _DMA_OPS:
            def dcall(*a, sem=None, **k):
                inst = fn(*a, **k)
                assert sem is not None, f"dma {nm} needs sem="
                sem.n += 16
                return inst.then_inc(sem.sem, 16)
            return dcall

        def ccall(*a, **k):
            inst = fn(*a, **k)
            self.c.n += 1
            return inst.then_inc(self.c.sem, 1)
        return ccall

    def fence(self):
        self.raw.wait_ge(self.c.sem, self.c.n)

    def wait(self, ctr, n=None):
        self.raw.wait_ge(ctr.sem, ctr.n if n is None else n)


class P:

    def __init__(self, nc, ctx, blk):
        self.nc, self.ctx, self.blk = nc, ctx, blk
        self.ctrs = {}
        self.shared = {}
        self.scope = ctx
        for nm in ("v", "a", "t", "g"):
            self.ctr(nm)

    def ctr(self, name):
        if name not in self.ctrs:
            h = self.ctx.enter_context(self.nc.semaphore(name))
            self.ctrs[name] = Ctr(h)
        return self.ctrs[name]

    def sbuf(self, name, shape, dt):
        return self.scope.enter_context(self.nc.sbuf_tensor(name, shape, dt))

    def psum(self, name, shape, dt=F32):
        return self.scope.enter_context(self.nc.psum_tensor(name, shape, dt))

    def sync(self, fn):
        self.blk.sync(lambda e: fn(Eng(e, self.ctr("sy"))))

    def vector(self, fn):
        self.blk.vector(lambda e: fn(Eng(e, self.ctr("v"))))

    def scalar(self, fn):
        self.blk.scalar(lambda e: fn(Eng(e, self.ctr("a"))))

    def tensor(self, fn):
        self.blk.tensor(lambda e: fn(Eng(e, self.ctr("t"))))

    def gpsimd(self, fn):
        self.blk.gpsimd(lambda e: fn(Eng(e, self.ctr("g"))))

    def barrier(self, engines="vatg"):
        vals = [(c, c.n) for c in self.ctrs.values()]
        emap = {"v": self.vector, "a": self.scalar, "t": self.tensor,
                "g": self.gpsimd, "s": self.sync}
        for ch in engines:
            def mk(e, vals=tuple(vals)):
                for ctr, n in vals:
                    if n:
                        e.wait(ctr, n)
            emap[ch](mk)


MM_T, YEV, A_SQ, V_SQRD, SLAB_T, PS_EV = {}, {}, {}, {}, {}, {}
V_UBF, V_CONVD, V_YD, V_RES, V_RESNG, MM_OUT = {}, {}, {}, {}, {}, {}


def _reset_markers():
    for dd in (MM_T, YEV, A_SQ, V_SQRD, SLAB_T, PS_EV,
               V_UBF, V_CONVD, V_YD, V_RES, V_RESNG, MM_OUT):
        dd.clear()




def emit_l0_attn(p, io, V):
    nc = p.nc
    _p1s = ExitStack()
    _oldscope = p.scope
    p.scope = _p1s
    x_in = io["x_in"]
    x1_hbm, xr_hbm, stats_hbm = io["x1_hbm"], io["xr_hbm"], io["stats_hbm"]
    inw_h, parw_h = io["w_inw_bf"], io["w_parw_bf"]

    outw = p.sbuf("p1_outw", [128, 8, D], BF16)
    wsl = [p.sbuf(f"p1_wsl{i}", [128, 8, 768], BF16) for i in range(2)]
    normw_f = p.sbuf("p1_normwf", [128, 8], F32)
    inb = p.sbuf("p1_inb", [128, 16], F32)
    convw = p.sbuf("p1_convw", [128, 8, 4], F32)
    convb = p.sbuf("p1_convb", [128, 8], F32)
    parb = p.sbuf("p1_parb", [128, 24], F32)
    skipw = p.sbuf("p1_skipw", [128, 8], F32)
    outb_row = p.sbuf("p1_outb", [1, D], BF16)
    ones_col = p.sbuf("p1_ones", [1, 128], BF16)

    v, a, t, g = p.ctr("v"), p.ctr("a"), p.ctr("t"), p.ctr("g")
    d_ld = p.ctr("d_ld")
    d_wv = p.ctr("d_wv")
    d_w = [p.ctr("d_w0"), p.ctr("d_w1")]
    d_tr = [p.ctr("d_tr0"), p.ctr("d_tr1")]
    d_xc = [p.ctr("d_xc0"), p.ctr("d_xc1")]
    d_st = p.ctr("d_st")
    d_sx1 = [p.ctr("d_sx10"), p.ctr("d_sx11")]
    d_g = p.ctr("d_g")

    def loads(e):
        e.dma_start(out=inb[:], in_=V["l0_inb_fm"], sem=d_wv)
        e.dma_start(out=convw[:], in_=V["l0_convw_fm"], sem=d_wv)
        e.dma_start(out=convb[:], in_=V["l0_convb_fm"], sem=d_wv)
        e.dma_start(out=parb[:], in_=V["l0_parb_fm"], sem=d_wv)
        e.dma_start(out=skipw[:], in_=V["l0_skip_fm"], sem=d_wv)
        e.dma_start(out=normw_f[:], in_=V["l0_normw_fm"], sem=d_wv)
    p.sync(loads)
    LD_W = d_wv.n

    def gload(e):
        e.dma_start(out=outb_row[:], in_=V["l0_out_b"], sem=d_g)
        e.dma_start(out=outw[:], in_=V["l0_out_w"].rearrange("(j q) e -> q j e", q=128), sem=d_g)
    p.gpsimd(gload)
    GD_OUTB = d_g.n

    with ExitStack() as actx:
        def asb(name, shape, dt):
            return actx.enter_context(nc.sbuf_tensor(name, shape, dt))
        xs = asb("p1_xs", [128, 8, D], F32)
        xrs = asb("p1_xrs", [128, 8, D], BF16)
        sqs = asb("p1_sqs", [128, 8, D], F32)
        bnag = asb("p1_bnag", [128, 8, 2], F32)
        stA = asb("p1_stA", [128, 8, 2], F32)
        z1 = asb("p1_z1", [128, 8], F32)
        z2 = asb("p1_z2", [128, 8], F32)
        z3 = asb("p1_z3", [128, 8], F32)
        s1 = asb("p1_s1", [128, 8], F32)
        s3 = asb("p1_s3", [128, 8], F32)
        Amul = asb("p1_Am", [128, 8], F32)
        rstd = asb("p1_rs", [128, 8], F32)

        def vinit(e):
            e.memset(ones_col[:], 1.0)
        p.vector(vinit)

        for grp in range(4):
            def ldx(e, grp=grp):
                if grp > 0:
                    e.wait(v)
                e.dma_start(out=xs[:], in_=x_in.rearrange("(q s) d -> q s d", q=128)[:, grp * 8:(grp + 1) * 8, :], sem=d_ld)
            p.sync(ldx)

            def v1(e, grp=grp):
                e.wait(d_ld)
                for k in range(8):
                    e.reduce_sum(bnag[:, k, 0:1], xs[:, k, :], axis=AX.X)
                e.fence()
            p.vector(v1)

            def v1b(e, grp=grp):
                for k in range(8):
                    e.tensor_tensor(sqs[:, k, :], xs[:, k, :], xs[:, k, :], ALU.mult)
                    e.fence()
                    e.reduce_sum(bnag[:, k, 1:2], sqs[:, k, :], axis=AX.X)
                e.fence()
                V_SQRD[(grp, 7)] = p.ctr("v").n
                e.tensor_scalar_mul(bnag[:, :, 0], bnag[:, :, 0], 1.0 / D)
                e.fence()
                e.tensor_scalar_mul(bnag[:, :, 1], bnag[:, :, 1], 1.0 / D)
                e.fence()
                e.scalar_tensor_tensor(z1[:], bnag[:, :, 0], -1.0, bnag[:, :, 0], ALU.mult, ALU.mult)
                e.fence()
                e.tensor_tensor(bnag[:, :, 1], bnag[:, :, 1], z1[:], ALU.add)
                e.fence()
                e.tensor_scalar_add(z1[:], bnag[:, :, 1], 1e-5)
                e.fence()
                e.reciprocal(z2[:], z1[:])
                e.fence()
                e.tensor_tensor(z2[:], bnag[:, :, 1], z2[:], ALU.mult)
                e.fence()
                e.tensor_scalar_add(z2[:], z2[:], 1e-6)
                e.fence()
                e.tensor_tensor(z3[:], z1[:], z2[:], ALU.mult)
                e.fence()
            p.vector(v1b)

            def a1(e, grp=grp):
                e.wait(v)
                e.activation(s3[:], z3[:], AF.Sqrt)
                e.activation(s1[:], z1[:], AF.Sqrt)
            p.scalar(a1)

            def v2(e, grp=grp):
                e.wait(a)
                e.wait(d_st)
                e.reciprocal(Amul[:], s3[:])
                e.reciprocal(rstd[:], s1[:])
                e.fence()
                e.tensor_scalar_add(stA[:, :, 0], rstd[:], 1.0)
                e.scalar_tensor_tensor(stA[:, :, 1], bnag[:, :, 0], -1.0, rstd[:], ALU.mult, ALU.mult)
                for k in range(8):
                    e.tensor_scalar(xrs[:, k, :], xs[:, k, :], bnag[:, k, 0:1], Amul[:, k:k + 1],
                                    ALU.subtract, ALU.mult)
                e.fence()
            p.vector(v2)

            def st1(e, grp=grp):
                e.wait(v)
                e.dma_start(out=xr_hbm.rearrange("(q s) d -> q s d", q=128)[:, grp * 8:(grp + 1) * 8, :], in_=xrs[:], sem=d_st)
                e.dma_start(out=stats_hbm.rearrange("(q s) c -> q s c", q=128)[:, grp * 8:(grp + 1) * 8, :], in_=stA[:], sem=d_st)
            p.scalar(st1)
    ST_PASSA = d_st.n
    p.barrier("vats")

    xr_fm = [p.sbuf(f"p1_xrfm{i}", [128, 8, CHUNK], BF16) for i in range(2)]
    u_pad = p.sbuf("p1_upad", [128, 8, CHUNK + 3], F32)
    tail = p.sbuf("p1_tail", [128, 8, 3], F32)
    uc = p.sbuf("p1_uc", [128, 8, CHUNK], F32)
    u_bf = p.sbuf("p1_ubf", [128, 8, CHUNK], BF16)
    gs_bf = p.sbuf("p1_gsbf", [128, 8, CHUNK], BF16)
    a_t = p.sbuf("p1_a", [128, 8, CHUNK], F32)
    bu_t = p.sbuf("p1_bu", [128, 8, CHUNK], F32)
    c_t = p.sbuf("p1_c", [128, 8, CHUNK], F32)
    stn = p.sbuf("p1_stn", [128, 8, CHUNK], F32)
    y_bf = p.sbuf("p1_ybf", [128, 8, CHUNK], BF16)
    carry = p.sbuf("p1_carry", [128, 8], F32)
    xc = [p.sbuf(f"p1_xc{i}", [128, D], F32) for i in range(2)]
    x1c = [p.sbuf(f"p1_x1c{i}", [128, D], F32) for i in range(2)]
    ab_t = p.sbuf("p1_ab", [128, 4, 2], F32)
    ps_a = [p.psum(f"p1_psa{i}", [128, 512]) for i in range(6)]
    ps_o = [p.psum(f"p1_pso{i}", [128, 512]) for i in range(2)]

    def vinit2(e):
        e.memset(carry[:], 0.0)
        e.memset(tail[:], 0.0)
        e.fence()
    p.vector(vinit2)

    SLABS = [("in", 0, 6), ("in", 6, 12), ("in", 12, 16),
             ("par", 0, 6), ("par", 6, 12), ("par", 12, 18), ("par", 18, 24)]

    for c in range(NCHUNK):
        t0 = c * CHUNK
        b = c % 2
        seq_start = (c % 4 == 0)

        def ldtr(e, c=c, t0=t0, b=b):
            e.wait(d_st, ST_PASSA)
            if c >= 2:
                e.wait(t)
            for j in range(8):
                e.dma_start_transpose(xr_fm[b][:, j, :], xr_hbm[t0:t0 + CHUNK, j * 128:(j + 1) * 128], sem=d_tr[b])
        p.sync(ldtr)

        def ldab(e, c=c, t0=t0):
            if c >= 1:
                e.wait(v)
            e.dma_start(out=ab_t[:], in_=stats_hbm[t0:t0 + CHUNK, :].rearrange("(k q) c -> q k c", q=128), sem=d_ld)
        p.sync(ldab)
        LD_AB = d_ld.n

        def vnw(e, c=c, b=b):
            e.wait(d_tr[b])
            e.wait(d_wv, LD_W)
            for j in range(8):
                e.tensor_scalar_mul(xr_fm[b][:, j, :], xr_fm[b][:, j, :], normw_f[:, j:j + 1])
            e.fence()
        p.vector(vnw)
        V_NW = v.n

        evac_plan = []
        for si, (kind, pf0, pf1) in enumerate(SLABS):
            sb = si % 2

            def ldsl(e, si=si, sb=sb, kind=kind, pf0=pf0, pf1=pf1, c=c):
                if si >= 2:
                    e.wait(t, SLAB_T[(c, si - 2)])
                elif c > 0:
                    e.wait(t, SLAB_T[(c - 1, len(SLABS) - 2 + si)])
                src = inw_h if kind == "in" else parw_h
                e.dma_start(out=wsl[sb][:, :, 0:(pf1 - pf0) * 128],
                            in_=src.rearrange("(j q) e -> q j e", q=128)[:, :, pf0 * 128:pf1 * 128], sem=d_w[sb])
            p.sync(ldsl)

            def mmsl(e, si=si, sb=sb, kind=kind, pf0=pf0, pf1=pf1, c=c, b=b):
                e.wait(d_w[sb])
                if kind == "in":
                    e.wait(v, V_NW)
                else:
                    e.wait(v, V_UBF[c])
                for pf in range(pf0, pf1):
                    gi = (pf if kind == "in" else 16 + pf)
                    pi = gi % 6
                    if gi >= 6:
                        e.wait(a, PS_EV[(c, gi - 6)])
                    elif c > 0:
                        e.wait(a, PS_EV[(c - 1, 34 + gi)])
                    for j in range(8):
                        e.matmul(ps_a[pi][:], wsl[sb][:, j, (pf - pf0) * 128:(pf - pf0 + 1) * 128],
                                 xr_fm[b][:, j, :] if kind == "in" else u_bf[:, j, :],
                                 start=(j == 0), stop=(j == 7))
                    MM_T[(c, gi)] = t.n
                SLAB_T[(c, si)] = t.n
            p.tensor(mmsl)

            def evsl(e, si=si, kind=kind, pf0=pf0, pf1=pf1, c=c):
                for pf in range(pf0, pf1):
                    gi = (pf if kind == "in" else 16 + pf)
                    pi = gi % 6
                    e.wait(t, MM_T[(c, gi)])
                    if kind == "in":
                        if pf < 8:
                            if c >= 1 and pf == 0:
                                e.wait(v, V_CONVD[c - 1])
                            e.activation(u_pad[:, pf, 3:], ps_a[pi][:], AF.Identity, bias=inb[:, pf:pf + 1])
                        else:
                            if c >= 1 and pf == 8:
                                e.wait(v, V_YD[c - 1])
                            e.activation(gs_bf[:, pf - 8, :], ps_a[pi][:], AF.Silu, bias=inb[:, pf:pf + 1])
                    else:
                        j = pf % 8
                        if pf < 8:
                            if c >= 1 and pf == 0:
                                e.wait(v, V_YD[c - 1])
                            e.activation(a_t[:, j, :], ps_a[pi][:], AF.Sigmoid, bias=parb[:, pf:pf + 1])
                        elif pf < 16:
                            e.activation(bu_t[:, j, :], ps_a[pi][:], AF.Tanh, bias=parb[:, pf:pf + 1])
                        else:
                            e.activation(c_t[:, j, :], ps_a[pi][:], AF.Sigmoid, bias=parb[:, pf:pf + 1])
                    PS_EV[(c, gi)] = a.n
            p.scalar(evsl)

            if si == 2:
                def convv(e, c=c, seq_start=seq_start):
                    e.wait(a)
                    if seq_start:
                        e.memset(u_pad[:, :, 0:3], 0.0)
                    else:
                        e.tensor_copy(u_pad[:, :, 0:3], tail[:])
                    e.fence()
                    for j in range(8):
                        e.tensor_scalar(uc[:, j, :], u_pad[:, j, 0:CHUNK], convw[:, j, 0:1], convb[:, j:j + 1], ALU.mult, ALU.add)
                    e.fence()
                    for k in range(1, 4):
                        for j in range(8):
                            e.scalar_tensor_tensor(uc[:, j, :], u_pad[:, j, k:k + CHUNK], convw[:, j, k:k + 1], uc[:, j, :], ALU.mult, ALU.add)
                        e.fence()
                    e.tensor_copy(tail[:], u_pad[:, :, CHUNK:CHUNK + 3])
                    e.fence()
                p.vector(convv)
                V_CONVD[c] = p.ctr("v").n

                def silua(e, c=c):
                    e.wait(v)
                    e.activation(uc[:], uc[:], AF.Silu)
                p.scalar(silua)

                def ubfv(e, c=c):
                    e.wait(a)
                    e.tensor_copy(u_bf[:], uc[:])
                    e.fence()
                p.vector(ubfv)
                V_UBF[c] = v.n

        def scanv(e, c=c, seq_start=seq_start):
            e.wait(a)
            if seq_start:
                e.memset(carry[:], 0.0)
                e.fence()
            for j in range(8):
                e.tensor_tensor(bu_t[:, j, :], bu_t[:, j, :], uc[:, j, :], ALU.mult)
            e.fence()
            for j in range(8):
                e.tensor_tensor_scan(stn[:, j, :], a_t[:, j, :], bu_t[:, j, :], carry[:, j:j + 1], ALU.mult, ALU.add)
                e.fence()
                e.tensor_copy(carry[:, j:j + 1], stn[:, j, CHUNK - 1:CHUNK])
                e.fence()
            for j in range(8):
                e.tensor_tensor(c_t[:, j, :], c_t[:, j, :], stn[:, j, :], ALU.mult)
            e.fence()
            for j in range(8):
                e.scalar_tensor_tensor(stn[:, j, :], uc[:, j, :], skipw[:, j:j + 1], c_t[:, j, :], ALU.mult, ALU.add)
            e.fence()
            for j in range(8):
                e.tensor_tensor(y_bf[:, j, :], stn[:, j, :], gs_bf[:, j, :], ALU.mult)
            e.fence()
        p.vector(scanv)
        V_YD[c] = v.n

        for tt_i in range(4):
            bb = tt_i % 2

            def ldxc(e, c=c, t0=t0, tt_i=tt_i, bb=bb):
                if c * 4 + tt_i >= 2:
                    e.wait(v, V_RES[(c * 4 + tt_i) - 2])
                e.dma_start(out=xc[bb][:], in_=x_in[t0 + tt_i * 128:t0 + (tt_i + 1) * 128, :], sem=d_xc[bb])
            p.sync(ldxc)

            def mmout(e, c=c, tt_i=tt_i):
                e.wait(v, V_YD[c])
                if c == 0 and tt_i == 0:
                    e.wait(d_g, GD_OUTB)
                for ng in range(2):
                    gi = tt_i * 2 + ng
                    if c * 8 + gi >= 2:
                        e.wait(v, V_RESNG[(c * 8 + gi) - 2])
                    for j in range(8):
                        e.matmul(ps_o[gi % 2][:], y_bf[:, j, tt_i * 128:(tt_i + 1) * 128],
                                 outw[:, j, ng * 512:(ng + 1) * 512], start=(j == 0), stop=False)
                    e.matmul(ps_o[gi % 2][:], ones_col[:], outb_row[0:1, ng * 512:(ng + 1) * 512],
                             start=False, stop=True)
                    MM_OUT[(c, gi)] = t.n
            p.tensor(mmout)

            def resv(e, c=c, tt_i=tt_i, bb=bb):
                e.wait(d_ld, LD_AB)
                e.wait(d_xc[bb])
                e.wait(d_sx1[bb])
                for ng in range(2):
                    gi = tt_i * 2 + ng
                    e.wait(t, MM_OUT[(c, gi)])
                    sl = slice(ng * 512, (ng + 1) * 512)
                    e.tensor_scalar(x1c[bb][:, sl], xc[bb][:, sl], ab_t[:, tt_i, 0:1], ab_t[:, tt_i, 1:2], ALU.mult, ALU.add)
                    e.fence()
                    e.tensor_tensor(x1c[bb][:, sl], x1c[bb][:, sl], ps_o[gi % 2][:], ALU.add)
                    e.fence()
                    V_RESNG[c * 8 + gi] = v.n
                V_RES[c * 4 + tt_i] = v.n
            p.vector(resv)

            def stx1(e, c=c, t0=t0, tt_i=tt_i, bb=bb):
                e.wait(v, V_RES[c * 4 + tt_i])
                e.dma_start(out=x1_hbm[t0 + tt_i * 128:t0 + (tt_i + 1) * 128, :], in_=x1c[bb][:], sem=d_sx1[bb])
            p.scalar(stx1)

    p.scope = _oldscope
    _p1s.close()




def emit_ln_plain(p, src_hbm, xn_hbm, pref, gates=()):
    nc = p.nc
    v, a = p.ctr("v"), p.ctr("a")
    d_ld = p.ctr("d_ld")
    d_st = p.ctr("d_st")
    with ExitStack() as sc:
        old = p.scope
        p.scope = sc
        xs = p.sbuf(pref + "_xs", [128, 8, D], F32)
        xrs = p.sbuf(pref + "_xrs", [128, 8, D], BF16)
        sqs = p.sbuf(pref + "_sqs", [128, 8, D], F32)
        mv = p.sbuf(pref + "_mv", [128, 8, 2], F32)
        z1 = p.sbuf(pref + "_z1", [128, 8], F32)
        s1 = p.sbuf(pref + "_s1", [128, 8], F32)
        Am = p.sbuf(pref + "_Am", [128, 8], F32)
        M = {}
        for grp in range(4):
            def ldx(e, grp=grp):
                for gnm in gates:
                    e.wait(p.ctr(gnm))
                if grp > 0:
                    e.wait(v)
                e.dma_start(out=xs[:], in_=src_hbm.rearrange("(q s) d -> q s d", q=128)[:, grp * 8:(grp + 1) * 8, :], sem=d_ld)
            p.sync(ldx)

            def v1(e, grp=grp):
                e.wait(d_ld)
                if grp > 0:
                    e.wait(d_st)
                for k in range(8):
                    e.reduce_sum(mv[:, k, 0:1], xs[:, k, :], axis=AX.X)
                e.fence()
            p.vector(v1)

            def asq(e, grp=grp):
                e.wait(d_ld)
                e.wait(v)
                for k in range(8):
                    e.tensor_tensor(sqs[:, k, :], xs[:, k, :], xs[:, k, :], ALU.mult)
                e.fence()
                M[("sq", grp)] = p.ctr("v").n
            p.vector(asq)

            def v2(e, grp=grp):
                e.wait(v, M[("sq", grp)])
                for k in range(8):
                    e.reduce_sum(mv[:, k, 1:2], sqs[:, k, :], axis=AX.X)
                e.fence()
                e.tensor_scalar_mul(mv[:, :, 0], mv[:, :, 0], 1.0 / D)
                e.fence()
                e.tensor_scalar_mul(mv[:, :, 1], mv[:, :, 1], 1.0 / D)
                e.fence()
                e.scalar_tensor_tensor(z1[:], mv[:, :, 0], -1.0, mv[:, :, 0], ALU.mult, ALU.mult)
                e.fence()
                e.tensor_tensor(z1[:], mv[:, :, 1], z1[:], ALU.add)
                e.fence()
                e.tensor_scalar_add(z1[:], z1[:], 1e-5)
                e.fence()
            p.vector(v2)

            def a2(e, grp=grp):
                e.wait(v)
                e.activation(s1[:], z1[:], AF.Sqrt)
            p.scalar(a2)

            def v3(e, grp=grp):
                e.wait(a)
                e.reciprocal(Am[:], s1[:])
                e.fence()
                for k in range(8):
                    e.tensor_scalar(xrs[:, k, :], xs[:, k, :], mv[:, k, 0:1], Am[:, k:k + 1],
                                    ALU.subtract, ALU.mult)
                e.fence()
            p.vector(v3)

            def st1(e, grp=grp):
                e.wait(v)
                e.dma_start(out=xn_hbm.rearrange("(q s) d -> q s d", q=128)[:, grp * 8:(grp + 1) * 8, :], in_=xrs[:], sem=d_st)
            p.scalar(st1)
        p.scope = old
    return d_st.n


N_ITER = 28
CPAY = 1024
PAYR = 8 * TLOC + 128
LCAP = 1024
PAYR4 = 64 * LCAP + 128
BIG = 1.0e6


def emit_l0_ffn(p, io, V, dbg=False):
    nc = p.nc
    v, a, t, g = p.ctr("v"), p.ctr("a"), p.ctr("t"), p.ctr("g")
    d_ld = p.ctr("d_ld")
    d_st = p.ctr("d_st")
    d_tr = [p.ctr("d_tr0"), p.ctr("d_tr1")]
    d_gg = p.ctr("p2_dgg")
    d_z = p.ctr("p2_dz")
    d_w = [p.ctr("d_w0"), p.ctr("d_w1")]
    d_y = [p.ctr("p2_dy0"), p.ctr("p2_dy1")]
    d_rp = p.ctr("p2_drp")

    x1_hbm, x2_hbm, xn2_hbm = io["x1_hbm"], io["x2_hbm"], io["xn2_hbm"]
    xall, sc_in, sc_all = io["p2_xall"], io["p2_scin"], io["p2_scall"]
    pay_in, pay_ar = io["p2_payin"], io["p2_payar"]
    mypay_hbm, xe_hbm = io["p2_mypay"], io["p2_xe"]
    y_in, y_all = io["p2_yin"], io["p2_yall"]
    acc_hbm, dw_hbm = io["p2_acc"], io["p2_dw"]
    w1t_h, w3t_h, w2t_h = io["w_w1t_bf"], io["w_w3t_bf"], io["w_w2t_bf"]

    me_i = p.sbuf("p2_mei", [1, 2], I32)
    REG = {}
    gidt = p.sbuf("p2_gidt", [128, 32], F32)
    riota = p.sbuf("p2_riota", [128, 8], F32)
    baset = p.sbuf("p2_baset", [128, 1], F32)
    dumpt = p.sbuf("p2_dumpt", [128, 8], F32)
    ones816 = p.sbuf("p2_o816", [8, 16], F32)
    ones88 = p.sbuf("p2_o88", [8, 8], F32)
    ctile = p.sbuf("p2_ctile", [8, 16], F32)
    cnt1 = p.sbuf("p2_cnt1", [128, 1], F32)
    thr128 = p.sbuf("p2_thr", [128, 1], F32)
    loB = [p.sbuf(f"p2_lo{i}", [8, 1], F32) for i in range(2)]
    hiB = [p.sbuf(f"p2_hi{i}", [8, 1], F32) for i in range(2)]
    loE, hiE = loB[0], hiB[0]
    midE = p.sbuf("p2_mid", [8, 1], F32)
    mid16 = p.sbuf("p2_mid16", [8, 16], F32)
    geE = p.sbuf("p2_ge", [8, 1], F32)
    ltE = p.sbuf("p2_lt", [8, 1], F32)
    sc8a = p.sbuf("p2_sc8a", [8, 1], F32)
    sc8b = p.sbuf("p2_sc8b", [8, 1], F32)
    cntE = p.sbuf("p2_cntE", [8, 1], F32)
    mE = p.sbuf("p2_mE", [8, 1], F32)
    fillE = p.sbuf("p2_fillE", [8, 1], F32)
    maxE = p.sbuf("p2_maxE", [8, 1], F32)
    nmaxE = p.sbuf("p2_nmaxE", [8, 1], F32)
    denE = p.sbuf("p2_denE", [8, 1], F32)
    rdenE = p.sbuf("p2_rdenE", [8, 1], F32)
    tievE = p.sbuf("p2_tievE", [8, 1], F32)
    mce = p.sbuf("p2_mce", [128, 1], F32)
    tice = p.sbuf("p2_tice", [128, 1], F32)
    hi128 = p.sbuf("p2_hi128", [128, 1], F32)
    lo128 = p.sbuf("p2_lo128", [128, 1], F32)
    nmx128 = p.sbuf("p2_nmx128", [128, 1], F32)
    tie_ec = p.sbuf("p2_tieec", [8, 16], F32)
    gt_ec = p.sbuf("p2_gtec", [8, 16], F32)
    tie_e8 = p.sbuf("p2_tie8", [8, 8], F32)
    tiecum = p.sbuf("p2_tcum", [8, 8], F32)
    take_ec = p.sbuf("p2_takeec", [8, 8], F32)
    m_e8 = p.sbuf("p2_me8", [8, 8], F32)
    off_gcT = p.sbuf("p2_offgcT", [8, 8], F32)
    take_my = p.sbuf("p2_takemy", [8, 1], F32)
    og_meE = p.sbuf("p2_ogmeE", [8, 1], F32)
    ogP = p.sbuf("p2_ogP", [8, 1], F32)
    epad8 = p.sbuf("p2_epad8", [8, 1], F32)
    smod = p.sbuf("p2_smod", [8, TLOC], F32)
    oge_row = p.sbuf("p2_ogerow", [1, 8], F32)
    ones_col = p.sbuf("p2_onescol", [1, 128], F32)
    ogbc = p.sbuf("p2_ogbc", [128, 8], F32)
    ccol64 = p.sbuf("p2_ccol64", [128, 64], F32)
    idx64 = p.sbuf("p2_idx64", [128, 64], F32)
    idx64_i = p.sbuf("p2_idx64i", [128, 64], I32)
    wg_tm2 = p.sbuf("p2_wgtm2", [128, 32, 2], F32)
    gidr = p.sbuf("p2_gidr", [128, 8], F32)
    tlr = p.sbuf("p2_tlr", [128, 8], F32)
    v1m = p.sbuf("p2_v1m", [128, 8], F32)
    v2m = p.sbuf("p2_v2m", [128, 8], F32)
    idxs2 = p.sbuf("p2_idxs2", [128, 8], I32)
    mask8 = p.sbuf("p2_mask8", [8, 8], F32)
    tl8 = p.sbuf("p2_tl8", [1, 8], F32)
    ztile = p.sbuf("p2_z", [128, D], F32)
    zb = p.sbuf("p2_zb", [128, 1040], BF16)
    p.shared.update(me_i=me_i, gidt=gidt, riota=riota, baset=baset, dumpt=dumpt,
                    tl8=tl8, epad8=epad8, smod=smod, ones_col=ones_col, ccol64=ccol64,
                    ztile=ztile, zb=zb, mask8=mask8, ones88=ones88)

    def ldconst(e):
        e.wait(p.ctr("d_sx10"))
        e.wait(p.ctr("d_sx11"))
        e.dma_start(out=me_i[:], in_=V["me_i"], sem=d_ld)
        e.dma_start(out=gidt[:], in_=V["gid_c"], sem=d_ld)
        e.dma_start(out=riota[:], in_=V["riota_c"], sem=d_ld)
        e.dma_start(out=baset[:], in_=V["base_c"], sem=d_ld)
        e.dma_start(out=mask8[:], in_=V["mask8_c"], sem=d_ld)
        e.dma_start(out=tl8[:], in_=V["tl8_c"], sem=d_ld)
        e.dma_start(out=epad8[:], in_=V["epad8_c"], sem=d_ld)
        e.dma_start(out=smod[:], in_=V["smod_c"], sem=d_ld)
        e.dma_start(out=ccol64[:], in_=V["ccol64_c"], sem=d_ld)
    p.sync(ldconst)
    LD_CONST = d_ld.n

    def vconst(e):
        e.memset(ones816[:], 1.0)
        e.memset(ones88[:], 1.0)
        for k in range(8):
            e.memset(dumpt[:, k:k + 1], float(TLOC + k))
        e.memset(ztile[:], 0.0)
        e.memset(zb[:], 0.0)
        e.memset(ones_col[:], 1.0)
        e.fence()
    p.vector(vconst)
    V_CONST = v.n

    def stz(e):
        e.wait(v, V_CONST)
        for i in range(33):
            e.dma_start(out=acc_hbm[i * 128:(i + 1) * 128, :], in_=ztile[:], sem=d_z)
        payv = pay_in.rearrange("(q x) f -> q x f", q=128)
        for j in range(4):
            e.dma_start(out=payv[:, j * 64:(j + 1) * 64, :], in_=ztile[:], sem=d_z)
        e.dma_start(out=payv[:, 256:257, :], in_=ztile[:, 0:16], sem=d_z)
        for i in range(CPAY // 128):
            e.dma_start(out=y_all[8 * TLOC + i * 128:8 * TLOC + (i + 1) * 128, :], in_=zb[:], sem=d_z)
    p.gpsimd(stz)
    ST_ZERO = d_z.n

    emit_ln_plain(p, x1_hbm, xn2_hbm, "p2ln", gates=("d_sx10", "d_sx11"))
    ST_LN = d_st.n

    with ExitStack() as ssc:
        old = p.scope; p.scope = ssc
        wgt = p.sbuf("p2_wgt", [128, 8, 8], F32)
        wgt_b = p.sbuf("p2_wgtb", [128, 8, 8], BF16)
        xnf = [p.sbuf(f"p2_xnf{i}", [128, 8, 512], BF16) for i in range(2)]
        ssb = p.sbuf("p2_ssb", [8, TLOC], F32)
        ps_s = p.psum("p2_pss", [8, 512])

        def lds(e):
            e.wait(d_st, ST_LN)
            e.dma_start(out=wgt[:], in_=V["wgt_fm"], sem=d_ld)
        p.sync(lds)
        LD_WGT = d_ld.n

        def vcvt(e):
            e.wait(d_ld, LD_WGT)
            e.tensor_copy(wgt_b[:], wgt[:])
            e.fence()
        p.vector(vcvt)
        V_WGT = v.n

        for cg in range(8):
            bb = cg % 2

            def ldtr(e, cg=cg, bb=bb):
                e.wait(d_st, ST_LN)
                if cg >= 2:
                    e.wait(t)
                for j in range(8):
                    e.dma_start_transpose(xnf[bb][:, j, :], xn2_hbm[cg * 512:(cg + 1) * 512, j * 128:(j + 1) * 128], sem=d_tr[bb])
            p.sync(ldtr)

            def mms(e, cg=cg, bb=bb):
                e.wait(d_tr[bb])
                e.wait(v, V_WGT)
                if cg >= 1:
                    e.wait(v)
                for j in range(8):
                    e.matmul(ps_s[:], wgt_b[:, j, :], xnf[bb][:, j, :], start=(j == 0), stop=(j == 7))
            p.tensor(mms)

            def evs(e, cg=cg):
                e.wait(t)
                e.tensor_copy(ssb[:, cg * 512:(cg + 1) * 512], ps_s[:])
                e.fence()
            p.vector(evs)

        def sts(e):
            e.wait(v)
            e.dma_start(out=sc_in, in_=ssb[:], sem=d_st)
        p.scalar(sts)
        p.scope = old
    ST_SC = d_st.n

    def ag_sc(e):
        e.wait(d_st, ST_SC)
        e.collective_compute("AllGather", ALU.bypass, replica_groups=[list(range(NCORES))],
                             ins=[sc_in], outs=[sc_all])
    p.gpsimd(ag_sc)
    CC_SC = g.n

    def ag_x(e):
        e.wait(d_st, ST_LN)
        e.collective_compute("AllGather", ALU.bypass, replica_groups=[list(range(NCORES))],
                             ins=[xn2_hbm], outs=[xall])
    p.gpsimd(ag_x)
    CC_X = g.n

    with ExitStack() as asc:
        old = p.scope; p.scope = asc
        srch = p.sbuf("p2_srch", [128, 2048], F32)
        cmpb = p.sbuf("p2_cmp", [128, 2048], BF16)
        gtb = p.sbuf("p2_gtb", [128, 2048], BF16)
        expb = p.sbuf("p2_expb", [128, 2048], F32)

        def ldsr(e):
            e.wait(g, CC_SC)
            e.dma_start(out=srch[:], in_=sc_all.rearrange("c eh f -> eh c f"), sem=d_ld)
        p.sync(ldsr)
        LD_SR = d_ld.n

        def vinit(e):
            e.wait(d_ld, LD_SR)
            e.reduce_max(cnt1[:], srch[:], axis=AX.X, apply_absolute_value=True)
            e.fence()
        p.vector(vinit)

        def rp0(e):
            e.wait(v)
            e.dma_start(out=ctile[:], in_=cnt1[:], sem=d_rp)
        p.sync(rp0)

        def vinit2(e):
            e.wait(d_rp)
            e.wait(v, V_CONST)
            e.reduce_max(hiE[:], ctile[:], axis=AX.X)
            e.fence()
            e.tensor_scalar_add(hiE[:], hiE[:], 1.0)
            e.fence()
            e.tensor_scalar_mul(loE[:], hiE[:], -1.0)
            e.fence()
            e.tensor_tensor(midE[:], loE[:], hiE[:], ALU.add)
            e.fence()
            e.tensor_scalar_mul(midE[:], midE[:], 0.5)
            e.fence()
            e.tensor_scalar(mid16[:], ones816[:], midE[:, 0:1], 0.0, ALU.mult, ALU.add)
            e.fence()
        p.vector(vinit2)

        def rp1(e):
            e.wait(v)
            e.dma_start(out=thr128[:], in_=mid16[:], sem=d_rp)
        p.sync(rp1)

        for it in range(N_ITER):
            def vc(e, it=it):
                e.wait(d_rp)
                e.tensor_scalar(cmpb[:], srch[:], thr128[:, 0:1], 0.0, ALU.is_gt, ALU.add,
                                accum_out=cnt1[:])
                e.fence()
            p.vector(vc)

            def rp2(e, it=it):
                e.wait(v)
                e.dma_start(out=ctile[:], in_=cnt1[:], sem=d_rp)
            p.sync(rp2)

            def vu(e, it=it):
                sb, db = it % 2, 1 - it % 2
                e.wait(d_rp)
                e.reduce_sum(cntE[:], ctile[:], axis=AX.X)
                e.fence()
                e.tensor_scalar(geE[:], cntE[:], float(TLOC), 0.0, ALU.is_ge, ALU.add)
                e.fence()
                e.tensor_scalar(ltE[:], cntE[:], float(TLOC), 0.0, ALU.is_lt, ALU.add)
                e.fence()
                e.tensor_tensor(sc8a[:], midE[:], geE[:], ALU.mult)
                e.tensor_tensor(sc8b[:], loB[sb][:], ltE[:], ALU.mult)
                e.fence()
                e.tensor_tensor(loB[db][:], sc8a[:], sc8b[:], ALU.add)
                e.fence()
                e.tensor_tensor(sc8a[:], hiB[sb][:], geE[:], ALU.mult)
                e.tensor_tensor(sc8b[:], midE[:], ltE[:], ALU.mult)
                e.fence()
                e.tensor_tensor(hiB[db][:], sc8a[:], sc8b[:], ALU.add)
                e.fence()
                e.tensor_tensor(midE[:], loB[db][:], hiB[db][:], ALU.add)
                e.fence()
                e.tensor_scalar_mul(midE[:], midE[:], 0.5)
                e.fence()
                e.tensor_scalar(mid16[:], ones816[:], midE[:, 0:1], 0.0, ALU.mult, ALU.add)
                e.fence()
            p.vector(vu)

            def rp3(e, it=it):
                e.wait(v)
                e.dma_start(out=thr128[:], in_=mid16[:], sem=d_rp)
            p.sync(rp3)

        def vhb(e):
            e.wait(d_rp)
            e.tensor_scalar(mid16[:], ones816[:], hiE[:, 0:1], 0.0, ALU.mult, ALU.add)
            e.fence()
        p.vector(vhb)

        def rp4(e):
            e.wait(v)
            e.dma_start(out=hi128[:], in_=mid16[:], sem=d_rp)
        p.sync(rp4)

        def vf1(e):
            e.wait(d_rp)
            e.tensor_scalar(gtb[:], srch[:], hi128[:, 0:1], 0.0, ALU.is_gt, ALU.add, accum_out=mce[:])
            e.fence()
            e.reduce_max(cnt1[:], srch[:], axis=AX.X)
            e.fence()
            e.tensor_scalar(mid16[:], ones816[:], loE[:, 0:1], 0.0, ALU.mult, ALU.add)
            e.fence()
        p.vector(vf1)

        def rp5(e):
            e.wait(v)
            e.dma_start(out=lo128[:], in_=mid16[:], sem=d_rp)
            e.dma_start(out=ctile[:], in_=cnt1[:], sem=d_rp)
            e.dma_start(out=gt_ec[:], in_=mce[:], sem=d_rp)
        p.sync(rp5)

        def vf2(e):
            e.wait(d_rp)
            e.reduce_max(maxE[:], ctile[:], axis=AX.X)
            e.fence()
            e.tensor_scalar_mul(nmaxE[:], maxE[:], -1.0)
            e.fence()
            e.tensor_scalar(mid16[:], ones816[:], nmaxE[:, 0:1], 0.0, ALU.mult, ALU.add)
            e.fence()
            e.tensor_scalar(cmpb[:], srch[:], lo128[:, 0:1], 0.0, ALU.is_gt, ALU.add, accum_out=tice[:])
            e.fence()
            e.tensor_tensor(tice[:], tice[:], mce[:], ALU.subtract)
            e.fence()
            e.reduce_sum(mE[:], gt_ec[:], axis=AX.X)
            e.fence()
            e.tensor_scalar(fillE[:], mE[:], float(TLOC), -1.0, ALU.subtract, ALU.mult)
            e.fence()
        p.vector(vf2)

        def rp6(e):
            e.wait(v)
            e.dma_start(out=nmx128[:], in_=mid16[:], sem=d_rp)
            e.dma_start(out=tie_ec[:], in_=tice[:], sem=d_rp)
        p.sync(rp6)

        def af1(e):
            e.wait(d_rp)
            e.activation(expb[:], srch[:], AF.Exp, bias=nmx128[:, 0:1])
        p.scalar(af1)

        def vf3(e):
            e.wait(a)
            e.tensor_tensor(expb[:], expb[:], gtb[:], ALU.mult)
            e.fence()
            e.reduce_sum(cnt1[:], expb[:], axis=AX.X)
            e.fence()
        p.vector(vf3)

        def rp7(e):
            e.wait(v)
            e.dma_start(out=ctile[:], in_=cnt1[:], sem=d_rp)
        p.sync(rp7)

        def vf4(e):
            e.wait(d_rp)
            e.reduce_sum(denE[:], ctile[:], axis=AX.X)
            e.fence()
            e.tensor_tensor(tievE[:], hiE[:], nmaxE[:], ALU.add)
            e.fence()
        p.vector(vf4)

        def af2(e):
            e.wait(v)
            e.activation(tievE[:], tievE[:], AF.Exp)
        p.scalar(af2)

        def vf5(e):
            e.wait(a)
            e.tensor_tensor(tievE[:], tievE[:], fillE[:], ALU.mult)
            e.fence()
            e.tensor_tensor(denE[:], denE[:], tievE[:], ALU.add)
            e.fence()
            e.reciprocal(rdenE[:], denE[:])
            e.fence()
            e.tensor_tensor(tie_e8[:], tie_ec[:, 0:8], tie_ec[:, 8:16], ALU.add)
            e.fence()
            e.tensor_tensor(m_e8[:], gt_ec[:, 0:8], gt_ec[:, 8:16], ALU.add)
            e.fence()
            e.tensor_tensor_scan(tiecum[:], ones88[:], tie_e8[:], 0.0, ALU.mult, ALU.add)
            e.fence()
            e.tensor_tensor(tiecum[:], tiecum[:], tie_e8[:], ALU.subtract)
            e.fence()
            e.tensor_scalar(take_ec[:], tiecum[:], fillE[:, 0:1], -1.0, ALU.subtract, ALU.mult)
            e.fence()
            e.tensor_scalar(take_ec[:], take_ec[:], 0.0, 0.0, ALU.max, ALU.add)
            e.fence()
            e.tensor_tensor(take_ec[:], take_ec[:], tie_e8[:], ALU.min)
            e.fence()
            e.tensor_tensor(m_e8[:], m_e8[:], take_ec[:], ALU.add)
            e.fence()
            e.tensor_tensor_scan(off_gcT[:], ones88[:], m_e8[:], 0.0, ALU.mult, ALU.add)
            e.fence()
            e.tensor_tensor(off_gcT[:], off_gcT[:], m_e8[:], ALU.subtract)
            e.fence()
        p.vector(vf5)
        V_STATS = v.n
        p.scope = old

    with ExitStack() as bsc:
        old = p.scope; p.scope = bsc
        s_my = p.sbuf("p2_smy", [8, TLOC], F32)
        gtw = p.sbuf("p2_gtw", [8, TLOC], F32)
        tied = p.sbuf("p2_tied", [8, TLOC], F32)
        sel_my = p.sbuf("p2_selmy", [8, TLOC], F32)
        rank_my = p.sbuf("p2_rankmy", [8, TLOC], F32)
        ones8 = p.sbuf("p2_ones8", [8, TLOC], F32)

        def vones(e):
            e.memset(ones8[:], 1.0)
            e.fence()
        p.vector(vones)

        def gcols(e):
            e.wait(g, CC_SC)
            e.wait(v, V_STATS)
            rh_me = e.alloc_register(name="p2mereg")
            REG["me"] = bass_rust.make_scalar_value(rh_me, min_val=0, max_val=7)
            REG["meh"] = rh_me
            e.reg_load(rh_me, me_i[0:1, 0:1])
            e.dma_start(out=s_my[:], in_=sc_all.rearrange("c eh f -> c (eh f)")[DynSlice(REG["me"], 1), :], sem=d_gg)
        p.gpsimd(gcols)
        GD_COLS = d_gg.n

        def vcols(e):
            e.wait(v, V_STATS)
            e.wait(d_ld)
            e.tensor_tensor(tiecum[:], take_ec[:], mask8[:], ALU.mult)
            e.fence()
            e.reduce_sum(take_my[:], tiecum[:], axis=AX.X)
            e.fence()
            e.tensor_tensor(tiecum[:], off_gcT[:], mask8[:], ALU.mult)
            e.fence()
            e.reduce_sum(og_meE[:], tiecum[:], axis=AX.X)
            e.fence()
            e.tensor_tensor(ogP[:], og_meE[:], epad8[:], ALU.add)
            e.fence()
        p.vector(vcols)
        V_COLS = v.n

        def rp8(e):
            e.wait(v, V_COLS)
            e.dma_start(out=oge_row[:], in_=og_meE[:], sem=d_rp)
        p.sync(rp8)
        RP_ROWS = d_rp.n

        def vpre(e):
            e.wait(d_rp, RP_ROWS)
            e.wait(d_ld)
            e.tensor_tensor(oge_row[:], oge_row[:], tl8[:], ALU.add)
            e.fence()
        p.vector(vpre)
        V_PRE = v.n

        if dbg:
            def stdbg(e):
                e.wait(v)
                e.wait(d_rp)
                e.wait(d_gg)
                dd = io["dbg"]
                for i, tt in enumerate([hiE, loE, mE, fillE, maxE, denE, rdenE, take_my,
                                        og_meE, ogP, midE, cntE]):
                    e.dma_start(out=dd[:, i:i + 1], in_=tt[:], sem=d_st, allow_slow_non_contiguous=True)
                e.dma_start(out=dd[:, 16:24], in_=m_e8[:], sem=d_st)
                e.dma_start(out=dd[:, 24:32], in_=take_ec[:], sem=d_st)
                e.dma_start(out=dd[:, 40:48], in_=off_gcT[:], sem=d_st)
                e.dma_start(out=dd[:, 48:56], in_=tie_e8[:], sem=d_st)
            p.scalar(stdbg)
            return

        def vmy1(e):
            e.wait(d_gg, GD_COLS)
            e.tensor_scalar(gtw[:], s_my[:], hiE[:, 0:1], 0.0, ALU.is_gt, ALU.add)
            e.fence()
            e.tensor_scalar(tied[:], s_my[:], loE[:, 0:1], 0.0, ALU.is_gt, ALU.add)
            e.fence()
            e.tensor_tensor(tied[:], tied[:], gtw[:], ALU.subtract)
            e.fence()
            e.tensor_tensor_scan(rank_my[:], ones8[:], tied[:], 0.0, ALU.mult, ALU.add)
            e.fence()
            e.tensor_tensor(rank_my[:], rank_my[:], tied[:], ALU.subtract)
            e.fence()
            e.tensor_scalar(sel_my[:], rank_my[:], take_my[:, 0:1], 0.0, ALU.is_lt, ALU.add)
            e.fence()
            e.tensor_tensor(sel_my[:], sel_my[:], tied[:], ALU.mult)
            e.fence()
            e.tensor_tensor(sel_my[:], sel_my[:], gtw[:], ALU.add)
            e.fence()
            e.tensor_tensor_scan(rank_my[:], ones8[:], sel_my[:], 0.0, ALU.mult, ALU.add)
            e.fence()
            e.tensor_tensor(rank_my[:], rank_my[:], sel_my[:], ALU.subtract)
            e.fence()
        p.vector(vmy1)
        V_MY1 = v.n

        def amy(e):
            e.wait(v, V_MY1)
            e.activation(gtw[:], s_my[:], AF.Exp, bias=nmaxE[:, 0:1])
        p.scalar(amy)
        A_WMY = a.n

        def vmy2(e):
            e.wait(a, A_WMY)
            e.wait(v, V_COLS)
            e.tensor_scalar_mul(gtw[:], gtw[:], rdenE[:, 0:1])
            e.fence()
            e.tensor_tensor(gtw[:], gtw[:], sel_my[:], ALU.mult)
            e.fence()
            e.tensor_scalar(tied[:], rank_my[:], ogP[:, 0:1], float(-8 * TLOC), ALU.add, ALU.add)
            e.fence()
            e.tensor_tensor(tied[:], tied[:], smod[:], ALU.subtract)
            e.fence()
            e.tensor_tensor(tied[:], tied[:], sel_my[:], ALU.mult)
            e.fence()
            e.tensor_tensor(tied[:], tied[:], smod[:], ALU.add)
            e.fence()
            e.tensor_scalar_add(tied[:], tied[:], float(8 * TLOC))
            e.fence()
        p.vector(vmy2)
        V_MY2 = v.n

        def stdw(e):
            e.wait(v, V_MY2)
            e.dma_start(out=dw_hbm[0], in_=tied[:], sem=d_st)
            e.dma_start(out=dw_hbm[1], in_=gtw[:], sem=d_st)
        p.scalar(stdw)
        p.scope = old
    ST_DW = d_st.n

    with ExitStack() as csc:
        old = p.scope; p.scope = csc
        dwt = p.sbuf("p2_dwt", [128, 2, 32], F32)
        dwi = p.sbuf("p2_dwi", [128, 32], I32)
        pay_src = p.sbuf("p2_psrc", [128, 32, 16], F32)

        def vpay0(e):
            e.wait(d_st, ST_DW)
            e.wait(d_ld)
            e.memset(pay_src[:], 0.0)
            e.fence()
            e.tensor_copy(pay_src[:, :, 0], gidt[:])
            e.fence()
        p.vector(vpay0)
        V_PAY0 = v.n

        for ee in range(8):
            def lddw(e, ee=ee):
                e.wait(d_st, ST_DW)
                if ee > 0:
                    e.wait(d_gg)
                e.dma_start(out=dwt[:], in_=dw_hbm.rearrange("k e t -> e t k")[ee].rearrange("(q s) k -> q k s", q=128), sem=d_ld)
            p.sync(lddw)
            LDDW = d_ld.n

            def vpay(e, ee=ee, LDDW=LDDW):
                e.wait(d_ld, LDDW)
                e.wait(v, V_PAY0)
                if ee > 0:
                    e.wait(d_gg)
                e.tensor_copy(dwi[:], dwt[:, 0, :])
                e.fence()
                e.tensor_copy(pay_src[:, :, 1], dwt[:, 1, :])
                e.fence()
            p.vector(vpay)
            VPAY = v.n

            def gpay(e, ee=ee, VPAY=VPAY):
                e.wait(v, VPAY)
                e.wait(d_z, ST_ZERO)
                for s in range(32):
                    e.indirect_dma_start(
                        out=pay_in, out_offset=IndirectOffsetOnAxis(dwi[:, s:s + 1], 0),
                        in_=pay_src[:, s, :], in_offset=None,
                        sem=d_gg)
            p.gpsimd(gpay)
        p.scope = old
    GD_PAY = d_gg.n

    def ag_pay(e):
        e.wait(d_gg, GD_PAY)
        e.collective_compute("AllReduce", ALU.add, replica_groups=[list(range(NCORES))],
                             ins=[pay_in], outs=[pay_ar])
    p.gpsimd(ag_pay)
    CC_PAY = g.n

    with ExitStack() as dsc:
        old = p.scope; p.scope = dsc
        mypay = p.sbuf("p2_mypaysb", [128, 32, 16], F32)
        idxg = p.sbuf("p2_idxg", [128, 32], I32)
        X_tm = p.sbuf("p2_xtm", [128, 32, D], BF16)

        def ldmp(e):
            e.wait(g, CC_PAY)
            e.dma_start(out=mypay[:], in_=pay_ar[0:8 * TLOC, :].rearrange("(b r) f -> b r f", b=8)[DynSlice(REG["me"], 1)].rearrange("o (q s) f -> o q s f", q=128), sem=d_gg)
        p.gpsimd(ldmp)
        GD_MP = d_gg.n

        def stmp(e):
            e.wait(d_gg, GD_MP)
            e.dma_start(out=mypay_hbm.rearrange("(q s) f -> q s f", q=128), in_=mypay[:], sem=d_st)
        p.scalar(stmp)
        LD_MP = d_gg.n

        def vidx(e):
            e.wait(d_gg, GD_MP)
            e.tensor_copy(idxg[:], mypay[:, :, 0])
            e.fence()
        p.vector(vidx)
        V_IDXG = v.n

        def ggx(e):
            e.wait(v, V_IDXG)
            for s in range(32):
                e.indirect_dma_start(
                    out=X_tm[:, s, :], out_offset=None,
                    in_=xall, in_offset=IndirectOffsetOnAxis(idxg[:, s:s + 1], 0),
                    sem=d_gg)
        p.gpsimd(ggx)
        GD_X = d_gg.n

        def stxe(e):
            e.wait(d_gg, GD_X)
            e.dma_start(out=xe_hbm.rearrange("(q s) d -> q s d", q=128), in_=X_tm[:], sem=d_st)
        p.scalar(stxe)
        p.scope = old
    ST_XE = d_st.n

    with ExitStack() as esc:
        old = p.scope; p.scope = esc
        w1t_sb = p.sbuf("p2_w1sb", [128, 8, 2048], BF16)
        w3t_sb = p.sbuf("p2_w3sb", [128, 8, 2048], BF16)
        w2t_sb = p.sbuf("p2_w2sb", [128, 16, D], BF16)
        xfc = [p.sbuf(f"p2_xfc{i}", [128, 8, 512], BF16) for i in range(2)]
        h_bf = p.sbuf("p2_hbf", [128, 16, 512], BF16)
        hs_f = p.sbuf("p2_hsf", [128, 512], F32)
        Y_cg = [p.sbuf(f"p2_ycg{i}", [128, 4, 1040], BF16) for i in range(2)]
        ps_h = [p.psum(f"p2_psh{i}", [128, 512]) for i in range(2)]
        ps_y = [p.psum(f"p2_psy{i}", [128, 512]) for i in range(2)]

        def ldw13(e):
            e.wait(d_st, ST_XE)
            e.dma_start(out=w1t_sb[:], in_=w1t_h.rearrange("(j q) h -> q j h", q=128), sem=d_w[0])
            e.dma_start(out=w3t_sb[:], in_=w3t_h.rearrange("(j q) h -> q j h", q=128), sem=d_w[0])
            e.dma_start(out=w2t_sb[:], in_=w2t_h.rearrange("(j q) d -> q j d", q=128), sem=d_w[0])
        p.sync(ldw13)
        LD_W13 = d_w[0].n

        def ldwg2(e):
            e.wait(d_st, ST_XE)
            e.dma_start(out=wg_tm2[:], in_=mypay_hbm[:, 0:2].rearrange("(s q) f -> q s f", q=128), sem=d_ld)
        p.sync(ldwg2)
        LD_WG2 = d_ld.n

        for cg in range(8):
            bb = cg % 2

            def ldxf(e, cg=cg, bb=bb):
                e.wait(d_st, ST_XE)
                if cg >= 2:
                    e.wait(t)
                for j in range(8):
                    e.dma_start_transpose(xfc[bb][:, j, :], xe_hbm[cg * 512:(cg + 1) * 512, j * 128:(j + 1) * 128], sem=d_tr[bb])
            p.sync(ldxf)
            LD_XF = d_tr[bb].n

            for hf in range(16):
                def mmh(e, cg=cg, hf=hf, bb=bb, LD_XF=LD_XF):
                    e.wait(d_tr[bb], LD_XF)
                    e.wait(d_w[0], LD_W13)
                    if cg > 0 or hf > 0:
                        e.wait(v)
                    for j in range(8):
                        e.matmul(ps_h[0][:], w1t_sb[:, j, hf * 128:(hf + 1) * 128], xfc[bb][:, j, :], start=(j == 0), stop=(j == 7))
                    for j in range(8):
                        e.matmul(ps_h[1][:], w3t_sb[:, j, hf * 128:(hf + 1) * 128], xfc[bb][:, j, :], start=(j == 0), stop=(j == 7))
                p.tensor(mmh)

                def evh(e):
                    e.wait(t)
                    e.activation(hs_f[:], ps_h[0][:], AF.Silu)
                p.scalar(evh)

                def evh2(e, hf=hf):
                    e.wait(a)
                    e.tensor_tensor(h_bf[:, hf, :], hs_f[:], ps_h[1][:], ALU.mult)
                    e.fence()
                p.vector(evh2)

            for tt_i in range(4):
                for ng in range(2):
                    gi = tt_i * 2 + ng

                    def mmy(e, cg=cg, tt_i=tt_i, ng=ng, gi=gi):
                        e.wait(v)
                        if cg * 8 + gi >= 2:
                            e.wait(v, YEV[cg * 8 + gi - 2])
                        for hf in range(16):
                            e.matmul(ps_y[gi % 2][:], h_bf[:, hf, tt_i * 128:(tt_i + 1) * 128], w2t_sb[:, hf, ng * 512:(ng + 1) * 512], start=(hf == 0), stop=(hf == 15))
                    p.tensor(mmy)

                    def evy(e, cg=cg, tt_i=tt_i, ng=ng, gi=gi, bb=bb):
                        e.wait(t)
                        e.wait(d_ld, LD_WG2)
                        if cg >= 2 and gi == 0:
                            e.wait(d_y[bb])
                        e.tensor_scalar_mul(Y_cg[bb][:, tt_i, ng * 512:(ng + 1) * 512], ps_y[gi % 2][:], wg_tm2[:, cg * 4 + tt_i, 1:2])
                        e.fence()
                        YEV[cg * 8 + gi] = p.ctr("v").n
                    p.vector(evy)

            def vecho(e, cg=cg, bb=bb):
                ycf = Y_cg[bb].bitcast(F32)
                for tt_i in range(4):
                    e.tensor_copy(ycf[:, tt_i, 512:513], wg_tm2[:, cg * 4 + tt_i, 0:1])
                e.memset(ycf[:, :, 513:520], 0.0)
                e.fence()
            p.vector(vecho)
            V_ECHO = v.n

            def sty(e, cg=cg, bb=bb, V_ECHO=V_ECHO):
                e.wait(v, V_ECHO)
                e.dma_start(out=y_in.rearrange("(s q) f -> q s f", q=128)[:, cg * 4:(cg + 1) * 4, :], in_=Y_cg[bb][:], sem=d_y[bb])
            p.scalar(sty)
        p.scope = old

    def ag_y(e):
        e.wait(d_y[0])
        e.wait(d_y[1])
        e.collective_compute("AllGather", ALU.bypass, replica_groups=[list(range(NCORES))],
                             ins=[y_in], outs=[y_all[0:8 * TLOC, :]])
    p.gpsimd(ag_y)
    CC_Y = g.n

    with ExitStack() as fsc:
        old = p.scope; p.scope = fsc
        slab = [p.sbuf(f"p2_slab{i}", [128, 8, 1040], BF16) for i in range(2)]
        yf = [p.sbuf(f"p2_yf{i}", [128, 8, D], F32) for i in range(2)]
        d_sl = [p.ctr("p2_dsl0"), p.ctr("p2_dsl1")]
        GSC2, VS2 = {}, {}
        ps_bc = p.psum("p2_psbc", [128, 8])

        def tbc(e):
            e.wait(v, V_PRE)
            e.matmul(ps_bc[:], ones_col[:], oge_row[:], start=True, stop=True)
        p.tensor(tbc)
        T_BC = t.n

        def vbc(e):
            e.wait(t, T_BC)
            e.tensor_copy(ogbc[:], ps_bc[:])
            e.fence()
            for ee in range(8):
                e.tensor_scalar(idx64[:, ee * 8:(ee + 1) * 8], ccol64[:, ee * 8:(ee + 1) * 8],
                                ogbc[:, ee:ee + 1], 0.0, ALU.add, ALU.add)
            e.fence()
            e.tensor_copy(idx64_i[:], idx64[:])
            e.fence()
        p.vector(vbc)
        V_BC = v.n

        idxs2b = [p.sbuf(f"p2_idxs2{i}", [128, 8], I32) for i in range(2)]
        for ee in range(8):
            cb = ee % 2

            def gslab(e, ee=ee, cb=cb):
                e.wait(g, CC_Y)
                e.wait(v, V_BC)
                if ee >= 2:
                    e.wait(d_gg, GSC2[ee - 2])
                for k in range(8):
                    e.indirect_dma_start(
                        out=slab[cb][:, k, :], out_offset=None,
                        in_=y_all, in_offset=IndirectOffsetOnAxis(idx64_i[:, ee * 8 + k:ee * 8 + k + 1], 0),
                        sem=d_sl[cb])
            p.gpsimd(gslab)
            GSLAB = d_sl[cb].n

            def vslab(e, ee=ee, cb=cb, GSLAB=GSLAB):
                e.wait(d_sl[cb], GSLAB)
                e.tensor_copy(yf[cb][:], slab[cb][:, :, 0:D])
                e.fence()
                e.tensor_copy(gidr[:], slab[cb].bitcast(F32)[:, :, 512:513].rearrange("q s o -> q (s o)"))
                e.fence()
                e.tensor_scalar(tlr[:], gidr[:], baset[:, 0:1], 0.0, ALU.subtract, ALU.add)
                e.fence()
                e.tensor_scalar(v1m[:], tlr[:], -0.5, 0.0, ALU.is_gt, ALU.add)
                e.fence()
                e.tensor_scalar(v2m[:], tlr[:], float(TLOC) - 0.5, 0.0, ALU.is_lt, ALU.add)
                e.fence()
                e.tensor_tensor(v1m[:], v1m[:], v2m[:], ALU.mult)
                e.fence()
                e.tensor_tensor(gidr[:], tlr[:], v1m[:], ALU.mult)
                e.fence()
                e.tensor_scalar(v2m[:], v1m[:], 1.0, -1.0, ALU.subtract, ALU.mult)
                e.fence()
                e.tensor_tensor(v2m[:], v2m[:], dumpt[:], ALU.mult)
                e.fence()
                e.tensor_tensor(gidr[:], gidr[:], v2m[:], ALU.add)
                e.fence()
                e.tensor_copy(idxs2b[cb][:], gidr[:])
                e.fence()
            p.vector(vslab)
            VSLAB = v.n

            def gsc(e, ee=ee, cb=cb, VSLAB=VSLAB):
                e.wait(v, VSLAB)
                e.wait(d_z, ST_ZERO)
                if ee > 0:
                    e.wait(d_gg)
                for k in range(8):
                    e.indirect_dma_start(
                        out=acc_hbm, out_offset=IndirectOffsetOnAxis(idxs2b[cb][:, k:k + 1], 0),
                        in_=yf[cb][:, k, :], in_offset=None, compute_op=ALU.add,
                        sem=d_gg)
                GSC2[ee] = d_gg.n
                e.wait(d_gg)
            p.gpsimd(gsc)
        p.scope = old
    GD_COMB = d_gg.n

    with ExitStack() as xsc:
        old = p.scope; p.scope = xsc
        xtile = [p.sbuf(f"p2_xt{i}", [128, D], F32) for i in range(2)]
        atile = [p.sbuf(f"p2_at{i}", [128, D], F32) for i in range(2)]
        d_x2 = [p.ctr("p2_dx20"), p.ctr("p2_dx21")]
        d_lx = [p.ctr("p2_dlx0"), p.ctr("p2_dlx1")]
        for i in range(32):
            bb = i % 2

            def ldx2(e, i=i, bb=bb):
                e.wait(d_gg, GD_COMB)
                if i >= 2:
                    e.wait(d_x2[bb])
                e.dma_start(out=xtile[bb][:], in_=x1_hbm[i * 128:(i + 1) * 128, :], sem=d_lx[bb])
                e.dma_start(out=atile[bb][:], in_=acc_hbm[i * 128:(i + 1) * 128, :], sem=d_lx[bb])
            p.sync(ldx2)
            LX2 = d_lx[bb].n

            def vx2(e, i=i, bb=bb, LX2=LX2):
                e.wait(d_lx[bb], LX2)
                e.tensor_tensor(xtile[bb][:], xtile[bb][:], atile[bb][:], ALU.add)
                e.fence()
            p.vector(vx2)
            VX2 = v.n

            def stx2(e, i=i, bb=bb, VX2=VX2):
                e.wait(v, VX2)
                e.dma_start(out=x2_hbm[i * 128:(i + 1) * 128, :], in_=xtile[bb][:], sem=d_x2[bb])
            p.scalar(stx2)
        p.scope = old


EPS32 = float(2.0 ** -23)


def _regions_of(qt):
    R = [(qt, "T", 128)]
    if qt >= 1:
        R.append((qt - 1, None, 128))
    if qt == 2:
        R.append((0, "BG", 128))
    if qt >= 3:
        R.append((qt - 2, "B2", 128))
        R.append((0, None, 64))
    return R


def emit_l1_attn(p, io, V):
    nc = p.nc
    v, a, t, g = p.ctr("v"), p.ctr("a"), p.ctr("t"), p.ctr("g")
    d_ld = p.ctr("d_ld")
    d_st = p.ctr("d_st")
    d_tr = [p.ctr("d_tr0"), p.ctr("d_tr1")]
    d_w = [p.ctr("d_w0"), p.ctr("d_w1")]
    d_y3 = [p.ctr("p3_dy0"), p.ctr("p3_dy1")]
    d_lx3 = [p.ctr("p3_dlx0"), p.ctr("p3_dlx1")]
    d_sx3 = [p.ctr("p3_dsx0"), p.ctr("p3_dsx1")]

    x2_hbm, x3_hbm, xn3_hbm = io["x2_hbm"], io["x3_hbm"], io["xn3_hbm"]
    qk_hbm, y_hbm = io["p3_qk"], io["p3_y"]
    wqkv_h, wo_h = io["w_qkv_bf"], io["w_wo_bf"]

    _p3s = ExitStack()
    _oldscope = p.scope
    p.scope = _p3s

    v_sb = p.sbuf("p3_vsb", [128, 32, 260], BF16)
    ones_cb = p.sbuf("p3_onescb", [128, 1], BF16)
    masks = p.sbuf("p3_masks", [128, 3, 128], F32)

    emit_ln_plain(p, x2_hbm, xn3_hbm, "p3ln", gates=("p2_dx20", "p2_dx21"))
    ST_LN3 = d_st.n

    d_msk = p.ctr("p3_dmsk")

    def ldc3(e):
        e.wait(p.ctr("p2_dx20"))
        e.wait(p.ctr("p2_dx21"))
        e.dma_start(out=masks[:], in_=V["masks_c"], sem=d_msk)
    p.sync(ldc3)
    LD_MSK = d_msk.n

    def vc3(e):
        e.wait(d_msk, LD_MSK)
        e.memset(ones_cb[:], 1.0)
        for gg in range(4):
            e.memset(v_sb[:, :, gg * 65 + 64:gg * 65 + 65], 1.0)
        e.fence()
    p.vector(vc3)

    with ExitStack() as qsc:
        old = p.scope; p.scope = qsc
        wqkv_sb = p.sbuf("p3_wqkvsb", [128, 8, 1536], BF16)
        xnf3 = [p.sbuf(f"p3_xnf{i}", [128, 8, 512], BF16) for i in range(2)]
        qkvt = [p.sbuf(f"p3_qkvt{i}", [128, 1536], F32) for i in range(2)]
        sqv = p.sbuf("p3_sqv", [128, 1280], F32)
        ssq = p.sbuf("p3_ssq", [128, 20], F32)
        rs20 = p.sbuf("p3_rs20", [128, 20], F32)
        qkro = p.sbuf("p3_qkro", [128, 1280], F32)
        tmp640 = p.sbuf("p3_tmp640", [128, 640], F32)
        qk_bf = [p.sbuf(f"p3_qkbf{i}", [128, 1280], BF16) for i in range(2)]
        ct = [p.sbuf(f"p3_ct{i}", [128, 32], F32) for i in range(2)]
        st_ = [p.sbuf(f"p3_st{i}", [128, 32], F32) for i in range(2)]
        ps_qkv = [p.psum(f"p3_psqkv{i}", [128, 1536]) for i in range(2)]
        d_qs = [p.ctr("p3_dqs0"), p.ctr("p3_dqs1")]
        VQC, VRP = {}, {}

        def ldw3(e):
            e.wait(d_st, ST_LN3)
            e.dma_start(out=wqkv_sb[:], in_=wqkv_h.rearrange("(j q) f -> q j f", q=128), sem=d_w[0])
        p.sync(ldw3)
        LD_W3 = d_w[0].n

        for cg in range(8):
            bb = cg % 2

            def ldtr3(e, cg=cg, bb=bb):
                e.wait(d_st, ST_LN3)
                if cg >= 2:
                    e.wait(t)
                for j in range(8):
                    e.dma_start_transpose(xnf3[bb][:, j, :], xn3_hbm[cg * 512:(cg + 1) * 512, j * 128:(j + 1) * 128], sem=d_tr[bb])
            p.sync(ldtr3)
            LD_XT3 = d_tr[bb].n

            for tt_i in range(4):
                gt = cg * 4 + tt_i
                qtl = gt % 16
                qb = gt % 2

                def mmq(e, cg=cg, tt_i=tt_i, bb=bb, qb=qb, gt=gt, LD_XT3=LD_XT3):
                    e.wait(d_tr[bb], LD_XT3)
                    e.wait(d_w[0], LD_W3)
                    if gt >= 2:
                        e.wait(v, VRP[gt - 2])
                    for part in range(3):
                        for j in range(8):
                            e.matmul(ps_qkv[qb][:, part * 512:(part + 1) * 512],
                                     xnf3[bb][:, j, tt_i * 128:(tt_i + 1) * 128],
                                     wqkv_sb[:, j, part * 512:(part + 1) * 512],
                                     start=(j == 0), stop=(j == 7))
                    MM_T[("p3q", gt)] = p.ctr("t").n
                p.tensor(mmq)

                def ldrope(e, qtl=qtl, qb=qb, gt=gt):
                    if gt >= 2:
                        e.wait(v, VRP[gt - 2])
                    e.dma_start(out=ct[qb][:], in_=V["rope_cos_c"][qtl * 128:(qtl + 1) * 128, :], sem=d_qs[qb])
                    e.dma_start(out=st_[qb][:], in_=V["rope_sin_c"][qtl * 128:(qtl + 1) * 128, :], sem=d_qs[qb])
                p.sync(ldrope)
                LD_RP = d_qs[qb].n

                def vqc(e, gt=gt, qb=qb):
                    e.wait(t, MM_T[("p3q", gt)])
                    e.tensor_copy(qkvt[qb][:], ps_qkv[qb][:, 0:1536])
                    e.fence()
                    VQC[gt] = p.ctr("v").n
                p.vector(vqc)

                def vrms(e, qb=qb):
                    e.tensor_tensor(sqv[:], qkvt[qb][:, 0:1280], qkvt[qb][:, 0:1280], ALU.mult)
                    e.fence()
                    e.reduce_sum(ssq[:], sqv.rearrange("p (h d) -> p h d", d=64), axis=AX.X)
                    e.fence()
                    e.tensor_scalar(ssq[:], ssq[:], 1.0 / 64.0, EPS32, ALU.mult, ALU.add)
                    e.fence()
                p.vector(vrms)

                def asq(e):
                    e.wait(v)
                    e.activation(rs20[:], ssq[:], AF.Sqrt)
                p.scalar(asq)

                def vrope(e, gt=gt, qb=qb, LD_RP=LD_RP):
                    e.wait(a)
                    e.wait(d_qs[qb], LD_RP)
                    e.reciprocal(rs20[:], rs20[:])
                    e.fence()
                    e.tensor_scalar_mul(rs20[:, 0:16], rs20[:, 0:16], 0.125)
                    e.fence()
                    rsb = rs20.rearrange("p (h o) -> p h o", o=1).broadcast_to([128, 20, 64])
                    e.tensor_tensor(qkro.rearrange("p (h d) -> p h d", d=64),
                                    qkvt[qb][:, 0:1280].rearrange("p (h d) -> p h d", d=64),
                                    rsb, ALU.mult)
                    e.fence()
                    x4 = qkro.rearrange("p (h c d) -> p h c d", c=2, d=32)
                    t2 = tmp640.rearrange("p (h d) -> p h d", d=32)
                    cb = ct[qb].rearrange("p (o d) -> p o d", o=1).broadcast_to([128, 20, 32])
                    sb = st_[qb].rearrange("p (o d) -> p o d", o=1).broadcast_to([128, 20, 32])
                    e.tensor_tensor(t2, x4[:, :, 0, :], sb, ALU.mult)
                    e.fence()
                    e.tensor_tensor(x4[:, :, 0, :], x4[:, :, 0, :], cb, ALU.mult)
                    e.fence()
                    e.tensor_tensor(sqv.rearrange("p (h d) -> p h d", d=64)[:, :, 0:32],
                                    x4[:, :, 1, :], sb, ALU.mult)
                    e.fence()
                    e.tensor_tensor(x4[:, :, 1, :], x4[:, :, 1, :], cb, ALU.mult)
                    e.fence()
                    e.tensor_tensor(x4[:, :, 0, :], x4[:, :, 0, :],
                                    sqv.rearrange("p (h d) -> p h d", d=64)[:, :, 0:32], ALU.add)
                    e.fence()
                    e.tensor_tensor(x4[:, :, 1, :], x4[:, :, 1, :], t2, ALU.subtract)
                    e.fence()
                    if gt >= 2:
                        e.wait(d_st)
                    e.tensor_copy(qk_bf[qb][:], qkro[:])
                    e.fence()
                    for gg in range(4):
                        e.tensor_copy(v_sb[:, gt, gg * 65:gg * 65 + 64], ps_qkv[qb][:, 1280 + gg * 64:1280 + (gg + 1) * 64])
                    e.fence()
                    VRP[gt] = p.ctr("v").n
                p.vector(vrope)
                V_QK = v.n

                def stqk(e, gt=gt, qb=qb, V_QK=V_QK):
                    e.wait(v, V_QK)
                    e.dma_start(out=qk_hbm[gt * 128:(gt + 1) * 128, :], in_=qk_bf[qb][:], sem=d_st)
                p.scalar(stqk)
        p.scope = old
    ST_QK = d_st.n

    with ExitStack() as asc3:
        old = p.scope; p.scope = asc3
        k_fmA = p.sbuf("p3_kfmA", [128, 2, 2048], BF16)
        d_qf = [p.ctr("p3_dqf0"), p.ctr("p3_dqf1")]
        d_msk = p.ctr("p3_dmsk2")
        k_fm = p.sbuf("p3_kfm", [128, 4, 2048], BF16)
        q_fm = [p.sbuf(f"p3_qfm{i}", [128, 8, 512], BF16) for i in range(2)]
        expS = [p.sbuf(f"p3_exps{i}", [128, 7, 512], BF16) for i in range(2)]
        den_f = p.sbuf("p3_den", [128, 4], F32)
        y_sb = [p.sbuf(f"p3_ysb{i}", [128, 4, 16, 64], BF16) for i in range(2)]
        amask = p.sbuf("p3_amask", [128, 8, 512], F32)
        ps_s = [p.psum(f"p3_pss{i}", [128, 512]) for i in range(2)]
        ps_av = [p.psum(f"p3_psav{i}", [128, 130]) for i in range(2)]

        def ldmsk(e):
            e.wait(d_st, ST_QK)
            e.dma_start(out=amask[:], in_=V["amask_c"], sem=d_msk)
        p.sync(ldmsk)
        LD_AM = d_msk.n

        TSC, AEXPR, AEXP, VMASK, TAV, VEV = {}, {}, {}, {}, {}, {}

        def kts_of(g):
            if g == 0:
                return [(0, 6, 128), (1, 3, 128), (2, 4, 128), (3, 5, 128)]
            out = [(4 * g - 2, 0, 128), (4 * g - 1, 1, 128)]
            for d in range(4):
                out.append((4 * g + d, 2 + d, 128))
            out.append((0, None, 64))
            return out

        def av_regions(g, j):
            if g == 0:
                return list(range(j + 1))
            return [j, 1 + j, 2 + j, 6]

        prev_last = {}
        for it in range(8):
            s2, gq = it // 4, it % 4
            yb = it % 2
            kts = kts_of(gq)

            def ldkf(e, s2=s2, gq=gq, it=it):
                if gq == 0:
                    e.wait(d_st, ST_QK)
                    if it > 0:
                        e.wait(t)
                    for o in range(4):
                        for c in range(2):
                            e.dma_start_transpose(k_fmA[:, c, o * 512:(o + 1) * 512],
                                                  qk_hbm[s2 * 2048 + o * 512:s2 * 2048 + (o + 1) * 512, 1024 + c * 128:1024 + (c + 1) * 128], sem=d_tr[0])
                qb = it % 2
                e.wait(d_st, ST_QK)
                if it >= 2:
                    e.wait(t)
                for j in range(8):
                    e.dma_start_transpose(q_fm[qb][:, j, :],
                                          qk_hbm[s2 * 2048 + gq * 512:s2 * 2048 + (gq + 1) * 512, j * 128:(j + 1) * 128], sem=d_qf[qb])
            p.sync(ldkf)
            LD_QF = d_qf[it % 2].n
            LD_KFA = d_tr[0].n

            if gq == 0:
                def dupk(e, LD_KFA=LD_KFA):
                    e.wait(d_tr[0], LD_KFA)
                    for gg in range(4):
                        for half in range(2):
                            e.dma_start(out=k_fm[half * 64:(half + 1) * 64, gg, :],
                                        in_=k_fmA[64 * (gg % 2):64 * (gg % 2) + 64, gg // 2, :], sem=d_ld)
                p.sync(dupk)
            LD_KF = d_ld.n

            for h in range(16):
                hb = h % 2
                gkv = h // 4
                hi = it * 16 + h

                for r, (kt, msk, kp) in enumerate(kts):
                    def tscr(e, h=h, gkv=gkv, kt=kt, kp=kp, r=r, it=it, hi=hi,
                             LD_QF=LD_QF, LD_KF=LD_KF):
                        if r == 0:
                            e.wait(d_qf[it % 2], LD_QF)
                            e.wait(d_ld, LD_KF)
                            if hi >= 2:
                                e.wait(t, TAV[hi - 2])
                        if r >= 2:
                            e.wait(a, AEXPR[(hi, r - 2)])
                        elif hi >= 1:
                            e.wait(a, AEXPR[prev_last[hi - 1][r]])
                        qb = it % 2
                        e.matmul(ps_s[r % 2][0:kp, :],
                                 k_fm[64 * (h % 2):64 * (h % 2) + 64, gkv, kt * 128:kt * 128 + kp],
                                 q_fm[qb][64 * (h % 2):64 * (h % 2) + 64, h // 2, :],
                                 start=True, stop=True)
                        TSC[(hi, r)] = p.ctr("t").n
                    p.tensor(tscr)

                    def aexpr(e, hb=hb, r=r, kp=kp, hi=hi):
                        e.wait(t, TSC[(hi, r)])
                        if r == 0 and hi >= 2:
                            e.wait(t, TAV[hi - 2])
                        e.activation(expS[hb][0:kp, r, :], ps_s[r % 2][0:kp, :], AF.Exp)
                        AEXPR[(hi, r)] = p.ctr("a").n
                    p.scalar(aexpr)
                nk = len(kts)
                lp0 = nk - 1 if (nk - 1) % 2 == 0 else nk - 2
                lp1 = nk - 1 if (nk - 1) % 2 == 1 else nk - 2
                prev_last[hi] = {0: (hi, lp0), 1: (hi, lp1)}
                AEXP[hi] = p.ctr("a").n

                def vmask(e, hb=hb, kts=kts, hi=hi):
                    e.wait(a, AEXP[hi])
                    e.wait(d_msk, LD_AM)
                    for r, (kt, msk, kp) in enumerate(kts):
                        if msk is not None:
                            e.tensor_tensor(expS[hb][0:kp, r, :], expS[hb][0:kp, r, :], amask[0:kp, msk, :], ALU.mult)
                    e.fence()
                    VMASK[hi] = p.ctr("v").n
                p.vector(vmask)

                def tav(e, s2=s2, gq=gq, hb=hb, gkv=gkv, kts=kts, hi=hi):
                    e.wait(v, VMASK[hi])
                    if hi >= 1:
                        e.wait(v, VEV[hi - 1])
                    for j in range(4):
                        regs = av_regions(gq, j)
                        pav = ps_av[j % 2]
                        col = (j // 2) * 65
                        nr = len(regs)
                        for ri, r in enumerate(regs):
                            kt, msk, kp = kts[r]
                            e.matmul(pav[:, col:col + 65],
                                     expS[hb][0:kp, r, j * 128:(j + 1) * 128],
                                     v_sb[0:kp, s2 * 16 + kt, gkv * 65:gkv * 65 + 65],
                                     start=(ri == 0), stop=(ri == nr - 1))
                    TAV[hi] = p.ctr("t").n
                p.tensor(tav)

                def vev(e, h=h, yb=yb, hi=hi, it=it):
                    e.wait(t, TAV[hi])
                    if h == 0 and it >= 2:
                        e.wait(d_y3[yb])
                    for j in range(4):
                        pav = ps_av[j % 2]
                        col = (j // 2) * 65
                        e.tensor_copy(den_f[:, j:j + 1], pav[:, col + 64:col + 65])
                    e.fence()
                    e.reciprocal(den_f[:], den_f[:])
                    e.fence()
                    for j in range(4):
                        pav = ps_av[j % 2]
                        col = (j // 2) * 65
                        e.tensor_scalar_mul(y_sb[yb][:, j, h, :], pav[:, col:col + 64], den_f[:, j:j + 1])
                    e.fence()
                    VEV[hi] = p.ctr("v").n
                p.vector(vev)

            def sty3(e, it=it, yb=yb):
                e.wait(v, VEV[it * 16 + 15])
                e.dma_start(out=y_hbm[it * 512:(it + 1) * 512, :].rearrange("(s q) d -> q s d", q=128),
                            in_=y_sb[yb].rearrange("p j h d -> p j (h d)"), sem=d_y3[yb])
            p.scalar(sty3)
        p.scope = old

    with ExitStack() as osc:
        old = p.scope; p.scope = osc
        wo_sb = p.sbuf("p3_wosb", [128, 8, D], BF16)
        y_fm = [p.sbuf(f"p3_yfm{i}", [128, 8, 512], BF16) for i in range(2)]
        xc3 = [p.sbuf(f"p3_xc{i}", [128, D], F32) for i in range(2)]
        x3c = [p.sbuf(f"p3_x3c{i}", [128, D], F32) for i in range(2)]
        ps_o = [p.psum(f"p3_pso{i}", [128, D]) for i in range(2)]

        def ldwo(e):
            e.wait(d_y3[0])
            e.wait(d_y3[1])
            e.dma_start(out=wo_sb[:], in_=wo_h.rearrange("(j q) f -> q j f", q=128), sem=d_w[1])
        p.sync(ldwo)
        LD_WO = d_w[1].n

        for cg in range(8):
            bb = cg % 2

            def ldtry(e, cg=cg, bb=bb):
                e.wait(d_y3[0])
                e.wait(d_y3[1])
                if cg >= 2:
                    e.wait(t)
                for j in range(8):
                    e.dma_start_transpose(y_fm[bb][:, j, :], y_hbm[cg * 512:(cg + 1) * 512, j * 128:(j + 1) * 128], sem=d_tr[bb])
            p.sync(ldtry)
            LD_YT = d_tr[bb].n

            for tt_i in range(4):
                i = cg * 4 + tt_i
                pb = i % 2

                def ldxc3(e, i=i, pb=pb):
                    e.wait(d_y3[0])
                    e.wait(d_y3[1])
                    if i >= 2:
                        e.wait(d_sx3[pb])
                    e.dma_start(out=xc3[pb][:], in_=x2_hbm[i * 128:(i + 1) * 128, :], sem=d_lx3[pb])
                p.sync(ldxc3)
                LD_XC3 = d_lx3[pb].n

                def mmo(e, cg=cg, tt_i=tt_i, bb=bb, pb=pb, LD_YT=LD_YT):
                    e.wait(d_tr[bb], LD_YT)
                    e.wait(d_w[1], LD_WO)
                    if cg * 4 + tt_i >= 2:
                        e.wait(v)
                    for ng in range(2):
                        for j in range(8):
                            e.matmul(ps_o[pb][:, ng * 512:(ng + 1) * 512],
                                     y_fm[bb][:, j, tt_i * 128:(tt_i + 1) * 128],
                                     wo_sb[:, j, ng * 512:(ng + 1) * 512],
                                     start=(j == 0), stop=(j == 7))
                p.tensor(mmo)

                def vres3(e, i=i, pb=pb, LD_XC3=LD_XC3):
                    e.wait(t)
                    e.wait(d_lx3[pb], LD_XC3)
                    e.tensor_tensor(x3c[pb][:], xc3[pb][:], ps_o[pb][:], ALU.add)
                    e.fence()
                p.vector(vres3)
                V_R3 = v.n

                def stx3(e, i=i, pb=pb, V_R3=V_R3):
                    e.wait(v, V_R3)
                    e.dma_start(out=x3_hbm[i * 128:(i + 1) * 128, :], in_=x3c[pb][:], sem=d_sx3[pb])
                p.scalar(stx3)
        p.scope = old

    p.scope = _oldscope
    _p3s.close()


def emit_l1_ffn(p, io, V):
    nc = p.nc
    v, a, t, g = p.ctr("v"), p.ctr("a"), p.ctr("t"), p.ctr("g")
    d_ld = p.ctr("d_ld")
    d_st = p.ctr("d_st")
    d_tr = [p.ctr("d_tr0"), p.ctr("d_tr1")]
    d_gg = p.ctr("p2_dgg")
    d_z = p.ctr("p2_dz")
    d_w = [p.ctr("d_w0"), p.ctr("d_w1")]
    d_y = [p.ctr("p4_dy0"), p.ctr("p4_dy1")]
    d_rp = p.ctr("p2_drp")
    d_c4 = p.ctr("p4_dc4")

    x3_hbm, x4_hbm, xn4_hbm = io["x3_hbm"], io["x4_hbm"], io["xn4_hbm"]
    xall4 = io["p4_xall"]
    ptok_hbm = io["p4_ptok"]
    cnt_in, cnt_all = io["p4_cntin"], io["p4_cntall"]
    pay_in4, pay_ar4 = io["p4_payin"], io["p4_payar"]
    mypay4_hbm, xe4_hbm = io["p4_mypay"], io["p4_xe"]
    y_in4, y_all4 = io["p4_yin"], io["p4_yall"]
    acc4_hbm, dw4_hbm = io["p4_acc"], io["p4_dw"]
    w1_h, w2_h = io["w_l1w1_bf"], io["w_l1w2_bf"]

    _p4s = ExitStack()
    _oldscope = p.scope
    p.scope = _p4s

    S = p.shared
    me_i, gidt, riota, baset, dumpt = S["me_i"], S["gidt"], S["riota"], S["baset"], S["dumpt"]
    tl8, epad8, smod, ones_col, ccol64 = S["tl8"], S["epad8"], S["smod"], S["ones_col"], S["ccol64"]
    ztile, zb, mask8, ones88 = S["ztile"], S["zb"], S["mask8"], S["ones88"]
    erow8 = p.sbuf("p4_erow8", [128, 8], F32)
    cmlt = p.sbuf("p4_cmlt", [8, 8], F32)
    cnt_ec = p.sbuf("p4_cntec", [8, 8], F32)
    cnt16 = p.sbuf("p4_cnt16", [16, 1], F32)
    off4 = p.sbuf("p4_off4", [8, 1], F32)
    len4 = p.sbuf("p4_len4", [8, 1], F32)
    og4 = p.sbuf("p4_og4", [8, 1], F32)
    sc81 = p.sbuf("p4_sc81", [8, 1], F32)
    oge4_row = p.sbuf("p4_oge4row", [1, 16], F32)
    cntrow = p.sbuf("p4_cntrow", [1, 16], F32)
    bcrow = p.sbuf("p4_bcrow", [1, 16], F32)
    cnt_cb = p.sbuf("p4_cntcb", [8, 16], F32)
    m_ce4 = p.sbuf("p4_mce4", [8, 8], F32)
    m_cme4 = p.sbuf("p4_mcme4", [8, 1], F32)
    ogrow = p.sbuf("p4_ogrow", [1, 16], F32)
    cmltc = p.sbuf("p4_cmltc", [8, 1], F32)
    bc128 = p.sbuf("p4_bc128", [128, 16], F32)
    baset4 = p.sbuf("p4_baset4", [128, 1], F32)
    srcidx = p.sbuf("p4_srcidx", [128, 32], F32)
    srct1 = p.sbuf("p4_srct1", [128, 32], F32)
    rloc4 = p.sbuf("p4_rloc", [128, 32], F32)

    ogbc4 = p.sbuf("p4_ogbc4", [128, 16], F32)
    idx644 = p.sbuf("p4_idx64", [128, 64], F32)
    idx644_i = p.sbuf("p4_idx64i", [128, 64], I32)
    wg4_tm2 = p.sbuf("p4_wgtm2", [128, 32, 2], F32)
    oh_em = p.sbuf("p4_ohem", [8, TLOC], BF16)
    idn_f = p.sbuf("p4_idnf", [128, 128], F32)
    idn_b = p.sbuf("p4_idnb", [128, 128], BF16)
    gidr4 = p.sbuf("p4_gidr", [128, 8], F32)
    tlr4 = p.sbuf("p4_tlr", [128, 8], F32)
    v1m4 = p.sbuf("p4_v1m", [128, 8], F32)
    v2m4 = p.sbuf("p4_v2m", [128, 8], F32)
    idxs4 = p.sbuf("p4_idxs", [128, 8], I32)
    d_cst4 = p.ctr("p4_dcst")

    def ldc4(e):
        e.wait(p.ctr("p3_dsx0"))
        e.wait(p.ctr("p3_dsx1"))
        e.dma_start(out=erow8[:], in_=V["erow8_c"], sem=d_cst4)
        e.dma_start(out=cmltc[:], in_=V["cmltc_c"], sem=d_cst4)
        e.dma_start(out=idn_f[:], in_=V["idn_c"], sem=d_cst4)
    p.sync(ldc4)
    LD_C4 = d_cst4.n

    def stz4(e):
        e.wait(d_cst4, LD_C4)
        for i in range(33):
            e.dma_start(out=acc4_hbm[i * 128:(i + 1) * 128, :], in_=ztile[:], sem=d_z)
        payv = pay_in4.rearrange("(q x) f -> q x f", q=128)
        for j in range(8):
            e.dma_start(out=payv[:, j * 64:(j + 1) * 64, :], in_=ztile[:], sem=d_z)
        e.dma_start(out=payv[:, 512:513, :], in_=ztile[:, 0:16], sem=d_z)
        for i in range(CPAY // 128):
            e.dma_start(out=y_all4[8 * TLOC + i * 128:8 * TLOC + (i + 1) * 128, :], in_=zb[:], sem=d_z)
    p.gpsimd(stz4)
    ST_Z4 = d_z.n

    emit_ln_plain(p, x3_hbm, xn4_hbm, "p4ln", gates=("p3_dsx0", "p3_dsx1"))
    ST_LN4 = d_st.n

    def ag_x4(e):
        e.wait(d_st, ST_LN4)
        e.collective_compute("AllGather", ALU.bypass, replica_groups=[list(range(NCORES))],
                             ins=[xn4_hbm], outs=[xall4])
    p.gpsimd(ag_x4)
    CC_X4 = g.n

    with ExitStack() as rsc:
        old = p.scope; p.scope = rsc
        wr_f = p.sbuf("p4_wrf", [128, 8, 8], F32)
        wr_b = p.sbuf("p4_wrb", [128, 8, 8], BF16)
        xnf4 = [p.sbuf(f"p4_xnf{i}", [128, 8, 512], BF16) for i in range(2)]
        lg = p.sbuf("p4_lg", [128, 8], F32)
        mx8 = p.sbuf("p4_mx8", [128, 8], F32)
        nmx = p.sbuf("p4_nmx", [128, 1], F32)
        midx = p.sbuf("p4_midx", [128, 8], U32)
        eidf = p.sbuf("p4_eidf", [128, 1], F32)
        exb = p.sbuf("p4_exb", [128, 8], F32)
        den1 = p.sbuf("p4_den1", [128, 1], F32)
        oh8 = [p.sbuf(f"p4_oh8{i}", [128, 8], BF16) for i in range(2)]
        ptk = [p.sbuf(f"p4_ptk{i}", [128, 1], F32) for i in range(2)]
        ps_r = p.psum("p4_psr", [128, 8])
        ps_t = p.psum("p4_pst", [8, 128])
        d_or = [p.ctr("p4_dor0"), p.ctr("p4_dor1")]
        TOH, VOH = {}, {}

        def vidn(e):
            e.wait(d_cst4, LD_C4)
            e.tensor_copy(idn_b[:], idn_f[:])
            e.fence()
        p.vector(vidn)
        V_IDN = v.n

        def ldwr(e):
            e.wait(d_st, ST_LN4)
            e.dma_start(out=wr_f[:], in_=V["p4_wr_fm"], sem=d_c4)
        p.sync(ldwr)
        LD_WR = d_c4.n

        def vwr(e):
            e.wait(d_c4, LD_WR)
            e.tensor_copy(wr_b[:], wr_f[:])
            e.fence()
        p.vector(vwr)
        V_WR = v.n

        for cg in range(8):
            bb = cg % 2

            def ldtr4(e, cg=cg, bb=bb):
                e.wait(d_st, ST_LN4)
                if cg >= 2:
                    e.wait(t)
                for j in range(8):
                    e.dma_start_transpose(xnf4[bb][:, j, :], xn4_hbm[cg * 512:(cg + 1) * 512, j * 128:(j + 1) * 128], sem=d_tr[bb])
            p.sync(ldtr4)
            LD_X4 = d_tr[bb].n

            for tt_i in range(4):
                i = cg * 4 + tt_i
                ob = i % 2

                def mmr(e, cg=cg, tt_i=tt_i, bb=bb, LD_X4=LD_X4):
                    e.wait(d_tr[bb], LD_X4)
                    e.wait(v, V_WR)
                    if cg * 4 + tt_i >= 1:
                        e.wait(v)
                    for j in range(8):
                        e.matmul(ps_r[:], xnf4[bb][:, j, tt_i * 128:(tt_i + 1) * 128], wr_b[:, j, :], start=(j == 0), stop=(j == 7))
                p.tensor(mmr)

                def vrt(e, i=i, ob=ob):
                    e.wait(t)
                    if i >= 2:
                        e.wait(d_or[ob])
                    e.tensor_copy(lg[:], ps_r[:])
                    e.fence()
                    e.max(out=mx8[:], in_=lg[:])
                    e.fence()
                    e.max_index(midx[:], mx8[:], lg[:])
                    e.fence()
                    e.tensor_copy(eidf[:], midx[:, 0:1])
                    e.fence()
                    e.tensor_scalar_mul(nmx[:], mx8[:, 0:1], -1.0)
                    e.fence()
                p.vector(vrt)

                def art(e):
                    e.wait(v)
                    e.activation(exb[:], lg[:], AF.Exp, bias=nmx[:, 0:1])
                p.scalar(art)

                def vrt2(e, i=i, ob=ob):
                    e.wait(a)
                    if i >= 2:
                        e.wait(t, TOH[i - 2])
                    e.reduce_sum(den1[:], exb[:], axis=AX.X)
                    e.fence()
                    e.reciprocal(ptk[ob][:], den1[:])
                    e.fence()
                    e.tensor_scalar(oh8[ob][:], erow8[:], eidf[:, 0:1], 0.0, ALU.is_equal, ALU.add)
                    e.fence()
                p.vector(vrt2)
                V_RT = v.n

                def tmoh(e, i=i, ob=ob, V_RT=V_RT):
                    e.wait(v, V_RT)
                    e.wait(v, V_IDN)
                    if i >= 1:
                        e.wait(v, VOH[i - 1])
                    e.matmul(ps_t[:], oh8[ob][:], idn_b[:], start=True, stop=True)
                    TOH[i] = p.ctr("t").n
                p.tensor(tmoh)

                def voh(e, i=i, ob=ob):
                    e.wait(t, TOH[i])
                    e.tensor_copy(oh_em[:, i * 128:(i + 1) * 128], ps_t[:])
                    e.fence()
                    VOH[i] = p.ctr("v").n
                p.vector(voh)

                def stor(e, i=i, ob=ob, V_RT=V_RT):
                    e.wait(v, V_RT)
                    e.dma_start(out=ptok_hbm[i * 128:(i + 1) * 128, :], in_=ptk[ob][:], sem=d_or[ob])
                p.scalar(stor)
        p.scope = old
        V_OHLAST = VOH[31]
    ST_RT0 = [p.ctr("p4_dor0").n, p.ctr("p4_dor1").n]

    with ExitStack() as bsc4:
        old = p.scope; p.scope = bsc4
        ohf = p.sbuf("p4_ohf", [8, TLOC], F32)
        rank4 = p.sbuf("p4_rank", [8, TLOC], F32)
        sel4 = p.sbuf("p4_sel", [8, TLOC], F32)
        dest4 = p.sbuf("p4_dest", [8, TLOC], F32)
        ones84 = p.sbuf("p4_ones8", [8, TLOC], F32)
        ones8c = p.sbuf("p4_ones8c", [8, 1], F32)
        drow = p.sbuf("p4_drow", [1, TLOC], F32)
        krow = p.sbuf("p4_krow", [1, TLOC], F32)
        epadL = p.sbuf("p4_epadL", [8, 1], F32)
        mebase4 = p.sbuf("p4_mebase", [8, 1], F32)
        ps_dk = p.psum("p4_psdk", [1, 1024])

        def ldc4b(e):
            e.dma_start(out=epadL[:], in_=V["epadL_c"], sem=d_c4)
            e.dma_start(out=mebase4[:], in_=V["mebase4_c"], sem=d_c4)
        p.sync(ldc4b)
        LD_C4B = d_c4.n

        def vrk1(e):
            e.wait(v, V_OHLAST)
            e.wait(p.ctr("p4_dor0"), ST_RT0[0])
            e.wait(p.ctr("p4_dor1"), ST_RT0[1])
            e.memset(ones84[:], 1.0)
            e.memset(cntrow[:], 0.0)
            e.fence()
            e.tensor_copy(ohf[:], oh_em[:, :])
            e.fence()
            e.reduce_sum(cnt16[0:8, :], ohf[:], axis=AX.X)
            e.fence()
        p.vector(vrk1)
        V_RK1 = v.n

        def rpc4(e):
            e.wait(v, V_RK1)
            e.dma_start(out=cntrow[:, 0:8], in_=cnt16[0:8, :], sem=d_rp)
        p.sync(rpc4)
        RP_C4 = d_rp.n

        def voff4(e):
            e.wait(d_c4, LD_C4B)
            e.wait(d_rp, RP_C4)
            e.tensor_tensor_scan(rank4[:], ones84[:], ohf[:], 0.0, ALU.mult, ALU.add)
            e.fence()
            e.tensor_tensor(rank4[:], rank4[:], ohf[:], ALU.subtract)
            e.fence()
            e.tensor_scalar(sel4[:], rank4[:], float(LCAP) - 0.5, 0.0, ALU.is_lt, ALU.add)
            e.fence()
            e.tensor_tensor(sel4[:], sel4[:], ohf[:], ALU.mult)
            e.fence()
            e.memset(ones8c[:], 1.0)
            e.fence()
            e.tensor_scalar(dest4[:], rank4[:], epadL[:, 0:1], 0.0, ALU.add, ALU.add)
            e.fence()
            e.tensor_scalar(dest4[:], dest4[:], mebase4[:, 0:1], 0.0, ALU.add, ALU.add)
            e.fence()
            e.tensor_tensor(dest4[:], dest4[:], sel4[:], ALU.mult)
            e.fence()
        p.vector(voff4)
        V_OFF4 = v.n

        DKT = {}
        for sl in range(8):
            def tdk(e, sl=sl):
                e.wait(v, V_OFF4)
                if sl >= 1:
                    e.wait(v)
                e.matmul(ps_dk[0:1, 0:512], ones8c[:], dest4[:, sl * 512:(sl + 1) * 512], start=True, stop=True)
                e.matmul(ps_dk[0:1, 512:1024], ones8c[:], sel4[:, sl * 512:(sl + 1) * 512], start=True, stop=True)
            p.tensor(tdk)

            def vdk(e, sl=sl):
                e.wait(t)
                e.tensor_copy(drow[:, sl * 512:(sl + 1) * 512], ps_dk[0:1, 0:512])
                e.tensor_copy(krow[:, sl * 512:(sl + 1) * 512], ps_dk[0:1, 512:1024])
                e.fence()
                DKT[sl] = p.ctr("v").n
            p.vector(vdk)

        def vfin4(e):
            e.wait(v, DKT[7])
            e.tensor_scalar(krow[:], krow[:], 1.0, -1.0, ALU.subtract, ALU.mult)
            e.fence()
            e.tensor_scalar(rank4[0:1, :], smod[0:1, :], float(64 * LCAP), 0.0, ALU.add, ALU.add)
            e.fence()
            e.tensor_tensor(krow[:], krow[:], rank4[0:1, :], ALU.mult)
            e.fence()
            e.tensor_tensor(drow[:], drow[:], krow[:], ALU.add)
            e.fence()
        p.vector(vfin4)
        V_OFF4b = v.n

        def stdw4(e):
            e.wait(v, V_OFF4b)
            e.dma_start(out=dw4_hbm[0][0], in_=drow[:], sem=d_st)
        p.scalar(stdw4)
        p.scope = old
    ST_DW4 = d_st.n

    with ExitStack() as csc4:
        old = p.scope; p.scope = csc4
        dwt4 = p.sbuf("p4_dwt", [128, 32], F32)
        dwi4 = p.sbuf("p4_dwi", [128, 32], I32)
        pay_src4 = p.sbuf("p4_psrc", [128, 32, 16], F32)
        ptt = p.sbuf("p4_ptt", [128, 32, 1], F32)

        def ldpt(e):
            e.wait(d_st, ST_DW4)
            e.dma_start(out=ptt[:], in_=ptok_hbm.rearrange("(q s) o -> q s o", q=128), sem=d_c4)
            e.dma_start(out=dwt4[:], in_=dw4_hbm[0][0].rearrange("(q s) -> q s", q=128), sem=d_c4)
        p.sync(ldpt)
        LD_PT = d_c4.n

        def vpay04(e):
            e.wait(d_c4, LD_PT)
            e.memset(pay_src4[:], 0.0)
            e.fence()
            e.tensor_copy(pay_src4[:, :, 0], gidt[:])
            e.fence()
            e.tensor_copy(pay_src4[:, :, 1], ptt[:, :, 0])
            e.fence()
            e.tensor_copy(dwi4[:], dwt4[:])
            e.fence()
        p.vector(vpay04)
        V_PAY04 = v.n

        def gpay4(e, V_PAY04=V_PAY04):
            e.wait(v, V_PAY04)
            e.wait(d_z, ST_Z4)
            rhc = e.alloc_register(name="p4cntreg")
            vc_ = bass_rust.make_scalar_value(rhc, min_val=64 * LCAP + 32, max_val=64 * LCAP + 40)
            e.reg_load(rhc, me_i[0:1, 1:2])
            e.dma_start(out=pay_in4[DynSlice(vc_, 1), :], in_=cntrow[:], sem=d_gg)
            for s in range(32):
                e.indirect_dma_start(
                    out=pay_in4, out_offset=IndirectOffsetOnAxis(dwi4[:, s:s + 1], 0),
                    in_=pay_src4[:, s, :], in_offset=None,
                    sem=d_gg)
        p.gpsimd(gpay4)
        p.scope = old
    GD_PAY4 = d_gg.n

    def ag_pay4(e):
        e.wait(d_gg, GD_PAY4)
        e.collective_compute("AllReduce", ALU.add, replica_groups=[list(range(NCORES))],
                             ins=[pay_in4], outs=[pay_ar4])
    p.gpsimd(ag_pay4)
    CC_PAY4 = g.n

    with ExitStack() as dsc4:
        old = p.scope; p.scope = dsc4
        mypay4 = p.sbuf("p4_mypaysb", [128, 32, 16], F32)
        idxg4 = p.sbuf("p4_idxg", [128, 32], I32)
        srci = p.sbuf("p4_srci", [128, 32], I32)
        X_tm4 = p.sbuf("p4_xtm", [128, 32, D], BF16)
        ps_bcr = p.psum("p4_psbcr", [128, 16])

        def ldcb4(e):
            e.wait(g, CC_PAY4)
            e.dma_start(out=cnt_cb[:], in_=pay_ar4[64 * LCAP + 32:64 * LCAP + 40, :], sem=d_c4)
        p.sync(ldcb4)
        LD_CB4 = d_c4.n

        def vmce4(e):
            e.wait(d_c4, LD_CB4)
            e.wait(d_cst4, LD_C4)
            e.tensor_scalar(m_ce4[:], cnt_cb[:, 0:8], float(LCAP), 0.0, ALU.min, ALU.add)
            e.fence()
            e.tensor_tensor(cnt_cb[:, 0:8], m_ce4[:], mask8[:], ALU.mult)
            e.fence()
            e.reduce_sum(m_cme4[:], cnt_cb[:, 0:8], axis=AX.X)
            e.fence()
            e.tensor_scalar(cnt_cb[:, 8:16], m_ce4[:], cmltc[:, 0:1], 0.0, ALU.mult, ALU.add)
            e.fence()
        p.vector(vmce4)
        V_MCE4 = v.n

        def tog4(e):
            e.wait(v, V_MCE4)
            e.matmul(ps_bcr[0:1, 0:8], ones8c[:], cnt_cb[:, 8:16], start=True, stop=True)
        p.tensor(tog4)
        T_OG4 = t.n

        def rpm4(e):
            e.wait(v, V_MCE4)
            e.dma_start(out=bcrow[:, 0:8], in_=m_cme4[:], sem=d_rp)
        p.sync(rpm4)
        RP_M4 = d_rp.n

        def vog4(e):
            e.wait(t, T_OG4)
            e.wait(d_rp, RP_M4)
            e.tensor_copy(ogrow[:, 0:8], ps_bcr[0:1, 0:8])
            e.fence()
            e.tensor_scalar(oge4_row[:, 0:8], ogrow[:, 0:8], float(TLOC), 0.0, ALU.min, ALU.add)
            e.fence()
            e.tensor_tensor(oge4_row[:, 0:8], oge4_row[:, 0:8], tl8[:], ALU.add)
            e.fence()
            e.tensor_scalar(oge4_row[:, 8:16], ogrow[:, 0:8], float(TLOC), -1.0, ALU.subtract, ALU.mult)
            e.fence()
            e.tensor_scalar(oge4_row[:, 8:16], oge4_row[:, 8:16], 0.0, 0.0, ALU.max, ALU.add)
            e.fence()
            e.tensor_scalar(cntrow[:, 8:16], cntrow[:, 0:8], float(LCAP), 0.0, ALU.min, ALU.add)
            e.fence()
            e.tensor_tensor(oge4_row[:, 8:16], oge4_row[:, 8:16], cntrow[:, 8:16], ALU.min)
            e.fence()
            e.tensor_tensor_scan(bcrow[:, 8:16], ones88[0:1, :], bcrow[:, 0:8], 0.0, ALU.mult, ALU.add)
            e.fence()
            e.tensor_scalar(bcrow[:, 0:8], bcrow[:, 0:8], float(8 * LCAP), -1.0, ALU.subtract, ALU.mult)
            e.fence()
            e.tensor_copy(srct1[0:1, 0:8], bcrow[:, 8:16])
            e.fence()
            e.tensor_copy(bcrow[:, 8:16], bcrow[:, 0:8])
            e.fence()
            e.tensor_copy(bcrow[:, 0:8], srct1[0:1, 0:8])
            e.fence()
        p.vector(vog4)
        V_OG4 = v.n
        V_FOLD4 = V_OG4

        def tbcr(e):
            e.wait(v, V_OG4)
            e.matmul(ps_bcr[:], ones_col[:], bcrow[:], start=True, stop=True)
        p.tensor(tbcr)
        T_BCR = t.n

        def vsrc4(e):
            e.wait(t, T_BCR)
            e.wait(d_cst4)
            e.tensor_copy(bc128[:], ps_bcr[:])
            e.fence()
            e.tensor_scalar_mul(baset4[:], baset[:, 0:1], 0.25)
            e.fence()
            e.tensor_scalar(rloc4[:], gidt[:], baset[:, 0:1], 0.0, ALU.subtract, ALU.add)
            e.fence()
            e.tensor_scalar(srcidx[:], rloc4[:], baset4[:, 0:1], 0.0, ALU.add, ALU.add)
            e.fence()
            for cc in range(7):
                e.tensor_scalar(srct1[:], rloc4[:], bc128[:, cc:cc + 1], 0.0, ALU.is_ge, ALU.add)
                e.fence()
                e.tensor_scalar(srct1[:], srct1[:], bc128[:, 8 + cc:9 + cc], 0.0, ALU.mult, ALU.add)
                e.fence()
                e.tensor_tensor(srcidx[:], srcidx[:], srct1[:], ALU.add)
                e.fence()
            e.tensor_scalar(srcidx[:], srcidx[:], float(PAYR4 - 1), 0.0, ALU.min, ALU.add)
            e.fence()
            e.tensor_copy(srci[:], srcidx[:])
            e.fence()
        p.vector(vsrc4)
        V_SRC4 = v.n

        def gmp4(e):
            e.wait(v, V_SRC4)
            for s in range(32):
                e.indirect_dma_start(
                    out=mypay4[:, s, :], out_offset=None,
                    in_=pay_ar4, in_offset=IndirectOffsetOnAxis(srci[:, s:s + 1], 0),
                    sem=d_gg)
        p.gpsimd(gmp4)
        GD_MP4 = d_gg.n

        def stmp4(e):
            e.wait(d_gg, GD_MP4)
            e.dma_start(out=mypay4_hbm.rearrange("(q s) f -> q s f", q=128), in_=mypay4[:], sem=d_st)
        p.scalar(stmp4)

        def vidx4(e):
            e.wait(d_gg, GD_MP4)
            e.tensor_copy(idxg4[:], mypay4[:, :, 0])
            e.fence()
        p.vector(vidx4)
        V_IDXG4 = v.n

        def ggx4(e):
            e.wait(v, V_IDXG4)
            for s in range(32):
                e.indirect_dma_start(
                    out=X_tm4[:, s, :], out_offset=None,
                    in_=xall4, in_offset=IndirectOffsetOnAxis(idxg4[:, s:s + 1], 0),
                    sem=d_gg)
        p.gpsimd(ggx4)
        GD_X4 = d_gg.n

        def stxe4(e):
            e.wait(d_gg, GD_X4)
            e.dma_start(out=xe4_hbm.rearrange("(q s) d -> q s d", q=128), in_=X_tm4[:], sem=d_st)
        p.scalar(stxe4)
        p.scope = old
    ST_XE4 = d_st.n

    with ExitStack() as esc4:
        old = p.scope; p.scope = esc4
        w2_sb = p.sbuf("p4_w2sb", [128, 32, D], BF16)
        w1sl = [p.sbuf(f"p4_w1sl{i}", [128, 8, 1024], BF16) for i in range(2)]
        xfc4 = [p.sbuf(f"p4_xfc{i}", [128, 8, 512], BF16) for i in range(2)]
        h_bf4 = p.sbuf("p4_hbf", [128, 32, 512], BF16)
        Y_cg4 = [p.sbuf(f"p4_ycg{i}", [128, 4, 1040], BF16) for i in range(2)]
        ps_h4 = [p.psum(f"p4_psh{i}", [128, 512]) for i in range(2)]
        ps_y4 = [p.psum(f"p4_psy{i}", [128, 512]) for i in range(2)]
        d_w1 = [p.ctr("p4_dw10"), p.ctr("p4_dw11")]

        def ldw24(e):
            e.wait(d_st, ST_XE4)
            e.dma_start(out=w2_sb[:], in_=w2_h.rearrange("(j q) f -> q j f", q=128), sem=d_w[1])
        p.sync(ldw24)
        LD_W24 = d_w[1].n

        def ldwg4(e):
            e.wait(d_st, ST_XE4)
            e.dma_start(out=wg4_tm2[:], in_=mypay4_hbm[:, 0:2].rearrange("(s q) f -> q s f", q=128), sem=d_ld)
        p.sync(ldwg4)
        LD_WG4 = d_ld.n

        for cg in range(8):
            bb = cg % 2

            def ldxf4(e, cg=cg, bb=bb):
                e.wait(d_st, ST_XE4)
                if cg >= 2:
                    e.wait(t)
                for j in range(8):
                    e.dma_start_transpose(xfc4[bb][:, j, :], xe4_hbm[cg * 512:(cg + 1) * 512, j * 128:(j + 1) * 128], sem=d_tr[bb])
            p.sync(ldxf4)
            LD_XF4 = d_tr[bb].n

            for hh in range(4):
                wb = hh % 2

                def ldw1(e, cg=cg, hh=hh, wb=wb):
                    e.wait(d_st, ST_XE4)
                    if cg > 0 or hh > 1:
                        e.wait(t)
                    e.dma_start(out=w1sl[wb][:], in_=w1_h[:, hh * 1024:(hh + 1) * 1024].rearrange("(j q) f -> q j f", q=128), sem=d_w1[wb])
                p.sync(ldw1)
                LW1 = d_w1[wb].n

                for hf in range(8):
                    hfg = hh * 8 + hf

                    def mmh4(e, cg=cg, hh=hh, hf=hf, bb=bb, wb=wb, LD_XF4=LD_XF4, LW1=LW1):
                        e.wait(d_tr[bb], LD_XF4)
                        e.wait(d_w1[wb], LW1)
                        if cg > 0 or hh > 0 or hf > 0:
                            e.wait(a)
                        for j in range(8):
                            e.matmul(ps_h4[hf % 2][:], w1sl[wb][:, j, hf * 128:(hf + 1) * 128], xfc4[bb][:, j, :], start=(j == 0), stop=(j == 7))
                    p.tensor(mmh4)

                    def evh4(e, hfg=hfg, hf=hf):
                        e.wait(t)
                        e.activation(h_bf4[:, hfg, :], ps_h4[hf % 2][:], AF.Relu)
                    p.scalar(evh4)
            A_H4 = a.n

            for tt_i in range(4):
                for ng in range(2):
                    gi = tt_i * 2 + ng

                    def mmy4(e, cg=cg, tt_i=tt_i, ng=ng, gi=gi, A_H4=A_H4, LD_W24=LD_W24):
                        e.wait(a, A_H4)
                        e.wait(d_w[1], LD_W24)
                        if cg * 8 + gi >= 2:
                            e.wait(v, YEV[1000 + cg * 8 + gi - 2])
                        for jf in range(32):
                            e.matmul(ps_y4[gi % 2][:], h_bf4[:, jf, tt_i * 128:(tt_i + 1) * 128], w2_sb[:, jf, ng * 512:(ng + 1) * 512], start=(jf == 0), stop=(jf == 31))
                    p.tensor(mmy4)

                    def evy4(e, cg=cg, tt_i=tt_i, ng=ng, gi=gi, bb=bb, LD_WG4=LD_WG4):
                        e.wait(t)
                        e.wait(d_ld, LD_WG4)
                        if cg >= 2 and gi == 0:
                            e.wait(d_y[bb])
                        e.tensor_scalar_mul(Y_cg4[bb][:, tt_i, ng * 512:(ng + 1) * 512], ps_y4[gi % 2][:], wg4_tm2[:, cg * 4 + tt_i, 1:2])
                        e.fence()
                        YEV[1000 + cg * 8 + gi] = p.ctr("v").n
                    p.vector(evy4)

            def vecho4(e, cg=cg, bb=bb):
                ycf = Y_cg4[bb].bitcast(F32)
                for tt_i in range(4):
                    e.tensor_copy(ycf[:, tt_i, 512:513], wg4_tm2[:, cg * 4 + tt_i, 0:1])
                e.memset(ycf[:, :, 513:520], 0.0)
                e.fence()
            p.vector(vecho4)
            V_ECHO4 = v.n

            def sty4(e, cg=cg, bb=bb, V_ECHO4=V_ECHO4):
                e.wait(v, V_ECHO4)
                e.dma_start(out=y_in4.rearrange("(s q) f -> q s f", q=128)[:, cg * 4:(cg + 1) * 4, :], in_=Y_cg4[bb][:], sem=d_y[bb])
            p.scalar(sty4)
        p.scope = old

    def ag_y4(e):
        e.wait(d_y[0])
        e.wait(d_y[1])
        e.collective_compute("AllGather", ALU.bypass, replica_groups=[list(range(NCORES))],
                             ins=[y_in4], outs=[y_all4[0:8 * TLOC, :]])
    p.gpsimd(ag_y4)
    CC_Y4 = g.n

    with ExitStack() as fsc4:
        old = p.scope; p.scope = fsc4
        slab4 = [p.sbuf(f"p4_slab{i}", [128, 8, 1040], BF16) for i in range(2)]
        yf4 = [p.sbuf(f"p4_yf{i}", [128, 8, D], F32) for i in range(2)]
        d_sl4 = [p.ctr("p4_dsl0"), p.ctr("p4_dsl1")]
        idxs4b = [p.sbuf(f"p4_idxs{i}", [128, 8], I32) for i in range(2)]
        GSC4, V_SL4 = {}, {}
        dsc8 = p.sbuf("p4_dsc8", [128, 8], F32)
        ps_bc4 = p.psum("p4_psbc", [128, 16])

        def tbc4(e):
            e.wait(v, V_FOLD4)
            e.matmul(ps_bc4[:], ones_col[:], oge4_row[:], start=True, stop=True)
        p.tensor(tbc4)
        T_BC4 = t.n

        def vbc4(e):
            e.wait(t, T_BC4)
            e.tensor_copy(ogbc4[:], ps_bc4[:])
            e.fence()
            for ee in range(8):
                e.tensor_scalar(idx644[:, ee * 8:(ee + 1) * 8], ccol64[:, ee * 8:(ee + 1) * 8],
                                ogbc4[:, ee:ee + 1], 0.0, ALU.add, ALU.add)
            e.fence()
            e.tensor_copy(idx644_i[:], idx644[:])
            e.fence()
        p.vector(vbc4)
        V_BC4 = v.n

        for ee in range(8):
            cb = ee % 2

            def gslab4(e, ee=ee, cb=cb):
                e.wait(g, CC_Y4)
                e.wait(v, V_BC4)
                if ee >= 2:
                    e.wait(d_gg, GSC4[ee - 2])
                for k in range(8):
                    e.indirect_dma_start(
                        out=slab4[cb][:, k, :], out_offset=None,
                        in_=y_all4, in_offset=IndirectOffsetOnAxis(idx644_i[:, ee * 8 + k:ee * 8 + k + 1], 0),
                        sem=d_sl4[cb])
            p.gpsimd(gslab4)
            GSLAB4 = d_sl4[cb].n

            def vslab4(e, ee=ee, cb=cb, GSLAB4=GSLAB4):
                e.wait(d_sl4[cb], GSLAB4)
                e.tensor_copy(yf4[cb][:], slab4[cb][:, :, 0:D])
                e.fence()
                e.tensor_copy(gidr4[:], slab4[cb].bitcast(F32)[:, :, 512:513].rearrange("q s o -> q (s o)"))
                e.fence()
                e.tensor_scalar(tlr4[:], gidr4[:], baset[:, 0:1], 0.0, ALU.subtract, ALU.add)
                e.fence()
                e.tensor_scalar(v1m4[:], tlr4[:], -0.5, 0.0, ALU.is_gt, ALU.add)
                e.fence()
                e.tensor_scalar(v2m4[:], tlr4[:], float(TLOC) - 0.5, 0.0, ALU.is_lt, ALU.add)
                e.fence()
                e.tensor_tensor(v1m4[:], v1m4[:], v2m4[:], ALU.mult)
                e.fence()
                e.tensor_scalar(v2m4[:], riota[:], ogbc4[:, 8 + ee:9 + ee], 0.0, ALU.is_lt, ALU.add)
                e.fence()
                e.tensor_tensor(v1m4[:], v1m4[:], v2m4[:], ALU.mult)
                e.fence()
                e.tensor_tensor(gidr4[:], tlr4[:], v1m4[:], ALU.mult)
                e.fence()
                e.tensor_scalar(v2m4[:], v1m4[:], 1.0, -1.0, ALU.subtract, ALU.mult)
                e.fence()
                e.tensor_scalar(dsc8[:], dumpt[:], float(8 * ee), 0.0, ALU.add, ALU.add)
                e.fence()
                e.tensor_tensor(v2m4[:], v2m4[:], dsc8[:], ALU.mult)
                e.fence()
                e.tensor_tensor(gidr4[:], gidr4[:], v2m4[:], ALU.add)
                e.fence()
                e.tensor_copy(idxs4b[cb][:], gidr4[:])
                e.fence()
            p.vector(vslab4)
            VSLAB4 = v.n

            def gsc4(e, ee=ee, cb=cb, VSLAB4=VSLAB4):
                e.wait(v, VSLAB4)
                e.wait(d_z, ST_Z4)
                for k in range(8):
                    e.indirect_dma_start(
                        out=acc4_hbm, out_offset=IndirectOffsetOnAxis(idxs4b[cb][:, k:k + 1], 0),
                        in_=yf4[cb][:, k, :], in_offset=None, compute_op=ALU.add,
                        sem=d_gg)
                GSC4[ee] = d_gg.n
                e.wait(d_gg)
            p.gpsimd(gsc4)
        p.scope = old
    GD_COMB4 = d_gg.n

    with ExitStack() as xsc4:
        old = p.scope; p.scope = xsc4
        xtile4 = [p.sbuf(f"p4_xt{i}", [128, D], F32) for i in range(2)]
        atile4 = [p.sbuf(f"p4_at{i}", [128, D], F32) for i in range(2)]
        d_x4 = [p.ctr("p4_dx40"), p.ctr("p4_dx41")]
        d_lx4 = [p.ctr("p4_dlx0"), p.ctr("p4_dlx1")]
        for i in range(32):
            bb = i % 2

            def ldx4(e, i=i, bb=bb):
                e.wait(d_gg, GD_COMB4)
                if i >= 2:
                    e.wait(d_x4[bb])
                e.dma_start(out=xtile4[bb][:], in_=x3_hbm[i * 128:(i + 1) * 128, :], sem=d_lx4[bb])
                e.dma_start(out=atile4[bb][:], in_=acc4_hbm[i * 128:(i + 1) * 128, :], sem=d_lx4[bb])
            p.sync(ldx4)
            LX4 = d_lx4[bb].n

            def vx4(e, i=i, bb=bb, LX4=LX4):
                e.wait(d_lx4[bb], LX4)
                e.tensor_tensor(xtile4[bb][:], xtile4[bb][:], atile4[bb][:], ALU.add)
                e.fence()
            p.vector(vx4)
            VX4 = v.n

            def stx4(e, i=i, bb=bb, VX4=VX4):
                e.wait(v, VX4)
                e.dma_start(out=x4_hbm[i * 128:(i + 1) * 128, :], in_=xtile4[bb][:], sem=d_x4[bb])
            p.scalar(stx4)
        p.scope = old

    p.scope = _oldscope
    _p4s.close()


def build_all(upto=4):
    _reset_markers()
    nc = bass.Bass("TRN2", target_bir_lowering=False, debug=False)
    ctx = ExitStack()
    blk = ctx.enter_context(nc.Block())
    p = P(nc, ctx, blk)

    io = {}
    V = {}
    io["x_in"] = nc.dram_tensor("x_in", [TLOC, D], F32, kind="ExternalInput").ap()
    for nm, shp in [
        ("l0_norm_w", [1, D]), ("l0_in_w", [D, 2 * D]), ("l0_in_b", [1, 2 * D]),
        ("l0_conv_w", [4, D]), ("l0_conv_b", [1, D]), ("l0_param_w", [D, 3 * D]),
        ("l0_param_b", [1, 3 * D]), ("l0_skip", [1, D]), ("l0_out_w", [D, D]),
        ("l0_out_b", [1, D]), ("l0_inb_fm", [128, 16]), ("l0_convw_fm", [128, 8, 4]),
        ("l0_convb_fm", [128, 8]), ("l0_parb_fm", [128, 24]), ("l0_skip_fm", [128, 8]),
        ("l0_normw_fm", [128, 8]),
    ]:
        V[nm] = nc.dram_tensor(nm, shp, F32, kind="ExternalInput").ap()
    io["x_out"] = nc.dram_tensor("x_out", [TLOC, D], F32, kind="ExternalOutput").ap()
    io["x1_hbm"] = nc.dram_tensor("x1_hbm", [TLOC, D], F32, kind="Internal").ap()
    io["xr_hbm"] = nc.dram_tensor("xr_hbm", [TLOC, D], BF16, kind="Internal").ap()
    io["stats_hbm"] = nc.dram_tensor("stats_hbm", [TLOC, 2], F32, kind="Internal").ap()
    io["w_inw_bf"] = nc.dram_tensor("w_inw_bf", [D, 2 * D], BF16, kind="Internal").ap()
    io["w_parw_bf"] = nc.dram_tensor("w_parw_bf", [D, 3 * D], BF16, kind="Internal").ap()

    if upto >= 2:
        for nm, shp in [
            ("wgt_fm", [128, 8, 8]), ("gid_c", [128, 32]), ("riota_c", [128, 8]),
            ("base_c", [128, 1]), ("mask8_c", [8, 8]), ("tl8_c", [1, 8]),
            ("epad8_c", [8, 1]), ("smod_c", [8, TLOC]), ("ccol64_c", [128, 64]),
            ("p2_w1t", [D, 2048]), ("p2_w3t", [D, 2048]), ("p2_w2t", [2048, D]),
        ]:
            V[nm] = nc.dram_tensor(nm, shp, F32, kind="ExternalInput").ap()
        V["me_i"] = nc.dram_tensor("me_i", [1, 2], I32, kind="ExternalInput").ap()
        io["x2_hbm"] = nc.dram_tensor("x2_hbm", [TLOC, D], F32, kind="Internal").ap()
        io["xn2_hbm"] = nc.dram_tensor("xn2_hbm", [TLOC, D], BF16, kind="Internal").ap()
        io["p2_xall"] = nc.dram_tensor("p2_xall", [8 * TLOC, D], BF16, kind="Internal").ap()
        io["p2_scin"] = nc.dram_tensor("p2_scin", [16, SEQ], F32, kind="Internal").ap()
        io["p2_scall"] = nc.dram_tensor("p2_scall", [8, 16, SEQ], F32, kind="Internal").ap()
        io["p2_payin"] = nc.dram_tensor("p2_payin", [PAYR, 16], F32, kind="Internal").ap()
        io["p2_payar"] = nc.dram_tensor("p2_payar", [PAYR, 16], F32, kind="Internal").ap()
        io["p2_mypay"] = nc.dram_tensor("p2_mypay", [TLOC, 16], F32, kind="Internal").ap()
        io["p2_xe"] = nc.dram_tensor("p2_xe", [TLOC, D], BF16, kind="Internal").ap()
        io["p2_yin"] = nc.dram_tensor("p2_yin", [TLOC, 1040], BF16, kind="Internal").ap()
        io["p2_yall"] = nc.dram_tensor("p2_yall", [8 * TLOC + CPAY, 1040], BF16, kind="Internal").ap()
        io["p2_acc"] = nc.dram_tensor("p2_acc", [TLOC + 128, D], F32, kind="Internal").ap()
        io["p2_dw"] = nc.dram_tensor("p2_dw", [2, 8, TLOC], F32, kind="Internal").ap()
        io["w_w1t_bf"] = nc.dram_tensor("w_w1t_bf", [D, 2048], BF16, kind="Internal").ap()
        io["w_w3t_bf"] = nc.dram_tensor("w_w3t_bf", [D, 2048], BF16, kind="Internal").ap()
        io["w_w2t_bf"] = nc.dram_tensor("w_w2t_bf", [2048, D], BF16, kind="Internal").ap()

    if upto >= 3 and upto != 15:
        for nm, shp in [
            ("p3_wqkv", [D, 1536]), ("p3_wo", [D, D]),
            ("rope_cos_c", [SEQ, 32]), ("rope_sin_c", [SEQ, 32]),
            ("masks_c", [128, 3, 128]), ("amask_c", [128, 8, 512]),
        ]:
            V[nm] = nc.dram_tensor(nm, shp, F32, kind="ExternalInput").ap()
        io["x3_hbm"] = nc.dram_tensor("x3_hbm", [TLOC, D], F32, kind="Internal").ap()
        io["xn3_hbm"] = nc.dram_tensor("xn3_hbm", [TLOC, D], BF16, kind="Internal").ap()
        io["p3_qk"] = nc.dram_tensor("p3_qk", [TLOC, 1280], BF16, kind="Internal").ap()
        io["p3_y"] = nc.dram_tensor("p3_y", [TLOC, D], BF16, kind="Internal").ap()
        io["w_qkv_bf"] = nc.dram_tensor("w_qkv_bf", [D, 1536], BF16, kind="Internal").ap()
        io["w_wo_bf"] = nc.dram_tensor("w_wo_bf", [D, D], BF16, kind="Internal").ap()

    if upto >= 4 and upto != 15:
        for nm, shp in [
            ("p4_wr_fm", [128, 8, 8]), ("erow8_c", [128, 8]), ("cmlt_c", [8, 8]),
            ("idn_c", [128, 128]), ("epadL_c", [8, 1]), ("mebase4_c", [8, 1]),
            ("cmltc_c", [8, 1]),
            ("p4_w1", [D, 4096]), ("p4_w2", [4096, D]),
        ]:
            V[nm] = nc.dram_tensor(nm, shp, F32, kind="ExternalInput").ap()
        io["x4_hbm"] = nc.dram_tensor("x4_hbm", [TLOC, D], F32, kind="Internal").ap()
        io["xn4_hbm"] = nc.dram_tensor("xn4_hbm", [TLOC, D], BF16, kind="Internal").ap()
        io["p4_xall"] = nc.dram_tensor("p4_xall", [8 * TLOC, D], BF16, kind="Internal").ap()
        io["p4_ptok"] = nc.dram_tensor("p4_ptok", [TLOC, 1], F32, kind="Internal").ap()
        io["p4_cntin"] = nc.dram_tensor("p4_cntin", [1, 8], F32, kind="Internal").ap()
        io["p4_cntall"] = nc.dram_tensor("p4_cntall", [8, 8], F32, kind="Internal").ap()
        io["p4_payin"] = nc.dram_tensor("p4_payin", [PAYR4, 16], F32, kind="Internal").ap()
        io["p4_payar"] = nc.dram_tensor("p4_payar", [PAYR4, 16], F32, kind="Internal").ap()
        io["p4_mypay"] = nc.dram_tensor("p4_mypay", [TLOC, 16], F32, kind="Internal").ap()
        io["p4_xe"] = nc.dram_tensor("p4_xe", [TLOC, D], BF16, kind="Internal").ap()
        io["p4_yin"] = nc.dram_tensor("p4_yin", [TLOC, 1040], BF16, kind="Internal").ap()
        io["p4_yall"] = nc.dram_tensor("p4_yall", [8 * TLOC + CPAY, 1040], BF16, kind="Internal").ap()
        io["p4_acc"] = nc.dram_tensor("p4_acc", [TLOC + 128, D], F32, kind="Internal").ap()
        io["p4_dw"] = nc.dram_tensor("p4_dw", [1, 8, TLOC], F32, kind="Internal").ap()
        io["w_l1w1_bf"] = nc.dram_tensor("w_l1w1_bf", [D, 4096], BF16, kind="Internal").ap()
        io["w_l1w2_bf"] = nc.dram_tensor("w_l1w2_bf", [4096, D], BF16, kind="Internal").ap()

    d_g = p.ctr("d_g")

    def stage(e):
        e.dma_start(out=io["w_inw_bf"], in_=V["l0_in_w"], sem=d_g)
        e.dma_start(out=io["w_parw_bf"], in_=V["l0_param_w"], sem=d_g)
        if upto >= 2:
            e.dma_start(out=io["w_w1t_bf"], in_=V["p2_w1t"], sem=d_g)
            e.dma_start(out=io["w_w3t_bf"], in_=V["p2_w3t"], sem=d_g)
            e.dma_start(out=io["w_w2t_bf"], in_=V["p2_w2t"], sem=d_g)
        if upto >= 3 and upto != 15:
            e.dma_start(out=io["w_qkv_bf"], in_=V["p3_wqkv"], sem=d_g)
            e.dma_start(out=io["w_wo_bf"], in_=V["p3_wo"], sem=d_g)
        if upto >= 4 and upto != 15:
            e.dma_start(out=io["w_l1w1_bf"], in_=V["p4_w1"], sem=d_g)
            e.dma_start(out=io["w_l1w2_bf"], in_=V["p4_w2"], sem=d_g)
        e.wait(d_g)
    p.gpsimd(stage)

    emit_l0_attn(p, io, V)
    if upto >= 2:
        if upto == 15:
            io["dbg"] = nc.dram_tensor("dbg", [8, 64], F32, kind="ExternalOutput").ap()
            emit_l0_ffn(p, io, V, dbg=True)
        else:
            emit_l0_ffn(p, io, V)
    if upto >= 3 and upto != 15:
        emit_l1_attn(p, io, V)
    if upto >= 4 and upto != 15:
        emit_l1_ffn(p, io, V)

    d_fin = p.ctr("d_fin")
    if upto >= 4 and upto != 15:
        src = io["x4_hbm"]
    elif upto >= 3 and upto != 15:
        src = io["x3_hbm"]
    elif upto >= 2 and upto != 15:
        src = io["x2_hbm"]
    else:
        src = io["x1_hbm"]

    def fin(e):
        for ctr in list(p.ctrs.values()):
            if ctr.n:
                e.wait(ctr)
        e.dma_start(out=io["x_out"], in_=src, sem=d_fin)
        e.wait(d_fin)
    p.sync(fin)

    ctx.close()
    return nc


def _shard_inputs(inputs, upto=4):
    x = np.ascontiguousarray(inputs["L0_attn_x"].reshape(NCORES, TLOC, D))
    base = {
        "l0_norm_w": np.ascontiguousarray(inputs["L0_attn_norm_w"].reshape(1, D)),
        "l0_in_w": np.ascontiguousarray(inputs["L0_attn_in_w"]),
        "l0_in_b": np.ascontiguousarray(inputs["L0_attn_in_b"].reshape(1, 2 * D)),
        "l0_conv_w": np.ascontiguousarray(inputs["L0_attn_conv_w"]),
        "l0_conv_b": np.ascontiguousarray(inputs["L0_attn_conv_b"].reshape(1, D)),
        "l0_param_w": np.ascontiguousarray(inputs["L0_attn_param_w"]),
        "l0_param_b": np.ascontiguousarray(inputs["L0_attn_param_b"].reshape(1, 3 * D)),
        "l0_skip": np.ascontiguousarray(inputs["L0_attn_skip"].reshape(1, D)),
        "l0_out_w": np.ascontiguousarray(inputs["L0_attn_out_w"]),
        "l0_out_b": np.ascontiguousarray(inputs["L0_attn_out_b"].reshape(1, D)),
    }
    def fm_vec(vrow, nj):
        return np.ascontiguousarray(vrow.reshape(nj, 128).T)
    base["l0_inb_fm"] = fm_vec(inputs["L0_attn_in_b"], 16)
    base["l0_convb_fm"] = fm_vec(inputs["L0_attn_conv_b"], 8)
    base["l0_parb_fm"] = fm_vec(inputs["L0_attn_param_b"], 24)
    base["l0_skip_fm"] = fm_vec(inputs["L0_attn_skip"], 8)
    base["l0_normw_fm"] = fm_vec(inputs["L0_attn_norm_w"], 8)
    base["l0_convw_fm"] = np.ascontiguousarray(
        inputs["L0_attn_conv_w"].reshape(4, 8, 128).transpose(2, 1, 0))
    if upto >= 2:
        Wg = inputs["L0_ffn_Wg"]
        base["wgt_fm"] = np.ascontiguousarray(Wg.T.reshape(8, 128, 8).transpose(1, 0, 2))
        base["tl8_c"] = (np.arange(8, dtype=np.float32) * TLOC).reshape(1, 8)
        base["epad8_c"] = (np.arange(8, dtype=np.float32) * TLOC).reshape(8, 1)
        base["smod_c"] = np.broadcast_to(
            np.tile(np.arange(32, dtype=np.float32), TLOC // 32), (8, TLOC)).copy()
        base["ccol64_c"] = np.ascontiguousarray(
            (np.arange(128)[:, None] + 128 * np.tile(np.arange(8), 8)[None, :]).astype(np.float32))
        base["riota_c"] = np.ascontiguousarray(
            (np.arange(128)[:, None] + 128 * np.arange(8)[None, :]).astype(np.float32))
    if upto >= 3:
        base["p3_wqkv"] = np.ascontiguousarray(np.concatenate(
            [inputs["L1_attn_Wq"], inputs["L1_attn_Wk"], inputs["L1_attn_Wv"]], axis=1))
        base["p3_wo"] = np.ascontiguousarray(inputs["L1_attn_Wo"])
        inv_freq = (1.0 / 10000.0 ** (np.arange(0, 64, 2, dtype=np.float32) / 64.0)).astype(np.float32)
        freqs = np.arange(SEQ, dtype=np.float32)[:, None] * inv_freq[None, :]
        base["rope_cos_c"] = np.cos(freqs).astype(np.float32)
        base["rope_sin_c"] = np.sin(freqs).astype(np.float32)
        kk = np.arange(128)[:, None]; qq = np.arange(128)[None, :]
        mT = (kk <= qq).astype(np.float32)
        mB2 = (kk >= qq + 1).astype(np.float32)
        mBG = np.maximum(mB2, (kk < 64).astype(np.float32))
        base["masks_c"] = np.ascontiguousarray(np.stack([mT, mB2, mBG], axis=1))
        kf = np.arange(128)[:, None]; qf = np.arange(512)[None, :]
        am = np.zeros((128, 8, 512), np.float32)
        for di, dlt in enumerate([-2, -1, 0, 1, 2, 3]):
            K = dlt * 128 + kf
            am[:, di, :] = ((K <= qf) & (qf - K <= 255)).astype(np.float32)
        am[:, 6, :] = ((kf <= qf) & ((qf - kf <= 255) | (kf < 64))).astype(np.float32)
        base["amask_c"] = np.ascontiguousarray(am)
    if upto >= 4:
        Wr = inputs["L1_ffn_Wr"]
        base["p4_wr_fm"] = np.ascontiguousarray(Wr.reshape(8, 128, 8).transpose(1, 0, 2))
        base["erow8_c"] = np.broadcast_to(np.arange(8, dtype=np.float32), (128, 8)).copy()
        base["idn_c"] = np.eye(128, dtype=np.float32)
        base["epadL_c"] = (np.arange(8, dtype=np.float32) * 1024.0).reshape(8, 1)
    maps = []
    for c in range(NCORES):
        m = dict(base, x_in=x[c])
        if upto >= 2:
            m["gid_c"] = np.ascontiguousarray(
                (c * TLOC + np.arange(128)[:, None] * 32 + np.arange(32)[None, :]).astype(np.float32))
            m["base_c"] = np.full((128, 1), float(c * TLOC), np.float32)
            mk = np.zeros((8, 8), np.float32); mk[:, c] = 1.0
            m["mask8_c"] = mk
            m["me_i"] = np.array([[c, 64 * 1024 + 32 + c]], np.int32)
            m["p2_w1t"] = np.ascontiguousarray(inputs["L0_ffn_W1"][c].T)
            m["p2_w3t"] = np.ascontiguousarray(inputs["L0_ffn_W3"][c].T)
            m["p2_w2t"] = np.ascontiguousarray(inputs["L0_ffn_W2"][c].T)
        if upto >= 4:
            m["p4_w1"] = np.ascontiguousarray(inputs["L1_ffn_W1"][c])
            m["p4_w2"] = np.ascontiguousarray(inputs["L1_ffn_W2"][c])
            cm = np.zeros((8, 8), np.float32); cm[:, :c] = 1.0
            m["cmlt_c"] = cm
            m["mebase4_c"] = np.full((8, 1), float(c * 8192), np.float32)
            m["cmltc_c"] = (np.arange(8) < c).astype(np.float32).reshape(8, 1)
        maps.append(m)
    return maps


def kernel(**inputs):
    nc = build_all()
    in_maps = _shard_inputs(inputs)
    res = run_bass_kernel_spmd(nc, in_maps, core_ids=list(range(NCORES)))
    out = np.concatenate([r["x_out"] for r in res.results], axis=0)
    return np.ascontiguousarray(out.reshape(16, SEQ, D))


# version 06
# speedup vs baseline: 1.1216x; 1.1108x over previous
import numpy as np
import ml_dtypes
import bass_rust
import concourse.bass as bass
import concourse.mybir as mybir
from concourse.bass_utils import run_bass_kernel_spmd
from concourse.bass_types import DynSlice
from concourse.bass_core_rust import IndirectOffsetOnAxis
from contextlib import ExitStack

F32 = mybir.dt.float32
BF16 = mybir.dt.bfloat16
I16 = mybir.dt.int16
I32 = mybir.dt.int32
U8 = mybir.dt.uint8
U32 = mybir.dt.uint32
AF = mybir.ActivationFunctionType
ALU = mybir.AluOpType
AX = mybir.AxisListType

NCORES = 8
TLOC = 4096
D = 1024
SEQ = 2048
CHUNK = 512
NCHUNK = TLOC // CHUNK

_DMA_OPS = ("dma_start", "dma_start_transpose", "indirect_dma_start")
_PASS = ("wait_ge", "value_load", "load_library", "register", "reg_load",
         "alloc_register", "free_register", "nop")


class Ctr:
    def __init__(self, sem):
        self.sem, self.n = sem, 0


class Eng:

    def __init__(self, raw, c):
        self.raw, self.c = raw, c

    def __getattr__(self, nm):
        fn = getattr(self.raw, nm)
        if nm in _PASS:
            return fn
        if nm in _DMA_OPS:
            def dcall(*a, sem=None, **k):
                inst = fn(*a, **k)
                assert sem is not None, f"dma {nm} needs sem="
                sem.n += 16
                return inst.then_inc(sem.sem, 16)
            return dcall

        def ccall(*a, **k):
            inst = fn(*a, **k)
            self.c.n += 1
            return inst.then_inc(self.c.sem, 1)
        return ccall

    def fence(self):
        self.raw.wait_ge(self.c.sem, self.c.n)

    def wait(self, ctr, n=None):
        self.raw.wait_ge(ctr.sem, ctr.n if n is None else n)


class P:

    def __init__(self, nc, ctx, blk):
        self.nc, self.ctx, self.blk = nc, ctx, blk
        self.ctrs = {}
        self.shared = {}
        self.scope = ctx
        for nm in ("v", "a", "t", "g"):
            self.ctr(nm)

    def ctr(self, name):
        if name not in self.ctrs:
            h = self.ctx.enter_context(self.nc.semaphore(name))
            self.ctrs[name] = Ctr(h)
        return self.ctrs[name]

    def sbuf(self, name, shape, dt):
        return self.scope.enter_context(self.nc.sbuf_tensor(name, shape, dt))

    def psum(self, name, shape, dt=F32):
        return self.scope.enter_context(self.nc.psum_tensor(name, shape, dt))

    def sync(self, fn):
        self.blk.sync(lambda e: fn(Eng(e, self.ctr("sy"))))

    def vector(self, fn):
        self.blk.vector(lambda e: fn(Eng(e, self.ctr("v"))))

    def scalar(self, fn):
        self.blk.scalar(lambda e: fn(Eng(e, self.ctr("a"))))

    def tensor(self, fn):
        self.blk.tensor(lambda e: fn(Eng(e, self.ctr("t"))))

    def gpsimd(self, fn):
        self.blk.gpsimd(lambda e: fn(Eng(e, self.ctr("g"))))

    def barrier(self, engines="vatg"):
        vals = [(c, c.n) for c in self.ctrs.values()]
        emap = {"v": self.vector, "a": self.scalar, "t": self.tensor,
                "g": self.gpsimd, "s": self.sync}
        for ch in engines:
            def mk(e, vals=tuple(vals)):
                for ctr, n in vals:
                    if n:
                        e.wait(ctr, n)
            emap[ch](mk)


MM_T, YEV, A_SQ, V_SQRD, SLAB_T, PS_EV = {}, {}, {}, {}, {}, {}
V_UBF, V_CONVD, V_YD, V_RES, V_RESNG, MM_OUT = {}, {}, {}, {}, {}, {}


def _reset_markers():
    for dd in (MM_T, YEV, A_SQ, V_SQRD, SLAB_T, PS_EV,
               V_UBF, V_CONVD, V_YD, V_RES, V_RESNG, MM_OUT):
        dd.clear()




def emit_l0_attn(p, io, V):
    nc = p.nc
    _p1s = ExitStack()
    _oldscope = p.scope
    p.scope = _p1s
    x_in = io["x_in"]
    x1_hbm, xr_hbm, stats_hbm = io["x1_hbm"], io["xr_hbm"], io["stats_hbm"]
    inw_h, parw_h = io["w_inw_bf"], io["w_parw_bf"]

    outw = p.sbuf("p1_outw", [128, 8, D], BF16)
    wsl = [p.sbuf(f"p1_wsl{i}", [128, 8, 768], BF16) for i in range(2)]
    normw_f = p.sbuf("p1_normwf", [128, 8], F32)
    inb = p.sbuf("p1_inb", [128, 16], F32)
    convw = p.sbuf("p1_convw", [128, 8, 4], F32)
    convb = p.sbuf("p1_convb", [128, 8], F32)
    parb = p.sbuf("p1_parb", [128, 24], F32)
    skipw = p.sbuf("p1_skipw", [128, 8], F32)
    outb_row = p.sbuf("p1_outb", [1, D], BF16)
    ones_col = p.sbuf("p1_ones", [1, 128], BF16)

    v, a, t, g = p.ctr("v"), p.ctr("a"), p.ctr("t"), p.ctr("g")
    d_ld = p.ctr("d_ld")
    d_wv = p.ctr("d_wv")
    d_w = [p.ctr("d_w0"), p.ctr("d_w1")]
    d_tr = [p.ctr("d_tr0"), p.ctr("d_tr1")]
    d_xc = [p.ctr("d_xc0"), p.ctr("d_xc1")]
    d_st = p.ctr("d_st")
    d_sx1 = [p.ctr("d_sx10"), p.ctr("d_sx11")]
    d_g = p.ctr("d_g")

    def loads(e):
        e.dma_start(out=inb[:], in_=V["l0_inb_fm"], sem=d_wv)
        e.dma_start(out=convw[:], in_=V["l0_convw_fm"], sem=d_wv)
        e.dma_start(out=convb[:], in_=V["l0_convb_fm"], sem=d_wv)
        e.dma_start(out=parb[:], in_=V["l0_parb_fm"], sem=d_wv)
        e.dma_start(out=skipw[:], in_=V["l0_skip_fm"], sem=d_wv)
        e.dma_start(out=normw_f[:], in_=V["l0_normw_fm"], sem=d_wv)
    p.sync(loads)
    LD_W = d_wv.n

    def gload(e):
        e.dma_start(out=outb_row[:], in_=V["l0_out_b"], sem=d_g)
        e.dma_start(out=outw[:], in_=V["l0_out_w"].rearrange("(j q) e -> q j e", q=128), sem=d_g)
    p.gpsimd(gload)
    GD_OUTB = d_g.n

    with ExitStack() as actx:
        def asb(name, shape, dt):
            return actx.enter_context(nc.sbuf_tensor(name, shape, dt))
        xs = asb("p1_xs", [128, 8, D], F32)
        xrs = asb("p1_xrs", [128, 8, D], BF16)
        sqs = asb("p1_sqs", [128, 8, D], F32)
        bnag = asb("p1_bnag", [128, 8, 2], F32)
        stA = asb("p1_stA", [128, 8, 2], F32)
        z1 = asb("p1_z1", [128, 8], F32)
        z2 = asb("p1_z2", [128, 8], F32)
        z3 = asb("p1_z3", [128, 8], F32)
        s1 = asb("p1_s1", [128, 8], F32)
        s3 = asb("p1_s3", [128, 8], F32)
        Amul = asb("p1_Am", [128, 8], F32)
        rstd = asb("p1_rs", [128, 8], F32)

        def vinit(e):
            e.memset(ones_col[:], 1.0)
        p.vector(vinit)

        for grp in range(4):
            def ldx(e, grp=grp):
                if grp > 0:
                    e.wait(v)
                e.dma_start(out=xs[:], in_=x_in.rearrange("(q s) d -> q s d", q=128)[:, grp * 8:(grp + 1) * 8, :], sem=d_ld)
            p.sync(ldx)

            def v1(e, grp=grp):
                e.wait(d_ld)
                for k in range(8):
                    e.reduce_sum(bnag[:, k, 0:1], xs[:, k, :], axis=AX.X)
                e.fence()
            p.vector(v1)

            def v1b(e, grp=grp):
                for k in range(8):
                    e.tensor_tensor(sqs[:, k, :], xs[:, k, :], xs[:, k, :], ALU.mult)
                    e.fence()
                    e.reduce_sum(bnag[:, k, 1:2], sqs[:, k, :], axis=AX.X)
                e.fence()
                V_SQRD[(grp, 7)] = p.ctr("v").n
                e.tensor_scalar_mul(bnag[:, :, 0], bnag[:, :, 0], 1.0 / D)
                e.fence()
                e.tensor_scalar_mul(bnag[:, :, 1], bnag[:, :, 1], 1.0 / D)
                e.fence()
                e.scalar_tensor_tensor(z1[:], bnag[:, :, 0], -1.0, bnag[:, :, 0], ALU.mult, ALU.mult)
                e.fence()
                e.tensor_tensor(bnag[:, :, 1], bnag[:, :, 1], z1[:], ALU.add)
                e.fence()
                e.tensor_scalar_add(z1[:], bnag[:, :, 1], 1e-5)
                e.fence()
                e.reciprocal(z2[:], z1[:])
                e.fence()
                e.tensor_tensor(z2[:], bnag[:, :, 1], z2[:], ALU.mult)
                e.fence()
                e.tensor_scalar_add(z2[:], z2[:], 1e-6)
                e.fence()
                e.tensor_tensor(z3[:], z1[:], z2[:], ALU.mult)
                e.fence()
            p.vector(v1b)

            def a1(e, grp=grp):
                e.wait(v)
                e.activation(s3[:], z3[:], AF.Sqrt)
                e.activation(s1[:], z1[:], AF.Sqrt)
            p.scalar(a1)

            def v2(e, grp=grp):
                e.wait(a)
                e.wait(d_st)
                e.reciprocal(Amul[:], s3[:])
                e.reciprocal(rstd[:], s1[:])
                e.fence()
                e.tensor_scalar_add(stA[:, :, 0], rstd[:], 1.0)
                e.scalar_tensor_tensor(stA[:, :, 1], bnag[:, :, 0], -1.0, rstd[:], ALU.mult, ALU.mult)
                for k in range(8):
                    e.tensor_scalar(xrs[:, k, :], xs[:, k, :], bnag[:, k, 0:1], Amul[:, k:k + 1],
                                    ALU.subtract, ALU.mult)
                e.fence()
            p.vector(v2)

            def st1(e, grp=grp):
                e.wait(v)
                e.dma_start(out=xr_hbm.rearrange("(q s) d -> q s d", q=128)[:, grp * 8:(grp + 1) * 8, :], in_=xrs[:], sem=d_st)
                e.dma_start(out=stats_hbm.rearrange("(q s) c -> q s c", q=128)[:, grp * 8:(grp + 1) * 8, :], in_=stA[:], sem=d_st)
            p.scalar(st1)
    ST_PASSA = d_st.n
    p.barrier("vats")

    xr_fm = [p.sbuf(f"p1_xrfm{i}", [128, 8, CHUNK], BF16) for i in range(2)]
    u_pad = p.sbuf("p1_upad", [128, 8, CHUNK + 3], F32)
    tail = p.sbuf("p1_tail", [128, 8, 3], F32)
    uc = p.sbuf("p1_uc", [128, 8, CHUNK], F32)
    u_bf = p.sbuf("p1_ubf", [128, 8, CHUNK], BF16)
    gs_bf = p.sbuf("p1_gsbf", [128, 8, CHUNK], BF16)
    a_t = p.sbuf("p1_a", [128, 8, CHUNK], F32)
    bu_t = p.sbuf("p1_bu", [128, 8, CHUNK], F32)
    c_t = p.sbuf("p1_c", [128, 8, CHUNK], F32)
    stn = p.sbuf("p1_stn", [128, 8, CHUNK], F32)
    y_bf = p.sbuf("p1_ybf", [128, 8, CHUNK], BF16)
    carry = p.sbuf("p1_carry", [128, 8], F32)
    xc = [p.sbuf(f"p1_xc{i}", [128, D], F32) for i in range(2)]
    x1c = [p.sbuf(f"p1_x1c{i}", [128, D], F32) for i in range(2)]
    ab_t = p.sbuf("p1_ab", [128, 4, 2], F32)
    ps_a = [p.psum(f"p1_psa{i}", [128, 512]) for i in range(6)]
    ps_o = [p.psum(f"p1_pso{i}", [128, 512]) for i in range(2)]

    def vinit2(e):
        e.memset(carry[:], 0.0)
        e.memset(tail[:], 0.0)
        e.fence()
    p.vector(vinit2)

    SLABS = [("in", 0, 6), ("in", 6, 12), ("in", 12, 16),
             ("par", 0, 6), ("par", 6, 12), ("par", 12, 18), ("par", 18, 24)]

    for c in range(NCHUNK):
        t0 = c * CHUNK
        b = c % 2
        seq_start = (c % 4 == 0)

        def ldtr(e, c=c, t0=t0, b=b):
            e.wait(d_st, ST_PASSA)
            if c >= 2:
                e.wait(t)
            for j in range(8):
                e.dma_start_transpose(xr_fm[b][:, j, :], xr_hbm[t0:t0 + CHUNK, j * 128:(j + 1) * 128], sem=d_tr[b])
        p.sync(ldtr)

        def ldab(e, c=c, t0=t0):
            if c >= 1:
                e.wait(v)
            e.dma_start(out=ab_t[:], in_=stats_hbm[t0:t0 + CHUNK, :].rearrange("(k q) c -> q k c", q=128), sem=d_ld)
        p.sync(ldab)
        LD_AB = d_ld.n

        def vnw(e, c=c, b=b):
            e.wait(d_tr[b])
            e.wait(d_wv, LD_W)
            for j in range(8):
                e.tensor_scalar_mul(xr_fm[b][:, j, :], xr_fm[b][:, j, :], normw_f[:, j:j + 1])
            e.fence()
        p.vector(vnw)
        V_NW = v.n

        evac_plan = []
        for si, (kind, pf0, pf1) in enumerate(SLABS):
            sb = si % 2

            def ldsl(e, si=si, sb=sb, kind=kind, pf0=pf0, pf1=pf1, c=c):
                if si >= 2:
                    e.wait(t, SLAB_T[(c, si - 2)])
                elif c > 0:
                    e.wait(t, SLAB_T[(c - 1, len(SLABS) - 2 + si)])
                src = inw_h if kind == "in" else parw_h
                e.dma_start(out=wsl[sb][:, :, 0:(pf1 - pf0) * 128],
                            in_=src.rearrange("(j q) e -> q j e", q=128)[:, :, pf0 * 128:pf1 * 128], sem=d_w[sb])
            p.sync(ldsl)

            def mmsl(e, si=si, sb=sb, kind=kind, pf0=pf0, pf1=pf1, c=c, b=b):
                e.wait(d_w[sb])
                if kind == "in":
                    e.wait(v, V_NW)
                else:
                    e.wait(v, V_UBF[c])
                for pf in range(pf0, pf1):
                    gi = (pf if kind == "in" else 16 + pf)
                    pi = gi % 6
                    if gi >= 6:
                        e.wait(a, PS_EV[(c, gi - 6)])
                    elif c > 0:
                        e.wait(a, PS_EV[(c - 1, 34 + gi)])
                    for j in range(8):
                        e.matmul(ps_a[pi][:], wsl[sb][:, j, (pf - pf0) * 128:(pf - pf0 + 1) * 128],
                                 xr_fm[b][:, j, :] if kind == "in" else u_bf[:, j, :],
                                 start=(j == 0), stop=(j == 7))
                    MM_T[(c, gi)] = t.n
                SLAB_T[(c, si)] = t.n
            p.tensor(mmsl)

            def evsl(e, si=si, kind=kind, pf0=pf0, pf1=pf1, c=c):
                for pf in range(pf0, pf1):
                    gi = (pf if kind == "in" else 16 + pf)
                    pi = gi % 6
                    e.wait(t, MM_T[(c, gi)])
                    if kind == "in":
                        if pf < 8:
                            if c >= 1 and pf == 0:
                                e.wait(v, V_CONVD[c - 1])
                            e.activation(u_pad[:, pf, 3:], ps_a[pi][:], AF.Identity, bias=inb[:, pf:pf + 1])
                        else:
                            if c >= 1 and pf == 8:
                                e.wait(v, V_YD[c - 1])
                            e.activation(gs_bf[:, pf - 8, :], ps_a[pi][:], AF.Silu, bias=inb[:, pf:pf + 1])
                    else:
                        j = pf % 8
                        if pf < 8:
                            if c >= 1 and pf == 0:
                                e.wait(v, V_YD[c - 1])
                            e.activation(a_t[:, j, :], ps_a[pi][:], AF.Sigmoid, bias=parb[:, pf:pf + 1])
                        elif pf < 16:
                            e.activation(bu_t[:, j, :], ps_a[pi][:], AF.Tanh, bias=parb[:, pf:pf + 1])
                        else:
                            e.activation(c_t[:, j, :], ps_a[pi][:], AF.Sigmoid, bias=parb[:, pf:pf + 1])
                    PS_EV[(c, gi)] = a.n
            p.scalar(evsl)

            if si == 2:
                def convv(e, c=c, seq_start=seq_start):
                    e.wait(a)
                    if seq_start:
                        e.memset(u_pad[:, :, 0:3], 0.0)
                    else:
                        e.tensor_copy(u_pad[:, :, 0:3], tail[:])
                    e.fence()
                    for j in range(8):
                        e.tensor_scalar(uc[:, j, :], u_pad[:, j, 0:CHUNK], convw[:, j, 0:1], convb[:, j:j + 1], ALU.mult, ALU.add)
                    e.fence()
                    for k in range(1, 4):
                        for j in range(8):
                            e.scalar_tensor_tensor(uc[:, j, :], u_pad[:, j, k:k + CHUNK], convw[:, j, k:k + 1], uc[:, j, :], ALU.mult, ALU.add)
                        e.fence()
                    e.tensor_copy(tail[:], u_pad[:, :, CHUNK:CHUNK + 3])
                    e.fence()
                p.vector(convv)
                V_CONVD[c] = p.ctr("v").n

                def silua(e, c=c):
                    e.wait(v)
                    e.activation(uc[:], uc[:], AF.Silu)
                p.scalar(silua)

                def ubfv(e, c=c):
                    e.wait(a)
                    e.tensor_copy(u_bf[:], uc[:])
                    e.fence()
                p.vector(ubfv)
                V_UBF[c] = v.n

        def scanv(e, c=c, seq_start=seq_start):
            e.wait(a)
            if seq_start:
                e.memset(carry[:], 0.0)
                e.fence()
            for j in range(8):
                e.tensor_tensor(bu_t[:, j, :], bu_t[:, j, :], uc[:, j, :], ALU.mult)
            e.fence()
            for j in range(8):
                e.tensor_tensor_scan(stn[:, j, :], a_t[:, j, :], bu_t[:, j, :], carry[:, j:j + 1], ALU.mult, ALU.add)
                e.fence()
                e.tensor_copy(carry[:, j:j + 1], stn[:, j, CHUNK - 1:CHUNK])
                e.fence()
            for j in range(8):
                e.tensor_tensor(c_t[:, j, :], c_t[:, j, :], stn[:, j, :], ALU.mult)
            e.fence()
            for j in range(8):
                e.scalar_tensor_tensor(stn[:, j, :], uc[:, j, :], skipw[:, j:j + 1], c_t[:, j, :], ALU.mult, ALU.add)
            e.fence()
            for j in range(8):
                e.tensor_tensor(y_bf[:, j, :], stn[:, j, :], gs_bf[:, j, :], ALU.mult)
            e.fence()
        p.vector(scanv)
        V_YD[c] = v.n

        for tt_i in range(4):
            bb = tt_i % 2

            def ldxc(e, c=c, t0=t0, tt_i=tt_i, bb=bb):
                if c * 4 + tt_i >= 2:
                    e.wait(v, V_RES[(c * 4 + tt_i) - 2])
                e.dma_start(out=xc[bb][:], in_=x_in[t0 + tt_i * 128:t0 + (tt_i + 1) * 128, :], sem=d_xc[bb])
            p.sync(ldxc)

            def mmout(e, c=c, tt_i=tt_i):
                e.wait(v, V_YD[c])
                if c == 0 and tt_i == 0:
                    e.wait(d_g, GD_OUTB)
                for ng in range(2):
                    gi = tt_i * 2 + ng
                    if c * 8 + gi >= 2:
                        e.wait(v, V_RESNG[(c * 8 + gi) - 2])
                    for j in range(8):
                        e.matmul(ps_o[gi % 2][:], y_bf[:, j, tt_i * 128:(tt_i + 1) * 128],
                                 outw[:, j, ng * 512:(ng + 1) * 512], start=(j == 0), stop=False)
                    e.matmul(ps_o[gi % 2][:], ones_col[:], outb_row[0:1, ng * 512:(ng + 1) * 512],
                             start=False, stop=True)
                    MM_OUT[(c, gi)] = t.n
            p.tensor(mmout)

            def resv(e, c=c, tt_i=tt_i, bb=bb):
                e.wait(d_ld, LD_AB)
                e.wait(d_xc[bb])
                e.wait(d_sx1[bb])
                for ng in range(2):
                    gi = tt_i * 2 + ng
                    e.wait(t, MM_OUT[(c, gi)])
                    sl = slice(ng * 512, (ng + 1) * 512)
                    e.tensor_scalar(x1c[bb][:, sl], xc[bb][:, sl], ab_t[:, tt_i, 0:1], ab_t[:, tt_i, 1:2], ALU.mult, ALU.add)
                    e.fence()
                    e.tensor_tensor(x1c[bb][:, sl], x1c[bb][:, sl], ps_o[gi % 2][:], ALU.add)
                    e.fence()
                    V_RESNG[c * 8 + gi] = v.n
                V_RES[c * 4 + tt_i] = v.n
            p.vector(resv)

            def stx1(e, c=c, t0=t0, tt_i=tt_i, bb=bb):
                e.wait(v, V_RES[c * 4 + tt_i])
                e.dma_start(out=x1_hbm[t0 + tt_i * 128:t0 + (tt_i + 1) * 128, :], in_=x1c[bb][:], sem=d_sx1[bb])
            p.scalar(stx1)

    p.scope = _oldscope
    _p1s.close()




def emit_ln_plain(p, src_hbm, xn_hbm, pref, gates=()):
    nc = p.nc
    v, a = p.ctr("v"), p.ctr("a")
    d_ld = p.ctr("d_ld")
    d_st = p.ctr("d_st")
    with ExitStack() as sc:
        old = p.scope
        p.scope = sc
        xs = p.sbuf(pref + "_xs", [128, 8, D], F32)
        xrs = p.sbuf(pref + "_xrs", [128, 8, D], BF16)
        sqs = p.sbuf(pref + "_sqs", [128, 8, D], F32)
        mv = p.sbuf(pref + "_mv", [128, 8, 2], F32)
        z1 = p.sbuf(pref + "_z1", [128, 8], F32)
        s1 = p.sbuf(pref + "_s1", [128, 8], F32)
        Am = p.sbuf(pref + "_Am", [128, 8], F32)
        M = {}
        for grp in range(4):
            def ldx(e, grp=grp):
                for gnm in gates:
                    e.wait(p.ctr(gnm))
                if grp > 0:
                    e.wait(v)
                e.dma_start(out=xs[:], in_=src_hbm.rearrange("(q s) d -> q s d", q=128)[:, grp * 8:(grp + 1) * 8, :], sem=d_ld)
            p.sync(ldx)

            def v1(e, grp=grp):
                e.wait(d_ld)
                if grp > 0:
                    e.wait(d_st)
                for k in range(8):
                    e.reduce_sum(mv[:, k, 0:1], xs[:, k, :], axis=AX.X)
                e.fence()
            p.vector(v1)

            def asq(e, grp=grp):
                e.wait(d_ld)
                e.wait(v)
                for k in range(8):
                    e.tensor_tensor(sqs[:, k, :], xs[:, k, :], xs[:, k, :], ALU.mult)
                e.fence()
                M[("sq", grp)] = p.ctr("v").n
            p.vector(asq)

            def v2(e, grp=grp):
                e.wait(v, M[("sq", grp)])
                for k in range(8):
                    e.reduce_sum(mv[:, k, 1:2], sqs[:, k, :], axis=AX.X)
                e.fence()
                e.tensor_scalar_mul(mv[:, :, 0], mv[:, :, 0], 1.0 / D)
                e.fence()
                e.tensor_scalar_mul(mv[:, :, 1], mv[:, :, 1], 1.0 / D)
                e.fence()
                e.scalar_tensor_tensor(z1[:], mv[:, :, 0], -1.0, mv[:, :, 0], ALU.mult, ALU.mult)
                e.fence()
                e.tensor_tensor(z1[:], mv[:, :, 1], z1[:], ALU.add)
                e.fence()
                e.tensor_scalar_add(z1[:], z1[:], 1e-5)
                e.fence()
            p.vector(v2)

            def a2(e, grp=grp):
                e.wait(v)
                e.activation(s1[:], z1[:], AF.Sqrt)
            p.scalar(a2)

            def v3(e, grp=grp):
                e.wait(a)
                e.reciprocal(Am[:], s1[:])
                e.fence()
                for k in range(8):
                    e.tensor_scalar(xrs[:, k, :], xs[:, k, :], mv[:, k, 0:1], Am[:, k:k + 1],
                                    ALU.subtract, ALU.mult)
                e.fence()
            p.vector(v3)

            def st1(e, grp=grp):
                e.wait(v)
                e.dma_start(out=xn_hbm.rearrange("(q s) d -> q s d", q=128)[:, grp * 8:(grp + 1) * 8, :], in_=xrs[:], sem=d_st)
            p.scalar(st1)
        p.scope = old
    return d_st.n


N_ITER = 28
CPAY = 1024
PAYR = 8 * TLOC + 128
LCAP = 1024
PAYR4 = 64 * LCAP + 128
BIG = 1.0e6


def emit_l0_ffn(p, io, V, dbg=False):
    nc = p.nc
    v, a, t, g = p.ctr("v"), p.ctr("a"), p.ctr("t"), p.ctr("g")
    d_ld = p.ctr("d_ld")
    d_st = p.ctr("d_st")
    d_tr = [p.ctr("d_tr0"), p.ctr("d_tr1")]
    d_gg = p.ctr("p2_dgg")
    d_z = p.ctr("p2_dz")
    d_w = [p.ctr("d_w0"), p.ctr("d_w1")]
    d_y = [p.ctr("p2_dy0"), p.ctr("p2_dy1")]
    d_rp = p.ctr("p2_drp")

    x1_hbm, x2_hbm, xn2_hbm = io["x1_hbm"], io["x2_hbm"], io["xn2_hbm"]
    xall, sc_in, sc_all = io["p2_xall"], io["p2_scin"], io["p2_scall"]
    pay_in, pay_ar = io["p2_payin"], io["p2_payar"]
    mypay_hbm, xe_hbm = io["p2_mypay"], io["p2_xe"]
    y_in, y_all = io["p2_yin"], io["p2_yall"]
    acc_hbm, dw_hbm = io["p2_acc"], io["p2_dw"]
    w1t_h, w3t_h, w2t_h = io["w_w1t_bf"], io["w_w3t_bf"], io["w_w2t_bf"]

    me_i = p.sbuf("p2_mei", [1, 2], I32)
    REG = {}
    gidt = p.sbuf("p2_gidt", [128, 32], F32)
    riota = p.sbuf("p2_riota", [128, 8], F32)
    baset = p.sbuf("p2_baset", [128, 1], F32)
    dumpt = p.sbuf("p2_dumpt", [128, 8], F32)
    ones816 = p.sbuf("p2_o816", [8, 16], F32)
    ones88 = p.sbuf("p2_o88", [8, 8], F32)
    ctile = p.sbuf("p2_ctile", [8, 16], F32)
    cnt1 = p.sbuf("p2_cnt1", [128, 1], F32)
    thr128 = p.sbuf("p2_thr", [128, 1], F32)
    loB = [p.sbuf(f"p2_lo{i}", [8, 1], F32) for i in range(2)]
    hiB = [p.sbuf(f"p2_hi{i}", [8, 1], F32) for i in range(2)]
    loE, hiE = loB[0], hiB[0]
    midE = p.sbuf("p2_mid", [8, 1], F32)
    mid16 = p.sbuf("p2_mid16", [8, 16], F32)
    geE = p.sbuf("p2_ge", [8, 1], F32)
    ltE = p.sbuf("p2_lt", [8, 1], F32)
    sc8a = p.sbuf("p2_sc8a", [8, 1], F32)
    sc8b = p.sbuf("p2_sc8b", [8, 1], F32)
    cntE = p.sbuf("p2_cntE", [8, 1], F32)
    mE = p.sbuf("p2_mE", [8, 1], F32)
    fillE = p.sbuf("p2_fillE", [8, 1], F32)
    maxE = p.sbuf("p2_maxE", [8, 1], F32)
    nmaxE = p.sbuf("p2_nmaxE", [8, 1], F32)
    denE = p.sbuf("p2_denE", [8, 1], F32)
    rdenE = p.sbuf("p2_rdenE", [8, 1], F32)
    tievE = p.sbuf("p2_tievE", [8, 1], F32)
    mce = p.sbuf("p2_mce", [128, 1], F32)
    tice = p.sbuf("p2_tice", [128, 1], F32)
    hi128 = p.sbuf("p2_hi128", [128, 1], F32)
    lo128 = p.sbuf("p2_lo128", [128, 1], F32)
    nmx128 = p.sbuf("p2_nmx128", [128, 1], F32)
    tie_ec = p.sbuf("p2_tieec", [8, 16], F32)
    gt_ec = p.sbuf("p2_gtec", [8, 16], F32)
    tie_e8 = p.sbuf("p2_tie8", [8, 8], F32)
    tiecum = p.sbuf("p2_tcum", [8, 8], F32)
    take_ec = p.sbuf("p2_takeec", [8, 8], F32)
    m_e8 = p.sbuf("p2_me8", [8, 8], F32)
    off_gcT = p.sbuf("p2_offgcT", [8, 8], F32)
    take_my = p.sbuf("p2_takemy", [8, 1], F32)
    og_meE = p.sbuf("p2_ogmeE", [8, 1], F32)
    ogP = p.sbuf("p2_ogP", [8, 1], F32)
    epad8 = p.sbuf("p2_epad8", [8, 1], F32)
    smod = p.sbuf("p2_smod", [8, TLOC], F32)
    oge_row = p.sbuf("p2_ogerow", [1, 8], F32)
    ones_col = p.sbuf("p2_onescol", [1, 128], F32)
    ogbc = p.sbuf("p2_ogbc", [128, 8], F32)
    ccol64 = p.sbuf("p2_ccol64", [128, 64], F32)
    idx64 = p.sbuf("p2_idx64", [128, 64], F32)
    idx64_i = p.sbuf("p2_idx64i", [128, 64], I32)
    wg_tm2 = p.sbuf("p2_wgtm2", [128, 32, 2], F32)
    gidr = p.sbuf("p2_gidr", [128, 8], F32)
    tlr = p.sbuf("p2_tlr", [128, 8], F32)
    v1m = p.sbuf("p2_v1m", [128, 8], F32)
    v2m = p.sbuf("p2_v2m", [128, 8], F32)
    idxs2 = p.sbuf("p2_idxs2", [128, 8], I32)
    mask8 = p.sbuf("p2_mask8", [8, 8], F32)
    tl8 = p.sbuf("p2_tl8", [1, 8], F32)
    ztile = p.sbuf("p2_z", [128, D], F32)
    zb = p.sbuf("p2_zb", [128, 1040], BF16)
    p.shared.update(me_i=me_i, gidt=gidt, riota=riota, baset=baset, dumpt=dumpt,
                    tl8=tl8, epad8=epad8, smod=smod, ones_col=ones_col, ccol64=ccol64,
                    ztile=ztile, zb=zb, mask8=mask8, ones88=ones88)

    def ldconst(e):
        e.wait(p.ctr("d_sx10"))
        e.wait(p.ctr("d_sx11"))
        e.dma_start(out=me_i[:], in_=V["me_i"], sem=d_ld)
        e.dma_start(out=gidt[:], in_=V["gid_c"], sem=d_ld)
        e.dma_start(out=riota[:], in_=V["riota_c"], sem=d_ld)
        e.dma_start(out=baset[:], in_=V["base_c"], sem=d_ld)
        e.dma_start(out=mask8[:], in_=V["mask8_c"], sem=d_ld)
        e.dma_start(out=tl8[:], in_=V["tl8_c"], sem=d_ld)
        e.dma_start(out=epad8[:], in_=V["epad8_c"], sem=d_ld)
        e.dma_start(out=smod[:], in_=V["smod_c"], sem=d_ld)
        e.dma_start(out=ccol64[:], in_=V["ccol64_c"], sem=d_ld)
    p.sync(ldconst)
    LD_CONST = d_ld.n

    def vconst(e):
        e.memset(ones816[:], 1.0)
        e.memset(ones88[:], 1.0)
        for k in range(8):
            e.memset(dumpt[:, k:k + 1], float(TLOC + k))
        e.memset(ztile[:], 0.0)
        e.memset(zb[:], 0.0)
        e.memset(ones_col[:], 1.0)
        e.fence()
    p.vector(vconst)
    V_CONST = v.n

    def stz(e):
        e.wait(v, V_CONST)
        for i in range(33):
            e.dma_start(out=acc_hbm[i * 128:(i + 1) * 128, :], in_=ztile[:], sem=d_z)
        payv = pay_in.rearrange("(q x) f -> q x f", q=128)
        for j in range(4):
            e.dma_start(out=payv[:, j * 64:(j + 1) * 64, :], in_=ztile[:], sem=d_z)
        e.dma_start(out=payv[:, 256:257, :], in_=ztile[:, 0:16], sem=d_z)
        for i in range(CPAY // 128):
            e.dma_start(out=y_all[8 * TLOC + i * 128:8 * TLOC + (i + 1) * 128, :], in_=zb[:], sem=d_z)
    p.gpsimd(stz)
    ST_ZERO = d_z.n

    emit_ln_plain(p, x1_hbm, xn2_hbm, "p2ln", gates=("d_sx10", "d_sx11"))
    ST_LN = d_st.n

    with ExitStack() as ssc:
        old = p.scope; p.scope = ssc
        wgt = p.sbuf("p2_wgt", [128, 8, 8], F32)
        wgt_b = p.sbuf("p2_wgtb", [128, 8, 8], BF16)
        xnf = [p.sbuf(f"p2_xnf{i}", [128, 8, 512], BF16) for i in range(2)]
        ssb = p.sbuf("p2_ssb", [8, TLOC], F32)
        ps_s = p.psum("p2_pss", [8, 512])

        def lds(e):
            e.wait(d_st, ST_LN)
            e.dma_start(out=wgt[:], in_=V["wgt_fm"], sem=d_ld)
        p.sync(lds)
        LD_WGT = d_ld.n

        def vcvt(e):
            e.wait(d_ld, LD_WGT)
            e.tensor_copy(wgt_b[:], wgt[:])
            e.fence()
        p.vector(vcvt)
        V_WGT = v.n

        for cg in range(8):
            bb = cg % 2

            def ldtr(e, cg=cg, bb=bb):
                e.wait(d_st, ST_LN)
                if cg >= 2:
                    e.wait(t)
                for j in range(8):
                    e.dma_start_transpose(xnf[bb][:, j, :], xn2_hbm[cg * 512:(cg + 1) * 512, j * 128:(j + 1) * 128], sem=d_tr[bb])
            p.sync(ldtr)

            def mms(e, cg=cg, bb=bb):
                e.wait(d_tr[bb])
                e.wait(v, V_WGT)
                if cg >= 1:
                    e.wait(v)
                for j in range(8):
                    e.matmul(ps_s[:], wgt_b[:, j, :], xnf[bb][:, j, :], start=(j == 0), stop=(j == 7))
            p.tensor(mms)

            def evs(e, cg=cg):
                e.wait(t)
                e.tensor_copy(ssb[:, cg * 512:(cg + 1) * 512], ps_s[:])
                e.fence()
            p.vector(evs)

        def sts(e):
            e.wait(v)
            e.dma_start(out=sc_in, in_=ssb[:], sem=d_st)
        p.scalar(sts)
        p.scope = old
    ST_SC = d_st.n

    def ag_sc(e):
        e.wait(d_st, ST_SC)
        e.collective_compute("AllGather", ALU.bypass, replica_groups=[list(range(NCORES))],
                             ins=[sc_in], outs=[sc_all])
    p.gpsimd(ag_sc)
    CC_SC = g.n

    def ag_x(e):
        e.wait(d_st, ST_LN)
        e.collective_compute("AllGather", ALU.bypass, replica_groups=[list(range(NCORES))],
                             ins=[xn2_hbm], outs=[xall])
    p.gpsimd(ag_x)
    CC_X = g.n

    with ExitStack() as asc:
        old = p.scope; p.scope = asc
        srch = p.sbuf("p2_srch", [128, 2048], F32)
        cmpb = p.sbuf("p2_cmp", [128, 2048], BF16)
        gtb = p.sbuf("p2_gtb", [128, 2048], BF16)
        expb = p.sbuf("p2_expb", [128, 2048], F32)

        def ldsr(e):
            e.wait(g, CC_SC)
            e.dma_start(out=srch[:], in_=sc_all.rearrange("c eh f -> eh c f"), sem=d_ld)
        p.sync(ldsr)
        LD_SR = d_ld.n

        def vinit(e):
            e.wait(d_ld, LD_SR)
            e.reduce_max(cnt1[:], srch[:], axis=AX.X, apply_absolute_value=True)
            e.fence()
        p.vector(vinit)

        def rp0(e):
            e.wait(v)
            e.dma_start(out=ctile[:], in_=cnt1[:], sem=d_rp)
        p.sync(rp0)

        def vinit2(e):
            e.wait(d_rp)
            e.wait(v, V_CONST)
            e.reduce_max(hiE[:], ctile[:], axis=AX.X)
            e.fence()
            e.tensor_scalar_add(hiE[:], hiE[:], 1.0)
            e.fence()
            e.tensor_scalar_mul(loE[:], hiE[:], -1.0)
            e.fence()
            e.tensor_tensor(midE[:], loE[:], hiE[:], ALU.add)
            e.fence()
            e.tensor_scalar_mul(midE[:], midE[:], 0.5)
            e.fence()
            e.tensor_scalar(mid16[:], ones816[:], midE[:, 0:1], 0.0, ALU.mult, ALU.add)
            e.fence()
        p.vector(vinit2)

        def rp1(e):
            e.wait(v)
            e.dma_start(out=thr128[:], in_=mid16[:], sem=d_rp)
        p.sync(rp1)

        for it in range(N_ITER):
            def vc(e, it=it):
                e.wait(d_rp)
                e.tensor_scalar(cmpb[:], srch[:], thr128[:, 0:1], 0.0, ALU.is_gt, ALU.add,
                                accum_out=cnt1[:])
                e.fence()
            p.vector(vc)

            def rp2(e, it=it):
                e.wait(v)
                e.dma_start(out=ctile[:], in_=cnt1[:], sem=d_rp)
            p.sync(rp2)

            def vu(e, it=it):
                sb, db = it % 2, 1 - it % 2
                e.wait(d_rp)
                e.reduce_sum(cntE[:], ctile[:], axis=AX.X)
                e.fence()
                e.tensor_scalar(geE[:], cntE[:], float(TLOC), 0.0, ALU.is_ge, ALU.add)
                e.fence()
                e.tensor_scalar(ltE[:], cntE[:], float(TLOC), 0.0, ALU.is_lt, ALU.add)
                e.fence()
                e.tensor_tensor(sc8a[:], midE[:], geE[:], ALU.mult)
                e.tensor_tensor(sc8b[:], loB[sb][:], ltE[:], ALU.mult)
                e.fence()
                e.tensor_tensor(loB[db][:], sc8a[:], sc8b[:], ALU.add)
                e.fence()
                e.tensor_tensor(sc8a[:], hiB[sb][:], geE[:], ALU.mult)
                e.tensor_tensor(sc8b[:], midE[:], ltE[:], ALU.mult)
                e.fence()
                e.tensor_tensor(hiB[db][:], sc8a[:], sc8b[:], ALU.add)
                e.fence()
                e.tensor_tensor(midE[:], loB[db][:], hiB[db][:], ALU.add)
                e.fence()
                e.tensor_scalar_mul(midE[:], midE[:], 0.5)
                e.fence()
                e.tensor_scalar(mid16[:], ones816[:], midE[:, 0:1], 0.0, ALU.mult, ALU.add)
                e.fence()
            p.vector(vu)

            def rp3(e, it=it):
                e.wait(v)
                e.dma_start(out=thr128[:], in_=mid16[:], sem=d_rp)
            p.sync(rp3)

        def vhb(e):
            e.wait(d_rp)
            e.tensor_scalar(mid16[:], ones816[:], hiE[:, 0:1], 0.0, ALU.mult, ALU.add)
            e.fence()
        p.vector(vhb)

        def rp4(e):
            e.wait(v)
            e.dma_start(out=hi128[:], in_=mid16[:], sem=d_rp)
        p.sync(rp4)

        def vf1(e):
            e.wait(d_rp)
            e.tensor_scalar(gtb[:], srch[:], hi128[:, 0:1], 0.0, ALU.is_gt, ALU.add, accum_out=mce[:])
            e.fence()
            e.reduce_max(cnt1[:], srch[:], axis=AX.X)
            e.fence()
            e.tensor_scalar(mid16[:], ones816[:], loE[:, 0:1], 0.0, ALU.mult, ALU.add)
            e.fence()
        p.vector(vf1)

        def rp5(e):
            e.wait(v)
            e.dma_start(out=lo128[:], in_=mid16[:], sem=d_rp)
            e.dma_start(out=ctile[:], in_=cnt1[:], sem=d_rp)
            e.dma_start(out=gt_ec[:], in_=mce[:], sem=d_rp)
        p.sync(rp5)

        def vf2(e):
            e.wait(d_rp)
            e.reduce_max(maxE[:], ctile[:], axis=AX.X)
            e.fence()
            e.tensor_scalar_mul(nmaxE[:], maxE[:], -1.0)
            e.fence()
            e.tensor_scalar(mid16[:], ones816[:], nmaxE[:, 0:1], 0.0, ALU.mult, ALU.add)
            e.fence()
            e.tensor_scalar(cmpb[:], srch[:], lo128[:, 0:1], 0.0, ALU.is_gt, ALU.add, accum_out=tice[:])
            e.fence()
            e.tensor_tensor(tice[:], tice[:], mce[:], ALU.subtract)
            e.fence()
            e.reduce_sum(mE[:], gt_ec[:], axis=AX.X)
            e.fence()
            e.tensor_scalar(fillE[:], mE[:], float(TLOC), -1.0, ALU.subtract, ALU.mult)
            e.fence()
        p.vector(vf2)

        def rp6(e):
            e.wait(v)
            e.dma_start(out=nmx128[:], in_=mid16[:], sem=d_rp)
            e.dma_start(out=tie_ec[:], in_=tice[:], sem=d_rp)
        p.sync(rp6)

        def af1(e):
            e.wait(d_rp)
            e.activation(expb[:], srch[:], AF.Exp, bias=nmx128[:, 0:1])
        p.scalar(af1)

        def vf3(e):
            e.wait(a)
            e.tensor_tensor(expb[:], expb[:], gtb[:], ALU.mult)
            e.fence()
            e.reduce_sum(cnt1[:], expb[:], axis=AX.X)
            e.fence()
        p.vector(vf3)

        def rp7(e):
            e.wait(v)
            e.dma_start(out=ctile[:], in_=cnt1[:], sem=d_rp)
        p.sync(rp7)

        def vf4(e):
            e.wait(d_rp)
            e.reduce_sum(denE[:], ctile[:], axis=AX.X)
            e.fence()
            e.tensor_tensor(tievE[:], hiE[:], nmaxE[:], ALU.add)
            e.fence()
        p.vector(vf4)

        def af2(e):
            e.wait(v)
            e.activation(tievE[:], tievE[:], AF.Exp)
        p.scalar(af2)

        def vf5(e):
            e.wait(a)
            e.tensor_tensor(tievE[:], tievE[:], fillE[:], ALU.mult)
            e.fence()
            e.tensor_tensor(denE[:], denE[:], tievE[:], ALU.add)
            e.fence()
            e.reciprocal(rdenE[:], denE[:])
            e.fence()
            e.tensor_tensor(tie_e8[:], tie_ec[:, 0:8], tie_ec[:, 8:16], ALU.add)
            e.fence()
            e.tensor_tensor(m_e8[:], gt_ec[:, 0:8], gt_ec[:, 8:16], ALU.add)
            e.fence()
            e.tensor_tensor_scan(tiecum[:], ones88[:], tie_e8[:], 0.0, ALU.mult, ALU.add)
            e.fence()
            e.tensor_tensor(tiecum[:], tiecum[:], tie_e8[:], ALU.subtract)
            e.fence()
            e.tensor_scalar(take_ec[:], tiecum[:], fillE[:, 0:1], -1.0, ALU.subtract, ALU.mult)
            e.fence()
            e.tensor_scalar(take_ec[:], take_ec[:], 0.0, 0.0, ALU.max, ALU.add)
            e.fence()
            e.tensor_tensor(take_ec[:], take_ec[:], tie_e8[:], ALU.min)
            e.fence()
            e.tensor_tensor(m_e8[:], m_e8[:], take_ec[:], ALU.add)
            e.fence()
            e.tensor_tensor_scan(off_gcT[:], ones88[:], m_e8[:], 0.0, ALU.mult, ALU.add)
            e.fence()
            e.tensor_tensor(off_gcT[:], off_gcT[:], m_e8[:], ALU.subtract)
            e.fence()
        p.vector(vf5)
        V_STATS = v.n
        p.scope = old

    with ExitStack() as bsc:
        old = p.scope; p.scope = bsc
        s_my = p.sbuf("p2_smy", [8, TLOC], F32)
        gtw = p.sbuf("p2_gtw", [8, TLOC], F32)
        tied = p.sbuf("p2_tied", [8, TLOC], F32)
        sel_my = p.sbuf("p2_selmy", [8, TLOC], F32)
        rank_my = p.sbuf("p2_rankmy", [8, TLOC], F32)
        ones8 = p.sbuf("p2_ones8", [8, TLOC], F32)

        def vones(e):
            e.memset(ones8[:], 1.0)
            e.fence()
        p.vector(vones)

        def gcols(e):
            e.wait(g, CC_SC)
            e.wait(v, V_STATS)
            rh_me = e.alloc_register(name="p2mereg")
            REG["me"] = bass_rust.make_scalar_value(rh_me, min_val=0, max_val=7)
            REG["meh"] = rh_me
            e.reg_load(rh_me, me_i[0:1, 0:1])
            e.dma_start(out=s_my[:], in_=sc_all.rearrange("c eh f -> c (eh f)")[DynSlice(REG["me"], 1), :], sem=d_gg)
        p.gpsimd(gcols)
        GD_COLS = d_gg.n

        def vcols(e):
            e.wait(v, V_STATS)
            e.wait(d_ld)
            e.tensor_tensor(tiecum[:], take_ec[:], mask8[:], ALU.mult)
            e.fence()
            e.reduce_sum(take_my[:], tiecum[:], axis=AX.X)
            e.fence()
            e.tensor_tensor(tiecum[:], off_gcT[:], mask8[:], ALU.mult)
            e.fence()
            e.reduce_sum(og_meE[:], tiecum[:], axis=AX.X)
            e.fence()
            e.tensor_tensor(ogP[:], og_meE[:], epad8[:], ALU.add)
            e.fence()
        p.vector(vcols)
        V_COLS = v.n

        def rp8(e):
            e.wait(v, V_COLS)
            e.dma_start(out=oge_row[:], in_=og_meE[:], sem=d_rp)
        p.sync(rp8)
        RP_ROWS = d_rp.n

        def vpre(e):
            e.wait(d_rp, RP_ROWS)
            e.wait(d_ld)
            e.tensor_tensor(oge_row[:], oge_row[:], tl8[:], ALU.add)
            e.fence()
        p.vector(vpre)
        V_PRE = v.n

        if dbg:
            def stdbg(e):
                e.wait(v)
                e.wait(d_rp)
                e.wait(d_gg)
                dd = io["dbg"]
                for i, tt in enumerate([hiE, loE, mE, fillE, maxE, denE, rdenE, take_my,
                                        og_meE, ogP, midE, cntE]):
                    e.dma_start(out=dd[:, i:i + 1], in_=tt[:], sem=d_st, allow_slow_non_contiguous=True)
                e.dma_start(out=dd[:, 16:24], in_=m_e8[:], sem=d_st)
                e.dma_start(out=dd[:, 24:32], in_=take_ec[:], sem=d_st)
                e.dma_start(out=dd[:, 40:48], in_=off_gcT[:], sem=d_st)
                e.dma_start(out=dd[:, 48:56], in_=tie_e8[:], sem=d_st)
            p.scalar(stdbg)
            return

        def vmy1(e):
            e.wait(d_gg, GD_COLS)
            e.tensor_scalar(gtw[:], s_my[:], hiE[:, 0:1], 0.0, ALU.is_gt, ALU.add)
            e.fence()
            e.tensor_scalar(tied[:], s_my[:], loE[:, 0:1], 0.0, ALU.is_gt, ALU.add)
            e.fence()
            e.tensor_tensor(tied[:], tied[:], gtw[:], ALU.subtract)
            e.fence()
            e.tensor_tensor_scan(rank_my[:], ones8[:], tied[:], 0.0, ALU.mult, ALU.add)
            e.fence()
            e.tensor_tensor(rank_my[:], rank_my[:], tied[:], ALU.subtract)
            e.fence()
            e.tensor_scalar(sel_my[:], rank_my[:], take_my[:, 0:1], 0.0, ALU.is_lt, ALU.add)
            e.fence()
            e.tensor_tensor(sel_my[:], sel_my[:], tied[:], ALU.mult)
            e.fence()
            e.tensor_tensor(sel_my[:], sel_my[:], gtw[:], ALU.add)
            e.fence()
            e.tensor_tensor_scan(rank_my[:], ones8[:], sel_my[:], 0.0, ALU.mult, ALU.add)
            e.fence()
            e.tensor_tensor(rank_my[:], rank_my[:], sel_my[:], ALU.subtract)
            e.fence()
        p.vector(vmy1)
        V_MY1 = v.n

        def amy(e):
            e.wait(v, V_MY1)
            e.activation(gtw[:], s_my[:], AF.Exp, bias=nmaxE[:, 0:1])
        p.scalar(amy)
        A_WMY = a.n

        def vmy2(e):
            e.wait(a, A_WMY)
            e.wait(v, V_COLS)
            e.tensor_scalar_mul(gtw[:], gtw[:], rdenE[:, 0:1])
            e.fence()
            e.tensor_tensor(gtw[:], gtw[:], sel_my[:], ALU.mult)
            e.fence()
            e.tensor_scalar(tied[:], rank_my[:], ogP[:, 0:1], float(-8 * TLOC), ALU.add, ALU.add)
            e.fence()
            e.tensor_tensor(tied[:], tied[:], smod[:], ALU.subtract)
            e.fence()
            e.tensor_tensor(tied[:], tied[:], sel_my[:], ALU.mult)
            e.fence()
            e.tensor_tensor(tied[:], tied[:], smod[:], ALU.add)
            e.fence()
            e.tensor_scalar_add(tied[:], tied[:], float(8 * TLOC))
            e.fence()
        p.vector(vmy2)
        V_MY2 = v.n

        def stdw(e):
            e.wait(v, V_MY2)
            e.dma_start(out=dw_hbm[0], in_=tied[:], sem=d_st)
            e.dma_start(out=dw_hbm[1], in_=gtw[:], sem=d_st)
        p.scalar(stdw)
        p.scope = old
    ST_DW = d_st.n

    with ExitStack() as csc:
        old = p.scope; p.scope = csc
        dwt = p.sbuf("p2_dwt", [128, 2, 32], F32)
        dwi = p.sbuf("p2_dwi", [128, 32], I32)
        pay_src = p.sbuf("p2_psrc", [128, 32, 16], F32)

        def vpay0(e):
            e.wait(d_st, ST_DW)
            e.wait(d_ld)
            e.memset(pay_src[:], 0.0)
            e.fence()
            e.tensor_copy(pay_src[:, :, 0], gidt[:])
            e.fence()
        p.vector(vpay0)
        V_PAY0 = v.n

        for ee in range(8):
            def lddw(e, ee=ee):
                e.wait(d_st, ST_DW)
                if ee > 0:
                    e.wait(d_gg)
                e.dma_start(out=dwt[:], in_=dw_hbm.rearrange("k e t -> e t k")[ee].rearrange("(q s) k -> q k s", q=128), sem=d_ld)
            p.sync(lddw)
            LDDW = d_ld.n

            def vpay(e, ee=ee, LDDW=LDDW):
                e.wait(d_ld, LDDW)
                e.wait(v, V_PAY0)
                if ee > 0:
                    e.wait(d_gg)
                e.tensor_copy(dwi[:], dwt[:, 0, :])
                e.fence()
                e.tensor_copy(pay_src[:, :, 1], dwt[:, 1, :])
                e.fence()
            p.vector(vpay)
            VPAY = v.n

            def gpay(e, ee=ee, VPAY=VPAY):
                e.wait(v, VPAY)
                e.wait(d_z, ST_ZERO)
                for s in range(32):
                    e.indirect_dma_start(
                        out=pay_in, out_offset=IndirectOffsetOnAxis(dwi[:, s:s + 1], 0),
                        in_=pay_src[:, s, :], in_offset=None,
                        sem=d_gg)
            p.gpsimd(gpay)
        p.scope = old
    GD_PAY = d_gg.n

    def ag_pay(e):
        e.wait(d_gg, GD_PAY)
        e.collective_compute("AllReduce", ALU.add, replica_groups=[list(range(NCORES))],
                             ins=[pay_in], outs=[pay_ar])
    p.gpsimd(ag_pay)
    CC_PAY = g.n

    with ExitStack() as dsc:
        old = p.scope; p.scope = dsc
        mypay = p.sbuf("p2_mypaysb", [128, 32, 16], F32)
        idxg = p.sbuf("p2_idxg", [128, 32], I32)
        X_tm = p.sbuf("p2_xtm", [128, 32, D], BF16)

        def ldmp(e):
            e.wait(g, CC_PAY)
            e.dma_start(out=mypay[:], in_=pay_ar[0:8 * TLOC, :].rearrange("(b r) f -> b r f", b=8)[DynSlice(REG["me"], 1)].rearrange("o (q s) f -> o q s f", q=128), sem=d_gg)
        p.gpsimd(ldmp)
        GD_MP = d_gg.n

        def stmp(e):
            e.wait(d_gg, GD_MP)
            e.dma_start(out=mypay_hbm.rearrange("(q s) f -> q s f", q=128), in_=mypay[:], sem=d_st)
        p.scalar(stmp)
        LD_MP = d_gg.n

        def vidx(e):
            e.wait(d_gg, GD_MP)
            e.tensor_copy(idxg[:], mypay[:, :, 0])
            e.fence()
        p.vector(vidx)
        V_IDXG = v.n

        def ggx(e):
            e.wait(v, V_IDXG)
            for s in range(32):
                e.indirect_dma_start(
                    out=X_tm[:, s, :], out_offset=None,
                    in_=xall, in_offset=IndirectOffsetOnAxis(idxg[:, s:s + 1], 0),
                    sem=d_gg)
        p.gpsimd(ggx)
        GD_X = d_gg.n

        def stxe(e):
            e.wait(d_gg, GD_X)
            e.dma_start(out=xe_hbm.rearrange("(q s) d -> q s d", q=128), in_=X_tm[:], sem=d_st)
        p.scalar(stxe)
        p.scope = old
    ST_XE = d_st.n

    with ExitStack() as esc:
        old = p.scope; p.scope = esc
        w1t_sb = p.sbuf("p2_w1sb", [128, 8, 2048], BF16)
        w3t_sb = p.sbuf("p2_w3sb", [128, 8, 2048], BF16)
        w2t_sb = p.sbuf("p2_w2sb", [128, 16, D], BF16)
        xfc = [p.sbuf(f"p2_xfc{i}", [128, 8, 512], BF16) for i in range(2)]
        h_bf = p.sbuf("p2_hbf", [128, 16, 512], BF16)
        hs_f = p.sbuf("p2_hsf", [128, 512], F32)
        Y_cg = [p.sbuf(f"p2_ycg{i}", [128, 4, 1040], BF16) for i in range(2)]
        ps_h = [p.psum(f"p2_psh{i}", [128, 512]) for i in range(2)]
        ps_y = [p.psum(f"p2_psy{i}", [128, 512]) for i in range(2)]

        def ldw13(e):
            e.wait(d_st, ST_XE)
            e.dma_start(out=w1t_sb[:], in_=w1t_h.rearrange("(j q) h -> q j h", q=128), sem=d_w[0])
            e.dma_start(out=w3t_sb[:], in_=w3t_h.rearrange("(j q) h -> q j h", q=128), sem=d_w[0])
            e.dma_start(out=w2t_sb[:], in_=w2t_h.rearrange("(j q) d -> q j d", q=128), sem=d_w[0])
        p.sync(ldw13)
        LD_W13 = d_w[0].n

        def ldwg2(e):
            e.wait(d_st, ST_XE)
            e.dma_start(out=wg_tm2[:], in_=mypay_hbm[:, 0:2].rearrange("(s q) f -> q s f", q=128), sem=d_ld)
        p.sync(ldwg2)
        LD_WG2 = d_ld.n

        for cg in range(8):
            bb = cg % 2

            def ldxf(e, cg=cg, bb=bb):
                e.wait(d_st, ST_XE)
                if cg >= 2:
                    e.wait(t)
                for j in range(8):
                    e.dma_start_transpose(xfc[bb][:, j, :], xe_hbm[cg * 512:(cg + 1) * 512, j * 128:(j + 1) * 128], sem=d_tr[bb])
            p.sync(ldxf)
            LD_XF = d_tr[bb].n

            for hf in range(16):
                def mmh(e, cg=cg, hf=hf, bb=bb, LD_XF=LD_XF):
                    e.wait(d_tr[bb], LD_XF)
                    e.wait(d_w[0], LD_W13)
                    if cg > 0 or hf > 0:
                        e.wait(v)
                    for j in range(8):
                        e.matmul(ps_h[0][:], w1t_sb[:, j, hf * 128:(hf + 1) * 128], xfc[bb][:, j, :], start=(j == 0), stop=(j == 7))
                    for j in range(8):
                        e.matmul(ps_h[1][:], w3t_sb[:, j, hf * 128:(hf + 1) * 128], xfc[bb][:, j, :], start=(j == 0), stop=(j == 7))
                p.tensor(mmh)

                def evh(e):
                    e.wait(t)
                    e.activation(hs_f[:], ps_h[0][:], AF.Silu)
                p.scalar(evh)

                def evh2(e, hf=hf):
                    e.wait(a)
                    e.tensor_tensor(h_bf[:, hf, :], hs_f[:], ps_h[1][:], ALU.mult)
                    e.fence()
                p.vector(evh2)

            for tt_i in range(4):
                for ng in range(2):
                    gi = tt_i * 2 + ng

                    def mmy(e, cg=cg, tt_i=tt_i, ng=ng, gi=gi):
                        e.wait(v)
                        if cg * 8 + gi >= 2:
                            e.wait(v, YEV[cg * 8 + gi - 2])
                        for hf in range(16):
                            e.matmul(ps_y[gi % 2][:], h_bf[:, hf, tt_i * 128:(tt_i + 1) * 128], w2t_sb[:, hf, ng * 512:(ng + 1) * 512], start=(hf == 0), stop=(hf == 15))
                    p.tensor(mmy)

                    def evy(e, cg=cg, tt_i=tt_i, ng=ng, gi=gi, bb=bb):
                        e.wait(t)
                        e.wait(d_ld, LD_WG2)
                        if cg >= 2 and gi == 0:
                            e.wait(d_y[bb])
                        e.tensor_scalar_mul(Y_cg[bb][:, tt_i, ng * 512:(ng + 1) * 512], ps_y[gi % 2][:], wg_tm2[:, cg * 4 + tt_i, 1:2])
                        e.fence()
                        YEV[cg * 8 + gi] = p.ctr("v").n
                    p.vector(evy)

            def vecho(e, cg=cg, bb=bb):
                ycf = Y_cg[bb].bitcast(F32)
                for tt_i in range(4):
                    e.tensor_copy(ycf[:, tt_i, 512:513], wg_tm2[:, cg * 4 + tt_i, 0:1])
                e.memset(ycf[:, :, 513:520], 0.0)
                e.fence()
            p.vector(vecho)
            V_ECHO = v.n

            def sty(e, cg=cg, bb=bb, V_ECHO=V_ECHO):
                e.wait(v, V_ECHO)
                e.dma_start(out=y_in.rearrange("(s q) f -> q s f", q=128)[:, cg * 4:(cg + 1) * 4, :], in_=Y_cg[bb][:], sem=d_y[bb])
            p.scalar(sty)
        p.scope = old

    def ag_y(e):
        e.wait(d_y[0])
        e.wait(d_y[1])
        e.collective_compute("AllGather", ALU.bypass, replica_groups=[list(range(NCORES))],
                             ins=[y_in], outs=[y_all[0:8 * TLOC, :]])
    p.gpsimd(ag_y)
    CC_Y = g.n

    with ExitStack() as fsc:
        old = p.scope; p.scope = fsc
        slab = [p.sbuf(f"p2_slab{i}", [128, 8, 1040], BF16) for i in range(2)]
        yf = [p.sbuf(f"p2_yf{i}", [128, 8, D], F32) for i in range(2)]
        d_sl = [p.ctr("p2_dsl0"), p.ctr("p2_dsl1")]
        GSC2, VS2 = {}, {}
        ps_bc = p.psum("p2_psbc", [128, 8])

        def tbc(e):
            e.wait(v, V_PRE)
            e.matmul(ps_bc[:], ones_col[:], oge_row[:], start=True, stop=True)
        p.tensor(tbc)
        T_BC = t.n

        def vbc(e):
            e.wait(t, T_BC)
            e.tensor_copy(ogbc[:], ps_bc[:])
            e.fence()
            for ee in range(8):
                e.tensor_scalar(idx64[:, ee * 8:(ee + 1) * 8], ccol64[:, ee * 8:(ee + 1) * 8],
                                ogbc[:, ee:ee + 1], 0.0, ALU.add, ALU.add)
            e.fence()
            e.tensor_copy(idx64_i[:], idx64[:])
            e.fence()
        p.vector(vbc)
        V_BC = v.n

        idxs2b = [p.sbuf(f"p2_idxs2{i}", [128, 8], I32) for i in range(2)]
        for ee in range(8):
            cb = ee % 2

            def gslab(e, ee=ee, cb=cb):
                e.wait(g, CC_Y)
                e.wait(v, V_BC)
                if ee >= 2:
                    e.wait(d_gg, GSC2[ee - 2])
                for k in range(8):
                    e.indirect_dma_start(
                        out=slab[cb][:, k, :], out_offset=None,
                        in_=y_all, in_offset=IndirectOffsetOnAxis(idx64_i[:, ee * 8 + k:ee * 8 + k + 1], 0),
                        sem=d_sl[cb])
            p.gpsimd(gslab)
            GSLAB = d_sl[cb].n

            def vslab(e, ee=ee, cb=cb, GSLAB=GSLAB):
                e.wait(d_sl[cb], GSLAB)
                e.tensor_copy(yf[cb][:], slab[cb][:, :, 0:D])
                e.fence()
                e.tensor_copy(gidr[:], slab[cb].bitcast(F32)[:, :, 512:513].rearrange("q s o -> q (s o)"))
                e.fence()
                e.tensor_scalar(tlr[:], gidr[:], baset[:, 0:1], 0.0, ALU.subtract, ALU.add)
                e.fence()
                e.tensor_scalar(v1m[:], tlr[:], -0.5, 0.0, ALU.is_gt, ALU.add)
                e.fence()
                e.tensor_scalar(v2m[:], tlr[:], float(TLOC) - 0.5, 0.0, ALU.is_lt, ALU.add)
                e.fence()
                e.tensor_tensor(v1m[:], v1m[:], v2m[:], ALU.mult)
                e.fence()
                e.tensor_tensor(gidr[:], tlr[:], v1m[:], ALU.mult)
                e.fence()
                e.tensor_scalar(v2m[:], v1m[:], 1.0, -1.0, ALU.subtract, ALU.mult)
                e.fence()
                e.tensor_tensor(v2m[:], v2m[:], dumpt[:], ALU.mult)
                e.fence()
                e.tensor_tensor(gidr[:], gidr[:], v2m[:], ALU.add)
                e.fence()
                e.tensor_copy(idxs2b[cb][:], gidr[:])
                e.fence()
            p.vector(vslab)
            VSLAB = v.n

            def gsc(e, ee=ee, cb=cb, VSLAB=VSLAB):
                e.wait(v, VSLAB)
                e.wait(d_z, ST_ZERO)
                if ee > 0:
                    e.wait(d_gg)
                for k in range(8):
                    e.indirect_dma_start(
                        out=acc_hbm, out_offset=IndirectOffsetOnAxis(idxs2b[cb][:, k:k + 1], 0),
                        in_=yf[cb][:, k, :], in_offset=None, compute_op=ALU.add,
                        sem=d_gg)
                GSC2[ee] = d_gg.n
                e.wait(d_gg)
            p.gpsimd(gsc)
        p.scope = old
    GD_COMB = d_gg.n

    with ExitStack() as xsc:
        old = p.scope; p.scope = xsc
        xtile = [p.sbuf(f"p2_xt{i}", [128, D], F32) for i in range(2)]
        atile = [p.sbuf(f"p2_at{i}", [128, D], F32) for i in range(2)]
        d_x2 = [p.ctr("p2_dx20"), p.ctr("p2_dx21")]
        d_lx = [p.ctr("p2_dlx0"), p.ctr("p2_dlx1")]
        for i in range(32):
            bb = i % 2

            def ldx2(e, i=i, bb=bb):
                e.wait(d_gg, GD_COMB)
                if i >= 2:
                    e.wait(d_x2[bb])
                e.dma_start(out=xtile[bb][:], in_=x1_hbm[i * 128:(i + 1) * 128, :], sem=d_lx[bb])
                e.dma_start(out=atile[bb][:], in_=acc_hbm[i * 128:(i + 1) * 128, :], sem=d_lx[bb])
            p.sync(ldx2)
            LX2 = d_lx[bb].n

            def vx2(e, i=i, bb=bb, LX2=LX2):
                e.wait(d_lx[bb], LX2)
                e.tensor_tensor(xtile[bb][:], xtile[bb][:], atile[bb][:], ALU.add)
                e.fence()
            p.vector(vx2)
            VX2 = v.n

            def stx2(e, i=i, bb=bb, VX2=VX2):
                e.wait(v, VX2)
                e.dma_start(out=x2_hbm[i * 128:(i + 1) * 128, :], in_=xtile[bb][:], sem=d_x2[bb])
            p.scalar(stx2)
        p.scope = old


EPS32 = float(2.0 ** -23)


def _regions_of(qt):
    R = [(qt, "T", 128)]
    if qt >= 1:
        R.append((qt - 1, None, 128))
    if qt == 2:
        R.append((0, "BG", 128))
    if qt >= 3:
        R.append((qt - 2, "B2", 128))
        R.append((0, None, 64))
    return R


def emit_l1_attn(p, io, V):
    nc = p.nc
    v, a, t, g = p.ctr("v"), p.ctr("a"), p.ctr("t"), p.ctr("g")
    d_ld = p.ctr("d_ld")
    d_st = p.ctr("d_st")
    d_tr = [p.ctr("d_tr0"), p.ctr("d_tr1")]
    d_w = [p.ctr("d_w0"), p.ctr("d_w1")]
    d_y3 = [p.ctr("p3_dy0"), p.ctr("p3_dy1")]
    d_lx3 = [p.ctr("p3_dlx0"), p.ctr("p3_dlx1")]
    d_sx3 = [p.ctr("p3_dsx0"), p.ctr("p3_dsx1")]

    x2_hbm, x3_hbm, xn3_hbm = io["x2_hbm"], io["x3_hbm"], io["xn3_hbm"]
    qk_hbm, y_hbm = io["p3_qk"], io["p3_y"]
    wqkv_h, wo_h = io["w_qkv_bf"], io["w_wo_bf"]

    _p3s = ExitStack()
    _oldscope = p.scope
    p.scope = _p3s

    v_sb = p.sbuf("p3_vsb", [128, 32, 260], BF16)
    ones_cb = p.sbuf("p3_onescb", [128, 1], BF16)
    masks = p.sbuf("p3_masks", [128, 3, 128], F32)

    emit_ln_plain(p, x2_hbm, xn3_hbm, "p3ln", gates=("p2_dx20", "p2_dx21"))
    ST_LN3 = d_st.n

    d_msk = p.ctr("p3_dmsk")

    def ldc3(e):
        e.wait(p.ctr("p2_dx20"))
        e.wait(p.ctr("p2_dx21"))
        e.dma_start(out=masks[:], in_=V["masks_c"], sem=d_msk)
    p.sync(ldc3)
    LD_MSK = d_msk.n

    def vc3(e):
        e.wait(d_msk, LD_MSK)
        e.memset(ones_cb[:], 1.0)
        for gg in range(4):
            e.memset(v_sb[:, :, gg * 65 + 64:gg * 65 + 65], 1.0)
        e.fence()
    p.vector(vc3)

    with ExitStack() as qsc:
        old = p.scope; p.scope = qsc
        wqkv_sb = p.sbuf("p3_wqkvsb", [128, 8, 1536], BF16)
        xnf3 = [p.sbuf(f"p3_xnf{i}", [128, 8, 512], BF16) for i in range(2)]
        qkvt = [p.sbuf(f"p3_qkvt{i}", [128, 1536], F32) for i in range(2)]
        sqv = p.sbuf("p3_sqv", [128, 1280], F32)
        ssq = p.sbuf("p3_ssq", [128, 20], F32)
        rs20 = p.sbuf("p3_rs20", [128, 20], F32)
        qkro = p.sbuf("p3_qkro", [128, 1280], F32)
        tmp640 = p.sbuf("p3_tmp640", [128, 640], F32)
        qk_bf = [p.sbuf(f"p3_qkbf{i}", [128, 1280], BF16) for i in range(2)]
        ct = [p.sbuf(f"p3_ct{i}", [128, 32], F32) for i in range(2)]
        st_ = [p.sbuf(f"p3_st{i}", [128, 32], F32) for i in range(2)]
        ps_qkv = [p.psum(f"p3_psqkv{i}", [128, 1536]) for i in range(2)]
        d_qs = [p.ctr("p3_dqs0"), p.ctr("p3_dqs1")]
        VQC, VRP = {}, {}

        def ldw3(e):
            e.wait(d_st, ST_LN3)
            e.dma_start(out=wqkv_sb[:], in_=wqkv_h.rearrange("(j q) f -> q j f", q=128), sem=d_w[0])
        p.sync(ldw3)
        LD_W3 = d_w[0].n

        for cg in range(8):
            bb = cg % 2

            def ldtr3(e, cg=cg, bb=bb):
                e.wait(d_st, ST_LN3)
                if cg >= 2:
                    e.wait(t)
                for j in range(8):
                    e.dma_start_transpose(xnf3[bb][:, j, :], xn3_hbm[cg * 512:(cg + 1) * 512, j * 128:(j + 1) * 128], sem=d_tr[bb])
            p.sync(ldtr3)
            LD_XT3 = d_tr[bb].n

            for tt_i in range(4):
                gt = cg * 4 + tt_i
                qtl = gt % 16
                qb = gt % 2

                def mmq(e, cg=cg, tt_i=tt_i, bb=bb, qb=qb, gt=gt, LD_XT3=LD_XT3):
                    e.wait(d_tr[bb], LD_XT3)
                    e.wait(d_w[0], LD_W3)
                    if gt >= 2:
                        e.wait(v, VRP[gt - 2])
                    for part in range(3):
                        for j in range(8):
                            e.matmul(ps_qkv[qb][:, part * 512:(part + 1) * 512],
                                     xnf3[bb][:, j, tt_i * 128:(tt_i + 1) * 128],
                                     wqkv_sb[:, j, part * 512:(part + 1) * 512],
                                     start=(j == 0), stop=(j == 7))
                    MM_T[("p3q", gt)] = p.ctr("t").n
                p.tensor(mmq)

                def ldrope(e, qtl=qtl, qb=qb, gt=gt):
                    if gt >= 2:
                        e.wait(v, VRP[gt - 2])
                    e.dma_start(out=ct[qb][:], in_=V["rope_cos_c"][qtl * 128:(qtl + 1) * 128, :], sem=d_qs[qb])
                    e.dma_start(out=st_[qb][:], in_=V["rope_sin_c"][qtl * 128:(qtl + 1) * 128, :], sem=d_qs[qb])
                p.sync(ldrope)
                LD_RP = d_qs[qb].n

                def vqc(e, gt=gt, qb=qb):
                    e.wait(t, MM_T[("p3q", gt)])
                    e.tensor_copy(qkvt[qb][:], ps_qkv[qb][:, 0:1536])
                    e.fence()
                    VQC[gt] = p.ctr("v").n
                p.vector(vqc)

                def vrms(e, qb=qb):
                    e.tensor_tensor(sqv[:], qkvt[qb][:, 0:1280], qkvt[qb][:, 0:1280], ALU.mult)
                    e.fence()
                    e.reduce_sum(ssq[:], sqv.rearrange("p (h d) -> p h d", d=64), axis=AX.X)
                    e.fence()
                    e.tensor_scalar(ssq[:], ssq[:], 1.0 / 64.0, EPS32, ALU.mult, ALU.add)
                    e.fence()
                p.vector(vrms)

                def asq(e):
                    e.wait(v)
                    e.activation(rs20[:], ssq[:], AF.Sqrt)
                p.scalar(asq)

                def vrope(e, gt=gt, qb=qb, LD_RP=LD_RP):
                    e.wait(a)
                    e.wait(d_qs[qb], LD_RP)
                    e.reciprocal(rs20[:], rs20[:])
                    e.fence()
                    e.tensor_scalar_mul(rs20[:, 0:16], rs20[:, 0:16], 0.125)
                    e.fence()
                    rsb = rs20.rearrange("p (h o) -> p h o", o=1).broadcast_to([128, 20, 64])
                    e.tensor_tensor(qkro.rearrange("p (h d) -> p h d", d=64),
                                    qkvt[qb][:, 0:1280].rearrange("p (h d) -> p h d", d=64),
                                    rsb, ALU.mult)
                    e.fence()
                    x4 = qkro.rearrange("p (h c d) -> p h c d", c=2, d=32)
                    t2 = tmp640.rearrange("p (h d) -> p h d", d=32)
                    cb = ct[qb].rearrange("p (o d) -> p o d", o=1).broadcast_to([128, 20, 32])
                    sb = st_[qb].rearrange("p (o d) -> p o d", o=1).broadcast_to([128, 20, 32])
                    e.tensor_tensor(t2, x4[:, :, 0, :], sb, ALU.mult)
                    e.fence()
                    e.tensor_tensor(x4[:, :, 0, :], x4[:, :, 0, :], cb, ALU.mult)
                    e.fence()
                    e.tensor_tensor(sqv.rearrange("p (h d) -> p h d", d=64)[:, :, 0:32],
                                    x4[:, :, 1, :], sb, ALU.mult)
                    e.fence()
                    e.tensor_tensor(x4[:, :, 1, :], x4[:, :, 1, :], cb, ALU.mult)
                    e.fence()
                    e.tensor_tensor(x4[:, :, 0, :], x4[:, :, 0, :],
                                    sqv.rearrange("p (h d) -> p h d", d=64)[:, :, 0:32], ALU.add)
                    e.fence()
                    e.tensor_tensor(x4[:, :, 1, :], x4[:, :, 1, :], t2, ALU.subtract)
                    e.fence()
                    if gt >= 2:
                        e.wait(d_st)
                    e.tensor_copy(qk_bf[qb][:], qkro[:])
                    e.fence()
                    for gg in range(4):
                        e.tensor_copy(v_sb[:, gt, gg * 65:gg * 65 + 64], ps_qkv[qb][:, 1280 + gg * 64:1280 + (gg + 1) * 64])
                    e.fence()
                    VRP[gt] = p.ctr("v").n
                p.vector(vrope)
                V_QK = v.n

                def stqk(e, gt=gt, qb=qb, V_QK=V_QK):
                    e.wait(v, V_QK)
                    e.dma_start(out=qk_hbm[gt * 128:(gt + 1) * 128, :], in_=qk_bf[qb][:], sem=d_st)
                p.scalar(stqk)
        p.scope = old
    ST_QK = d_st.n

    with ExitStack() as asc3:
        old = p.scope; p.scope = asc3
        k_fmA = p.sbuf("p3_kfmA", [128, 2, 2048], BF16)
        d_qf = [p.ctr("p3_dqf0"), p.ctr("p3_dqf1")]
        d_msk = p.ctr("p3_dmsk2")
        k_fm = p.sbuf("p3_kfm", [128, 4, 2048], BF16)
        q_fm = [p.sbuf(f"p3_qfm{i}", [128, 8, 512], BF16) for i in range(2)]
        expS = [p.sbuf(f"p3_exps{i}", [128, 7, 512], BF16) for i in range(2)]
        den_f = p.sbuf("p3_den", [128, 4], F32)
        y_sb = [p.sbuf(f"p3_ysb{i}", [128, 4, 16, 64], BF16) for i in range(2)]
        amask = p.sbuf("p3_amask", [128, 8, 512], F32)
        ps_s = [p.psum(f"p3_pss{i}", [128, 512]) for i in range(2)]
        ps_av = [p.psum(f"p3_psav{i}", [128, 130]) for i in range(2)]

        def ldmsk(e):
            e.wait(d_st, ST_QK)
            e.dma_start(out=amask[:], in_=V["amask_c"], sem=d_msk)
        p.sync(ldmsk)
        LD_AM = d_msk.n

        TSC, AEXPR, AEXP, VMASK, TAV, VEV = {}, {}, {}, {}, {}, {}

        def kts_of(g):
            if g == 0:
                return [(0, 6, 128), (1, 3, 128), (2, 4, 128), (3, 5, 128)]
            out = [(4 * g - 2, 0, 128), (4 * g - 1, 1, 128)]
            for d in range(4):
                out.append((4 * g + d, 2 + d, 128))
            out.append((0, None, 64))
            return out

        def av_regions(g, j):
            if g == 0:
                return list(range(j + 1))
            return [j, 1 + j, 2 + j, 6]

        prev_last = {}
        for it in range(8):
            s2, gq = it // 4, it % 4
            yb = it % 2
            kts = kts_of(gq)

            def ldkf(e, s2=s2, gq=gq, it=it):
                if gq == 0:
                    e.wait(d_st, ST_QK)
                    if it > 0:
                        e.wait(t)
                    for o in range(4):
                        for c in range(2):
                            e.dma_start_transpose(k_fmA[:, c, o * 512:(o + 1) * 512],
                                                  qk_hbm[s2 * 2048 + o * 512:s2 * 2048 + (o + 1) * 512, 1024 + c * 128:1024 + (c + 1) * 128], sem=d_tr[0])
                qb = it % 2
                e.wait(d_st, ST_QK)
                if it >= 2:
                    e.wait(t)
                for j in range(8):
                    e.dma_start_transpose(q_fm[qb][:, j, :],
                                          qk_hbm[s2 * 2048 + gq * 512:s2 * 2048 + (gq + 1) * 512, j * 128:(j + 1) * 128], sem=d_qf[qb])
            p.sync(ldkf)
            LD_QF = d_qf[it % 2].n
            LD_KFA = d_tr[0].n

            if gq == 0:
                def dupk(e, LD_KFA=LD_KFA):
                    e.wait(d_tr[0], LD_KFA)
                    for gg in range(4):
                        for half in range(2):
                            e.dma_start(out=k_fm[half * 64:(half + 1) * 64, gg, :],
                                        in_=k_fmA[64 * (gg % 2):64 * (gg % 2) + 64, gg // 2, :], sem=d_ld)
                p.sync(dupk)
            LD_KF = d_ld.n

            for h in range(16):
                hb = h % 2
                gkv = h // 4
                hi = it * 16 + h

                A_BASE = p.ctr("a").n

                def tsch(e, h=h, gkv=gkv, kts=kts, it=it, hi=hi, A_BASE=A_BASE,
                         LD_QF=LD_QF, LD_KF=LD_KF):
                    e.wait(d_qf[it % 2], LD_QF)
                    e.wait(d_ld, LD_KF)
                    if hi >= 2:
                        e.wait(t, TAV[hi - 2])
                    qb = it % 2
                    for r, (kt, msk, kp) in enumerate(kts):
                        if r < 2:
                            e.wait(a, A_BASE)
                        else:
                            e.wait(a, A_BASE + r - 1)
                        e.matmul(ps_s[r % 2][0:kp, :],
                                 k_fm[64 * (h % 2):64 * (h % 2) + 64, gkv, kt * 128:kt * 128 + kp],
                                 q_fm[qb][64 * (h % 2):64 * (h % 2) + 64, h // 2, :],
                                 start=True, stop=True)
                        TSC[(hi, r)] = p.ctr("t").n
                p.tensor(tsch)

                def aexph(e, hb=hb, kts=kts, hi=hi):
                    if hi >= 2:
                        e.wait(t, TAV[hi - 2])
                    for r, (kt, msk, kp) in enumerate(kts):
                        e.wait(t, TSC[(hi, r)])
                        e.activation(expS[hb][0:kp, r, :], ps_s[r % 2][0:kp, :], AF.Exp)
                p.scalar(aexph)
                AEXP[hi] = p.ctr("a").n

                def vmask(e, hb=hb, kts=kts, hi=hi):
                    e.wait(a, AEXP[hi])
                    e.wait(d_msk, LD_AM)
                    for r, (kt, msk, kp) in enumerate(kts):
                        if msk is not None:
                            e.tensor_tensor(expS[hb][0:kp, r, :], expS[hb][0:kp, r, :], amask[0:kp, msk, :], ALU.mult)
                    e.fence()
                    VMASK[hi] = p.ctr("v").n
                p.vector(vmask)

                def tav(e, s2=s2, gq=gq, hb=hb, gkv=gkv, kts=kts, hi=hi):
                    e.wait(v, VMASK[hi])
                    if hi >= 1:
                        e.wait(v, VEV[hi - 1])
                    for j in range(4):
                        regs = av_regions(gq, j)
                        pav = ps_av[j % 2]
                        col = (j // 2) * 65
                        nr = len(regs)
                        for ri, r in enumerate(regs):
                            kt, msk, kp = kts[r]
                            e.matmul(pav[:, col:col + 65],
                                     expS[hb][0:kp, r, j * 128:(j + 1) * 128],
                                     v_sb[0:kp, s2 * 16 + kt, gkv * 65:gkv * 65 + 65],
                                     start=(ri == 0), stop=(ri == nr - 1))
                    TAV[hi] = p.ctr("t").n
                p.tensor(tav)

                def vev(e, h=h, yb=yb, hi=hi, it=it):
                    e.wait(t, TAV[hi])
                    if h == 0 and it >= 2:
                        e.wait(d_y3[yb])
                    for j in range(4):
                        pav = ps_av[j % 2]
                        col = (j // 2) * 65
                        e.tensor_copy(den_f[:, j:j + 1], pav[:, col + 64:col + 65])
                    e.fence()
                    e.reciprocal(den_f[:], den_f[:])
                    e.fence()
                    for j in range(4):
                        pav = ps_av[j % 2]
                        col = (j // 2) * 65
                        e.tensor_scalar_mul(y_sb[yb][:, j, h, :], pav[:, col:col + 64], den_f[:, j:j + 1])
                    e.fence()
                    VEV[hi] = p.ctr("v").n
                p.vector(vev)

            def sty3(e, it=it, yb=yb):
                e.wait(v, VEV[it * 16 + 15])
                e.dma_start(out=y_hbm[it * 512:(it + 1) * 512, :].rearrange("(s q) d -> q s d", q=128),
                            in_=y_sb[yb].rearrange("p j h d -> p j (h d)"), sem=d_y3[yb])
            p.scalar(sty3)
        p.scope = old

    with ExitStack() as osc:
        old = p.scope; p.scope = osc
        wo_sb = p.sbuf("p3_wosb", [128, 8, D], BF16)
        y_fm = [p.sbuf(f"p3_yfm{i}", [128, 8, 512], BF16) for i in range(2)]
        xc3 = [p.sbuf(f"p3_xc{i}", [128, D], F32) for i in range(2)]
        x3c = [p.sbuf(f"p3_x3c{i}", [128, D], F32) for i in range(2)]
        ps_o = [p.psum(f"p3_pso{i}", [128, D]) for i in range(2)]

        def ldwo(e):
            e.wait(d_y3[0])
            e.wait(d_y3[1])
            e.dma_start(out=wo_sb[:], in_=wo_h.rearrange("(j q) f -> q j f", q=128), sem=d_w[1])
        p.sync(ldwo)
        LD_WO = d_w[1].n

        for cg in range(8):
            bb = cg % 2

            def ldtry(e, cg=cg, bb=bb):
                e.wait(d_y3[0])
                e.wait(d_y3[1])
                if cg >= 2:
                    e.wait(t)
                for j in range(8):
                    e.dma_start_transpose(y_fm[bb][:, j, :], y_hbm[cg * 512:(cg + 1) * 512, j * 128:(j + 1) * 128], sem=d_tr[bb])
            p.sync(ldtry)
            LD_YT = d_tr[bb].n

            for tt_i in range(4):
                i = cg * 4 + tt_i
                pb = i % 2

                def ldxc3(e, i=i, pb=pb):
                    e.wait(d_y3[0])
                    e.wait(d_y3[1])
                    if i >= 2:
                        e.wait(d_sx3[pb])
                    e.dma_start(out=xc3[pb][:], in_=x2_hbm[i * 128:(i + 1) * 128, :], sem=d_lx3[pb])
                p.sync(ldxc3)
                LD_XC3 = d_lx3[pb].n

                def mmo(e, cg=cg, tt_i=tt_i, bb=bb, pb=pb, LD_YT=LD_YT):
                    e.wait(d_tr[bb], LD_YT)
                    e.wait(d_w[1], LD_WO)
                    if cg * 4 + tt_i >= 2:
                        e.wait(v)
                    for ng in range(2):
                        for j in range(8):
                            e.matmul(ps_o[pb][:, ng * 512:(ng + 1) * 512],
                                     y_fm[bb][:, j, tt_i * 128:(tt_i + 1) * 128],
                                     wo_sb[:, j, ng * 512:(ng + 1) * 512],
                                     start=(j == 0), stop=(j == 7))
                p.tensor(mmo)

                def vres3(e, i=i, pb=pb, LD_XC3=LD_XC3):
                    e.wait(t)
                    e.wait(d_lx3[pb], LD_XC3)
                    e.tensor_tensor(x3c[pb][:], xc3[pb][:], ps_o[pb][:], ALU.add)
                    e.fence()
                p.vector(vres3)
                V_R3 = v.n

                def stx3(e, i=i, pb=pb, V_R3=V_R3):
                    e.wait(v, V_R3)
                    e.dma_start(out=x3_hbm[i * 128:(i + 1) * 128, :], in_=x3c[pb][:], sem=d_sx3[pb])
                p.scalar(stx3)
        p.scope = old

    p.scope = _oldscope
    _p3s.close()


def emit_l1_ffn(p, io, V):
    nc = p.nc
    v, a, t, g = p.ctr("v"), p.ctr("a"), p.ctr("t"), p.ctr("g")
    d_ld = p.ctr("d_ld")
    d_st = p.ctr("d_st")
    d_tr = [p.ctr("d_tr0"), p.ctr("d_tr1")]
    d_gg = p.ctr("p2_dgg")
    d_z = p.ctr("p2_dz")
    d_w = [p.ctr("d_w0"), p.ctr("d_w1")]
    d_y = [p.ctr("p4_dy0"), p.ctr("p4_dy1")]
    d_rp = p.ctr("p2_drp")
    d_c4 = p.ctr("p4_dc4")

    x3_hbm, x4_hbm, xn4_hbm = io["x3_hbm"], io["x4_hbm"], io["xn4_hbm"]
    xall4 = io["p4_xall"]
    ptok_hbm = io["p4_ptok"]
    cnt_in, cnt_all = io["p4_cntin"], io["p4_cntall"]
    pay_in4, pay_ar4 = io["p4_payin"], io["p4_payar"]
    mypay4_hbm, xe4_hbm = io["p4_mypay"], io["p4_xe"]
    y_in4, y_all4 = io["p4_yin"], io["p4_yall"]
    acc4_hbm, dw4_hbm = io["p4_acc"], io["p4_dw"]
    w1_h, w2_h = io["w_l1w1_bf"], io["w_l1w2_bf"]

    _p4s = ExitStack()
    _oldscope = p.scope
    p.scope = _p4s

    S = p.shared
    me_i, gidt, riota, baset, dumpt = S["me_i"], S["gidt"], S["riota"], S["baset"], S["dumpt"]
    tl8, epad8, smod, ones_col, ccol64 = S["tl8"], S["epad8"], S["smod"], S["ones_col"], S["ccol64"]
    ztile, zb, mask8, ones88 = S["ztile"], S["zb"], S["mask8"], S["ones88"]
    erow8 = p.sbuf("p4_erow8", [128, 8], F32)
    cmlt = p.sbuf("p4_cmlt", [8, 8], F32)
    cnt_ec = p.sbuf("p4_cntec", [8, 8], F32)
    cnt16 = p.sbuf("p4_cnt16", [16, 1], F32)
    off4 = p.sbuf("p4_off4", [8, 1], F32)
    len4 = p.sbuf("p4_len4", [8, 1], F32)
    og4 = p.sbuf("p4_og4", [8, 1], F32)
    sc81 = p.sbuf("p4_sc81", [8, 1], F32)
    oge4_row = p.sbuf("p4_oge4row", [1, 16], F32)
    cntrow = p.sbuf("p4_cntrow", [1, 16], F32)
    bcrow = p.sbuf("p4_bcrow", [1, 16], F32)
    cnt_cb = p.sbuf("p4_cntcb", [8, 16], F32)
    m_ce4 = p.sbuf("p4_mce4", [8, 8], F32)
    m_cme4 = p.sbuf("p4_mcme4", [8, 1], F32)
    ogrow = p.sbuf("p4_ogrow", [1, 16], F32)
    cmltc = p.sbuf("p4_cmltc", [8, 1], F32)
    bc128 = p.sbuf("p4_bc128", [128, 16], F32)
    baset4 = p.sbuf("p4_baset4", [128, 1], F32)
    srcidx = p.sbuf("p4_srcidx", [128, 32], F32)
    srct1 = p.sbuf("p4_srct1", [128, 32], F32)
    rloc4 = p.sbuf("p4_rloc", [128, 32], F32)

    ogbc4 = p.sbuf("p4_ogbc4", [128, 16], F32)
    idx644 = p.sbuf("p4_idx64", [128, 64], F32)
    idx644_i = p.sbuf("p4_idx64i", [128, 64], I32)
    wg4_tm2 = p.sbuf("p4_wgtm2", [128, 32, 2], F32)
    oh_em = p.sbuf("p4_ohem", [8, TLOC], BF16)
    idn_f = p.sbuf("p4_idnf", [128, 128], F32)
    idn_b = p.sbuf("p4_idnb", [128, 128], BF16)
    gidr4 = p.sbuf("p4_gidr", [128, 8], F32)
    tlr4 = p.sbuf("p4_tlr", [128, 8], F32)
    v1m4 = p.sbuf("p4_v1m", [128, 8], F32)
    v2m4 = p.sbuf("p4_v2m", [128, 8], F32)
    idxs4 = p.sbuf("p4_idxs", [128, 8], I32)
    d_cst4 = p.ctr("p4_dcst")

    def ldc4(e):
        e.wait(p.ctr("p3_dsx0"))
        e.wait(p.ctr("p3_dsx1"))
        e.dma_start(out=erow8[:], in_=V["erow8_c"], sem=d_cst4)
        e.dma_start(out=cmltc[:], in_=V["cmltc_c"], sem=d_cst4)
        e.dma_start(out=idn_f[:], in_=V["idn_c"], sem=d_cst4)
    p.sync(ldc4)
    LD_C4 = d_cst4.n

    def stz4(e):
        e.wait(d_cst4, LD_C4)
        for i in range(33):
            e.dma_start(out=acc4_hbm[i * 128:(i + 1) * 128, :], in_=ztile[:], sem=d_z)
        payv = pay_in4.rearrange("(q x) f -> q x f", q=128)
        for j in range(8):
            e.dma_start(out=payv[:, j * 64:(j + 1) * 64, :], in_=ztile[:], sem=d_z)
        e.dma_start(out=payv[:, 512:513, :], in_=ztile[:, 0:16], sem=d_z)
        for i in range(CPAY // 128):
            e.dma_start(out=y_all4[8 * TLOC + i * 128:8 * TLOC + (i + 1) * 128, :], in_=zb[:], sem=d_z)
    p.gpsimd(stz4)
    ST_Z4 = d_z.n

    emit_ln_plain(p, x3_hbm, xn4_hbm, "p4ln", gates=("p3_dsx0", "p3_dsx1"))
    ST_LN4 = d_st.n

    def ag_x4(e):
        e.wait(d_st, ST_LN4)
        e.collective_compute("AllGather", ALU.bypass, replica_groups=[list(range(NCORES))],
                             ins=[xn4_hbm], outs=[xall4])
    p.gpsimd(ag_x4)
    CC_X4 = g.n

    with ExitStack() as rsc:
        old = p.scope; p.scope = rsc
        wr_f = p.sbuf("p4_wrf", [128, 8, 8], F32)
        wr_b = p.sbuf("p4_wrb", [128, 8, 8], BF16)
        xnf4 = [p.sbuf(f"p4_xnf{i}", [128, 8, 512], BF16) for i in range(2)]
        lg = p.sbuf("p4_lg", [128, 8], F32)
        mx8 = p.sbuf("p4_mx8", [128, 8], F32)
        nmx = p.sbuf("p4_nmx", [128, 1], F32)
        midx = p.sbuf("p4_midx", [128, 8], U32)
        eidf = p.sbuf("p4_eidf", [128, 1], F32)
        exb = p.sbuf("p4_exb", [128, 8], F32)
        den1 = p.sbuf("p4_den1", [128, 1], F32)
        oh8 = [p.sbuf(f"p4_oh8{i}", [128, 8], BF16) for i in range(2)]
        ptk = [p.sbuf(f"p4_ptk{i}", [128, 1], F32) for i in range(2)]
        ps_r = p.psum("p4_psr", [128, 8])
        ps_t = p.psum("p4_pst", [8, 128])
        d_or = [p.ctr("p4_dor0"), p.ctr("p4_dor1")]
        TOH, VOH = {}, {}

        def vidn(e):
            e.wait(d_cst4, LD_C4)
            e.tensor_copy(idn_b[:], idn_f[:])
            e.fence()
        p.vector(vidn)
        V_IDN = v.n

        def ldwr(e):
            e.wait(d_st, ST_LN4)
            e.dma_start(out=wr_f[:], in_=V["p4_wr_fm"], sem=d_c4)
        p.sync(ldwr)
        LD_WR = d_c4.n

        def vwr(e):
            e.wait(d_c4, LD_WR)
            e.tensor_copy(wr_b[:], wr_f[:])
            e.fence()
        p.vector(vwr)
        V_WR = v.n

        for cg in range(8):
            bb = cg % 2

            def ldtr4(e, cg=cg, bb=bb):
                e.wait(d_st, ST_LN4)
                if cg >= 2:
                    e.wait(t)
                for j in range(8):
                    e.dma_start_transpose(xnf4[bb][:, j, :], xn4_hbm[cg * 512:(cg + 1) * 512, j * 128:(j + 1) * 128], sem=d_tr[bb])
            p.sync(ldtr4)
            LD_X4 = d_tr[bb].n

            for tt_i in range(4):
                i = cg * 4 + tt_i
                ob = i % 2

                def mmr(e, cg=cg, tt_i=tt_i, bb=bb, LD_X4=LD_X4):
                    e.wait(d_tr[bb], LD_X4)
                    e.wait(v, V_WR)
                    if cg * 4 + tt_i >= 1:
                        e.wait(v)
                    for j in range(8):
                        e.matmul(ps_r[:], xnf4[bb][:, j, tt_i * 128:(tt_i + 1) * 128], wr_b[:, j, :], start=(j == 0), stop=(j == 7))
                p.tensor(mmr)

                def vrt(e, i=i, ob=ob):
                    e.wait(t)
                    if i >= 2:
                        e.wait(d_or[ob])
                    e.tensor_copy(lg[:], ps_r[:])
                    e.fence()
                    e.max(out=mx8[:], in_=lg[:])
                    e.fence()
                    e.max_index(midx[:], mx8[:], lg[:])
                    e.fence()
                    e.tensor_copy(eidf[:], midx[:, 0:1])
                    e.fence()
                    e.tensor_scalar_mul(nmx[:], mx8[:, 0:1], -1.0)
                    e.fence()
                p.vector(vrt)

                def art(e):
                    e.wait(v)
                    e.activation(exb[:], lg[:], AF.Exp, bias=nmx[:, 0:1])
                p.scalar(art)

                def vrt2(e, i=i, ob=ob):
                    e.wait(a)
                    if i >= 2:
                        e.wait(t, TOH[i - 2])
                    e.reduce_sum(den1[:], exb[:], axis=AX.X)
                    e.fence()
                    e.reciprocal(ptk[ob][:], den1[:])
                    e.fence()
                    e.tensor_scalar(oh8[ob][:], erow8[:], eidf[:, 0:1], 0.0, ALU.is_equal, ALU.add)
                    e.fence()
                p.vector(vrt2)
                V_RT = v.n

                def tmoh(e, i=i, ob=ob, V_RT=V_RT):
                    e.wait(v, V_RT)
                    e.wait(v, V_IDN)
                    if i >= 1:
                        e.wait(v, VOH[i - 1])
                    e.matmul(ps_t[:], oh8[ob][:], idn_b[:], start=True, stop=True)
                    TOH[i] = p.ctr("t").n
                p.tensor(tmoh)

                def voh(e, i=i, ob=ob):
                    e.wait(t, TOH[i])
                    e.tensor_copy(oh_em[:, i * 128:(i + 1) * 128], ps_t[:])
                    e.fence()
                    VOH[i] = p.ctr("v").n
                p.vector(voh)

                def stor(e, i=i, ob=ob, V_RT=V_RT):
                    e.wait(v, V_RT)
                    e.dma_start(out=ptok_hbm[i * 128:(i + 1) * 128, :], in_=ptk[ob][:], sem=d_or[ob])
                p.scalar(stor)
        p.scope = old
        V_OHLAST = VOH[31]
    ST_RT0 = [p.ctr("p4_dor0").n, p.ctr("p4_dor1").n]

    with ExitStack() as bsc4:
        old = p.scope; p.scope = bsc4
        ohf = p.sbuf("p4_ohf", [8, TLOC], F32)
        rank4 = p.sbuf("p4_rank", [8, TLOC], F32)
        sel4 = p.sbuf("p4_sel", [8, TLOC], F32)
        dest4 = p.sbuf("p4_dest", [8, TLOC], F32)
        ones84 = p.sbuf("p4_ones8", [8, TLOC], F32)
        ones8c = p.sbuf("p4_ones8c", [8, 1], F32)
        drow = p.sbuf("p4_drow", [1, TLOC], F32)
        krow = p.sbuf("p4_krow", [1, TLOC], F32)
        epadL = p.sbuf("p4_epadL", [8, 1], F32)
        mebase4 = p.sbuf("p4_mebase", [8, 1], F32)
        ps_dk = p.psum("p4_psdk", [1, 1024])

        def ldc4b(e):
            e.dma_start(out=epadL[:], in_=V["epadL_c"], sem=d_c4)
            e.dma_start(out=mebase4[:], in_=V["mebase4_c"], sem=d_c4)
        p.sync(ldc4b)
        LD_C4B = d_c4.n

        def vrk1(e):
            e.wait(v, V_OHLAST)
            e.wait(p.ctr("p4_dor0"), ST_RT0[0])
            e.wait(p.ctr("p4_dor1"), ST_RT0[1])
            e.memset(ones84[:], 1.0)
            e.memset(cntrow[:], 0.0)
            e.fence()
            e.tensor_copy(ohf[:], oh_em[:, :])
            e.fence()
            e.reduce_sum(cnt16[0:8, :], ohf[:], axis=AX.X)
            e.fence()
        p.vector(vrk1)
        V_RK1 = v.n

        def rpc4(e):
            e.wait(v, V_RK1)
            e.dma_start(out=cntrow[:, 0:8], in_=cnt16[0:8, :], sem=d_rp)
        p.sync(rpc4)
        RP_C4 = d_rp.n

        def voff4(e):
            e.wait(d_c4, LD_C4B)
            e.wait(d_rp, RP_C4)
            e.tensor_tensor_scan(rank4[:], ones84[:], ohf[:], 0.0, ALU.mult, ALU.add)
            e.fence()
            e.tensor_tensor(rank4[:], rank4[:], ohf[:], ALU.subtract)
            e.fence()
            e.tensor_scalar(sel4[:], rank4[:], float(LCAP) - 0.5, 0.0, ALU.is_lt, ALU.add)
            e.fence()
            e.tensor_tensor(sel4[:], sel4[:], ohf[:], ALU.mult)
            e.fence()
            e.memset(ones8c[:], 1.0)
            e.fence()
            e.tensor_scalar(dest4[:], rank4[:], epadL[:, 0:1], 0.0, ALU.add, ALU.add)
            e.fence()
            e.tensor_scalar(dest4[:], dest4[:], mebase4[:, 0:1], 0.0, ALU.add, ALU.add)
            e.fence()
            e.tensor_tensor(dest4[:], dest4[:], sel4[:], ALU.mult)
            e.fence()
        p.vector(voff4)
        V_OFF4 = v.n

        DKT = {}
        for sl in range(8):
            def tdk(e, sl=sl):
                e.wait(v, V_OFF4)
                if sl >= 1:
                    e.wait(v)
                e.matmul(ps_dk[0:1, 0:512], ones8c[:], dest4[:, sl * 512:(sl + 1) * 512], start=True, stop=True)
                e.matmul(ps_dk[0:1, 512:1024], ones8c[:], sel4[:, sl * 512:(sl + 1) * 512], start=True, stop=True)
            p.tensor(tdk)

            def vdk(e, sl=sl):
                e.wait(t)
                e.tensor_copy(drow[:, sl * 512:(sl + 1) * 512], ps_dk[0:1, 0:512])
                e.tensor_copy(krow[:, sl * 512:(sl + 1) * 512], ps_dk[0:1, 512:1024])
                e.fence()
                DKT[sl] = p.ctr("v").n
            p.vector(vdk)

        def vfin4(e):
            e.wait(v, DKT[7])
            e.tensor_scalar(krow[:], krow[:], 1.0, -1.0, ALU.subtract, ALU.mult)
            e.fence()
            e.tensor_scalar(rank4[0:1, :], smod[0:1, :], float(64 * LCAP), 0.0, ALU.add, ALU.add)
            e.fence()
            e.tensor_tensor(krow[:], krow[:], rank4[0:1, :], ALU.mult)
            e.fence()
            e.tensor_tensor(drow[:], drow[:], krow[:], ALU.add)
            e.fence()
        p.vector(vfin4)
        V_OFF4b = v.n

        def stdw4(e):
            e.wait(v, V_OFF4b)
            e.dma_start(out=dw4_hbm[0][0], in_=drow[:], sem=d_st)
        p.scalar(stdw4)
        p.scope = old
    ST_DW4 = d_st.n

    with ExitStack() as csc4:
        old = p.scope; p.scope = csc4
        dwt4 = p.sbuf("p4_dwt", [128, 32], F32)
        dwi4 = p.sbuf("p4_dwi", [128, 32], I32)
        pay_src4 = p.sbuf("p4_psrc", [128, 32, 16], F32)
        ptt = p.sbuf("p4_ptt", [128, 32, 1], F32)

        def ldpt(e):
            e.wait(d_st, ST_DW4)
            e.dma_start(out=ptt[:], in_=ptok_hbm.rearrange("(q s) o -> q s o", q=128), sem=d_c4)
            e.dma_start(out=dwt4[:], in_=dw4_hbm[0][0].rearrange("(q s) -> q s", q=128), sem=d_c4)
        p.sync(ldpt)
        LD_PT = d_c4.n

        def vpay04(e):
            e.wait(d_c4, LD_PT)
            e.memset(pay_src4[:], 0.0)
            e.fence()
            e.tensor_copy(pay_src4[:, :, 0], gidt[:])
            e.fence()
            e.tensor_copy(pay_src4[:, :, 1], ptt[:, :, 0])
            e.fence()
            e.tensor_copy(dwi4[:], dwt4[:])
            e.fence()
        p.vector(vpay04)
        V_PAY04 = v.n

        def gpay4(e, V_PAY04=V_PAY04):
            e.wait(v, V_PAY04)
            e.wait(d_z, ST_Z4)
            rhc = e.alloc_register(name="p4cntreg")
            vc_ = bass_rust.make_scalar_value(rhc, min_val=64 * LCAP + 32, max_val=64 * LCAP + 40)
            e.reg_load(rhc, me_i[0:1, 1:2])
            e.dma_start(out=pay_in4[DynSlice(vc_, 1), :], in_=cntrow[:], sem=d_gg)
            for s in range(32):
                e.indirect_dma_start(
                    out=pay_in4, out_offset=IndirectOffsetOnAxis(dwi4[:, s:s + 1], 0),
                    in_=pay_src4[:, s, :], in_offset=None,
                    sem=d_gg)
        p.gpsimd(gpay4)
        p.scope = old
    GD_PAY4 = d_gg.n

    def ag_pay4(e):
        e.wait(d_gg, GD_PAY4)
        e.collective_compute("AllReduce", ALU.add, replica_groups=[list(range(NCORES))],
                             ins=[pay_in4], outs=[pay_ar4])
    p.gpsimd(ag_pay4)
    CC_PAY4 = g.n

    with ExitStack() as dsc4:
        old = p.scope; p.scope = dsc4
        mypay4 = p.sbuf("p4_mypaysb", [128, 32, 16], F32)
        idxg4 = p.sbuf("p4_idxg", [128, 32], I32)
        srci = p.sbuf("p4_srci", [128, 32], I32)
        X_tm4 = p.sbuf("p4_xtm", [128, 32, D], BF16)
        ps_bcr = p.psum("p4_psbcr", [128, 16])

        def ldcb4(e):
            e.wait(g, CC_PAY4)
            e.dma_start(out=cnt_cb[:], in_=pay_ar4[64 * LCAP + 32:64 * LCAP + 40, :], sem=d_c4)
        p.sync(ldcb4)
        LD_CB4 = d_c4.n

        def vmce4(e):
            e.wait(d_c4, LD_CB4)
            e.wait(d_cst4, LD_C4)
            e.tensor_scalar(m_ce4[:], cnt_cb[:, 0:8], float(LCAP), 0.0, ALU.min, ALU.add)
            e.fence()
            e.tensor_tensor(cnt_cb[:, 0:8], m_ce4[:], mask8[:], ALU.mult)
            e.fence()
            e.reduce_sum(m_cme4[:], cnt_cb[:, 0:8], axis=AX.X)
            e.fence()
            e.tensor_scalar(cnt_cb[:, 8:16], m_ce4[:], cmltc[:, 0:1], 0.0, ALU.mult, ALU.add)
            e.fence()
        p.vector(vmce4)
        V_MCE4 = v.n

        def tog4(e):
            e.wait(v, V_MCE4)
            e.matmul(ps_bcr[0:1, 0:8], ones8c[:], cnt_cb[:, 8:16], start=True, stop=True)
        p.tensor(tog4)
        T_OG4 = t.n

        def rpm4(e):
            e.wait(v, V_MCE4)
            e.dma_start(out=bcrow[:, 0:8], in_=m_cme4[:], sem=d_rp)
        p.sync(rpm4)
        RP_M4 = d_rp.n

        def vog4(e):
            e.wait(t, T_OG4)
            e.wait(d_rp, RP_M4)
            e.tensor_copy(ogrow[:, 0:8], ps_bcr[0:1, 0:8])
            e.fence()
            e.tensor_scalar(oge4_row[:, 0:8], ogrow[:, 0:8], float(TLOC), 0.0, ALU.min, ALU.add)
            e.fence()
            e.tensor_tensor(oge4_row[:, 0:8], oge4_row[:, 0:8], tl8[:], ALU.add)
            e.fence()
            e.tensor_scalar(oge4_row[:, 8:16], ogrow[:, 0:8], float(TLOC), -1.0, ALU.subtract, ALU.mult)
            e.fence()
            e.tensor_scalar(oge4_row[:, 8:16], oge4_row[:, 8:16], 0.0, 0.0, ALU.max, ALU.add)
            e.fence()
            e.tensor_scalar(cntrow[:, 8:16], cntrow[:, 0:8], float(LCAP), 0.0, ALU.min, ALU.add)
            e.fence()
            e.tensor_tensor(oge4_row[:, 8:16], oge4_row[:, 8:16], cntrow[:, 8:16], ALU.min)
            e.fence()
            e.tensor_tensor_scan(bcrow[:, 8:16], ones88[0:1, :], bcrow[:, 0:8], 0.0, ALU.mult, ALU.add)
            e.fence()
            e.tensor_scalar(bcrow[:, 0:8], bcrow[:, 0:8], float(8 * LCAP), -1.0, ALU.subtract, ALU.mult)
            e.fence()
            e.tensor_copy(srct1[0:1, 0:8], bcrow[:, 8:16])
            e.fence()
            e.tensor_copy(bcrow[:, 8:16], bcrow[:, 0:8])
            e.fence()
            e.tensor_copy(bcrow[:, 0:8], srct1[0:1, 0:8])
            e.fence()
        p.vector(vog4)
        V_OG4 = v.n
        V_FOLD4 = V_OG4

        def tbcr(e):
            e.wait(v, V_OG4)
            e.matmul(ps_bcr[:], ones_col[:], bcrow[:], start=True, stop=True)
        p.tensor(tbcr)
        T_BCR = t.n

        def vsrc4(e):
            e.wait(t, T_BCR)
            e.wait(d_cst4)
            e.tensor_copy(bc128[:], ps_bcr[:])
            e.fence()
            e.tensor_scalar_mul(baset4[:], baset[:, 0:1], 0.25)
            e.fence()
            e.tensor_scalar(rloc4[:], gidt[:], baset[:, 0:1], 0.0, ALU.subtract, ALU.add)
            e.fence()
            e.tensor_scalar(srcidx[:], rloc4[:], baset4[:, 0:1], 0.0, ALU.add, ALU.add)
            e.fence()
            for cc in range(7):
                e.tensor_scalar(srct1[:], rloc4[:], bc128[:, cc:cc + 1], 0.0, ALU.is_ge, ALU.add)
                e.fence()
                e.tensor_scalar(srct1[:], srct1[:], bc128[:, 8 + cc:9 + cc], 0.0, ALU.mult, ALU.add)
                e.fence()
                e.tensor_tensor(srcidx[:], srcidx[:], srct1[:], ALU.add)
                e.fence()
            e.tensor_scalar(srcidx[:], srcidx[:], float(PAYR4 - 1), 0.0, ALU.min, ALU.add)
            e.fence()
            e.tensor_copy(srci[:], srcidx[:])
            e.fence()
        p.vector(vsrc4)
        V_SRC4 = v.n

        def gmp4(e):
            e.wait(v, V_SRC4)
            for s in range(32):
                e.indirect_dma_start(
                    out=mypay4[:, s, :], out_offset=None,
                    in_=pay_ar4, in_offset=IndirectOffsetOnAxis(srci[:, s:s + 1], 0),
                    sem=d_gg)
        p.gpsimd(gmp4)
        GD_MP4 = d_gg.n

        def stmp4(e):
            e.wait(d_gg, GD_MP4)
            e.dma_start(out=mypay4_hbm.rearrange("(q s) f -> q s f", q=128), in_=mypay4[:], sem=d_st)
        p.scalar(stmp4)

        def vidx4(e):
            e.wait(d_gg, GD_MP4)
            e.tensor_copy(idxg4[:], mypay4[:, :, 0])
            e.fence()
        p.vector(vidx4)
        V_IDXG4 = v.n

        def ggx4(e):
            e.wait(v, V_IDXG4)
            for s in range(32):
                e.indirect_dma_start(
                    out=X_tm4[:, s, :], out_offset=None,
                    in_=xall4, in_offset=IndirectOffsetOnAxis(idxg4[:, s:s + 1], 0),
                    sem=d_gg)
        p.gpsimd(ggx4)
        GD_X4 = d_gg.n

        def stxe4(e):
            e.wait(d_gg, GD_X4)
            e.dma_start(out=xe4_hbm.rearrange("(q s) d -> q s d", q=128), in_=X_tm4[:], sem=d_st)
        p.scalar(stxe4)
        p.scope = old
    ST_XE4 = d_st.n

    with ExitStack() as esc4:
        old = p.scope; p.scope = esc4
        w2_sb = p.sbuf("p4_w2sb", [128, 32, D], BF16)
        w1sl = [p.sbuf(f"p4_w1sl{i}", [128, 8, 1024], BF16) for i in range(2)]
        xfc4 = [p.sbuf(f"p4_xfc{i}", [128, 8, 512], BF16) for i in range(2)]
        h_bf4 = p.sbuf("p4_hbf", [128, 32, 512], BF16)
        Y_cg4 = [p.sbuf(f"p4_ycg{i}", [128, 4, 1040], BF16) for i in range(2)]
        ps_h4 = [p.psum(f"p4_psh{i}", [128, 512]) for i in range(2)]
        ps_y4 = [p.psum(f"p4_psy{i}", [128, 512]) for i in range(2)]
        d_w1 = [p.ctr("p4_dw10"), p.ctr("p4_dw11")]

        def ldw24(e):
            e.wait(d_st, ST_XE4)
            e.dma_start(out=w2_sb[:], in_=w2_h.rearrange("(j q) f -> q j f", q=128), sem=d_w[1])
        p.sync(ldw24)
        LD_W24 = d_w[1].n

        def ldwg4(e):
            e.wait(d_st, ST_XE4)
            e.dma_start(out=wg4_tm2[:], in_=mypay4_hbm[:, 0:2].rearrange("(s q) f -> q s f", q=128), sem=d_ld)
        p.sync(ldwg4)
        LD_WG4 = d_ld.n

        for cg in range(8):
            bb = cg % 2

            def ldxf4(e, cg=cg, bb=bb):
                e.wait(d_st, ST_XE4)
                if cg >= 2:
                    e.wait(t)
                for j in range(8):
                    e.dma_start_transpose(xfc4[bb][:, j, :], xe4_hbm[cg * 512:(cg + 1) * 512, j * 128:(j + 1) * 128], sem=d_tr[bb])
            p.sync(ldxf4)
            LD_XF4 = d_tr[bb].n

            for hh in range(4):
                wb = hh % 2

                def ldw1(e, cg=cg, hh=hh, wb=wb):
                    e.wait(d_st, ST_XE4)
                    if cg > 0 or hh > 1:
                        e.wait(t)
                    e.dma_start(out=w1sl[wb][:], in_=w1_h[:, hh * 1024:(hh + 1) * 1024].rearrange("(j q) f -> q j f", q=128), sem=d_w1[wb])
                p.sync(ldw1)
                LW1 = d_w1[wb].n

                for hf in range(8):
                    hfg = hh * 8 + hf

                    def mmh4(e, cg=cg, hh=hh, hf=hf, bb=bb, wb=wb, LD_XF4=LD_XF4, LW1=LW1):
                        e.wait(d_tr[bb], LD_XF4)
                        e.wait(d_w1[wb], LW1)
                        if cg > 0 or hh > 0 or hf > 0:
                            e.wait(v)
                        for j in range(8):
                            e.matmul(ps_h4[hf % 2][:], w1sl[wb][:, j, hf * 128:(hf + 1) * 128], xfc4[bb][:, j, :], start=(j == 0), stop=(j == 7))
                    p.tensor(mmh4)

                    def evh4(e, hfg=hfg, hf=hf):
                        e.wait(t)
                        e.tensor_scalar(h_bf4[:, hfg, :], ps_h4[hf % 2][:], 0.0, 0.0, ALU.max, ALU.add)
                        e.fence()
                    p.vector(evh4)
            A_H4 = p.ctr("v").n

            for tt_i in range(4):
                for ng in range(2):
                    gi = tt_i * 2 + ng

                    def mmy4(e, cg=cg, tt_i=tt_i, ng=ng, gi=gi, A_H4=A_H4, LD_W24=LD_W24):
                        e.wait(v, A_H4)
                        e.wait(d_w[1], LD_W24)
                        if cg * 8 + gi >= 2:
                            e.wait(v, YEV[1000 + cg * 8 + gi - 2])
                        for jf in range(32):
                            e.matmul(ps_y4[gi % 2][:], h_bf4[:, jf, tt_i * 128:(tt_i + 1) * 128], w2_sb[:, jf, ng * 512:(ng + 1) * 512], start=(jf == 0), stop=(jf == 31))
                    p.tensor(mmy4)

                    def evy4(e, cg=cg, tt_i=tt_i, ng=ng, gi=gi, bb=bb, LD_WG4=LD_WG4):
                        e.wait(t)
                        e.wait(d_ld, LD_WG4)
                        if cg >= 2 and gi == 0:
                            e.wait(d_y[bb])
                        e.tensor_scalar_mul(Y_cg4[bb][:, tt_i, ng * 512:(ng + 1) * 512], ps_y4[gi % 2][:], wg4_tm2[:, cg * 4 + tt_i, 1:2])
                        e.fence()
                        YEV[1000 + cg * 8 + gi] = p.ctr("v").n
                    p.vector(evy4)

            def vecho4(e, cg=cg, bb=bb):
                ycf = Y_cg4[bb].bitcast(F32)
                for tt_i in range(4):
                    e.tensor_copy(ycf[:, tt_i, 512:513], wg4_tm2[:, cg * 4 + tt_i, 0:1])
                e.memset(ycf[:, :, 513:520], 0.0)
                e.fence()
            p.vector(vecho4)
            V_ECHO4 = v.n

            def sty4(e, cg=cg, bb=bb, V_ECHO4=V_ECHO4):
                e.wait(v, V_ECHO4)
                e.dma_start(out=y_in4.rearrange("(s q) f -> q s f", q=128)[:, cg * 4:(cg + 1) * 4, :], in_=Y_cg4[bb][:], sem=d_y[bb])
            p.scalar(sty4)
        p.scope = old

    def ag_y4(e):
        e.wait(d_y[0])
        e.wait(d_y[1])
        e.collective_compute("AllGather", ALU.bypass, replica_groups=[list(range(NCORES))],
                             ins=[y_in4], outs=[y_all4[0:8 * TLOC, :]])
    p.gpsimd(ag_y4)
    CC_Y4 = g.n

    with ExitStack() as fsc4:
        old = p.scope; p.scope = fsc4
        slab4 = [p.sbuf(f"p4_slab{i}", [128, 8, 1040], BF16) for i in range(2)]
        yf4 = [p.sbuf(f"p4_yf{i}", [128, 8, D], F32) for i in range(2)]
        d_sl4 = [p.ctr("p4_dsl0"), p.ctr("p4_dsl1")]
        idxs4b = [p.sbuf(f"p4_idxs{i}", [128, 8], I32) for i in range(2)]
        GSC4, V_SL4 = {}, {}
        dsc8 = p.sbuf("p4_dsc8", [128, 8], F32)
        ps_bc4 = p.psum("p4_psbc", [128, 16])

        def tbc4(e):
            e.wait(v, V_FOLD4)
            e.matmul(ps_bc4[:], ones_col[:], oge4_row[:], start=True, stop=True)
        p.tensor(tbc4)
        T_BC4 = t.n

        def vbc4(e):
            e.wait(t, T_BC4)
            e.tensor_copy(ogbc4[:], ps_bc4[:])
            e.fence()
            for ee in range(8):
                e.tensor_scalar(idx644[:, ee * 8:(ee + 1) * 8], ccol64[:, ee * 8:(ee + 1) * 8],
                                ogbc4[:, ee:ee + 1], 0.0, ALU.add, ALU.add)
            e.fence()
            e.tensor_copy(idx644_i[:], idx644[:])
            e.fence()
        p.vector(vbc4)
        V_BC4 = v.n

        for ee in range(8):
            cb = ee % 2

            def gslab4(e, ee=ee, cb=cb):
                e.wait(g, CC_Y4)
                e.wait(v, V_BC4)
                if ee >= 2:
                    e.wait(d_gg, GSC4[ee - 2])
                for k in range(8):
                    e.indirect_dma_start(
                        out=slab4[cb][:, k, :], out_offset=None,
                        in_=y_all4, in_offset=IndirectOffsetOnAxis(idx644_i[:, ee * 8 + k:ee * 8 + k + 1], 0),
                        sem=d_sl4[cb])
            p.gpsimd(gslab4)
            GSLAB4 = d_sl4[cb].n

            def vslab4(e, ee=ee, cb=cb, GSLAB4=GSLAB4):
                e.wait(d_sl4[cb], GSLAB4)
                e.tensor_copy(yf4[cb][:], slab4[cb][:, :, 0:D])
                e.fence()
                e.tensor_copy(gidr4[:], slab4[cb].bitcast(F32)[:, :, 512:513].rearrange("q s o -> q (s o)"))
                e.fence()
                e.tensor_scalar(tlr4[:], gidr4[:], baset[:, 0:1], 0.0, ALU.subtract, ALU.add)
                e.fence()
                e.tensor_scalar(v1m4[:], tlr4[:], -0.5, 0.0, ALU.is_gt, ALU.add)
                e.fence()
                e.tensor_scalar(v2m4[:], tlr4[:], float(TLOC) - 0.5, 0.0, ALU.is_lt, ALU.add)
                e.fence()
                e.tensor_tensor(v1m4[:], v1m4[:], v2m4[:], ALU.mult)
                e.fence()
                e.tensor_scalar(v2m4[:], riota[:], ogbc4[:, 8 + ee:9 + ee], 0.0, ALU.is_lt, ALU.add)
                e.fence()
                e.tensor_tensor(v1m4[:], v1m4[:], v2m4[:], ALU.mult)
                e.fence()
                e.tensor_tensor(gidr4[:], tlr4[:], v1m4[:], ALU.mult)
                e.fence()
                e.tensor_scalar(v2m4[:], v1m4[:], 1.0, -1.0, ALU.subtract, ALU.mult)
                e.fence()
                e.tensor_scalar(dsc8[:], dumpt[:], float(8 * ee), 0.0, ALU.add, ALU.add)
                e.fence()
                e.tensor_tensor(v2m4[:], v2m4[:], dsc8[:], ALU.mult)
                e.fence()
                e.tensor_tensor(gidr4[:], gidr4[:], v2m4[:], ALU.add)
                e.fence()
                e.tensor_copy(idxs4b[cb][:], gidr4[:])
                e.fence()
            p.vector(vslab4)
            VSLAB4 = v.n

            def gsc4(e, ee=ee, cb=cb, VSLAB4=VSLAB4):
                e.wait(v, VSLAB4)
                e.wait(d_z, ST_Z4)
                for k in range(8):
                    e.indirect_dma_start(
                        out=acc4_hbm, out_offset=IndirectOffsetOnAxis(idxs4b[cb][:, k:k + 1], 0),
                        in_=yf4[cb][:, k, :], in_offset=None, compute_op=ALU.add,
                        sem=d_gg)
                GSC4[ee] = d_gg.n
                e.wait(d_gg)
            p.gpsimd(gsc4)
        p.scope = old
    GD_COMB4 = d_gg.n

    with ExitStack() as xsc4:
        old = p.scope; p.scope = xsc4
        xtile4 = [p.sbuf(f"p4_xt{i}", [128, D], F32) for i in range(2)]
        atile4 = [p.sbuf(f"p4_at{i}", [128, D], F32) for i in range(2)]
        d_x4 = [p.ctr("p4_dx40"), p.ctr("p4_dx41")]
        d_lx4 = [p.ctr("p4_dlx0"), p.ctr("p4_dlx1")]
        for i in range(32):
            bb = i % 2

            def ldx4(e, i=i, bb=bb):
                e.wait(d_gg, GD_COMB4)
                if i >= 2:
                    e.wait(d_x4[bb])
                e.dma_start(out=xtile4[bb][:], in_=x3_hbm[i * 128:(i + 1) * 128, :], sem=d_lx4[bb])
                e.dma_start(out=atile4[bb][:], in_=acc4_hbm[i * 128:(i + 1) * 128, :], sem=d_lx4[bb])
            p.sync(ldx4)
            LX4 = d_lx4[bb].n

            def vx4(e, i=i, bb=bb, LX4=LX4):
                e.wait(d_lx4[bb], LX4)
                e.tensor_tensor(xtile4[bb][:], xtile4[bb][:], atile4[bb][:], ALU.add)
                e.fence()
            p.vector(vx4)
            VX4 = v.n

            def stx4(e, i=i, bb=bb, VX4=VX4):
                e.wait(v, VX4)
                e.dma_start(out=x4_hbm[i * 128:(i + 1) * 128, :], in_=xtile4[bb][:], sem=d_x4[bb])
            p.scalar(stx4)
        p.scope = old

    p.scope = _oldscope
    _p4s.close()


def build_all(upto=4):
    _reset_markers()
    nc = bass.Bass("TRN2", target_bir_lowering=False, debug=False)
    ctx = ExitStack()
    blk = ctx.enter_context(nc.Block())
    p = P(nc, ctx, blk)

    io = {}
    V = {}
    io["x_in"] = nc.dram_tensor("x_in", [TLOC, D], F32, kind="ExternalInput").ap()
    for nm, shp in [
        ("l0_norm_w", [1, D]), ("l0_in_w", [D, 2 * D]), ("l0_in_b", [1, 2 * D]),
        ("l0_conv_w", [4, D]), ("l0_conv_b", [1, D]), ("l0_param_w", [D, 3 * D]),
        ("l0_param_b", [1, 3 * D]), ("l0_skip", [1, D]), ("l0_out_w", [D, D]),
        ("l0_out_b", [1, D]), ("l0_inb_fm", [128, 16]), ("l0_convw_fm", [128, 8, 4]),
        ("l0_convb_fm", [128, 8]), ("l0_parb_fm", [128, 24]), ("l0_skip_fm", [128, 8]),
        ("l0_normw_fm", [128, 8]),
    ]:
        V[nm] = nc.dram_tensor(nm, shp, F32, kind="ExternalInput").ap()
    io["x_out"] = nc.dram_tensor("x_out", [TLOC, D], F32, kind="ExternalOutput").ap()
    io["x1_hbm"] = nc.dram_tensor("x1_hbm", [TLOC, D], F32, kind="Internal").ap()
    io["xr_hbm"] = nc.dram_tensor("xr_hbm", [TLOC, D], BF16, kind="Internal").ap()
    io["stats_hbm"] = nc.dram_tensor("stats_hbm", [TLOC, 2], F32, kind="Internal").ap()
    io["w_inw_bf"] = nc.dram_tensor("w_inw_bf", [D, 2 * D], BF16, kind="Internal").ap()
    io["w_parw_bf"] = nc.dram_tensor("w_parw_bf", [D, 3 * D], BF16, kind="Internal").ap()

    if upto >= 2:
        for nm, shp in [
            ("wgt_fm", [128, 8, 8]), ("gid_c", [128, 32]), ("riota_c", [128, 8]),
            ("base_c", [128, 1]), ("mask8_c", [8, 8]), ("tl8_c", [1, 8]),
            ("epad8_c", [8, 1]), ("smod_c", [8, TLOC]), ("ccol64_c", [128, 64]),
            ("p2_w1t", [D, 2048]), ("p2_w3t", [D, 2048]), ("p2_w2t", [2048, D]),
        ]:
            V[nm] = nc.dram_tensor(nm, shp, F32, kind="ExternalInput").ap()
        V["me_i"] = nc.dram_tensor("me_i", [1, 2], I32, kind="ExternalInput").ap()
        io["x2_hbm"] = nc.dram_tensor("x2_hbm", [TLOC, D], F32, kind="Internal").ap()
        io["xn2_hbm"] = nc.dram_tensor("xn2_hbm", [TLOC, D], BF16, kind="Internal").ap()
        io["p2_xall"] = nc.dram_tensor("p2_xall", [8 * TLOC, D], BF16, kind="Internal").ap()
        io["p2_scin"] = nc.dram_tensor("p2_scin", [16, SEQ], F32, kind="Internal").ap()
        io["p2_scall"] = nc.dram_tensor("p2_scall", [8, 16, SEQ], F32, kind="Internal").ap()
        io["p2_payin"] = nc.dram_tensor("p2_payin", [PAYR, 16], F32, kind="Internal").ap()
        io["p2_payar"] = nc.dram_tensor("p2_payar", [PAYR, 16], F32, kind="Internal").ap()
        io["p2_mypay"] = nc.dram_tensor("p2_mypay", [TLOC, 16], F32, kind="Internal").ap()
        io["p2_xe"] = nc.dram_tensor("p2_xe", [TLOC, D], BF16, kind="Internal").ap()
        io["p2_yin"] = nc.dram_tensor("p2_yin", [TLOC, 1040], BF16, kind="Internal").ap()
        io["p2_yall"] = nc.dram_tensor("p2_yall", [8 * TLOC + CPAY, 1040], BF16, kind="Internal").ap()
        io["p2_acc"] = nc.dram_tensor("p2_acc", [TLOC + 128, D], F32, kind="Internal").ap()
        io["p2_dw"] = nc.dram_tensor("p2_dw", [2, 8, TLOC], F32, kind="Internal").ap()
        io["w_w1t_bf"] = nc.dram_tensor("w_w1t_bf", [D, 2048], BF16, kind="Internal").ap()
        io["w_w3t_bf"] = nc.dram_tensor("w_w3t_bf", [D, 2048], BF16, kind="Internal").ap()
        io["w_w2t_bf"] = nc.dram_tensor("w_w2t_bf", [2048, D], BF16, kind="Internal").ap()

    if upto >= 3 and upto != 15:
        for nm, shp in [
            ("p3_wqkv", [D, 1536]), ("p3_wo", [D, D]),
            ("rope_cos_c", [SEQ, 32]), ("rope_sin_c", [SEQ, 32]),
            ("masks_c", [128, 3, 128]), ("amask_c", [128, 8, 512]),
        ]:
            V[nm] = nc.dram_tensor(nm, shp, F32, kind="ExternalInput").ap()
        io["x3_hbm"] = nc.dram_tensor("x3_hbm", [TLOC, D], F32, kind="Internal").ap()
        io["xn3_hbm"] = nc.dram_tensor("xn3_hbm", [TLOC, D], BF16, kind="Internal").ap()
        io["p3_qk"] = nc.dram_tensor("p3_qk", [TLOC, 1280], BF16, kind="Internal").ap()
        io["p3_y"] = nc.dram_tensor("p3_y", [TLOC, D], BF16, kind="Internal").ap()
        io["w_qkv_bf"] = nc.dram_tensor("w_qkv_bf", [D, 1536], BF16, kind="Internal").ap()
        io["w_wo_bf"] = nc.dram_tensor("w_wo_bf", [D, D], BF16, kind="Internal").ap()

    if upto >= 4 and upto != 15:
        for nm, shp in [
            ("p4_wr_fm", [128, 8, 8]), ("erow8_c", [128, 8]), ("cmlt_c", [8, 8]),
            ("idn_c", [128, 128]), ("epadL_c", [8, 1]), ("mebase4_c", [8, 1]),
            ("cmltc_c", [8, 1]),
            ("p4_w1", [D, 4096]), ("p4_w2", [4096, D]),
        ]:
            V[nm] = nc.dram_tensor(nm, shp, F32, kind="ExternalInput").ap()
        io["x4_hbm"] = nc.dram_tensor("x4_hbm", [TLOC, D], F32, kind="Internal").ap()
        io["xn4_hbm"] = nc.dram_tensor("xn4_hbm", [TLOC, D], BF16, kind="Internal").ap()
        io["p4_xall"] = nc.dram_tensor("p4_xall", [8 * TLOC, D], BF16, kind="Internal").ap()
        io["p4_ptok"] = nc.dram_tensor("p4_ptok", [TLOC, 1], F32, kind="Internal").ap()
        io["p4_cntin"] = nc.dram_tensor("p4_cntin", [1, 8], F32, kind="Internal").ap()
        io["p4_cntall"] = nc.dram_tensor("p4_cntall", [8, 8], F32, kind="Internal").ap()
        io["p4_payin"] = nc.dram_tensor("p4_payin", [PAYR4, 16], F32, kind="Internal").ap()
        io["p4_payar"] = nc.dram_tensor("p4_payar", [PAYR4, 16], F32, kind="Internal").ap()
        io["p4_mypay"] = nc.dram_tensor("p4_mypay", [TLOC, 16], F32, kind="Internal").ap()
        io["p4_xe"] = nc.dram_tensor("p4_xe", [TLOC, D], BF16, kind="Internal").ap()
        io["p4_yin"] = nc.dram_tensor("p4_yin", [TLOC, 1040], BF16, kind="Internal").ap()
        io["p4_yall"] = nc.dram_tensor("p4_yall", [8 * TLOC + CPAY, 1040], BF16, kind="Internal").ap()
        io["p4_acc"] = nc.dram_tensor("p4_acc", [TLOC + 128, D], F32, kind="Internal").ap()
        io["p4_dw"] = nc.dram_tensor("p4_dw", [1, 8, TLOC], F32, kind="Internal").ap()
        io["w_l1w1_bf"] = nc.dram_tensor("w_l1w1_bf", [D, 4096], BF16, kind="Internal").ap()
        io["w_l1w2_bf"] = nc.dram_tensor("w_l1w2_bf", [4096, D], BF16, kind="Internal").ap()

    d_g = p.ctr("d_g")

    def stage(e):
        e.dma_start(out=io["w_inw_bf"], in_=V["l0_in_w"], sem=d_g)
        e.dma_start(out=io["w_parw_bf"], in_=V["l0_param_w"], sem=d_g)
        if upto >= 2:
            e.dma_start(out=io["w_w1t_bf"], in_=V["p2_w1t"], sem=d_g)
            e.dma_start(out=io["w_w3t_bf"], in_=V["p2_w3t"], sem=d_g)
            e.dma_start(out=io["w_w2t_bf"], in_=V["p2_w2t"], sem=d_g)
        if upto >= 3 and upto != 15:
            e.dma_start(out=io["w_qkv_bf"], in_=V["p3_wqkv"], sem=d_g)
            e.dma_start(out=io["w_wo_bf"], in_=V["p3_wo"], sem=d_g)
        if upto >= 4 and upto != 15:
            e.dma_start(out=io["w_l1w1_bf"], in_=V["p4_w1"], sem=d_g)
            e.dma_start(out=io["w_l1w2_bf"], in_=V["p4_w2"], sem=d_g)
        e.wait(d_g)
    p.gpsimd(stage)

    emit_l0_attn(p, io, V)
    if upto >= 2:
        if upto == 15:
            io["dbg"] = nc.dram_tensor("dbg", [8, 64], F32, kind="ExternalOutput").ap()
            emit_l0_ffn(p, io, V, dbg=True)
        else:
            emit_l0_ffn(p, io, V)
    if upto >= 3 and upto != 15:
        emit_l1_attn(p, io, V)
    if upto >= 4 and upto != 15:
        emit_l1_ffn(p, io, V)

    d_fin = p.ctr("d_fin")
    if upto >= 4 and upto != 15:
        src = io["x4_hbm"]
    elif upto >= 3 and upto != 15:
        src = io["x3_hbm"]
    elif upto >= 2 and upto != 15:
        src = io["x2_hbm"]
    else:
        src = io["x1_hbm"]

    def fin(e):
        for ctr in list(p.ctrs.values()):
            if ctr.n:
                e.wait(ctr)
        e.dma_start(out=io["x_out"], in_=src, sem=d_fin)
        e.wait(d_fin)
    p.sync(fin)

    ctx.close()
    return nc


def _shard_inputs(inputs, upto=4):
    x = np.ascontiguousarray(inputs["L0_attn_x"].reshape(NCORES, TLOC, D))
    base = {
        "l0_norm_w": np.ascontiguousarray(inputs["L0_attn_norm_w"].reshape(1, D)),
        "l0_in_w": np.ascontiguousarray(inputs["L0_attn_in_w"]),
        "l0_in_b": np.ascontiguousarray(inputs["L0_attn_in_b"].reshape(1, 2 * D)),
        "l0_conv_w": np.ascontiguousarray(inputs["L0_attn_conv_w"]),
        "l0_conv_b": np.ascontiguousarray(inputs["L0_attn_conv_b"].reshape(1, D)),
        "l0_param_w": np.ascontiguousarray(inputs["L0_attn_param_w"]),
        "l0_param_b": np.ascontiguousarray(inputs["L0_attn_param_b"].reshape(1, 3 * D)),
        "l0_skip": np.ascontiguousarray(inputs["L0_attn_skip"].reshape(1, D)),
        "l0_out_w": np.ascontiguousarray(inputs["L0_attn_out_w"]),
        "l0_out_b": np.ascontiguousarray(inputs["L0_attn_out_b"].reshape(1, D)),
    }
    def fm_vec(vrow, nj):
        return np.ascontiguousarray(vrow.reshape(nj, 128).T)
    base["l0_inb_fm"] = fm_vec(inputs["L0_attn_in_b"], 16)
    base["l0_convb_fm"] = fm_vec(inputs["L0_attn_conv_b"], 8)
    base["l0_parb_fm"] = fm_vec(inputs["L0_attn_param_b"], 24)
    base["l0_skip_fm"] = fm_vec(inputs["L0_attn_skip"], 8)
    base["l0_normw_fm"] = fm_vec(inputs["L0_attn_norm_w"], 8)
    base["l0_convw_fm"] = np.ascontiguousarray(
        inputs["L0_attn_conv_w"].reshape(4, 8, 128).transpose(2, 1, 0))
    if upto >= 2:
        Wg = inputs["L0_ffn_Wg"]
        base["wgt_fm"] = np.ascontiguousarray(Wg.T.reshape(8, 128, 8).transpose(1, 0, 2))
        base["tl8_c"] = (np.arange(8, dtype=np.float32) * TLOC).reshape(1, 8)
        base["epad8_c"] = (np.arange(8, dtype=np.float32) * TLOC).reshape(8, 1)
        base["smod_c"] = np.broadcast_to(
            np.tile(np.arange(32, dtype=np.float32), TLOC // 32), (8, TLOC)).copy()
        base["ccol64_c"] = np.ascontiguousarray(
            (np.arange(128)[:, None] + 128 * np.tile(np.arange(8), 8)[None, :]).astype(np.float32))
        base["riota_c"] = np.ascontiguousarray(
            (np.arange(128)[:, None] + 128 * np.arange(8)[None, :]).astype(np.float32))
    if upto >= 3:
        base["p3_wqkv"] = np.ascontiguousarray(np.concatenate(
            [inputs["L1_attn_Wq"], inputs["L1_attn_Wk"], inputs["L1_attn_Wv"]], axis=1))
        base["p3_wo"] = np.ascontiguousarray(inputs["L1_attn_Wo"])
        inv_freq = (1.0 / 10000.0 ** (np.arange(0, 64, 2, dtype=np.float32) / 64.0)).astype(np.float32)
        freqs = np.arange(SEQ, dtype=np.float32)[:, None] * inv_freq[None, :]
        base["rope_cos_c"] = np.cos(freqs).astype(np.float32)
        base["rope_sin_c"] = np.sin(freqs).astype(np.float32)
        kk = np.arange(128)[:, None]; qq = np.arange(128)[None, :]
        mT = (kk <= qq).astype(np.float32)
        mB2 = (kk >= qq + 1).astype(np.float32)
        mBG = np.maximum(mB2, (kk < 64).astype(np.float32))
        base["masks_c"] = np.ascontiguousarray(np.stack([mT, mB2, mBG], axis=1))
        kf = np.arange(128)[:, None]; qf = np.arange(512)[None, :]
        am = np.zeros((128, 8, 512), np.float32)
        for di, dlt in enumerate([-2, -1, 0, 1, 2, 3]):
            K = dlt * 128 + kf
            am[:, di, :] = ((K <= qf) & (qf - K <= 255)).astype(np.float32)
        am[:, 6, :] = ((kf <= qf) & ((qf - kf <= 255) | (kf < 64))).astype(np.float32)
        base["amask_c"] = np.ascontiguousarray(am)
    if upto >= 4:
        Wr = inputs["L1_ffn_Wr"]
        base["p4_wr_fm"] = np.ascontiguousarray(Wr.reshape(8, 128, 8).transpose(1, 0, 2))
        base["erow8_c"] = np.broadcast_to(np.arange(8, dtype=np.float32), (128, 8)).copy()
        base["idn_c"] = np.eye(128, dtype=np.float32)
        base["epadL_c"] = (np.arange(8, dtype=np.float32) * 1024.0).reshape(8, 1)
    maps = []
    for c in range(NCORES):
        m = dict(base, x_in=x[c])
        if upto >= 2:
            m["gid_c"] = np.ascontiguousarray(
                (c * TLOC + np.arange(128)[:, None] * 32 + np.arange(32)[None, :]).astype(np.float32))
            m["base_c"] = np.full((128, 1), float(c * TLOC), np.float32)
            mk = np.zeros((8, 8), np.float32); mk[:, c] = 1.0
            m["mask8_c"] = mk
            m["me_i"] = np.array([[c, 64 * 1024 + 32 + c]], np.int32)
            m["p2_w1t"] = np.ascontiguousarray(inputs["L0_ffn_W1"][c].T)
            m["p2_w3t"] = np.ascontiguousarray(inputs["L0_ffn_W3"][c].T)
            m["p2_w2t"] = np.ascontiguousarray(inputs["L0_ffn_W2"][c].T)
        if upto >= 4:
            m["p4_w1"] = np.ascontiguousarray(inputs["L1_ffn_W1"][c])
            m["p4_w2"] = np.ascontiguousarray(inputs["L1_ffn_W2"][c])
            cm = np.zeros((8, 8), np.float32); cm[:, :c] = 1.0
            m["cmlt_c"] = cm
            m["mebase4_c"] = np.full((8, 1), float(c * 8192), np.float32)
            m["cmltc_c"] = (np.arange(8) < c).astype(np.float32).reshape(8, 1)
        maps.append(m)
    return maps


def kernel(**inputs):
    nc = build_all()
    in_maps = _shard_inputs(inputs)
    res = run_bass_kernel_spmd(nc, in_maps, core_ids=list(range(NCORES)))
    out = np.concatenate([r["x_out"] for r in res.results], axis=0)
    return np.ascontiguousarray(out.reshape(16, SEQ, D))
